# version 03
# baseline (speedup 1.0000x reference)
import math
import numpy as np
import concourse.bass as bass
import concourse.mybir as mybir
from concourse.bass_utils import run_bass_kernel_spmd

F32 = mybir.dt.float32
BF16 = mybir.dt.bfloat16
AF = mybir.ActivationFunctionType
AL = mybir.AluOpType

NCORE = 8
T = 2048
D = 1024
NT = 8
SW_F = 256
NSTRIP_F = T // SW_F
NSPAN = 16
EXPSH = -40.0


def _bucket_of(d):
    n = -d
    ret = 16 if n < 0 else 0
    na = abs(n)
    if na < 8:
        return ret + na
    v = 8 + int(math.floor(math.log(na / 8.0) / math.log(16.0) * 8.0))
    return ret + min(v, 15)


def _host_consts():
    c = {}
    mf = np.zeros((96, 16384), np.float32)
    for di, delta in enumerate((-1, 0, 1)):
        for dd in range(-90, 91):
            b = _bucket_of(dd)
            for jl in range(128):
                il = dd + 128 * delta + jl
                if 0 <= il < 128:
                    col = 128 * jl + il
                    mf[32 * di + b, col] += 1.0
                    if dd >= 1:
                        mf[32 * di + 31, col] -= 1.0
                    else:
                        mf[32 * di + 15, col] -= 1.0
    c["mflat"] = mf
    eye = np.zeros((128, 256), np.float32)
    eye[:, 0:128] = np.eye(128, dtype=np.float32)
    eye[:, 128:256] = (np.arange(128)[:, None] < np.arange(128)[None, :]).astype(np.float32)
    c["eyetri"] = eye
    l64 = np.zeros((128, 65), np.float32)
    l64[0:64, 0:64] = (np.arange(64)[:, None] < np.arange(64)[None, :]).astype(np.float32)
    l64[:, 64:65] = 1.0
    c["l64"] = l64
    g = np.zeros((16, 1024), np.float32)
    for ht in range(8):
        for pp in range(128):
            g[2 * ht + pp // 64, 128 * ht + pp] = 1.0
    c["gmask"] = g
    return c


_CONSTS = None


def _get_consts():
    global _CONSTS
    if _CONSTS is None:
        _CONSTS = _host_consts()
    return _CONSTS



ENGS = ("t", "v", "s", "y", "g")


class P:
    def __init__(self):
        self.ops = {e: [] for e in ENGS}
        self.ct = {}
        self.semnames = []

    def sem(self, name):
        if name not in self.ct:
            self.ct[name] = 0
            self.semnames.append(name)

    def add(self, eng, fn, sem=None, inc=0):
        if sem is not None:
            self.sem(sem)
            self.ct[sem] += inc

            def wrapped(e, S, fn=fn, sem=sem, inc=inc):
                fn(e, S).then_inc(S[sem], inc)
            self.ops[eng].append(wrapped)
            return self.ct[sem]
        self.ops[eng].append(fn)
        return None

    def wait(self, eng, sem, val=None):
        self.sem(sem)
        v = self.ct[sem] if val is None else val
        assert v <= self.ct[sem], f"{sem}: wait {v} > emitted {self.ct[sem]}"
        if v > 0:
            self.ops[eng].append(lambda e, S, sem=sem, v=v: e.wait_ge(S[sem], v))
        return v



def build_nc(debug=False, parts=('tp', 'ln', 'kv', 'gather', 'flash', 'attn', 'ffn2'), attn_nn=4, attn_stage=10):
    nc = bass.Bass("TRN2", target_bir_lowering=False)
    p = P()

    din = {}

    def DI(name, shape):
        din[name] = nc.dram_tensor(name, list(shape), F32, kind="ExternalInput").ap()

    DI("xt", (D, T))
    for w in ("wqT", "wkT", "wvT", "woT", "w1aT", "w2aT"):
        DI(w, (D, D))
    for b in ("bq4", "bk4", "bv4", "bo4", "b1a4", "b2a4"):
        DI(b, (128, 8))
    for pfx in ("q", "k", "v"):
        DI(f"{pfx}nwT", (D, 15))
        DI(f"{pfx}nb", (1, 15))
        DI(f"{pfx}w1c", (D, D))
        DI(f"{pfx}b1f4", (128, 8))
        DI(f"{pfx}w2c", (D, D))
        DI(f"{pfx}b2", (16, D))
    DI("relbT", (16, 32))
    DI("wogT", (D, D))
    for b in ("bq4s", "bk4s", "bv4s"):
        DI(b, (128, 8))
    DI("fconst", (128, 8))
    DI("hsel", (16, 4))
    DI("mflat", (96, 16384))
    DI("eyetri", (128, 256))
    DI("l64", (128, 65))
    DI("gmask", (16, 1024))

    yout = nc.dram_tensor("y", [D, T], F32, kind="ExternalOutput").ap()
    dbg = {}
    if debug:
        for nm, shape, dt_ in (
            ("dxa", (128, 16384), BF16), ("dfq", (128, 2048), BF16),
            ("de", (128, 16384), BF16), ("dtp", (128, 2048), BF16),
            ("doh", (128, 256), BF16), ("dht", (128, 4096), BF16),
            ("dqk", (128, 16384), BF16), ("dvmo", (128, 8256), BF16),
            ("dyt", (128, 2048), F32), ("do1", (128, 8192), BF16),
            ("do2", (128, 4096), BF16),
        ):
            dbg[nm] = nc.dram_tensor(nm, list(shape), dt_, kind="ExternalOutput").ap()

    xspill = nc.dram_tensor("xspill", [D, T], F32).ap()
    kb = nc.dram_tensor("kb", [D, T], BF16).ap()
    vb = nc.dram_tensor("vb", [T, D], BF16).ap()
    kg = nc.dram_tensor("kg", [4 * D, T], BF16).ap()
    vg = nc.dram_tensor("vg", [4 * T, D], BF16).ap()
    tpscr = nc.dram_tensor("tpscr", [12, 16384], BF16).ap()

    X = nc.alloc_sbuf_tensor("X", [128, 16384], F32)
    XA = nc.alloc_sbuf_tensor("XA", [128, 16384], BF16)
    AR1 = nc.alloc_sbuf_tensor("AR1", [128, 16384], BF16)
    AR2 = nc.alloc_sbuf_tensor("AR2", [128, 4096], BF16)
    AR3 = nc.alloc_sbuf_tensor("AR3", [128, 8256], BF16)
    WA = nc.alloc_sbuf_tensor("WA", [128, 16384], BF16)
    CON = nc.alloc_sbuf_tensor("CON", [128, 2512], BF16)
    ROWS = nc.alloc_sbuf_tensor("ROWS", [128, 1536], F32)
    SPK = nc.alloc_sbuf_tensor("SPK", [128, 2560], BF16)
    OHQ = nc.alloc_sbuf_tensor("OHQ", [128, 256], BF16)
    NWB = nc.alloc_sbuf_tensor("NWB", [128, 1792], BF16)
    MISC = nc.alloc_sbuf_tensor("MISC", [128, 480], F32)
    FCON = nc.alloc_sbuf_tensor("FCON", [128, 8], F32)
    FCONSTS = [0.0, 1.0, 1e-5, -0.5, 1.0 / 32.0, -40.0, 0.5, 2.0]
    for j, v in enumerate(FCONSTS):
        nc.const_aps.register(F32, v, FCON[:, j:j + 1])

    AR1f = AR1.ap().bitcast(F32)

    def Xf(dt_, lo, n):
        return X[:, 2048 * dt_ + lo: 2048 * dt_ + lo + n]

    def XAf(dt_, lo, n):
        return XA[:, 2048 * dt_ + lo: 2048 * dt_ + lo + n]

    CON_EYE, CON_TRI, CON_L64, CON_ONEC, CON_ONEB, CON_TP = \
        0, 128, 256, 321, 324, 464
    ONEROW = CON[0:1, CON_ONEB:CON_ONEB + 128]
    ONECOL = CON[:, CON_ONEC:CON_ONEC + 1]

    def TPR(di, n):
        o = CON_TP + 128 * (4 * di + n)
        return CON[:, o:o + 128]

    def TSN(n):
        o = CON_TP + 128 * (12 + n)
        return CON[:, o:o + 128]

    def ROW0(lo, n):
        return ROWS[0:1, lo:lo + n]


    def NWT(pi, dt_):
        return NWB[:, 120 * pi + 15 * dt_: 120 * pi + 15 * dt_ + 15]

    def NBROW(pi):
        return NWB[0:1, 360 + 15 * pi: 375 + 15 * pi]

    RELBT = NWB[0:16, 440:472]
    NW_CROW = 480
    NW_REC = 512
    NW_TPS = [768, 1280]

    def B2S(lo, n):
        return SPK[0:16, 1536 + lo:1536 + lo + n]

    def OHT(lo, n):
        return SPK[0:16, 1024 + lo:1024 + lo + n]

    HSEL = NWB[0:16, 412:416]
    RBSEL = NWB[0:32, 408:412]

    MI_B = {"bq": 0, "bk": 8, "bv": 16, "bo": 24, "b1a": 32, "b2a": 40,
            "qb1": 48, "kb1": 56, "vb1": 64, "bqs": 448, "bks": 456,
            "bvs": 464}
    MI_F = 80
    MI_REP = 144
    MI_ONE = 160
    MI_BST = 288
    MI_OCF = 352
    MI_SM, MI_SV, MI_SS, MI_SR = 356, 372, 388, 404
    MISCb = MISC.ap().bitcast(BF16)
    MB_M, MB_R = 856, 872

    def WSl(r, kt, lo, n):
        o = 8192 * r + 1024 * kt + lo
        return WA[:, o:o + n]

    pb = [nc.alloc_psum_tensor(f"pb{i}", [128, 512], F32) for i in range(4)]
    pS1 = nc.alloc_psum_tensor("pS1", [128, 1024], F32)
    pS2 = nc.alloc_psum_tensor("pS2", [128, 1024], F32)

    OHB = 14336
    OH_SC, OH_SCR, OH_KV, OH_Q = OHB, OHB + 256, OHB + 512, -1

    LNSf = [1536, 2048]
    MSKf = [2560, 3072]
    STTf = [3584, 4096]
    BST = [13312, 13824]
    MF = [9216, 11264]

    for s_ in ("XL", "SPL", "WLC", "WA0", "WA1", "WB", "GC", "GF0", "GF1",
               "GT0", "GT1", "GS", "KL0", "KL1", "SB0", "SB1", "SO",
               "MM", "EV", "SE", "CC", "XS", "XR"):
        p.sem(s_)

    def g_dma(dst, src, sem="WLC"):
        return p.add("g", lambda e, S, d=dst, s=src: e.dma_start(out=d, in_=s), sem, 16)

    def y_dma(dst, src, sem="SPL"):
        return p.add("y", lambda e, S, d=dst, s=src: e.dma_start(out=d, in_=s), sem, 16)

    def mm(out, lhsT, rhs, start, stop, tp=None, sem="MM"):
        return p.add("t", lambda e, S, o=out, l=lhsT, r=rhs, a=start, b=stop, tp=tp:
                     e.matmul(o, l, r, start=a, stop=b, tile_position=tp), sem, 1)

    def mm0(out, lhsT, rhs, start, stop, tp=None):
        p.add("t", lambda e, S, o=out, l=lhsT, r=rhs, a=start, b=stop, tp=tp:
              e.matmul(o, l, r, start=a, stop=b, tile_position=tp))

    def ttrans(out, in_, ident):
        p.add("t", lambda e, S, o=out, i=in_, d=ident: e.transpose(o, i, d))

    def tnop(sem, inc=1):
        return p.add("t", lambda e, S: e.nop(), sem, inc)

    def vop(fn, sem="EV", inc=1):
        p.wait("v", "EV", p.ct["EV"])
        return p.add("v", fn, "EV", 1)

    def vop0(fn):
        p.wait("v", "EV", p.ct["EV"])
        p.add("v", fn, "EV", 1)

    def sop(fn, sem="SE", inc=1):
        p.wait("s", "SE", p.ct["SE"])
        return p.add("s", fn, "SE", 1)

    def wload(reg, name):
        src = din[name][:, :].rearrange("(a p) f -> p a f", p=128)
        dst = WA[:, 8192 * reg:8192 * reg + 8192].rearrange("p (a f) -> p a f", a=8)
        return (f"WA{reg}", g_dma(dst, src, sem=f"WA{reg}"))

    for dt_ in range(NT):
        y_dma(Xf(dt_, 0, 2048), din["xt"][128 * dt_:128 * (dt_ + 1), :], sem="XL")
    xload = p.ct["XL"]

    g_dma(CON[:, CON_EYE:CON_EYE + 256], din["eyetri"][:, :])
    g_dma(CON[0:128, CON_L64:CON_L64 + 65], din["l64"][:, :])
    g_dma(SPK[0:16, 0:1024], din["gmask"][:, :])
    for pi, pfx in enumerate(("q", "k", "v")):
        g_dma(NWB[:, 120 * pi:120 * pi + 120].rearrange("p (a n) -> p a n", a=8),
              din[f"{pfx}nwT"][:, :].rearrange("(a p) n -> p a n", p=128))
        g_dma(NBROW(pi), din[f"{pfx}nb"][:, :])
    g_dma(HSEL, din["hsel"][:, :])
    g_dma(RELBT, din["relbT"][:, :])
    con_done = p.ct["WLC"]

    y_dma(FCON[:, :], din["fconst"][:, :], sem="SPL")
    for nm, key in (("bq4", "bq"), ("bk4", "bk"), ("bv4", "bv"), ("bo4", "bo"),
                    ("b1a4", "b1a"), ("b2a4", "b2a")):
        y_dma(MISC[:, MI_B[key]:MI_B[key] + 8], din[nm][:, :], sem="SPL")
    for pfx, key in (("q", "qb1"), ("k", "kb1"), ("v", "vb1")):
        y_dma(MISC[:, MI_B[key]:MI_B[key] + 8], din[f"{pfx}b1f4"][:, :], sem="SPL")
    for nm, key in (("bq4s", "bqs"), ("bk4s", "bks"), ("bv4s", "bvs")):
        y_dma(MISC[:, MI_B[key]:MI_B[key] + 8], din[nm][:, :], sem="SPL")
    smalls = p.ct["SPL"]

    if debug:
        vop0(lambda e, S: e.memset(AR3[:, :], 0.0))
        vop0(lambda e, S: e.memset(CON[:, CON_TP:CON_TP + 2048], 0.0))
        vop0(lambda e, S: e.memset(OHQ[:, :], 0.0))
        vop0(lambda e, S: e.memset(AR1[:, :], 0.0))
        vop0(lambda e, S: e.memset(AR2[:, :], 0.0))
        vop0(lambda e, S: e.memset(XA[:, :], 0.0))
    vop0(lambda e, S: e.memset(CON[:, CON_ONEC:CON_ONEC + 1], 1.0))
    vop0(lambda e, S: e.memset(CON[:, CON_ONEB:CON_ONEB + 128], 1.0))
    vop0(lambda e, S: e.memset(MISC[:, MI_ONE:MI_ONE + 128], 1.0))
    ones_v = vop(lambda e, S: e.memset(MISC[:, MI_OCF:MI_OCF + 1], 1.0))

    wf = {}
    wf["kw1"] = wload(0, "kw1c")
    wf["kw2"] = wload(1, "kw2c")

    def build_tprime():
        p.wait("t", "WLC", con_done)
        p.wait("t", "EV", ones_v)
        m1 = mm(pS2[0:32, 0:4], RELBT, HSEL, True, True)
        p.wait("v", "MM", m1)
        e1 = vop(lambda e, S: e.tensor_copy(RBSEL, pS2[0:32, 0:4]))
        p.wait("g", "EV", e1)
        p.add("g", lambda e, S: e.dma_start(out=NWB[0:1, NW_CROW:NW_CROW + 4],
                                            in_=RBSEL[15:16, :]), "GC", 16)
        c2 = p.add("g", lambda e, S: e.dma_start(out=NWB[0:1, NW_CROW + 4:NW_CROW + 8],
                                                 in_=RBSEL[31:32, :]), "GC", 16)
        p.wait("v", "GC", c2)
        e2 = vop(lambda e, S: e.tensor_sub(NWB[0:1, NW_CROW + 8:NW_CROW + 12],
                                           NWB[0:1, NW_CROW + 4:NW_CROW + 8],
                                           NWB[0:1, NW_CROW:NW_CROW + 4]))
        p.wait("t", "EV", e2)
        last = None
        for ni in range(4):
            mm0(pS2[0:128, 8 + ni:9 + ni], ONEROW,
                NWB[0:1, NW_CROW + ni:NW_CROW + ni + 1], True, True)
            last = mm(pS2[0:128, 12 + ni:13 + ni], ONEROW,
                      NWB[0:1, NW_CROW + 8 + ni:NW_CROW + 9 + ni], True, True)
        p.wait("v", "MM", last)
        er = vop(lambda e, S: e.tensor_copy(MISC[:, MI_REP:MI_REP + 8],
                                            pS2[0:128, 8:16]))
        for ni in range(4):
            vop(lambda e, S, ni=ni: e.tensor_scalar(
                TSN(ni), CON[:, CON_TRI:CON_TRI + 128],
                MISC[:, MI_REP + 4 + ni:MI_REP + 5 + ni], None, op0=AL.mult))
        return er

    def tprime_rtiles(gate_ev):
        ps1_free = [gate_ev, gate_ev]
        tps_free = [0, 0]
        mf_mm = [0, 0]
        for di in range(3):
            for half in range(8):
                hb = half % 2
                p.wait("g", "MM", mf_mm[hb])
                ld = g_dma(AR1[0:32, MF[hb]:MF[hb] + 2048],
                           din["mflat"][32 * di:32 * di + 32,
                                        2048 * half:2048 * half + 2048],
                           sem=f"GF{hb}")
                p.wait("t", f"GF{hb}", ld)
                for cq in range(4):
                    h2 = cq % 2
                    p.wait("t", "EV", ps1_free[h2])
                    m_ = mm(pS1[0:4, 512 * h2:512 * h2 + 512], RBSEL,
                            AR1[0:32, MF[hb] + 512 * cq:MF[hb] + 512 * cq + 512],
                            True, True)
                    p.wait("v", "MM", m_)
                    p.wait("v", f"GT{h2}", tps_free[h2])
                    ps1_free[h2] = vop(lambda e, S, h2=h2: e.tensor_copy(
                        NWB[0:4, NW_TPS[h2]:NW_TPS[h2] + 512],
                        pS1[0:4, 512 * h2:512 * h2 + 512]))
                    p.wait("g", "EV", ps1_free[h2])
                    tps_free[h2] = g_dma(
                        tpscr[4 * di:4 * di + 4,
                              2048 * half + 512 * cq:2048 * half + 512 * cq + 512],
                        NWB[0:4, NW_TPS[h2]:NW_TPS[h2] + 512], sem=f"GT{h2}")
                mf_mm[hb] = p.ct["MM"]
            p.wait("g", "GT0", p.ct["GT0"])
            p.wait("g", "GT1", p.ct["GT1"])
            for ni in range(4):
                g_dma(TPR(di, ni),
                      tpscr[4 * di + ni:4 * di + ni + 1, :]
                      .rearrange("o (pp f) -> (o pp) f", pp=128), sem="GS")
        return p.ct["EV"], p.ct["GS"]

    if 'tp' in parts:
        tp_er = build_tprime()
    else:
        tp_er = p.ct["EV"]
    tp_state = {"ev": p.ct["EV"], "gs": p.ct["GS"]}

    def emit_ln(eps, write_xa, gate_ev, src=None):
        src = src or Xf
        p.wait("s", "XL", xload)
        p.wait("s", "SPL", smalls)
        p.wait("s", "EV", p.ct["EV"])
        p.wait("s", "MM", p.ct["MM"])
        p.wait("t", "SPL", smalls)
        p.wait("t", "WLC", con_done)
        p.wait("t", "XL", xload)
        p.wait("t", "EV", ones_v)
        p.wait("t", "EV", gate_ev)
        SQR = [pS1[:, 0:512], pS1[:, 512:1024], pS2[:, 0:512], pS2[:, 512:1024]]
        sqfree = {}
        for st in range(4):
            for dt_ in range(NT):
                g_ = 8 * st + dt_
                slot = LNSf[g_ % 2]
                p.wait("s", "MM", sqfree.get(g_ - 2, 0))
                sq = sop(lambda e, S, dt_=dt_, st=st, slot=slot: e.activation(
                    AR1f[:, slot:slot + 512], src(dt_, 512 * st, 512), AF.Square))
                p.wait("t", "SE", sq)
                for c4 in range(4):
                    mm0(pb[c4][:, st:st + 1], src(dt_, 512 * st + 128 * c4, 128),
                        MISC[:, MI_OCF:MI_OCF + 1], dt_ == 0, dt_ == NT - 1)
                    fn_ = mm0 if c4 < 3 else (lambda *a, **k: mm(*a, **k))
                    fn_(SQR[c4][:, st:st + 1],
                        AR1f[:, slot + 128 * c4:slot + 128 * c4 + 128],
                        MISC[:, MI_OCF:MI_OCF + 1], dt_ == 0, dt_ == NT - 1)
                sqfree[g_] = p.ct["MM"]
        lastmm = p.ct["MM"]
        p.wait("v", "MM", lastmm)
        for c4 in range(4):
            vop0(lambda e, S, c4=c4: e.tensor_scalar(
                MISC[:, MI_SM + c4:MI_SM + 13 + c4:4], pb[c4][:, 0:4],
                1.0 / 1024, None, op0=AL.mult))
            vop0(lambda e, S, c4=c4: e.tensor_scalar(
                MISC[:, MI_SV + c4:MI_SV + 13 + c4:4], SQR[c4][:, 0:4],
                1.0 / 1024, None, op0=AL.mult))
        vop0(lambda e, S: e.tensor_mul(
            MISC[:, MI_SS:MI_SS + 16], MISC[:, MI_SM:MI_SM + 16],
            MISC[:, MI_SM:MI_SM + 16]))
        stats_ev = vop(lambda e, S: e.tensor_sub(
            MISC[:, MI_SV:MI_SV + 16], MISC[:, MI_SV:MI_SV + 16],
            MISC[:, MI_SS:MI_SS + 16]))
        p.wait("s", "EV", stats_ev)
        sop(lambda e, S: e.activation(MISC[:, MI_SS:MI_SS + 16],
                                      MISC[:, MI_SV:MI_SV + 16], AF.Ln,
                                      bias=float(eps)))
        rstd = sop(lambda e, S: e.activation(MISC[:, MI_SR:MI_SR + 16],
                                             MISC[:, MI_SS:MI_SS + 16], AF.Exp,
                                             scale=-0.5))
        p.wait("v", "SE", rstd)
        mr16 = vop(lambda e, S: e.tensor_copy(MISCb[:, MB_M:MB_M + 16],
                                              MISC[:, MI_SM:MI_SM + 16]))
        mr16 = vop(lambda e, S: e.tensor_copy(MISCb[:, MB_R:MB_R + 16],
                                              MISC[:, MI_SR:MI_SR + 16]))
        if not write_xa:
            return mr16
        p.wait("t", "EV", mr16)
        apply_ev = mr16
        for st in range(4):
            lo = 512 * st
            bk = (st % 2) * 2
            p.wait("t", "EV", apply_ev)
            for c4 in range(4):
                c = 4 * st + c4
                mm0(pb[bk][0:1, 128 * c4:128 * c4 + 128],
                    MISCb[:, MB_M + c:MB_M + c + 1],
                    CON[:, CON_EYE:CON_EYE + 128], True, True)
                fn_ = mm0 if c4 < 3 else (lambda *a, **k: mm(*a, **k))
                fn_(pb[bk + 1][0:1, 128 * c4:128 * c4 + 128],
                    MISCb[:, MB_R + c:MB_R + c + 1],
                    CON[:, CON_EYE:CON_EYE + 128], True, True)
            p.wait("v", "MM")
            vop0(lambda e, S, bk=bk: e.tensor_copy(ROW0(0, 512), pb[bk][0:1, 0:512]))
            rowev = vop(lambda e, S, bk=bk: e.tensor_copy(ROW0(512, 512),
                                                          pb[bk + 1][0:1, 0:512]))
            p.wait("t", "EV", rowev)
            mm0(pb[bk][:, 0:512], MISC[0:1, MI_ONE:MI_ONE + 128],
                ROW0(0, 512), True, True)
            mrr = mm(pb[bk + 1][:, 0:512], MISC[0:1, MI_ONE:MI_ONE + 128],
                     ROW0(512, 512), True, True)
            p.wait("v", "MM", mrr)
            for dt_ in range(NT):
                vop0(lambda e, S, dt_=dt_, lo=lo, bk=bk: e.tensor_sub(
                    AR1f[:, STTf[dt_ % 2]:STTf[dt_ % 2] + 512],
                    src(dt_, lo, 512), pb[bk][:, 0:512]))
                apply_ev = vop(lambda e, S, dt_=dt_, lo=lo, bk=bk: e.tensor_mul(
                    XAf(dt_, lo, 512),
                    AR1f[:, STTf[dt_ % 2]:STTf[dt_ % 2] + 512],
                    pb[bk + 1][:, 0:512]))
        return apply_ev

    def fff_scores_traversal(pi, oh_dst, psum_ready_ev):
        p.wait("t", "WLC", con_done)
        p.wait("t", "EV", psum_ready_ev)
        for c in range(16):
            for dt_ in range(NT):
                mm0(pS2[0:128, 15 * c:15 * c + 15], XAf(dt_, 128 * c, 128),
                    NWT(pi, dt_), dt_ == 0, False)
            last = mm(pS2[0:128, 15 * c:15 * c + 15], ONEROW, NBROW(pi), False, True)
        p.wait("v", "MM", last)
        sc_ev = vop(lambda e, S: e.tensor_copy(AR1[:, OH_SC:OH_SC + 240],
                                               pS2[0:128, 0:240]))

        def sl(i):
            return AR1[:, OH_SCR + 16 * i:OH_SCR + 16 * i + 16]

        def scc(j):
            return AR1[:, OH_SC + j: OH_SC + j + 15 * 16:15]

        B = [sl(i) for i in range(4)]
        E_ = [sl(4 + i) for i in range(4)]
        TMP = [sl(8 + i) for i in range(8)]
        P2 = [AR1[:, OH_SC + 16 * i:OH_SC + 16 * i + 16] for i in range(4)]
        P4 = [AR1[:, OH_SC + 64 + 16 * i:OH_SC + 64 + 16 * i + 16] for i in range(8)]

        def ts1(out, in_, s, op):
            vop0(lambda e, S, o=out, i=in_, s=s, op=op:
                 e.tensor_single_scalar(o, i, s, op=op))

        MXT = sl(24)

        def selv(out, mask, tt, ff):
            vop0(lambda e, S, o=MXT, a=tt, b=ff: e.tensor_sub(o, a, b))
            vop0(lambda e, S, o=MXT, m=mask: e.tensor_mul(o, o, m))
            vop0(lambda e, S, o=out, b=ff: e.tensor_add(o, b, MXT))

        def mulv(out, a, b):
            vop0(lambda e, S, o=out, a=a, b=b: e.tensor_mul(o, a, b))

        ts1(B[0], scc(0), 0.0, AL.is_ge)
        selv(TMP[0], B[0], scc(2), scc(1))
        ts1(B[1], TMP[0], 0.0, AL.is_ge)
        selv(TMP[1], B[1], scc(4), scc(3))
        selv(TMP[2], B[1], scc(6), scc(5))
        selv(TMP[3], B[0], TMP[2], TMP[1])
        ts1(B[2], TMP[3], 0.0, AL.is_ge)
        selv(TMP[4], B[2], scc(8), scc(7))
        selv(TMP[5], B[2], scc(10), scc(9))
        selv(TMP[6], B[2], scc(12), scc(11))
        selv(TMP[7], B[2], scc(14), scc(13))
        selv(TMP[1], B[1], TMP[5], TMP[4])
        selv(TMP[2], B[1], TMP[7], TMP[6])
        selv(TMP[0], B[0], TMP[2], TMP[1])
        ts1(B[3], TMP[0], 0.0, AL.is_ge)
        for i in range(4):
            vop0(lambda e, S, i=i: e.tensor_scalar(
                E_[i], B[i], -1.0, 1.0, op0=AL.mult, op1=AL.add))
        mulv(P2[0], E_[0], E_[1]); mulv(P2[1], E_[0], B[1])
        mulv(P2[2], B[0], E_[1]); mulv(P2[3], B[0], B[1])
        for a in range(4):
            mulv(P4[2 * a], P2[a], E_[2]); mulv(P4[2 * a + 1], P2[a], B[2])
        for l in range(16):
            src_ = P4[l // 2]
            bit = E_[3] if l % 2 == 0 else B[3]
            if oh_dst >= 0:
                dst = AR1[:, oh_dst + l: oh_dst + l + 241:16]
            else:
                dst = OHQ[:, l: l + 241:16]
            vop0(lambda e, S, dst=dst, src_=src_, bit=bit: e.tensor_mul(
                dst, src_, bit))
        done = p.add("v", lambda e, S: e.engine_nop(), "EV", 1)
        return done, sc_ev

    def fff_ohT(oh_dst, chunks, dst_lo, gate_ev):
        pS2b = pS2.ap().bitcast(BF16)
        slots = [pS2b[0:16, 0:128], pS2b[0:16, 1024:1152]]
        slot_free = [gate_ev, gate_ev]
        ev = gate_ev
        for idx, c in enumerate(chunks):
            sl_ = idx % 2
            p.wait("t", "EV", slot_free[sl_])
            if oh_dst >= 0:
                src_oh = AR1[:, oh_dst + 16 * c:oh_dst + 16 * c + 16]
            else:
                src_oh = OHQ[:, 16 * c:16 * c + 16]
            p.add("t", lambda e, S, src_oh=src_oh, sl_=sl_, slots=slots:
                  e.transpose(slots[sl_], src_oh,
                              CON[:, CON_EYE:CON_EYE + 128]), "MM", 1)
            mv = p.ct["MM"]
            p.wait("v", "MM", mv)
            ev = vop(lambda e, S, idx=idx, sl_=sl_, slots=slots, dst_lo=dst_lo:
                     e.tensor_copy(OHT(dst_lo + 128 * idx, 128), slots[sl_]))
            slot_free[sl_] = ev
        return ev

    def fff_hidden(st_lo, n, b1key, oht_lo, pbfree, maskfree, h1s=512):
        for ht in range(8):
            bk = ht % 2
            p.wait("t", "EV", pbfree[bk])
            for dt_ in range(NT):
                mm0(pb[bk][:, 0:n], WSl(0, dt_, 128 * ht, 128),
                    XAf(dt_, st_lo, n), dt_ == 0, dt_ == NT - 1)
            p.wait("t", "SE", maskfree[bk])
            mk = mm(pb[2 + bk][:, 0:n],
                    SPK[0:16, 128 * ht:128 * ht + 128],
                    OHT(oht_lo, n), True, True)
            p.wait("s", "MM", mk)
            maskfree[bk] = sop(lambda e, S, bk=bk, n=n: e.activation(
                AR1f[:, MSKf[bk]:MSKf[bk] + n], pb[2 + bk][:, 0:n], AF.Copy))
            p.wait("v", "SE", maskfree[bk])
            p.wait("v", "MM", mk)
            vop0(lambda e, S, bk=bk, n=n, b1key=b1key, ht=ht: e.scalar_tensor_tensor(
                AR1f[:, STTf[bk]:STTf[bk] + n], pb[bk][:, 0:n],
                MISC[:, MI_B[b1key] + ht:MI_B[b1key] + ht + 1],
                AR1f[:, MSKf[bk]:MSKf[bk] + n], op0=AL.add, op1=AL.mult))
            pbfree[bk] = vop(lambda e, S, bk=bk, n=n, ht=ht, h1s=h1s: e.tensor_scalar(
                AR2[:, h1s * ht:h1s * ht + n],
                AR1f[:, STTf[bk]:STTf[bk] + n], 0.0, None, op0=AL.max))
        return pbfree, maskfree

    def emit_ffn(li, gate_ev):
        if 'ln' not in parts:
            return gate_ev
        ln_ev = emit_ln(1e-5, True, max(gate_ev, tp_er))
        if li == 0 and 'tp' in parts:
            tp_state["ev"], tp_state["gs"] = tprime_rtiles(ln_ev)
        if 'kv' not in parts:
            p.wait("t", "EV", tp_state["ev"])
            p.wait("t", "GS", tp_state["gs"])
            return ln_ev

        pbfree = [ln_ev, ln_ev, ln_ev, ln_ev]
        maskfree = [0, 0]
        scores_gate = tp_er if li == 0 else gate_ev
        bounce_last = 0
        bst_free = [0, 0]
        for pi, pfx, b1key in ((1, "k", "kb1"), (2, "v", "vb1")):
            p.wait("g", "MM", p.ct["MM"])
            b2load = g_dma(B2S(0, 1024), din[f"{pfx}b2"][:, :], sem="WB")
            trav, sc_ev = fff_scores_traversal(pi, OH_KV, max(scores_gate, ln_ev))
            p.wait("t", "EV", trav)
            p.wait("t", *wf[f"{pfx}w1"])
            p.wait("t", *wf[f"{pfx}w2"])
            p.wait("t", "WB", b2load)
            ofree = [pbfree[0], pbfree[1]]
            ohgate = sc_ev
            for st in range(4):
                lo = 512 * st
                oht_ev = fff_ohT(OH_KV, range(4 * st, 4 * st + 4), 0, ohgate)
                ohgate = oht_ev
                p.wait("t", "EV", oht_ev)
                pbfree, maskfree = fff_hidden(lo, 512, b1key, 0, pbfree, maskfree)
                p.wait("t", "EV", max(pbfree[0], pbfree[1]))
                ofree = [max(ofree[0], pbfree[0]), max(ofree[1], pbfree[1])]
                if pi == 1:
                    jobs = [(dc,) for dc in range(8)]
                else:
                    jobs = [(tc, ds) for tc in range(4) for ds in range(2)]
                for j, job in enumerate(jobs):
                    h2 = j % 2
                    p.wait("t", "EV", ofree[h2])
                    if pi == 1:
                        dc = job[0]
                        for hti in range(NT):
                            mm0(pb[h2][:, 0:512], WSl(1, hti, 128 * dc, 128),
                                AR2[:, 512 * hti:512 * hti + 512], hti == 0, False)
                        lastm = mm(pb[h2][:, 0:512], B2S(128 * dc, 128),
                                   OHT(0, 512), False, True)
                    else:
                        tc, ds = job
                        for hti in range(NT):
                            mm0(pb[h2][:, 0:512],
                                AR2[:, 512 * hti + 128 * tc:512 * hti + 128 * tc + 128],
                                WSl(1, hti, 512 * ds, 512), hti == 0, False)
                        lastm = mm(pb[h2][:, 0:512], OHT(128 * tc, 128),
                                   B2S(512 * ds, 512), False, True)
                    p.wait("v", "MM", lastm)
                    p.wait("v", f"SB{h2}", bst_free[h2])
                    ev = vop(lambda e, S, h2=h2: e.tensor_copy(
                        AR1[:, BST[h2]:BST[h2] + 512], pb[h2][:, 0:512]))
                    ofree[h2] = ev
                    pbfree[h2] = ev
                    p.wait("y", "EV", ev)
                    if pi == 1:
                        dc = job[0]
                        bst_free[h2] = y_dma(kb[128 * dc:128 * dc + 128, lo:lo + 512],
                                             AR1[:, BST[h2]:BST[h2] + 512],
                                             sem=f"SB{h2}")
                    else:
                        tc, ds = job
                        bst_free[h2] = y_dma(
                            vb[lo + 128 * tc:lo + 128 * tc + 128, 512 * ds:512 * ds + 512],
                            AR1[:, BST[h2]:BST[h2] + 512], sem=f"SB{h2}")
            scores_gate = sc_ev
            p.wait("g", "MM", p.ct["MM"])
            if pi == 1:
                wf["vw1"] = wload(0, "vw1c")
                wf["vw2"] = wload(1, "vw2c")
            else:
                wf["qw1"] = wload(0, "qw1c")
                wf["qw2"] = wload(1, "qw2c")
                wf["qb2"] = ("WB", g_dma(B2S(0, 1024), din["qb2"][:, :], sem="WB"))

        if 'gather' not in parts:
            return pbfree[0]
        p.wait("g", "SB0", p.ct["SB0"])
        p.wait("g", "SB1", p.ct["SB1"])
        p.add("g", lambda e, S: e.collective_compute(
            "AllGather", AL.bypass,
            replica_groups=[[0, 1, 2, 3], [4, 5, 6, 7]],
            ins=[kb[:, :]], outs=[kg[:, :]]), "CC", 1)
        cc = p.add("g", lambda e, S: e.collective_compute(
            "AllGather", AL.bypass,
            replica_groups=[[0, 1, 2, 3], [4, 5, 6, 7]],
            ins=[vb[:, :]], outs=[vg[:, :]]), "CC", 1)

        travq, sc_ev_q = fff_scores_traversal(0, OH_Q, max(scores_gate, ln_ev))
        p.wait("t", "EV", travq)
        p.wait("t", *wf["qw1"])
        p.wait("t", *wf["qw2"])
        p.wait("t", *wf["qb2"])
        if li == 0:
            p.wait("t", "EV", tp_state["ev"])
            p.wait("t", "GS", tp_state["gs"])
            p.wait("s", "GS", tp_state["gs"])
            p.wait("s", "GF0", p.ct["GF0"])
            p.wait("s", "GF1", p.ct["GF1"])

        if 'flash' not in parts:
            return pbfree[0]
        p.wait("y", "CC", cc)
        p.wait("s", "SB0", p.ct["SB0"])
        p.wait("s", "SB1", p.ct["SB1"])
        span_free = {0: span_free_g["c"], 1: span_free_g["c"]}

        def span_load(which, sp, half):
            p.wait("y", "MM", span_free[half])
            if which == "k":
                r_, lo_ = sp // 4, 512 * (sp % 4)
                src = kg[1024 * r_:1024 * r_ + 1024, lo_:lo_ + 512] \
                    .rearrange("(a p) f -> p a f", p=128)
                na = 8
            else:
                g0 = 512 * sp
                src = vg[g0:g0 + 512, :].rearrange("(a p) f -> p a f", p=128)
                na = 4
            dst = AR3[:, 4096 * half:4096 * half + 4096] \
                .rearrange("p (a f) -> p a f", a=na)
            return y_dma(dst, src, sem=f"KL{half}")

        AVR = [pb[0][:, 0:256], pb[1][:, 0:256], pb[2][:, 0:256], pb[3][:, 0:256],
               pS1[:, 0:256], pS1[:, 512:768], pS2[:, 0:256], pS2[:, 512:768]]
        SUMR = pb[0][0:1, 384:640] if False else pb[0][0:1, 256:512]
        qofree = {0: 0, 1: 0}
        av_ev = 0
        ohgate = sc_ev_q
        for s in range(NSTRIP_F):
            qlo = SW_F * s
            p.wait("t", "EV", av_ev)
            oht_ev = fff_ohT(OH_Q, [2 * s, 2 * s + 1], 0, max(ohgate, av_ev))
            ohgate = oht_ev
            p.wait("t", "EV", oht_ev)
            fff_hidden(qlo, SW_F, "qb1", 0, pbfree, maskfree, h1s=256)
            p.wait("t", "EV", max(pbfree[0], pbfree[1]))
            for dc in range(8):
                h2 = dc % 2
                p.wait("t", "SE", qofree[h2])
                for hti in range(NT):
                    mm0(pS1[:, 512 * h2:512 * h2 + 256], WSl(1, hti, 128 * dc, 128),
                        AR2[:, 256 * hti:256 * hti + 256], hti == 0, False)
                lastm = mm(pS1[:, 512 * h2:512 * h2 + 256],
                           B2S(128 * dc, 128), OHT(0, 256), False, True)
                p.wait("s", "MM", lastm)
                qofree[h2] = sop(lambda e, S, h2=h2, dc=dc: e.activation(
                    AR2[:, 2048 + 256 * dc:2048 + 256 * dc + 256],
                    pS1[:, 512 * h2:512 * h2 + 256], AF.Copy, scale=1.0 / 32.0))
            fq_se = qofree[1]
            p.wait("t", "SE", fq_se)
            p.wait("t", "EV", p.ct["EV"])
            p.wait("t", "SE", p.ct["SE"])
            exp_se = {}
            for sp in range(NSPAN):
                half = sp % 2
                ld = span_load("k", sp, half)
                p.wait("t", f"KL{half}", ld)
                pS = pS1 if half == 0 else pS2
                p.wait("t", "SE", exp_se.get(sp - 2, 0))
                for kc in range(4):
                    for dt_ in range(NT):
                        fn_ = mm if (kc == 3 and dt_ == NT - 1) else mm0
                        fn_(pS[:, 256 * kc:256 * kc + 256],
                            AR3[:, 4096 * half + 512 * dt_ + 128 * kc:
                                4096 * half + 512 * dt_ + 128 * kc + 128],
                            AR2[:, 2048 + 256 * dt_:2048 + 256 * dt_ + 256],
                            dt_ == 0, dt_ == NT - 1)
                mmv = p.ct["MM"]
                span_free[half] = mmv
                p.wait("s", "MM", mmv)
                exp_se[sp] = sop(lambda e, S, sp=sp, pS=pS: e.activation(
                    AR1[:, 1024 * sp:1024 * sp + 1024], pS[:, 0:1024], AF.Exp))
            p.wait("t", "SE", exp_se[NSPAN - 1])
            for sp in range(NSPAN):
                half = sp % 2
                ld = span_load("v", sp, half)
                p.wait("t", f"KL{half}", ld)
                for kc in range(4):
                    first = (sp == 0 and kc == 0)
                    lastf = (sp == NSPAN - 1 and kc == 3)
                    for dc in range(8):
                        fn_ = mm if (kc == 3 and dc == 7) else mm0
                        fn_(AVR[dc],
                            AR3[:, 4096 * half + 1024 * kc + 128 * dc:
                                4096 * half + 1024 * kc + 128 * dc + 128],
                            AR1[:, 1024 * sp + 256 * kc:1024 * sp + 256 * kc + 256],
                            first, lastf)
                span_free[half] = p.ct["MM"]
            span_free_g["c"] = max(span_free.values())
            for K in range(64):
                fn_ = mm if K == 63 else mm0
                fn_(SUMR, ONECOL, AR1[:, 256 * K:256 * K + 256], K == 0, K == 63)
            msum = p.ct["MM"]
            p.wait("v", "MM", msum)
            vop0(lambda e, S: e.reciprocal(ROW0(1024, 256), SUMR))
            vop0(lambda e, S: e.tensor_scalar(ROW0(1280, 256), ROW0(1024, 256),
                                              0.5, None, op0=AL.mult))
            rec_ev = vop(lambda e, S: e.tensor_copy(
                NWB[0:1, NW_REC:NW_REC + 256], ROW0(1280, 256)))
            p.wait("t", "EV", rec_ev)
            mrep = mm(pb[1][:, 256:512], ONEROW, NWB[0:1, NW_REC:NW_REC + 256],
                      True, True)
            p.wait("v", "MM", mrep)
            vop0(lambda e, S: e.tensor_copy(AR1f[:, 0:256], pb[1][:, 256:512]))
            for dc in range(8):
                vop0(lambda e, S, dc=dc: e.tensor_mul(
                    AR1f[:, 256:512], AVR[dc], AR1f[:, 0:256]))
                av_ev = vop(lambda e, S, dc=dc, qlo=qlo: e.tensor_add(
                    Xf(dc, qlo, 256), Xf(dc, qlo, 256), AR1f[:, 256:512]))
            pbfree[0] = pbfree[1] = pbfree[2] = pbfree[3] = av_ev
        return av_ev


    Xb = X.ap().bitcast(BF16)
    QMT, KMT, O1C = 0, 8192, 16384
    SW_A = 256
    eyeT = CON[:, CON_EYE:CON_EYE + 64]
    eyeB = CON[:, CON_EYE + 64:CON_EYE + 128]
    SCRA = NWB[64:128, 768:1280]
    SCRF = NWB[:, 768:1280]
    REPS = NWB[:, 1280:1536]
    BSB = NWB[0:64, 1536:1600]
    BSTB = NWB[0:64, 1600:1664]
    RECROW = NWB[64:65, 512:768]

    def Yf(slot, dt_, lo, n):
        o = 12288 + 512 * dt_ + lo
        return X[:, o:o + n]

    def emit_attn(gate_ev):
        trunc = [False]
        ln_ev = emit_ln(1e-5, True, gate_ev)
        p.wait("y", "EV", ln_ev)
        for dt_ in range(NT):
            y_dma(xspill[128 * dt_:128 * (dt_ + 1), :], Xf(dt_, 0, 2048), sem="XS")
        xs_done = p.ct["XS"]
        p.wait("v", "XS", xs_done)
        ev0 = vop(lambda e, S: e.memset(NWB[0:64, 768:1280], 0.0))
        ev0 = vop(lambda e, S: e.memset(Xb[:, O1C:O1C + 8192], 0.0))
        wreg = {"i": 0}

        def next_wload(name):
            r = wreg["i"] % 2
            wreg["i"] += 1
            p.wait("g", "MM", p.ct["MM"])
            p.wait("g", f"WA{r}", p.ct[f"WA{r}"])
            return wload(r, name), r

        state = {"shift": [ev0, ev0], "pb23": [ev0, ev0], "ps1b": [ev0, ev0],
                 "av": ev0, "exp": {}, "qo": 0}

        def proj(nn, wl, reg, bkey, dst_base, dst_tensor):
            p.wait("t", *wl)
            for i in range(8):
                bk = 2 + (i % 2)
                p.wait("t", "EV", state["pb23"][i % 2])
                p.wait("t", "SE", state.get("pb23se", [0, 0])[i % 2])
                for dt_ in range(NT):
                    fn_ = mm if dt_ == NT - 1 else mm0
                    fn_(pb[bk][:, 0:512], WSl(reg, dt_, 128 * i, 128),
                        XAf(dt_, 512 * nn, 512), dt_ == 0, dt_ == NT - 1)
                mmv = p.ct["MM"]
                p.wait("s", "MM", mmv)
                se1 = sop(lambda e, S, bk=bk: e.activation(
                    SCRA, pb[bk][64:128, 0:512], AF.Copy))
                state.setdefault("pb23se", [0, 0])[i % 2] = se1
                p.wait("v", "MM", mmv)
                ev1 = vop(lambda e, S, bk=bk, i=i, bkey=bkey, dst_base=dst_base,
                          dst_tensor=dst_tensor: e.tensor_scalar(
                    dst_tensor[0:64, dst_base + 2 * i:dst_base + 2 * i + 8177:16],
                    pb[bk][0:64, 0:512],
                    MISC[0:64, MI_B[bkey] + i:MI_B[bkey] + i + 1], None,
                    op0=AL.add))
                p.wait("t", "SE", se1)
                h2 = i % 2
                p.wait("t", "EV", state["ps1b"][h2])
                ms = mm(pS1[0:64, 512 * h2:512 * h2 + 512], eyeB, SCRF,
                        True, True)
                p.wait("v", "MM", ms)
                ev2 = vop(lambda e, S, i=i, h2=h2, bkey=bkey, dst_base=dst_base,
                          dst_tensor=dst_tensor: e.tensor_scalar(
                    dst_tensor[0:64, dst_base + 2 * i + 1:dst_base + 2 * i + 8178:16],
                    pS1[0:64, 512 * h2:512 * h2 + 512],
                    MISC[0:64, MI_B[bkey + "s"] + i:MI_B[bkey + "s"] + i + 1],
                    None, op0=AL.add))
                state["ps1b"][h2] = ev2
                state["pb23"][i % 2] = ev1
            return p.ct["EV"]


        def score_shift(nn):
            p.wait("t", "EV", p.ct["EV"])
            p.wait("t", "SE", p.ct["SE"])
            p.wait("s", "EV", p.ct["EV"])
            p.wait("s", "MM", p.ct["MM"])
            vop0(lambda e, S: e.memset(Xb[64:65, KMT:KMT + 8192], -1.0))
            expse = {}
            dense_ = {}

            def emit_ones(bb):
                for k4 in range(4):
                    fn_ = mm if (bb == 15 and k4 == 3) else mm0
                    fn_(pb[0][0:1, 0:256], ONECOL,
                        AR1[:, 1024 * (bb % 4) + 256 * k4:
                            1024 * (bb % 4) + 256 * k4 + 256],
                        bb == 0 and k4 == 0, bb == 15 and k4 == 3)

            for s_ in range(32):
                ilo = SW_A * s_
                p.wait("t", "SE", dense_.get("ln", 0))
                for b_ in range(16):
                    pS = pS1 if b_ % 2 == 0 else pS2
                    p.wait("t", "SE", expse.get(b_ - 2, 0))
                    for k4 in range(4):
                        kc = 4 * b_ + k4
                        fn_ = mm if k4 == 3 else mm0
                        fn_(pS[:, 256 * k4:256 * k4 + 256],
                            Xb[0:64, KMT + 128 * kc:KMT + 128 * kc + 128],
                            Xb[0:64, QMT + ilo:QMT + ilo + SW_A],
                            True, True)
                    if b_ >= 2:
                        emit_ones(b_ - 2)
                    mmv = p.ct["MM"]
                    p.wait("s", "MM", mmv)
                    expse[b_] = sop(lambda e, S, b_=b_, pS=pS: e.activation(
                        AR1[:, 1024 * (b_ % 4):1024 * (b_ % 4) + 1024],
                        pS[:, 0:1024], AF.Exp, scale=0.125))
                for bb in (14, 15):
                    p.wait("t", "SE", expse[bb])
                    emit_ones(bb)
                smm = p.ct["MM"]
                p.wait("s", "MM", smm)
                p.wait("s", "EV", p.ct["EV"])
                lnse = sop(lambda e, S: e.activation(ROWS[0:1, 0:256],
                                                     pb[0][0:1, 0:256],
                                                     AF.Ln, bias=0.0))
                dense_["ln"] = lnse
                p.wait("v", "SE", lnse)
                cev = vop(lambda e, S, s_=s_: e.tensor_scalar(
                    AR1[0:1, 8192 + 256 * s_:8192 + 256 * s_ + 256],
                    ROWS[0:1, 0:256], 8.0, -76.0, op0=AL.mult, op1=AL.add))
                expse = {}
            p.wait("g", "EV", cev)
            cdma = g_dma(Xb[64:65, QMT:QMT + 8192], AR1[0:1, 8192:16384],
                         sem="GC")
            p.wait("t", "GC", cdma)
            p.wait("t", "EV", cev)
            p.wait("t", "SE", p.ct["SE"])

        def bias_pass(nn, ihalf, gate):
            o2g = gate
            for ii in range(32):
                I = 32 * ihalf + ii
                h2 = ii % 2
                p.wait("t", "EV", state["ps1b"][h2])
                first = True
                for di, dlt in ((0, -1), (1, 0), (2, 1)):
                    J = I + dlt
                    if J < 0 or J > 63:
                        continue
                    mm0(pS1[0:64, 512 * h2:512 * h2 + 128], AR3[:, 65 * J:65 * J + 64],
                        TPR(di, nn), first, False)
                    first = False
                lmm = mm(pS1[0:64, 512 * h2:512 * h2 + 128], AR3[:, 65 * I:65 * I + 64],
                         TSN(nn), False, True)
                p.wait("v", "MM", lmm)
                o2g = vop(lambda e, S, I=I, h2=h2: e.tensor_scalar(
                    AR2[0:64, 128 * (I % 32):128 * (I % 32) + 128],
                    pS1[0:64, 512 * h2:512 * h2 + 128],
                    MISC[0:64, MI_F + I:MI_F + I + 1], None, op0=AL.add))
                state["ps1b"][h2] = o2g
            return o2g

        ab_ev = 0
        for nn in range(attn_nn):
            lastb = (nn == attn_nn - 1)
            tokb = 512 * nn
            wl_q, reg_q = next_wload("wqT")
            wl_k, reg_k = next_wload("wkT")
            if nn == 0:
                p.wait("t", "EV", ln_ev)
                p.wait("t", "XS", xs_done)
                p.wait("v", "XS", xs_done)
            proj(nn, wl_q, reg_q, "bq", QMT, Xb)
            if lastb and attn_stage <= 1:
                trunc[0] = True; break
            wl_v, reg_v = next_wload("wvT")
            proj(nn, wl_k, reg_k, "bk", KMT, Xb)
            if lastb and attn_stage <= 2:
                trunc[0] = True; break
            score_shift(nn)
            if lastb and attn_stage <= 3:
                trunc[0] = True; break
            wl_wog, reg_wog = next_wload("wogT")
            vev = proj(nn, wl_v, reg_v, "bv", 0, AR3)
            if lastb and attn_stage <= 4:
                trunc[0] = True; break
            pS2b = pS2.ap().bitcast(BF16)
            slots = [pS2b[0:128, 0:64], pS2b[0:128, 1024:1088]]
            sfree = [vev, vev]
            for c in range(64):
                sl_ = c % 2
                p.wait("t", "EV", sfree[sl_])
                p.add("t", lambda e, S, c=c, sl_=sl_: e.transpose(
                    slots[sl_], AR3[0:64, 128 * c:128 * c + 128],
                    CON[0:64, CON_EYE:CON_EYE + 64]), "MM", 1)
                mv = p.ct["MM"]
                p.wait("v", "MM", mv)
                vop0(lambda e, S, c=c, sl_=sl_: e.tensor_copy(
                    AR3[:, 65 * c:65 * c + 64], slots[sl_]))
                sfree[sl_] = vop(lambda e, S, c=c: e.memset(
                    AR3[:, 65 * c + 64:65 * c + 65], 1.0))
            vmo_ev = p.ct["EV"]
            if lastb and attn_stage <= 5:
                trunc[0] = True; break
            p.wait("t", "EV", vmo_ev)
            p.wait("t", "EV", state["pb23"][0])
            p.wait("t", "EV", state["pb23"][1])
            for J in range(64):
                fn_ = mm if J == 63 else mm0
                fn_(pb[2][0:64, J:J + 1], AR3[:, 65 * J:65 * J + 64], ONECOL,
                    True, True)
            bsmm = p.ct["MM"]
            p.wait("v", "MM", bsmm)
            bse = vop(lambda e, S: e.tensor_copy(BSB, pb[2][0:64, 0:64]))
            p.wait("t", "EV", bse)
            p.add("t", lambda e, S: e.transpose(
                pS2.ap().bitcast(BF16)[0:64, 0:64], BSB,
                CON[0:64, CON_EYE:CON_EYE + 64]), "MM", 1)
            btr = p.ct["MM"]
            p.wait("v", "MM", btr)
            bte = vop(lambda e, S: e.tensor_copy(
                BSTB, pS2.ap().bitcast(BF16)[0:64, 0:64]))
            p.wait("t", "EV", bte)
            mm0(pb[3][0:64, 0:64], BSTB, CON[0:64, CON_L64:CON_L64 + 64],
                True, True)
            lsmm = mm(pb[3][0:64, 64:65], BSTB,
                      CON[0:64, CON_L64 + 64:CON_L64 + 65], True, True)
            p.wait("v", "MM", lsmm)
            vop0(lambda e, S, nn=nn: e.tensor_scalar(
                MISC[0:64, 72:73], pb[3][0:64, 64:65],
                MISC[0:64, MI_REP + nn:MI_REP + nn + 1], None, op0=AL.mult))
            fev = vop(lambda e, S, nn=nn: e.tensor_scalar(
                MISC[0:64, MI_F:MI_F + 64], pb[3][0:64, 0:64],
                MISC[0:64, MI_REP + 4 + nn:MI_REP + 4 + nn + 1],
                MISC[0:64, 72:73], op0=AL.mult, op1=AL.add))
            state["pb23"] = [fev, fev]
            if lastb and attn_stage <= 6:
                trunc[0] = True; break
            o2g = bias_pass(nn, 0, fev)
            for s in range(32):
                if s == 16:
                    o2g = bias_pass(nn, 1, o2g)
                ilo = SW_A * s

                def emit_av(bb):
                    if bb == 0:
                        p.wait("t", "EV", state["av"])
                    for k4 in range(4):
                        kc = 4 * bb + k4
                        fn_ = mm if kc == 63 else mm0
                        fn_(pb[0][0:65, 0:256], AR3[:, 65 * kc:65 * kc + 65],
                            AR1[:, 256 * kc:256 * kc + 256], kc == 0, kc == 63)

                for b_ in range(16):
                    pS = pS1 if b_ % 2 == 0 else pS2
                    p.wait("t", "SE", state["exp"].get(("b", nn, s, b_ - 2), 0))
                    if b_ < 2:
                        p.wait("t", "EV", state["ps1b"][0])
                        p.wait("t", "EV", state["ps1b"][1])
                        p.wait("t", "EV", o2g)
                    for k4 in range(4):
                        kc = 4 * b_ + k4
                        fn_ = mm if k4 == 3 else mm0
                        fn_(pS[:, 256 * k4:256 * k4 + 256],
                            Xb[0:65, KMT + 128 * kc:KMT + 128 * kc + 128],
                            Xb[0:65, QMT + ilo:QMT + ilo + SW_A],
                            True, True)
                    if b_ >= 2:
                        emit_av(b_ - 2)
                    mmv = p.ct["MM"]
                    p.wait("s", "MM", mmv)
                    state["exp"][("b", nn, s, b_)] = sop(
                        lambda e, S, b_=b_, pS=pS: e.activation(
                            AR1[:, 1024 * b_:1024 * b_ + 1024], pS[:, 0:1024],
                            AF.Exp, bias=0.0))
                for bb in (14, 15):
                    p.wait("t", "SE", state["exp"][("b", nn, s, bb)])
                    emit_av(bb)
                avm = p.ct["MM"]
                p.wait("v", "MM", avm)
                vop0(lambda e, S: e.tensor_scalar(
                    ROWS[64:65, 1024:1280], pb[0][64:65, 0:256], 1e-30, None,
                    op0=AL.max))
                vop0(lambda e, S: e.reciprocal(ROWS[64:65, 1024:1280],
                                               ROWS[64:65, 1024:1280]))
                rec = vop(lambda e, S: e.tensor_copy(RECROW,
                                                     ROWS[64:65, 1024:1280]))
                p.wait("t", "EV", rec)
                mrep = mm(pb[1][:, 0:256], CON[64:65, CON_ONEB:CON_ONEB + 128],
                          RECROW, True, True)
                p.wait("v", "MM", mrep)
                vop0(lambda e, S: e.tensor_copy(REPS, pb[1][:, 0:256]))
                vop0(lambda e, S, ilo=ilo: e.tensor_mul(
                    Xb[0:64, O1C + ilo:O1C + ilo + 256], pb[0][0:64, 0:256],
                    REPS[0:64, :]))
                state["av"] = vop(lambda e, S, ilo=ilo, s=s: e.tensor_add(
                    Xb[0:64, O1C + ilo:O1C + ilo + 256],
                    Xb[0:64, O1C + ilo:O1C + ilo + 256],
                    AR2[0:64, 128 * ((2 * s) % 32):128 * ((2 * s) % 32) + 256]))
            if lastb and attn_stage <= 7:
                trunc[0] = True; break
            dupev = state["av"]
            for i in range(16):
                h2 = i % 2
                p.wait("t", "EV", state["ps1b"][h2])
                p.wait("t", "EV", dupev)
                msd = p.add("t", lambda e, S, i=i, h2=h2: e.matmul(
                    pS1[64:128, 512 * h2:512 * h2 + 512], eyeT,
                    Xb[:, O1C + 512 * i:O1C + 512 * i + 512],
                    start=True, stop=True, tile_position=(0, 64)), "MM", 1)
                p.wait("v", "MM", p.ct["MM"])
                state["ps1b"][h2] = vop(lambda e, S, i=i, h2=h2: e.tensor_copy(
                    Xb[64:128, O1C + 512 * i:O1C + 512 * i + 512],
                    pS1[64:128, 512 * h2:512 * h2 + 512]))
            dup_ev = p.ct["EV"]
            if lastb and attn_stage <= 8:
                trunc[0] = True; break
            p.wait("t", *wl_wog)
            p.wait("t", "EV", dup_ev)
            slot = 0
            yev = 0
            for dc in range(8):
                bk = 2 + (dc % 2)
                h = dc % 2
                p.wait("t", "EV", state["pb23"][h])
                p.wait("t", "EV", state["ps1b"][h])
                for par in range(2):
                    base = 64 * par
                    dst = pb[bk][:, 0:512] if par == 0 else \
                        pS1[:, 512 * h:512 * h + 512]
                    for gi in range(8):
                        g = 2 * gi + par
                        fn_ = mm if (par == 1 and gi == 7) else mm0
                        fn_(dst,
                            WA[base:base + 64,
                               8192 * reg_wog + 1024 * gi + 128 * dc:
                               8192 * reg_wog + 1024 * gi + 128 * dc + 128],
                            Xb[base:base + 64, O1C + g:O1C + g + 8177:16],
                            gi == 0, gi == 7)
                wmm = p.ct["MM"]
                p.wait("s", "MM", wmm)
                se = sop(lambda e, S, h=h: e.activation(
                    AR1f[:, 5120 + 512 * h:5120 + 512 * h + 512],
                    pS1[:, 512 * h:512 * h + 512], AF.Copy))
                p.wait("v", "MM", wmm)
                p.wait("v", "SE", se)
                vop0(lambda e, S, dc=dc, bk=bk, h=h: e.tensor_add(
                    AR1f[:, 2048 + 512 * h:2048 + 512 * h + 512],
                    pb[bk][:, 0:512], AR1f[:, 5120 + 512 * h:5120 + 512 * h + 512]))
                vop0(lambda e, S, dc=dc, h=h: e.tensor_scalar(
                    AR1f[:, 2048 + 512 * h:2048 + 512 * h + 512],
                    AR1f[:, 2048 + 512 * h:2048 + 512 * h + 512],
                    MISC[:, MI_B["bo"] + dc:MI_B["bo"] + dc + 1],
                    None, op0=AL.add))
                yev = vop(lambda e, S, dc=dc, slot=slot, nn=nn, h=h: e.tensor_add(
                    Yf(slot, dc, 0, 512),
                    AR1f[:, 2048 + 512 * h:2048 + 512 * h + 512],
                    XAf(dc, 512 * nn, 512)))
                state["pb23"][h] = yev
                state["ps1b"][h] = yev
            if debug and nn == 0:
                p.wait("y", "EV", yev)
                y_dma(dbg["dyt"][:, :].bitcast(mybir.dt.uint32),
                      X[:, 12288:14336].bitcast(mybir.dt.uint32), sem="SO")
                y_dma(dbg["do1"][:, :].bitcast(mybir.dt.uint16),
                      Xb[:, O1C:O1C + 8192].bitcast(mybir.dt.uint16), sem="SO")
                y_dma(dbg["do2"][:, :].bitcast(mybir.dt.uint16),
                      AR2[:, 0:4096].bitcast(mybir.dt.uint16), sem="SO")
                p.wait("t", "SO", p.ct["SO"])
            if lastb and attn_stage <= 9:
                trunc[0] = True; break
            ab_ev = ln_rms_block(nn, slot, yev)
            state["ps1b"] = [ab_ev, ab_ev]
            state["pb23"] = [ab_ev, ab_ev]
            state["av"] = ab_ev

        if trunc[0] or attn_nn < 4:
            return p.ct["EV"], xs_done
        wl1, reg1 = next_wload("w1aT")
        wl2, reg2 = next_wload("w2aT")
        p.wait("t", *wl1)
        p.wait("t", "EV", ab_ev)
        pbf = [ab_ev, ab_ev]
        h1f = [0, 0]
        for st in range(4):
            lo = 512 * st
            for ht in range(8):
                bk = 2 + (ht % 2)
                p.wait("t", "EV", pbf[ht % 2])
                for dt_ in range(NT):
                    fn_ = mm if dt_ == NT - 1 else mm0
                    fn_(pb[bk][:, 0:512], WSl(reg1, dt_, 128 * ht, 128),
                        XAf(dt_, lo, 512), dt_ == 0, dt_ == NT - 1)
                hm = p.ct["MM"]
                p.wait("v", "MM", hm)
                pbf[ht % 2] = vop(lambda e, S, bk=bk, ht=ht: e.tensor_scalar(
                    AR2[:, 512 * ht:512 * ht + 512], pb[bk][:, 0:512],
                    MISC[:, MI_B["b1a"] + ht:MI_B["b1a"] + ht + 1], None,
                    op0=AL.add))
            if st == 0:
                p.wait("t", *wl2)
            p.wait("t", "EV", max(pbf))
            for dc in range(8):
                bk = 2 + (dc % 2)
                p.wait("t", "EV", pbf[dc % 2])
                for hti in range(NT):
                    fn_ = mm if hti == NT - 1 else mm0
                    fn_(pb[bk][:, 0:512], WSl(reg2, hti, 128 * dc, 128),
                        AR2[:, 512 * hti:512 * hti + 512], hti == 0,
                        hti == NT - 1)
                zm = p.ct["MM"]
                p.wait("v", "MM", zm)
                vop0(lambda e, S, bk=bk, dc=dc: e.tensor_scalar(
                    AR1f[:, 2048 + 512 * (dc % 2):2048 + 512 * (dc % 2) + 512],
                    pb[bk][:, 0:512],
                    MISC[:, MI_B["b2a"] + dc:MI_B["b2a"] + dc + 1], None,
                    op0=AL.add))
                pbf[dc % 2] = vop(lambda e, S, bk=bk, dc=dc, lo=lo: e.tensor_add(
                    Xf(dc, lo, 512),
                    AR1f[:, 2048 + 512 * (dc % 2):2048 + 512 * (dc % 2) + 512],
                    XAf(dc, lo, 512)))
        z_ev = max(pbf)
        return z_ev, xs_done


    def ln_rms_block(nn, slot, gate_ev):
        p.wait("t", "EV", gate_ev)
        p.wait("s", "MM", p.ct["MM"])
        sq = {}
        for dt_ in range(NT):
            slt = LNSf[dt_ % 2]
            p.wait("s", "MM", sq.get(dt_ - 2, 0))
            p.wait("s", "EV", gate_ev)
            sev = sop(lambda e, S, dt_=dt_, slt=slt, slot=slot: e.activation(
                AR1f[:, slt:slt + 512], Yf(slot, dt_, 0, 512), AF.Square))
            p.wait("t", "SE", sev)
            for c4 in range(4):
                mm0(pb[c4][:, 4 + nn:5 + nn], Yf(slot, dt_, 128 * c4, 128),
                    MISC[:, MI_OCF:MI_OCF + 1], dt_ == 0, dt_ == NT - 1)
                fn_ = mm0 if c4 < 3 else (lambda *a, **k: mm(*a, **k))
                fn_([pS1, pS1, pS2, pS2][c4][:, [0, 512, 0, 512][c4] + 4 + nn:
                                             [0, 512, 0, 512][c4] + 5 + nn],
                    AR1f[:, slt + 128 * c4:slt + 128 * c4 + 128],
                    MISC[:, MI_OCF:MI_OCF + 1], dt_ == 0, dt_ == NT - 1)
            sq[dt_] = p.ct["MM"]
        p.wait("v", "MM", p.ct["MM"])
        SQR = [pS1[:, 0:512], pS1[:, 512:1024], pS2[:, 0:512], pS2[:, 512:1024]]
        for c4 in range(4):
            vop0(lambda e, S, c4=c4, nn=nn: e.tensor_scalar(
                MISC[:, MI_SM + 4 * nn + c4:MI_SM + 4 * nn + c4 + 1],
                pb[c4][:, 4 + nn:5 + nn], 1.0 / 1024, None, op0=AL.mult))
            vop0(lambda e, S, c4=c4, nn=nn: e.tensor_scalar(
                MISC[:, MI_SV + 4 * nn + c4:MI_SV + 4 * nn + c4 + 1],
                SQR[c4][:, 4 + nn:5 + nn], 1.0 / 1024, None, op0=AL.mult))
        cm, cv = MISC[:, MI_SM + 4 * nn:MI_SM + 4 * nn + 4], \
            MISC[:, MI_SV + 4 * nn:MI_SV + 4 * nn + 4]
        vop0(lambda e, S, cm=cm, nn=nn: e.tensor_mul(
            MISC[:, MI_SS:MI_SS + 4], cm, cm))
        sev2 = vop(lambda e, S, cv=cv: e.tensor_sub(
            cv, cv, MISC[:, MI_SS:MI_SS + 4]))
        p.wait("s", "EV", sev2)
        sop(lambda e, S, cv=cv: e.activation(MISC[:, MI_SS:MI_SS + 4], cv,
                                             AF.Ln, bias=0.0))
        rst = sop(lambda e, S, nn=nn: e.activation(
            MISC[:, MI_SR + 4 * nn:MI_SR + 4 * nn + 4],
            MISC[:, MI_SS:MI_SS + 4], AF.Exp, scale=-0.5))
        p.wait("v", "SE", rst)
        m16 = vop(lambda e, S, nn=nn: e.tensor_copy(
            MISCb[:, MB_M + 4 * nn:MB_M + 4 * nn + 4],
            MISC[:, MI_SM + 4 * nn:MI_SM + 4 * nn + 4]))
        m16 = vop(lambda e, S, nn=nn: e.tensor_copy(
            MISCb[:, MB_R + 4 * nn:MB_R + 4 * nn + 4],
            MISC[:, MI_SR + 4 * nn:MI_SR + 4 * nn + 4]))
        p.wait("t", "EV", m16)
        for c4 in range(4):
            c = 4 * nn + c4
            mm0(pb[0][0:1, 128 * c4:128 * c4 + 128], MISCb[:, MB_M + c:MB_M + c + 1],
                CON[:, CON_EYE:CON_EYE + 128], True, True)
            fn_ = mm0 if c4 < 3 else (lambda *a, **k: mm(*a, **k))
            fn_(pb[1][0:1, 128 * c4:128 * c4 + 128], MISCb[:, MB_R + c:MB_R + c + 1],
                CON[:, CON_EYE:CON_EYE + 128], True, True)
        p.wait("v", "MM")
        vop0(lambda e, S: e.tensor_copy(ROW0(0, 512), pb[0][0:1, 0:512]))
        rowev = vop(lambda e, S: e.tensor_copy(ROW0(512, 512), pb[1][0:1, 0:512]))
        p.wait("t", "EV", rowev)
        mm0(pb[0][:, 0:512], MISC[0:1, MI_ONE:MI_ONE + 128], ROW0(0, 512),
            True, True)
        mrr = mm(pb[1][:, 0:512], MISC[0:1, MI_ONE:MI_ONE + 128], ROW0(512, 512),
                 True, True)
        p.wait("v", "MM", mrr)
        ab_ev = 0
        for dt_ in range(NT):
            vop0(lambda e, S, dt_=dt_, slot=slot: e.tensor_sub(
                AR1f[:, STTf[dt_ % 2]:STTf[dt_ % 2] + 512],
                Yf(slot, dt_, 0, 512), pb[0][:, 0:512]))
            ab_ev = vop(lambda e, S, dt_=dt_, nn=nn: e.tensor_mul(
                XAf(dt_, 512 * nn, 512),
                AR1f[:, STTf[dt_ % 2]:STTf[dt_ % 2] + 512], pb[1][:, 0:512]))
        return ab_ev


    def final_residual(gate_ev):
        mr = emit_ln(0.0, False, gate_ev)
        p.wait("y", "XS", p.ct["XS"])
        p.wait("y", "EV", mr)
        xr_free = {}
        for st in range(4):
            lo = 512 * st
            bk = (st % 2) * 2
            for dt_ in range(NT):
                p.wait("y", "EV", xr_free.get((st - 1, dt_), 0))
                y_dma(AR1f[:, 4096 + 512 * dt_:4096 + 512 * dt_ + 512],
                      xspill[128 * dt_:128 * (dt_ + 1), lo:lo + 512], sem="XR")
            xr = p.ct["XR"]
            p.wait("t", "EV", mr if st == 0 else 0)
            for c4 in range(4):
                c = 4 * st + c4
                mm0(pb[bk][0:1, 128 * c4:128 * c4 + 128],
                    MISCb[:, MB_M + c:MB_M + c + 1],
                    CON[:, CON_EYE:CON_EYE + 128], True, True)
                fn_ = mm0 if c4 < 3 else (lambda *a, **k: mm(*a, **k))
                fn_(pb[bk + 1][0:1, 128 * c4:128 * c4 + 128],
                    MISCb[:, MB_R + c:MB_R + c + 1],
                    CON[:, CON_EYE:CON_EYE + 128], True, True)
            p.wait("v", "MM")
            vop0(lambda e, S, bk=bk: e.tensor_copy(ROW0(0, 512), pb[bk][0:1, 0:512]))
            rowev = vop(lambda e, S, bk=bk: e.tensor_copy(ROW0(512, 512),
                                                          pb[bk + 1][0:1, 0:512]))
            p.wait("t", "EV", rowev)
            mm0(pb[bk][:, 0:512], MISC[0:1, MI_ONE:MI_ONE + 128], ROW0(0, 512),
                True, True)
            mrr = mm(pb[bk + 1][:, 0:512], MISC[0:1, MI_ONE:MI_ONE + 128],
                     ROW0(512, 512), True, True)
            p.wait("v", "MM", mrr)
            p.wait("v", "XR", xr)
            last = 0
            for dt_ in range(NT):
                hm = 512 * dt_
                vop0(lambda e, S, dt_=dt_, lo=lo, bk=bk: e.tensor_sub(
                    AR1f[:, LNSf[dt_ % 2]:LNSf[dt_ % 2] + 512],
                    Xf(dt_, lo, 512), pb[bk][:, 0:512]))
                vop0(lambda e, S, dt_=dt_, bk=bk: e.tensor_mul(
                    AR1f[:, LNSf[dt_ % 2]:LNSf[dt_ % 2] + 512],
                    AR1f[:, LNSf[dt_ % 2]:LNSf[dt_ % 2] + 512],
                    pb[bk + 1][:, 0:512]))
                last = vop(lambda e, S, dt_=dt_, lo=lo, hm=hm: e.tensor_add(
                    Xf(dt_, lo, 512),
                    AR1f[:, LNSf[dt_ % 2]:LNSf[dt_ % 2] + 512],
                    AR1f[:, 4096 + hm:4096 + hm + 512]))
                xr_free[(st, dt_)] = last
        return last

    span_free_g = {"c": 0}
    last_ev = emit_ffn(0, ones_v)

    if 'attn' in parts:
        z_ev, _xs = emit_attn(last_ev)
        if attn_nn == 4 and attn_stage >= 10:
            last_ev = final_residual(z_ev)
        if 'ffn2' in parts and attn_nn == 4 and attn_stage >= 10:
            p.wait("g", "MM", p.ct["MM"])
            wf["kw1"] = wload(0, "kw1c")
            wf["kw2"] = wload(1, "kw2c")
            last_ev = emit_ffn(1, last_ev)

    p.wait("y", "EV", p.ct["EV"])
    p.wait("y", "MM", p.ct["MM"])
    p.wait("y", "SE", p.ct["SE"])
    for s_ in ("GS", "GT0", "GT1", "GF0", "GF1", "WB", "WA0", "WA1", "WLC"):
        p.wait("y", s_, p.ct[s_])
    U32 = mybir.dt.uint32
    for dt_ in range(NT):
        y_dma(yout[128 * dt_:128 * (dt_ + 1), :].bitcast(U32) if debug else
              yout[128 * dt_:128 * (dt_ + 1), :],
              Xf(dt_, 0, 2048).bitcast(U32) if debug else Xf(dt_, 0, 2048),
              sem="SO")
    if debug:
        y_dma(dbg["dxa"][:, :], XA[:, :], sem="SO")
        y_dma(dbg["dtp"][:, :], CON[:, CON_TP:CON_TP + 2048], sem="SO")
        if 'gather' in parts:
            y_dma(dbg["doh"][:, 0:256], OHQ[:, :], sem="SO")
        if 'attn' in parts:
            U16 = mybir.dt.uint16
            y_dma(dbg["dqk"][:, :].bitcast(U16), Xb[:, 0:16384].bitcast(U16), sem="SO")
            y_dma(dbg["dvmo"][:, :].bitcast(U16), AR3[:, :].bitcast(U16), sem="SO")
        if 'flash' in parts:
            y_dma(dbg["dfq"][:, :], AR2[:, 2048:4096], sem="SO")
            U16 = mybir.dt.uint16
            y_dma(dbg["de"][:, :].bitcast(U16), AR1[:, :].bitcast(U16), sem="SO")
            y_dma(dbg["dht"][:, :], AR2[:, 0:4096], sem="SO")
    p.wait("y", "SO", p.ct["SO"])

    with nc.Block() as blk:
        semcms = [nc.semaphore(n) for n in p.semnames]
        S = {}
        for n_, cm in zip(p.semnames, semcms):
            S[n_] = cm.__enter__()

        @blk.tensor
        def _(e):
            for fn in p.ops["t"]:
                fn(e, S)

        @blk.vector
        def _(e):
            for fn in p.ops["v"]:
                fn(e, S)

        @blk.scalar
        def _(e):
            for fn in p.ops["s"]:
                fn(e, S)

        @blk.sync
        def _(e):
            for fn in p.ops["y"]:
                fn(e, S)

        @blk.gpsimd
        def _(e):
            for fn in p.ops["g"]:
                fn(e, S)

    return nc



def _wog(wo):
    out = np.zeros((1024, 1024), np.float32)
    w = np.asarray(wo).reshape(1024, 16, 64)
    for g in range(16):
        out[128 * (g // 2) + 64 * (g % 2):128 * (g // 2) + 64 * (g % 2) + 64, :] \
            = w[:, g, :].T
    return np.ascontiguousarray(out)


def _prep_core_inputs(core, inp):
    b, qd = core // 4, core % 4
    sl = slice(2048 * qd, 2048 * (qd + 1))
    C = _get_consts()

    def f32(a):
        return np.ascontiguousarray(np.asarray(a, dtype=np.float32))

    d = {
        "xt": f32(np.asarray(inp["L0_attn_x"])[b, sl, :].T),
        "wqT": f32(np.asarray(inp["L0_attn_Wq"]).T),
        "wkT": f32(np.asarray(inp["L0_attn_Wk"]).T),
        "wvT": f32(np.asarray(inp["L0_attn_Wv"]).T),
        "woT": f32(np.asarray(inp["L0_attn_Wo"]).T),
        "wogT": _wog(np.asarray(inp["L0_attn_Wo"])),
        "w1aT": f32(np.asarray(inp["L0_attn_W1"]).T),
        "w2aT": f32(np.asarray(inp["L0_attn_W2"]).T),
        "relbT": f32(np.asarray(inp["L0_attn_rel_bias"]).T),
        "mflat": C["mflat"], "eyetri": C["eyetri"], "l64": C["l64"],
        "gmask": C["gmask"],
        "fconst": np.tile(np.array([0.0, 1.0, 1e-5, -0.5, 1.0 / 32.0, -40.0,
                                    0.5, 2.0], np.float32), (128, 1)),
    }
    for nm, key in (("bq4", "L0_attn_bq"), ("bk4", "L0_attn_bk"),
                    ("bv4", "L0_attn_bv"), ("bo4", "L0_attn_bo"),
                    ("b1a4", "L0_attn_b1"), ("b2a4", "L0_attn_b2")):
        d[nm] = f32(np.asarray(inp[key]).reshape(8, 128).T)
    for nm in ("bq4", "bk4", "bv4"):
        sh = np.zeros((128, 8), np.float32)
        sh[0:64, :] = d[nm][64:128, :]
        d[nm + "s"] = sh
    for pfx in ("q", "k", "v"):
        d[f"{pfx}nwT"] = f32(np.asarray(inp[f"L0_ffn_{pfx}_node_w"]).T)
        d[f"{pfx}nb"] = f32(np.asarray(inp[f"L0_ffn_{pfx}_node_b"]).reshape(1, 15))
        d[f"{pfx}w1c"] = f32(np.asarray(inp[f"L0_ffn_{pfx}_w1"]).transpose(1, 0, 2)
                             .reshape(1024, 1024))
        d[f"{pfx}b1f4"] = f32(np.asarray(inp[f"L0_ffn_{pfx}_b1"]).reshape(1024)
                              .reshape(8, 128).T)
        d[f"{pfx}w2c"] = f32(np.asarray(inp[f"L0_ffn_{pfx}_w2"]).reshape(1024, 1024))
        d[f"{pfx}b2"] = f32(np.asarray(inp[f"L0_ffn_{pfx}_b2"]))
    hs = np.zeros((16, 4), np.float32)
    for ln in range(4):
        hs[4 * qd + ln, ln] = 1.0
    d["hsel"] = hs
    return d


def kernel(**inputs):
    nc = build_nc()
    in_maps = [_prep_core_inputs(c, inputs) for c in range(NCORE)]
    res = run_bass_kernel_spmd(nc, in_maps, core_ids=list(range(NCORE)))
    out = np.zeros((2, 8192, 1024), np.float32)
    for c in range(NCORE):
        b, qd = c // 4, c % 4
        out[b, 2048 * qd:2048 * (qd + 1), :] = res.results[c]["y"].T
    return out


# version 04
# speedup vs baseline: 1.0163x; 1.0163x over previous
import math
import numpy as np
import concourse.bass as bass
import concourse.mybir as mybir
from concourse.bass_utils import run_bass_kernel_spmd

F32 = mybir.dt.float32
BF16 = mybir.dt.bfloat16
AF = mybir.ActivationFunctionType
AL = mybir.AluOpType

NCORE = 8
T = 2048
D = 1024
NT = 8
SW_F = 256
NSTRIP_F = T // SW_F
NSPAN = 16
EXPSH = -40.0


def _bucket_of(d):
    n = -d
    ret = 16 if n < 0 else 0
    na = abs(n)
    if na < 8:
        return ret + na
    v = 8 + int(math.floor(math.log(na / 8.0) / math.log(16.0) * 8.0))
    return ret + min(v, 15)


def _host_consts():
    c = {}
    mf = np.zeros((96, 16384), np.float32)
    for di, delta in enumerate((-1, 0, 1)):
        for dd in range(-90, 91):
            b = _bucket_of(dd)
            for jl in range(128):
                il = dd + 128 * delta + jl
                if 0 <= il < 128:
                    col = 128 * jl + il
                    mf[32 * di + b, col] += 1.0
                    if dd >= 1:
                        mf[32 * di + 31, col] -= 1.0
                    else:
                        mf[32 * di + 15, col] -= 1.0
    c["mflat"] = mf
    eye = np.zeros((128, 256), np.float32)
    eye[:, 0:128] = np.eye(128, dtype=np.float32)
    eye[:, 128:256] = (np.arange(128)[:, None] < np.arange(128)[None, :]).astype(np.float32)
    c["eyetri"] = eye
    l64 = np.zeros((128, 65), np.float32)
    l64[0:64, 0:64] = (np.arange(64)[:, None] < np.arange(64)[None, :]).astype(np.float32)
    l64[:, 64:65] = 1.0
    c["l64"] = l64
    g = np.zeros((16, 1024), np.float32)
    for ht in range(8):
        for pp in range(128):
            g[2 * ht + pp // 64, 128 * ht + pp] = 1.0
    c["gmask"] = g
    return c


_CONSTS = None


def _get_consts():
    global _CONSTS
    if _CONSTS is None:
        _CONSTS = _host_consts()
    return _CONSTS



ENGS = ("t", "v", "s", "y", "g")


class P:
    def __init__(self):
        self.ops = {e: [] for e in ENGS}
        self.ct = {}
        self.semnames = []

    def sem(self, name):
        if name not in self.ct:
            self.ct[name] = 0
            self.semnames.append(name)

    def add(self, eng, fn, sem=None, inc=0):
        if sem is not None:
            self.sem(sem)
            self.ct[sem] += inc

            def wrapped(e, S, fn=fn, sem=sem, inc=inc):
                fn(e, S).then_inc(S[sem], inc)
            self.ops[eng].append(wrapped)
            return self.ct[sem]
        self.ops[eng].append(fn)
        return None

    def wait(self, eng, sem, val=None):
        self.sem(sem)
        v = self.ct[sem] if val is None else val
        assert v <= self.ct[sem], f"{sem}: wait {v} > emitted {self.ct[sem]}"
        if v > 0:
            self.ops[eng].append(lambda e, S, sem=sem, v=v: e.wait_ge(S[sem], v))
        return v



def build_nc(debug=False, parts=('tp', 'ln', 'kv', 'gather', 'flash', 'attn', 'ffn2'), attn_nn=4, attn_stage=10):
    nc = bass.Bass("TRN2", target_bir_lowering=False)
    p = P()

    din = {}

    def DI(name, shape):
        din[name] = nc.dram_tensor(name, list(shape), F32, kind="ExternalInput").ap()

    DI("xt", (D, T))
    for w in ("wqT", "wkT", "wvT", "woT", "w1aT", "w2aT"):
        DI(w, (D, D))
    for b in ("bq4", "bk4", "bv4", "bo4", "b1a4", "b2a4"):
        DI(b, (128, 8))
    for pfx in ("q", "k", "v"):
        DI(f"{pfx}nwT", (D, 15))
        DI(f"{pfx}nb", (1, 15))
        DI(f"{pfx}w1c", (D, D))
        DI(f"{pfx}b1f4", (128, 8))
        DI(f"{pfx}w2c", (D, D))
        DI(f"{pfx}b2", (16, D))
    DI("relbT", (16, 32))
    DI("wogT", (D, D))
    for b in ("bq4s", "bk4s", "bv4s"):
        DI(b, (128, 8))
    DI("fconst", (128, 8))
    DI("hsel", (16, 4))
    DI("mflat", (96, 16384))
    DI("eyetri", (128, 256))
    DI("l64", (128, 65))
    DI("gmask", (16, 1024))

    yout = nc.dram_tensor("y", [D, T], F32, kind="ExternalOutput").ap()
    dbg = {}
    if debug:
        for nm, shape, dt_ in (
            ("dxa", (128, 16384), BF16), ("dfq", (128, 2048), BF16),
            ("de", (128, 16384), BF16), ("dtp", (128, 2048), BF16),
            ("doh", (128, 256), BF16), ("dht", (128, 4096), BF16),
            ("dqk", (128, 16384), BF16), ("dvmo", (128, 8256), BF16),
            ("dyt", (128, 2048), F32), ("do1", (128, 8192), BF16),
            ("do2", (128, 4096), BF16),
        ):
            dbg[nm] = nc.dram_tensor(nm, list(shape), dt_, kind="ExternalOutput").ap()

    xspill = nc.dram_tensor("xspill", [D, T], F32).ap()
    kb = nc.dram_tensor("kb", [D, T], BF16).ap()
    vb = nc.dram_tensor("vb", [T, D], BF16).ap()
    kg = nc.dram_tensor("kg", [4 * D, T], BF16).ap()
    vg = nc.dram_tensor("vg", [4 * T, D], BF16).ap()
    tpscr = nc.dram_tensor("tpscr", [12, 16384], BF16).ap()

    X = nc.alloc_sbuf_tensor("X", [128, 16384], F32)
    XA = nc.alloc_sbuf_tensor("XA", [128, 16384], BF16)
    AR1 = nc.alloc_sbuf_tensor("AR1", [128, 16384], BF16)
    AR2 = nc.alloc_sbuf_tensor("AR2", [128, 4096], BF16)
    AR3 = nc.alloc_sbuf_tensor("AR3", [128, 8256], BF16)
    WA = nc.alloc_sbuf_tensor("WA", [128, 16384], BF16)
    CON = nc.alloc_sbuf_tensor("CON", [128, 2512], BF16)
    ROWS = nc.alloc_sbuf_tensor("ROWS", [128, 1536], F32)
    SPK = nc.alloc_sbuf_tensor("SPK", [128, 2560], BF16)
    OHQ = nc.alloc_sbuf_tensor("OHQ", [128, 256], BF16)
    NWB = nc.alloc_sbuf_tensor("NWB", [128, 1792], BF16)
    MISC = nc.alloc_sbuf_tensor("MISC", [128, 480], F32)
    FCON = nc.alloc_sbuf_tensor("FCON", [128, 8], F32)
    FCONSTS = [0.0, 1.0, 1e-5, -0.5, 1.0 / 32.0, -40.0, 0.5, 2.0]
    for j, v in enumerate(FCONSTS):
        nc.const_aps.register(F32, v, FCON[:, j:j + 1])

    AR1f = AR1.ap().bitcast(F32)

    def Xf(dt_, lo, n):
        return X[:, 2048 * dt_ + lo: 2048 * dt_ + lo + n]

    def XAf(dt_, lo, n):
        return XA[:, 2048 * dt_ + lo: 2048 * dt_ + lo + n]

    CON_EYE, CON_TRI, CON_L64, CON_ONEC, CON_ONEB, CON_TP = \
        0, 128, 256, 321, 324, 464
    ONEROW = CON[0:1, CON_ONEB:CON_ONEB + 128]
    ONECOL = CON[:, CON_ONEC:CON_ONEC + 1]

    def TPR(di, n):
        o = CON_TP + 128 * (4 * di + n)
        return CON[:, o:o + 128]

    def TSN(n):
        o = CON_TP + 128 * (12 + n)
        return CON[:, o:o + 128]

    def ROW0(lo, n):
        return ROWS[0:1, lo:lo + n]


    def NWT(pi, dt_):
        return NWB[:, 120 * pi + 15 * dt_: 120 * pi + 15 * dt_ + 15]

    def NBROW(pi):
        return NWB[0:1, 360 + 15 * pi: 375 + 15 * pi]

    RELBT = NWB[0:16, 440:472]
    NW_CROW = 480
    NW_REC = 512
    NW_TPS = [768, 1280]

    def B2S(lo, n):
        return SPK[0:16, 1536 + lo:1536 + lo + n]

    def OHT(lo, n):
        return SPK[0:16, 1024 + lo:1024 + lo + n]

    HSEL = NWB[0:16, 412:416]
    RBSEL = NWB[0:32, 408:412]

    MI_B = {"bq": 0, "bk": 8, "bv": 16, "bo": 24, "b1a": 32, "b2a": 40,
            "qb1": 48, "kb1": 56, "vb1": 64, "bqs": 448, "bks": 456,
            "bvs": 464}
    MI_F = 80
    MI_REP = 144
    MI_ONE = 160
    MI_BST = 288
    MI_OCF = 352
    MI_SM, MI_SV, MI_SS, MI_SR = 356, 372, 388, 404
    MISCb = MISC.ap().bitcast(BF16)
    MB_M, MB_R = 856, 872

    def WSl(r, kt, lo, n):
        o = 8192 * r + 1024 * kt + lo
        return WA[:, o:o + n]

    pb = [nc.alloc_psum_tensor(f"pb{i}", [128, 512], F32) for i in range(4)]
    pS1 = nc.alloc_psum_tensor("pS1", [128, 1024], F32)
    pS2 = nc.alloc_psum_tensor("pS2", [128, 1024], F32)

    OHB = 14336
    OH_SC, OH_SCR, OH_KV, OH_Q = OHB, OHB + 256, OHB + 512, -1

    LNSf = [1536, 2048]
    MSKf = [2560, 3072]
    STTf = [3584, 4096]
    BST = [13312, 13824]
    MF = [9216, 11264]

    for s_ in ("XL", "SPL", "WLC", "WA0", "WA1", "WB", "GC", "GF0", "GF1",
               "GT0", "GT1", "GS", "KL0", "KL1", "SB0", "SB1", "SO",
               "MM", "EV", "SE", "CC", "XS", "XR"):
        p.sem(s_)

    def g_dma(dst, src, sem="WLC"):
        return p.add("g", lambda e, S, d=dst, s=src: e.dma_start(out=d, in_=s), sem, 16)

    def y_dma(dst, src, sem="SPL"):
        return p.add("y", lambda e, S, d=dst, s=src: e.dma_start(out=d, in_=s), sem, 16)

    def mm(out, lhsT, rhs, start, stop, tp=None, sem="MM"):
        return p.add("t", lambda e, S, o=out, l=lhsT, r=rhs, a=start, b=stop, tp=tp:
                     e.matmul(o, l, r, start=a, stop=b, tile_position=tp), sem, 1)

    def mm0(out, lhsT, rhs, start, stop, tp=None):
        p.add("t", lambda e, S, o=out, l=lhsT, r=rhs, a=start, b=stop, tp=tp:
              e.matmul(o, l, r, start=a, stop=b, tile_position=tp))

    def ttrans(out, in_, ident):
        p.add("t", lambda e, S, o=out, i=in_, d=ident: e.transpose(o, i, d))

    def tnop(sem, inc=1):
        return p.add("t", lambda e, S: e.nop(), sem, inc)

    def vop(fn, sem="EV", inc=1):
        p.wait("v", "EV", p.ct["EV"])
        return p.add("v", fn, "EV", 1)

    def vop0(fn):
        p.wait("v", "EV", p.ct["EV"])
        p.add("v", fn, "EV", 1)

    def sop(fn, sem="SE", inc=1):
        p.wait("s", "SE", p.ct["SE"])
        return p.add("s", fn, "SE", 1)

    def wload(reg, name):
        src = din[name][:, :].rearrange("(a p) f -> p a f", p=128)
        dst = WA[:, 8192 * reg:8192 * reg + 8192].rearrange("p (a f) -> p a f", a=8)
        return (f"WA{reg}", g_dma(dst, src, sem=f"WA{reg}"))

    for dt_ in range(NT):
        y_dma(Xf(dt_, 0, 2048), din["xt"][128 * dt_:128 * (dt_ + 1), :], sem="XL")
    xload = p.ct["XL"]

    g_dma(CON[:, CON_EYE:CON_EYE + 256], din["eyetri"][:, :])
    g_dma(CON[0:128, CON_L64:CON_L64 + 65], din["l64"][:, :])
    g_dma(SPK[0:16, 0:1024], din["gmask"][:, :])
    for pi, pfx in enumerate(("q", "k", "v")):
        g_dma(NWB[:, 120 * pi:120 * pi + 120].rearrange("p (a n) -> p a n", a=8),
              din[f"{pfx}nwT"][:, :].rearrange("(a p) n -> p a n", p=128))
        g_dma(NBROW(pi), din[f"{pfx}nb"][:, :])
    g_dma(HSEL, din["hsel"][:, :])
    g_dma(RELBT, din["relbT"][:, :])
    con_done = p.ct["WLC"]

    y_dma(FCON[:, :], din["fconst"][:, :], sem="SPL")
    for nm, key in (("bq4", "bq"), ("bk4", "bk"), ("bv4", "bv"), ("bo4", "bo"),
                    ("b1a4", "b1a"), ("b2a4", "b2a")):
        y_dma(MISC[:, MI_B[key]:MI_B[key] + 8], din[nm][:, :], sem="SPL")
    for pfx, key in (("q", "qb1"), ("k", "kb1"), ("v", "vb1")):
        y_dma(MISC[:, MI_B[key]:MI_B[key] + 8], din[f"{pfx}b1f4"][:, :], sem="SPL")
    for nm, key in (("bq4s", "bqs"), ("bk4s", "bks"), ("bv4s", "bvs")):
        y_dma(MISC[:, MI_B[key]:MI_B[key] + 8], din[nm][:, :], sem="SPL")
    smalls = p.ct["SPL"]

    if debug:
        vop0(lambda e, S: e.memset(AR3[:, :], 0.0))
        vop0(lambda e, S: e.memset(CON[:, CON_TP:CON_TP + 2048], 0.0))
        vop0(lambda e, S: e.memset(OHQ[:, :], 0.0))
        vop0(lambda e, S: e.memset(AR1[:, :], 0.0))
        vop0(lambda e, S: e.memset(AR2[:, :], 0.0))
        vop0(lambda e, S: e.memset(XA[:, :], 0.0))
    vop0(lambda e, S: e.memset(CON[:, CON_ONEC:CON_ONEC + 1], 1.0))
    vop0(lambda e, S: e.memset(CON[:, CON_ONEB:CON_ONEB + 128], 1.0))
    vop0(lambda e, S: e.memset(MISC[:, MI_ONE:MI_ONE + 128], 1.0))
    ones_v = vop(lambda e, S: e.memset(MISC[:, MI_OCF:MI_OCF + 1], 1.0))

    wf = {}
    wf["kw1"] = wload(0, "kw1c")
    wf["kw2"] = wload(1, "kw2c")

    def build_tprime():
        p.wait("t", "WLC", con_done)
        p.wait("t", "EV", ones_v)
        m1 = mm(pS2[0:32, 0:4], RELBT, HSEL, True, True)
        p.wait("v", "MM", m1)
        e1 = vop(lambda e, S: e.tensor_copy(RBSEL, pS2[0:32, 0:4]))
        p.wait("g", "EV", e1)
        p.add("g", lambda e, S: e.dma_start(out=NWB[0:1, NW_CROW:NW_CROW + 4],
                                            in_=RBSEL[15:16, :]), "GC", 16)
        c2 = p.add("g", lambda e, S: e.dma_start(out=NWB[0:1, NW_CROW + 4:NW_CROW + 8],
                                                 in_=RBSEL[31:32, :]), "GC", 16)
        p.wait("v", "GC", c2)
        e2 = vop(lambda e, S: e.tensor_sub(NWB[0:1, NW_CROW + 8:NW_CROW + 12],
                                           NWB[0:1, NW_CROW + 4:NW_CROW + 8],
                                           NWB[0:1, NW_CROW:NW_CROW + 4]))
        p.wait("t", "EV", e2)
        last = None
        for ni in range(4):
            mm0(pS2[0:128, 8 + ni:9 + ni], ONEROW,
                NWB[0:1, NW_CROW + ni:NW_CROW + ni + 1], True, True)
            last = mm(pS2[0:128, 12 + ni:13 + ni], ONEROW,
                      NWB[0:1, NW_CROW + 8 + ni:NW_CROW + 9 + ni], True, True)
        p.wait("v", "MM", last)
        er = vop(lambda e, S: e.tensor_copy(MISC[:, MI_REP:MI_REP + 8],
                                            pS2[0:128, 8:16]))
        for ni in range(4):
            vop(lambda e, S, ni=ni: e.tensor_scalar(
                TSN(ni), CON[:, CON_TRI:CON_TRI + 128],
                MISC[:, MI_REP + 4 + ni:MI_REP + 5 + ni], None, op0=AL.mult))
        return er

    def tprime_rtiles(gate_ev):
        ps1_free = [gate_ev, gate_ev]
        tps_free = [0, 0]
        mf_mm = [0, 0]
        for di in range(3):
            for half in range(8):
                hb = half % 2
                p.wait("g", "MM", mf_mm[hb])
                ld = g_dma(AR1[0:32, MF[hb]:MF[hb] + 2048],
                           din["mflat"][32 * di:32 * di + 32,
                                        2048 * half:2048 * half + 2048],
                           sem=f"GF{hb}")
                p.wait("t", f"GF{hb}", ld)
                for cq in range(4):
                    h2 = cq % 2
                    p.wait("t", "EV", ps1_free[h2])
                    m_ = mm(pS1[0:4, 512 * h2:512 * h2 + 512], RBSEL,
                            AR1[0:32, MF[hb] + 512 * cq:MF[hb] + 512 * cq + 512],
                            True, True)
                    p.wait("v", "MM", m_)
                    p.wait("v", f"GT{h2}", tps_free[h2])
                    ps1_free[h2] = vop(lambda e, S, h2=h2: e.tensor_copy(
                        NWB[0:4, NW_TPS[h2]:NW_TPS[h2] + 512],
                        pS1[0:4, 512 * h2:512 * h2 + 512]))
                    p.wait("g", "EV", ps1_free[h2])
                    tps_free[h2] = g_dma(
                        tpscr[4 * di:4 * di + 4,
                              2048 * half + 512 * cq:2048 * half + 512 * cq + 512],
                        NWB[0:4, NW_TPS[h2]:NW_TPS[h2] + 512], sem=f"GT{h2}")
                mf_mm[hb] = p.ct["MM"]
            p.wait("g", "GT0", p.ct["GT0"])
            p.wait("g", "GT1", p.ct["GT1"])
            for ni in range(4):
                g_dma(TPR(di, ni),
                      tpscr[4 * di + ni:4 * di + ni + 1, :]
                      .rearrange("o (pp f) -> (o pp) f", pp=128), sem="GS")
        return p.ct["EV"], p.ct["GS"]

    if 'tp' in parts:
        tp_er = build_tprime()
    else:
        tp_er = p.ct["EV"]
    tp_state = {"ev": p.ct["EV"], "gs": p.ct["GS"]}

    def emit_ln(eps, write_xa, gate_ev, src=None):
        src = src or Xf
        p.wait("s", "XL", xload)
        p.wait("s", "SPL", smalls)
        p.wait("s", "EV", p.ct["EV"])
        p.wait("s", "MM", p.ct["MM"])
        p.wait("t", "SPL", smalls)
        p.wait("t", "WLC", con_done)
        p.wait("t", "XL", xload)
        p.wait("t", "EV", ones_v)
        p.wait("t", "EV", gate_ev)
        SQR = [pS1[:, 0:512], pS1[:, 512:1024], pS2[:, 0:512], pS2[:, 512:1024]]
        sqfree = {}
        for st in range(4):
            for dt_ in range(NT):
                g_ = 8 * st + dt_
                slot = LNSf[g_ % 2]
                p.wait("s", "MM", sqfree.get(g_ - 2, 0))
                sq = sop(lambda e, S, dt_=dt_, st=st, slot=slot: e.activation(
                    AR1f[:, slot:slot + 512], src(dt_, 512 * st, 512), AF.Square))
                p.wait("t", "SE", sq)
                for c4 in range(4):
                    mm0(pb[c4][:, st:st + 1], src(dt_, 512 * st + 128 * c4, 128),
                        MISC[:, MI_OCF:MI_OCF + 1], dt_ == 0, dt_ == NT - 1)
                    fn_ = mm0 if c4 < 3 else (lambda *a, **k: mm(*a, **k))
                    fn_(SQR[c4][:, st:st + 1],
                        AR1f[:, slot + 128 * c4:slot + 128 * c4 + 128],
                        MISC[:, MI_OCF:MI_OCF + 1], dt_ == 0, dt_ == NT - 1)
                sqfree[g_] = p.ct["MM"]
        lastmm = p.ct["MM"]
        p.wait("v", "MM", lastmm)
        for c4 in range(4):
            vop0(lambda e, S, c4=c4: e.tensor_scalar(
                MISC[:, MI_SM + c4:MI_SM + 13 + c4:4], pb[c4][:, 0:4],
                1.0 / 1024, None, op0=AL.mult))
            vop0(lambda e, S, c4=c4: e.tensor_scalar(
                MISC[:, MI_SV + c4:MI_SV + 13 + c4:4], SQR[c4][:, 0:4],
                1.0 / 1024, None, op0=AL.mult))
        vop0(lambda e, S: e.tensor_mul(
            MISC[:, MI_SS:MI_SS + 16], MISC[:, MI_SM:MI_SM + 16],
            MISC[:, MI_SM:MI_SM + 16]))
        stats_ev = vop(lambda e, S: e.tensor_sub(
            MISC[:, MI_SV:MI_SV + 16], MISC[:, MI_SV:MI_SV + 16],
            MISC[:, MI_SS:MI_SS + 16]))
        p.wait("s", "EV", stats_ev)
        sop(lambda e, S: e.activation(MISC[:, MI_SS:MI_SS + 16],
                                      MISC[:, MI_SV:MI_SV + 16], AF.Ln,
                                      bias=float(eps)))
        rstd = sop(lambda e, S: e.activation(MISC[:, MI_SR:MI_SR + 16],
                                             MISC[:, MI_SS:MI_SS + 16], AF.Exp,
                                             scale=-0.5))
        p.wait("v", "SE", rstd)
        mr16 = vop(lambda e, S: e.tensor_copy(MISCb[:, MB_M:MB_M + 16],
                                              MISC[:, MI_SM:MI_SM + 16]))
        mr16 = vop(lambda e, S: e.tensor_copy(MISCb[:, MB_R:MB_R + 16],
                                              MISC[:, MI_SR:MI_SR + 16]))
        if not write_xa:
            return mr16
        p.wait("t", "EV", mr16)
        apply_ev = mr16
        for st in range(4):
            lo = 512 * st
            bk = (st % 2) * 2
            p.wait("t", "EV", apply_ev)
            for c4 in range(4):
                c = 4 * st + c4
                mm0(pb[bk][0:1, 128 * c4:128 * c4 + 128],
                    MISCb[:, MB_M + c:MB_M + c + 1],
                    CON[:, CON_EYE:CON_EYE + 128], True, True)
                fn_ = mm0 if c4 < 3 else (lambda *a, **k: mm(*a, **k))
                fn_(pb[bk + 1][0:1, 128 * c4:128 * c4 + 128],
                    MISCb[:, MB_R + c:MB_R + c + 1],
                    CON[:, CON_EYE:CON_EYE + 128], True, True)
            p.wait("v", "MM")
            vop0(lambda e, S, bk=bk: e.tensor_copy(ROW0(0, 512), pb[bk][0:1, 0:512]))
            rowev = vop(lambda e, S, bk=bk: e.tensor_copy(ROW0(512, 512),
                                                          pb[bk + 1][0:1, 0:512]))
            p.wait("t", "EV", rowev)
            mm0(pb[bk][:, 0:512], MISC[0:1, MI_ONE:MI_ONE + 128],
                ROW0(0, 512), True, True)
            mrr = mm(pb[bk + 1][:, 0:512], MISC[0:1, MI_ONE:MI_ONE + 128],
                     ROW0(512, 512), True, True)
            p.wait("v", "MM", mrr)
            for dt_ in range(NT):
                vop0(lambda e, S, dt_=dt_, lo=lo, bk=bk: e.tensor_sub(
                    AR1f[:, STTf[dt_ % 2]:STTf[dt_ % 2] + 512],
                    src(dt_, lo, 512), pb[bk][:, 0:512]))
                apply_ev = vop(lambda e, S, dt_=dt_, lo=lo, bk=bk: e.tensor_mul(
                    XAf(dt_, lo, 512),
                    AR1f[:, STTf[dt_ % 2]:STTf[dt_ % 2] + 512],
                    pb[bk + 1][:, 0:512]))
        return apply_ev

    def fff_scores_traversal(pi, oh_dst, psum_ready_ev):
        p.wait("t", "WLC", con_done)
        p.wait("t", "EV", psum_ready_ev)
        for c in range(16):
            for dt_ in range(NT):
                mm0(pS2[0:128, 15 * c:15 * c + 15], XAf(dt_, 128 * c, 128),
                    NWT(pi, dt_), dt_ == 0, False)
            last = mm(pS2[0:128, 15 * c:15 * c + 15], ONEROW, NBROW(pi), False, True)
        p.wait("v", "MM", last)
        sc_ev = vop(lambda e, S: e.tensor_copy(AR1[:, OH_SC:OH_SC + 240],
                                               pS2[0:128, 0:240]))

        def sl(i):
            return AR1[:, OH_SCR + 16 * i:OH_SCR + 16 * i + 16]

        def scc(j):
            return AR1[:, OH_SC + j: OH_SC + j + 15 * 16:15]

        B = [sl(i) for i in range(4)]
        E_ = [sl(4 + i) for i in range(4)]
        TMP = [sl(8 + i) for i in range(8)]
        P2 = [AR1[:, OH_SC + 16 * i:OH_SC + 16 * i + 16] for i in range(4)]
        P4 = [AR1[:, OH_SC + 64 + 16 * i:OH_SC + 64 + 16 * i + 16] for i in range(8)]

        def ts1(out, in_, s, op):
            vop0(lambda e, S, o=out, i=in_, s=s, op=op:
                 e.tensor_single_scalar(o, i, s, op=op))

        MXT = sl(24)

        def selv(out, mask, tt, ff):
            vop0(lambda e, S, o=MXT, a=tt, b=ff: e.tensor_sub(o, a, b))
            vop0(lambda e, S, o=MXT, m=mask: e.tensor_mul(o, o, m))
            vop0(lambda e, S, o=out, b=ff: e.tensor_add(o, b, MXT))

        def mulv(out, a, b):
            vop0(lambda e, S, o=out, a=a, b=b: e.tensor_mul(o, a, b))

        ts1(B[0], scc(0), 0.0, AL.is_ge)
        selv(TMP[0], B[0], scc(2), scc(1))
        ts1(B[1], TMP[0], 0.0, AL.is_ge)
        selv(TMP[1], B[1], scc(4), scc(3))
        selv(TMP[2], B[1], scc(6), scc(5))
        selv(TMP[3], B[0], TMP[2], TMP[1])
        ts1(B[2], TMP[3], 0.0, AL.is_ge)
        selv(TMP[4], B[2], scc(8), scc(7))
        selv(TMP[5], B[2], scc(10), scc(9))
        selv(TMP[6], B[2], scc(12), scc(11))
        selv(TMP[7], B[2], scc(14), scc(13))
        selv(TMP[1], B[1], TMP[5], TMP[4])
        selv(TMP[2], B[1], TMP[7], TMP[6])
        selv(TMP[0], B[0], TMP[2], TMP[1])
        ts1(B[3], TMP[0], 0.0, AL.is_ge)
        for i in range(4):
            vop0(lambda e, S, i=i: e.tensor_scalar(
                E_[i], B[i], -1.0, 1.0, op0=AL.mult, op1=AL.add))
        mulv(P2[0], E_[0], E_[1]); mulv(P2[1], E_[0], B[1])
        mulv(P2[2], B[0], E_[1]); mulv(P2[3], B[0], B[1])
        for a in range(4):
            mulv(P4[2 * a], P2[a], E_[2]); mulv(P4[2 * a + 1], P2[a], B[2])
        for l in range(16):
            src_ = P4[l // 2]
            bit = E_[3] if l % 2 == 0 else B[3]
            if oh_dst >= 0:
                dst = AR1[:, oh_dst + l: oh_dst + l + 241:16]
            else:
                dst = OHQ[:, l: l + 241:16]
            vop0(lambda e, S, dst=dst, src_=src_, bit=bit: e.tensor_mul(
                dst, src_, bit))
        done = p.add("v", lambda e, S: e.engine_nop(), "EV", 1)
        return done, sc_ev

    def fff_ohT(oh_dst, chunks, dst_lo, gate_ev):
        pS2b = pS2.ap().bitcast(BF16)
        slots = [pS2b[0:16, 0:128], pS2b[0:16, 1024:1152]]
        slot_free = [gate_ev, gate_ev]
        ev = gate_ev
        for idx, c in enumerate(chunks):
            sl_ = idx % 2
            p.wait("t", "EV", slot_free[sl_])
            if oh_dst >= 0:
                src_oh = AR1[:, oh_dst + 16 * c:oh_dst + 16 * c + 16]
            else:
                src_oh = OHQ[:, 16 * c:16 * c + 16]
            p.add("t", lambda e, S, src_oh=src_oh, sl_=sl_, slots=slots:
                  e.transpose(slots[sl_], src_oh,
                              CON[:, CON_EYE:CON_EYE + 128]), "MM", 1)
            mv = p.ct["MM"]
            p.wait("v", "MM", mv)
            ev = vop(lambda e, S, idx=idx, sl_=sl_, slots=slots, dst_lo=dst_lo:
                     e.tensor_copy(OHT(dst_lo + 128 * idx, 128), slots[sl_]))
            slot_free[sl_] = ev
        return ev

    def fff_hidden(st_lo, n, b1key, oht_lo, pbfree, maskfree, h1s=512):
        for ht in range(8):
            bk = ht % 2
            p.wait("t", "EV", pbfree[bk])
            for dt_ in range(NT):
                mm0(pb[bk][:, 0:n], WSl(0, dt_, 128 * ht, 128),
                    XAf(dt_, st_lo, n), dt_ == 0, dt_ == NT - 1)
            p.wait("t", "SE", maskfree[bk])
            mk = mm(pb[2 + bk][:, 0:n],
                    SPK[0:16, 128 * ht:128 * ht + 128],
                    OHT(oht_lo, n), True, True)
            p.wait("s", "MM", mk)
            maskfree[bk] = sop(lambda e, S, bk=bk, n=n: e.activation(
                AR1f[:, MSKf[bk]:MSKf[bk] + n], pb[2 + bk][:, 0:n], AF.Copy))
            p.wait("v", "SE", maskfree[bk])
            p.wait("v", "MM", mk)
            vop0(lambda e, S, bk=bk, n=n, b1key=b1key, ht=ht: e.scalar_tensor_tensor(
                AR1f[:, STTf[bk]:STTf[bk] + n], pb[bk][:, 0:n],
                MISC[:, MI_B[b1key] + ht:MI_B[b1key] + ht + 1],
                AR1f[:, MSKf[bk]:MSKf[bk] + n], op0=AL.add, op1=AL.mult))
            pbfree[bk] = vop(lambda e, S, bk=bk, n=n, ht=ht, h1s=h1s: e.tensor_scalar(
                AR2[:, h1s * ht:h1s * ht + n],
                AR1f[:, STTf[bk]:STTf[bk] + n], 0.0, None, op0=AL.max))
        return pbfree, maskfree

    def emit_ffn(li, gate_ev):
        if 'ln' not in parts:
            return gate_ev
        ln_ev = emit_ln(1e-5, True, max(gate_ev, tp_er))
        if li == 0 and 'tp' in parts:
            tp_state["ev"], tp_state["gs"] = tprime_rtiles(ln_ev)
        if 'kv' not in parts:
            p.wait("t", "EV", tp_state["ev"])
            p.wait("t", "GS", tp_state["gs"])
            return ln_ev

        pbfree = [ln_ev, ln_ev, ln_ev, ln_ev]
        maskfree = [0, 0]
        scores_gate = tp_er if li == 0 else gate_ev
        bounce_last = 0
        bst_free = [0, 0]
        vb2_pre = [None]
        for pi, pfx, b1key in ((1, "k", "kb1"), (2, "v", "vb1")):
            if vb2_pre[0] is None:
                p.wait("g", "MM", p.ct["MM"])
                b2load = g_dma(B2S(0, 1024), din[f"{pfx}b2"][:, :], sem="WB")
            else:
                b2load = vb2_pre[0]
            trav, sc_ev = fff_scores_traversal(pi, OH_KV, max(scores_gate, ln_ev))
            p.wait("t", "EV", trav)
            p.wait("t", *wf[f"{pfx}w1"])
            p.wait("t", *wf[f"{pfx}w2"])
            p.wait("t", "WB", b2load)
            ofree = [pbfree[0], pbfree[1]]
            ohgate = sc_ev
            for st in range(4):
                lo = 512 * st
                oht_ev = fff_ohT(OH_KV, range(4 * st, 4 * st + 4), 0, ohgate)
                ohgate = oht_ev
                p.wait("t", "EV", oht_ev)
                pbfree, maskfree = fff_hidden(lo, 512, b1key, 0, pbfree, maskfree)
                p.wait("t", "EV", max(pbfree[0], pbfree[1]))
                ofree = [max(ofree[0], pbfree[0]), max(ofree[1], pbfree[1])]
                if pi == 1:
                    jobs = [(dc,) for dc in range(8)]
                else:
                    jobs = [(tc, ds) for tc in range(4) for ds in range(2)]
                for j, job in enumerate(jobs):
                    h2 = j % 2
                    p.wait("t", "EV", ofree[h2])
                    if pi == 1:
                        dc = job[0]
                        for hti in range(NT):
                            mm0(pb[h2][:, 0:512], WSl(1, hti, 128 * dc, 128),
                                AR2[:, 512 * hti:512 * hti + 512], hti == 0, False)
                        lastm = mm(pb[h2][:, 0:512], B2S(128 * dc, 128),
                                   OHT(0, 512), False, True)
                    else:
                        tc, ds = job
                        for hti in range(NT):
                            mm0(pb[h2][:, 0:512],
                                AR2[:, 512 * hti + 128 * tc:512 * hti + 128 * tc + 128],
                                WSl(1, hti, 512 * ds, 512), hti == 0, False)
                        lastm = mm(pb[h2][:, 0:512], OHT(128 * tc, 128),
                                   B2S(512 * ds, 512), False, True)
                    p.wait("v", "MM", lastm)
                    p.wait("v", f"SB{h2}", bst_free[h2])
                    ev = vop(lambda e, S, h2=h2: e.tensor_copy(
                        AR1[:, BST[h2]:BST[h2] + 512], pb[h2][:, 0:512]))
                    ofree[h2] = ev
                    pbfree[h2] = ev
                    p.wait("y", "EV", ev)
                    if pi == 1:
                        dc = job[0]
                        bst_free[h2] = y_dma(kb[128 * dc:128 * dc + 128, lo:lo + 512],
                                             AR1[:, BST[h2]:BST[h2] + 512],
                                             sem=f"SB{h2}")
                    else:
                        tc, ds = job
                        bst_free[h2] = y_dma(
                            vb[lo + 128 * tc:lo + 128 * tc + 128, 512 * ds:512 * ds + 512],
                            AR1[:, BST[h2]:BST[h2] + 512], sem=f"SB{h2}")
            scores_gate = sc_ev
            p.wait("g", "MM", p.ct["MM"])
            if pi == 1:
                wf["vw1"] = wload(0, "vw1c")
                wf["vw2"] = wload(1, "vw2c")
                vb2_pre[0] = g_dma(B2S(0, 1024), din["vb2"][:, :], sem="WB")
                if 'gather' in parts:
                    p.wait("g", "SB0", p.ct["SB0"])
                    p.wait("g", "SB1", p.ct["SB1"])
                    p.add("g", lambda e, S: e.collective_compute(
                        "AllGather", AL.bypass,
                        replica_groups=[[0, 1, 2, 3], [4, 5, 6, 7]],
                        ins=[kb[:, :]], outs=[kg[:, :]]), "CC", 1)
            else:
                wf["qw1"] = wload(0, "qw1c")
                wf["qw2"] = wload(1, "qw2c")
                wf["qb2"] = ("WB", g_dma(B2S(0, 1024), din["qb2"][:, :], sem="WB"))

        if 'gather' not in parts:
            return pbfree[0]
        p.wait("g", "SB0", p.ct["SB0"])
        p.wait("g", "SB1", p.ct["SB1"])
        cc = p.add("g", lambda e, S: e.collective_compute(
            "AllGather", AL.bypass,
            replica_groups=[[0, 1, 2, 3], [4, 5, 6, 7]],
            ins=[vb[:, :]], outs=[vg[:, :]]), "CC", 1)

        travq, sc_ev_q = fff_scores_traversal(0, OH_Q, max(scores_gate, ln_ev))
        p.wait("t", "EV", travq)
        p.wait("t", *wf["qw1"])
        p.wait("t", *wf["qw2"])
        p.wait("t", *wf["qb2"])
        if li == 0:
            p.wait("t", "EV", tp_state["ev"])
            p.wait("t", "GS", tp_state["gs"])
            p.wait("s", "GS", tp_state["gs"])
            p.wait("s", "GF0", p.ct["GF0"])
            p.wait("s", "GF1", p.ct["GF1"])

        if 'flash' not in parts:
            return pbfree[0]
        p.wait("y", "CC", cc)
        p.wait("s", "SB0", p.ct["SB0"])
        p.wait("s", "SB1", p.ct["SB1"])
        span_free = {0: span_free_g["c"], 1: span_free_g["c"]}

        def span_load(which, sp, half):
            p.wait("y", "MM", span_free[half])
            if which == "k":
                r_, lo_ = sp // 4, 512 * (sp % 4)
                src = kg[1024 * r_:1024 * r_ + 1024, lo_:lo_ + 512] \
                    .rearrange("(a p) f -> p a f", p=128)
                na = 8
            else:
                g0 = 512 * sp
                src = vg[g0:g0 + 512, :].rearrange("(a p) f -> p a f", p=128)
                na = 4
            dst = AR3[:, 4096 * half:4096 * half + 4096] \
                .rearrange("p (a f) -> p a f", a=na)
            return y_dma(dst, src, sem=f"KL{half}")

        AVR = [pb[0][:, 0:256], pb[1][:, 0:256], pb[2][:, 0:256], pb[3][:, 0:256],
               pS1[:, 0:256], pS1[:, 512:768], pS2[:, 0:256], pS2[:, 512:768]]
        SUMR = pb[0][0:1, 384:640] if False else pb[0][0:1, 256:512]
        qofree = {0: 0, 1: 0}
        av_ev = 0
        ohgate = sc_ev_q
        for s in range(NSTRIP_F):
            qlo = SW_F * s
            p.wait("t", "EV", av_ev)
            oht_ev = fff_ohT(OH_Q, [2 * s, 2 * s + 1], 0, max(ohgate, av_ev))
            ohgate = oht_ev
            p.wait("t", "EV", oht_ev)
            fff_hidden(qlo, SW_F, "qb1", 0, pbfree, maskfree, h1s=256)
            p.wait("t", "EV", max(pbfree[0], pbfree[1]))
            for dc in range(8):
                h2 = dc % 2
                p.wait("t", "SE", qofree[h2])
                for hti in range(NT):
                    mm0(pS1[:, 512 * h2:512 * h2 + 256], WSl(1, hti, 128 * dc, 128),
                        AR2[:, 256 * hti:256 * hti + 256], hti == 0, False)
                lastm = mm(pS1[:, 512 * h2:512 * h2 + 256],
                           B2S(128 * dc, 128), OHT(0, 256), False, True)
                p.wait("s", "MM", lastm)
                qofree[h2] = sop(lambda e, S, h2=h2, dc=dc: e.activation(
                    AR2[:, 2048 + 256 * dc:2048 + 256 * dc + 256],
                    pS1[:, 512 * h2:512 * h2 + 256], AF.Copy, scale=1.0 / 32.0))
            fq_se = qofree[1]
            p.wait("t", "SE", fq_se)
            p.wait("t", "EV", p.ct["EV"])
            p.wait("t", "SE", p.ct["SE"])
            exp_se = {}
            for sp in range(NSPAN):
                half = sp % 2
                ld = span_load("k", sp, half)
                p.wait("t", f"KL{half}", ld)
                pS = pS1 if half == 0 else pS2
                p.wait("t", "SE", exp_se.get(sp - 2, 0))
                for kc in range(4):
                    for dt_ in range(NT):
                        fn_ = mm if (kc == 3 and dt_ == NT - 1) else mm0
                        fn_(pS[:, 256 * kc:256 * kc + 256],
                            AR3[:, 4096 * half + 512 * dt_ + 128 * kc:
                                4096 * half + 512 * dt_ + 128 * kc + 128],
                            AR2[:, 2048 + 256 * dt_:2048 + 256 * dt_ + 256],
                            dt_ == 0, dt_ == NT - 1)
                mmv = p.ct["MM"]
                span_free[half] = mmv
                p.wait("s", "MM", mmv)
                exp_se[sp] = sop(lambda e, S, sp=sp, pS=pS: e.activation(
                    AR1[:, 1024 * sp:1024 * sp + 1024], pS[:, 0:1024], AF.Exp))
            p.wait("t", "SE", exp_se[NSPAN - 1])
            for sp in range(NSPAN):
                half = sp % 2
                ld = span_load("v", sp, half)
                p.wait("t", f"KL{half}", ld)
                for kc in range(4):
                    first = (sp == 0 and kc == 0)
                    lastf = (sp == NSPAN - 1 and kc == 3)
                    for dc in range(8):
                        fn_ = mm if (kc == 3 and dc == 7) else mm0
                        fn_(AVR[dc],
                            AR3[:, 4096 * half + 1024 * kc + 128 * dc:
                                4096 * half + 1024 * kc + 128 * dc + 128],
                            AR1[:, 1024 * sp + 256 * kc:1024 * sp + 256 * kc + 256],
                            first, lastf)
                span_free[half] = p.ct["MM"]
            span_free_g["c"] = max(span_free.values())
            for K in range(64):
                fn_ = mm if K == 63 else mm0
                fn_(SUMR, ONECOL, AR1[:, 256 * K:256 * K + 256], K == 0, K == 63)
            msum = p.ct["MM"]
            p.wait("v", "MM", msum)
            vop0(lambda e, S: e.reciprocal(ROW0(1024, 256), SUMR))
            vop0(lambda e, S: e.tensor_scalar(ROW0(1280, 256), ROW0(1024, 256),
                                              0.5, None, op0=AL.mult))
            rec_ev = vop(lambda e, S: e.tensor_copy(
                NWB[0:1, NW_REC:NW_REC + 256], ROW0(1280, 256)))
            p.wait("t", "EV", rec_ev)
            mrep = mm(pb[1][:, 256:512], ONEROW, NWB[0:1, NW_REC:NW_REC + 256],
                      True, True)
            p.wait("v", "MM", mrep)
            vop0(lambda e, S: e.tensor_copy(AR1f[:, 0:256], pb[1][:, 256:512]))
            for dc in range(8):
                vop0(lambda e, S, dc=dc: e.tensor_mul(
                    AR1f[:, 256:512], AVR[dc], AR1f[:, 0:256]))
                av_ev = vop(lambda e, S, dc=dc, qlo=qlo: e.tensor_add(
                    Xf(dc, qlo, 256), Xf(dc, qlo, 256), AR1f[:, 256:512]))
            pbfree[0] = pbfree[1] = pbfree[2] = pbfree[3] = av_ev
        return av_ev


    Xb = X.ap().bitcast(BF16)
    QMT, KMT, O1C = 0, 8192, 16384
    SW_A = 256
    eyeT = CON[:, CON_EYE:CON_EYE + 64]
    eyeB = CON[:, CON_EYE + 64:CON_EYE + 128]
    SCRA = NWB[64:128, 768:1280]
    SCRF = NWB[:, 768:1280]
    REPS = NWB[:, 1280:1536]
    BSB = NWB[0:64, 1536:1600]
    BSTB = NWB[0:64, 1600:1664]
    RECROW = NWB[64:65, 512:768]

    def Yf(slot, dt_, lo, n):
        o = 12288 + 512 * dt_ + lo
        return X[:, o:o + n]

    def emit_attn(gate_ev):
        trunc = [False]
        ln_ev = emit_ln(1e-5, True, gate_ev)
        p.wait("y", "EV", ln_ev)
        for dt_ in range(NT):
            y_dma(xspill[128 * dt_:128 * (dt_ + 1), :], Xf(dt_, 0, 2048), sem="XS")
        xs_done = p.ct["XS"]
        p.wait("v", "XS", xs_done)
        ev0 = vop(lambda e, S: e.memset(NWB[0:64, 768:1280], 0.0))
        ev0 = vop(lambda e, S: e.memset(Xb[:, O1C:O1C + 8192], 0.0))
        wreg = {"i": 0}
        wuse = {0: p.ct["MM"], 1: p.ct["MM"]}

        def next_wload(name):
            r = wreg["i"] % 2
            wreg["i"] += 1
            p.wait("g", "MM", wuse[r])
            p.wait("g", f"WA{r}", p.ct[f"WA{r}"])
            return wload(r, name), r

        state = {"shift": [ev0, ev0], "pb23": [ev0, ev0], "ps1b": [ev0, ev0],
                 "av": ev0, "exp": {}, "qo": 0}

        def proj(nn, wl, reg, bkey, dst_base, dst_tensor):
            p.wait("t", *wl)
            for i in range(8):
                bk = 2 + (i % 2)
                p.wait("t", "EV", state["pb23"][i % 2])
                p.wait("t", "SE", state.get("pb23se", [0, 0])[i % 2])
                for dt_ in range(NT):
                    fn_ = mm if dt_ == NT - 1 else mm0
                    fn_(pb[bk][:, 0:512], WSl(reg, dt_, 128 * i, 128),
                        XAf(dt_, 512 * nn, 512), dt_ == 0, dt_ == NT - 1)
                mmv = p.ct["MM"]
                p.wait("s", "MM", mmv)
                se1 = sop(lambda e, S, bk=bk: e.activation(
                    SCRA, pb[bk][64:128, 0:512], AF.Copy))
                state.setdefault("pb23se", [0, 0])[i % 2] = se1
                p.wait("v", "MM", mmv)
                ev1 = vop(lambda e, S, bk=bk, i=i, bkey=bkey, dst_base=dst_base,
                          dst_tensor=dst_tensor: e.tensor_scalar(
                    dst_tensor[0:64, dst_base + 2 * i:dst_base + 2 * i + 8177:16],
                    pb[bk][0:64, 0:512],
                    MISC[0:64, MI_B[bkey] + i:MI_B[bkey] + i + 1], None,
                    op0=AL.add))
                p.wait("t", "SE", se1)
                h2 = i % 2
                p.wait("t", "EV", state["ps1b"][h2])
                ms = mm(pS1[0:64, 512 * h2:512 * h2 + 512], eyeB, SCRF,
                        True, True)
                p.wait("v", "MM", ms)
                ev2 = vop(lambda e, S, i=i, h2=h2, bkey=bkey, dst_base=dst_base,
                          dst_tensor=dst_tensor: e.tensor_scalar(
                    dst_tensor[0:64, dst_base + 2 * i + 1:dst_base + 2 * i + 8178:16],
                    pS1[0:64, 512 * h2:512 * h2 + 512],
                    MISC[0:64, MI_B[bkey + "s"] + i:MI_B[bkey + "s"] + i + 1],
                    None, op0=AL.add))
                state["ps1b"][h2] = ev2
                state["pb23"][i % 2] = ev1
            wuse[reg] = p.ct["MM"]
            return p.ct["EV"]


        def score_shift(nn):
            p.wait("t", "EV", p.ct["EV"])
            p.wait("t", "SE", p.ct["SE"])
            p.wait("s", "EV", p.ct["EV"])
            p.wait("s", "MM", p.ct["MM"])
            vop0(lambda e, S: e.memset(Xb[64:65, KMT:KMT + 8192], -1.0))
            expse = {}
            dense_ = {}

            def emit_ones(bb):
                for k4 in range(4):
                    fn_ = mm if (bb == 15 and k4 == 3) else mm0
                    fn_(pb[0][0:1, 0:256], ONECOL,
                        AR1[:, 1024 * (bb % 4) + 256 * k4:
                            1024 * (bb % 4) + 256 * k4 + 256],
                        bb == 0 and k4 == 0, bb == 15 and k4 == 3)

            for s_ in range(32):
                ilo = SW_A * s_
                p.wait("t", "SE", dense_.get("ln", 0))
                for b_ in range(16):
                    pS = pS1 if b_ % 2 == 0 else pS2
                    p.wait("t", "SE", expse.get(b_ - 2, 0))
                    for k4 in range(4):
                        kc = 4 * b_ + k4
                        fn_ = mm if k4 == 3 else mm0
                        fn_(pS[:, 256 * k4:256 * k4 + 256],
                            Xb[0:64, KMT + 128 * kc:KMT + 128 * kc + 128],
                            Xb[0:64, QMT + ilo:QMT + ilo + SW_A],
                            True, True)
                    if b_ >= 2:
                        emit_ones(b_ - 2)
                    mmv = p.ct["MM"]
                    p.wait("s", "MM", mmv)
                    expse[b_] = sop(lambda e, S, b_=b_, pS=pS: e.activation(
                        AR1[:, 1024 * (b_ % 4):1024 * (b_ % 4) + 1024],
                        pS[:, 0:1024], AF.Exp, scale=0.125))
                for bb in (14, 15):
                    p.wait("t", "SE", expse[bb])
                    emit_ones(bb)
                smm = p.ct["MM"]
                p.wait("s", "MM", smm)
                p.wait("s", "EV", p.ct["EV"])
                lnse = sop(lambda e, S: e.activation(ROWS[0:1, 0:256],
                                                     pb[0][0:1, 0:256],
                                                     AF.Ln, bias=0.0))
                dense_["ln"] = lnse
                p.wait("v", "SE", lnse)
                cev = vop(lambda e, S, s_=s_: e.tensor_scalar(
                    AR1[0:1, 8192 + 256 * s_:8192 + 256 * s_ + 256],
                    ROWS[0:1, 0:256], 8.0, -76.0, op0=AL.mult, op1=AL.add))
                expse = {}
            p.wait("g", "EV", cev)
            cdma = g_dma(Xb[64:65, QMT:QMT + 8192], AR1[0:1, 8192:16384],
                         sem="GC")
            p.wait("t", "GC", cdma)
            p.wait("t", "EV", cev)
            p.wait("t", "SE", p.ct["SE"])

        def bias_pass(nn, ihalf, gate):
            o2g = gate
            for ii in range(32):
                I = 32 * ihalf + ii
                h2 = ii % 2
                p.wait("t", "EV", state["ps1b"][h2])
                first = True
                for di, dlt in ((0, -1), (1, 0), (2, 1)):
                    J = I + dlt
                    if J < 0 or J > 63:
                        continue
                    mm0(pS1[0:64, 512 * h2:512 * h2 + 128], AR3[:, 65 * J:65 * J + 64],
                        TPR(di, nn), first, False)
                    first = False
                lmm = mm(pS1[0:64, 512 * h2:512 * h2 + 128], AR3[:, 65 * I:65 * I + 64],
                         TSN(nn), False, True)
                p.wait("v", "MM", lmm)
                o2g = vop(lambda e, S, I=I, h2=h2: e.tensor_scalar(
                    AR2[0:64, 128 * (I % 32):128 * (I % 32) + 128],
                    pS1[0:64, 512 * h2:512 * h2 + 128],
                    MISC[0:64, MI_F + I:MI_F + I + 1], None, op0=AL.add))
                state["ps1b"][h2] = o2g
            return o2g

        ab_ev = 0
        for nn in range(attn_nn):
            lastb = (nn == attn_nn - 1)
            tokb = 512 * nn
            wl_q, reg_q = next_wload("wqT")
            wl_k, reg_k = next_wload("wkT")
            if nn == 0:
                p.wait("t", "EV", ln_ev)
                p.wait("t", "XS", xs_done)
                p.wait("v", "XS", xs_done)
            proj(nn, wl_q, reg_q, "bq", QMT, Xb)
            if lastb and attn_stage <= 1:
                trunc[0] = True; break
            wl_v, reg_v = next_wload("wvT")
            proj(nn, wl_k, reg_k, "bk", KMT, Xb)
            if lastb and attn_stage <= 2:
                trunc[0] = True; break
            score_shift(nn)
            if lastb and attn_stage <= 3:
                trunc[0] = True; break
            wl_wog, reg_wog = next_wload("wogT")
            vev = proj(nn, wl_v, reg_v, "bv", 0, AR3)
            if lastb and attn_stage <= 4:
                trunc[0] = True; break
            pS2b = pS2.ap().bitcast(BF16)
            slots = [pS2b[0:128, 0:64], pS2b[0:128, 1024:1088]]
            sfree = [vev, vev]
            for c in range(64):
                sl_ = c % 2
                p.wait("t", "EV", sfree[sl_])
                p.add("t", lambda e, S, c=c, sl_=sl_: e.transpose(
                    slots[sl_], AR3[0:64, 128 * c:128 * c + 128],
                    CON[0:64, CON_EYE:CON_EYE + 64]), "MM", 1)
                mv = p.ct["MM"]
                p.wait("v", "MM", mv)
                vop0(lambda e, S, c=c, sl_=sl_: e.tensor_copy(
                    AR3[:, 65 * c:65 * c + 64], slots[sl_]))
                sfree[sl_] = vop(lambda e, S, c=c: e.memset(
                    AR3[:, 65 * c + 64:65 * c + 65], 1.0))
            vmo_ev = p.ct["EV"]
            if lastb and attn_stage <= 5:
                trunc[0] = True; break
            p.wait("t", "EV", vmo_ev)
            p.wait("t", "EV", state["pb23"][0])
            p.wait("t", "EV", state["pb23"][1])
            for J in range(64):
                fn_ = mm if J == 63 else mm0
                fn_(pb[2][0:64, J:J + 1], AR3[:, 65 * J:65 * J + 64], ONECOL,
                    True, True)
            bsmm = p.ct["MM"]
            p.wait("v", "MM", bsmm)
            bse = vop(lambda e, S: e.tensor_copy(BSB, pb[2][0:64, 0:64]))
            p.wait("t", "EV", bse)
            p.add("t", lambda e, S: e.transpose(
                pS2.ap().bitcast(BF16)[0:64, 0:64], BSB,
                CON[0:64, CON_EYE:CON_EYE + 64]), "MM", 1)
            btr = p.ct["MM"]
            p.wait("v", "MM", btr)
            bte = vop(lambda e, S: e.tensor_copy(
                BSTB, pS2.ap().bitcast(BF16)[0:64, 0:64]))
            p.wait("t", "EV", bte)
            mm0(pb[3][0:64, 0:64], BSTB, CON[0:64, CON_L64:CON_L64 + 64],
                True, True)
            lsmm = mm(pb[3][0:64, 64:65], BSTB,
                      CON[0:64, CON_L64 + 64:CON_L64 + 65], True, True)
            p.wait("v", "MM", lsmm)
            vop0(lambda e, S, nn=nn: e.tensor_scalar(
                MISC[0:64, 72:73], pb[3][0:64, 64:65],
                MISC[0:64, MI_REP + nn:MI_REP + nn + 1], None, op0=AL.mult))
            fev = vop(lambda e, S, nn=nn: e.tensor_scalar(
                MISC[0:64, MI_F:MI_F + 64], pb[3][0:64, 0:64],
                MISC[0:64, MI_REP + 4 + nn:MI_REP + 4 + nn + 1],
                MISC[0:64, 72:73], op0=AL.mult, op1=AL.add))
            state["pb23"] = [fev, fev]
            if lastb and attn_stage <= 6:
                trunc[0] = True; break
            o2g = bias_pass(nn, 0, fev)
            for s in range(32):
                if s == 16:
                    o2g = bias_pass(nn, 1, o2g)
                ilo = SW_A * s

                def emit_av(bb):
                    if bb == 0:
                        p.wait("t", "EV", state["av"])
                    for k4 in range(4):
                        kc = 4 * bb + k4
                        fn_ = mm if kc == 63 else mm0
                        fn_(pb[0][0:65, 0:256], AR3[:, 65 * kc:65 * kc + 65],
                            AR1[:, 256 * kc:256 * kc + 256], kc == 0, kc == 63)

                for b_ in range(16):
                    pS = pS1 if b_ % 2 == 0 else pS2
                    p.wait("t", "SE", state["exp"].get(("b", nn, s, b_ - 2), 0))
                    if b_ < 2:
                        p.wait("t", "EV", state["ps1b"][0])
                        p.wait("t", "EV", state["ps1b"][1])
                        p.wait("t", "EV", o2g)
                    for k4 in range(4):
                        kc = 4 * b_ + k4
                        fn_ = mm if k4 == 3 else mm0
                        fn_(pS[:, 256 * k4:256 * k4 + 256],
                            Xb[0:65, KMT + 128 * kc:KMT + 128 * kc + 128],
                            Xb[0:65, QMT + ilo:QMT + ilo + SW_A],
                            True, True)
                    if b_ >= 2:
                        emit_av(b_ - 2)
                    mmv = p.ct["MM"]
                    p.wait("s", "MM", mmv)
                    state["exp"][("b", nn, s, b_)] = sop(
                        lambda e, S, b_=b_, pS=pS: e.activation(
                            AR1[:, 1024 * b_:1024 * b_ + 1024], pS[:, 0:1024],
                            AF.Exp, bias=0.0))
                for bb in (14, 15):
                    p.wait("t", "SE", state["exp"][("b", nn, s, bb)])
                    emit_av(bb)
                avm = p.ct["MM"]
                p.wait("v", "MM", avm)
                vop0(lambda e, S: e.tensor_scalar(
                    ROWS[64:65, 1024:1280], pb[0][64:65, 0:256], 1e-30, None,
                    op0=AL.max))
                vop0(lambda e, S: e.reciprocal(ROWS[64:65, 1024:1280],
                                               ROWS[64:65, 1024:1280]))
                rec = vop(lambda e, S: e.tensor_copy(RECROW,
                                                     ROWS[64:65, 1024:1280]))
                p.wait("t", "EV", rec)
                mrep = mm(pb[1][:, 0:256], CON[64:65, CON_ONEB:CON_ONEB + 128],
                          RECROW, True, True)
                p.wait("v", "MM", mrep)
                vop0(lambda e, S: e.tensor_copy(REPS, pb[1][:, 0:256]))
                vop0(lambda e, S, ilo=ilo: e.tensor_mul(
                    Xb[0:64, O1C + ilo:O1C + ilo + 256], pb[0][0:64, 0:256],
                    REPS[0:64, :]))
                state["av"] = vop(lambda e, S, ilo=ilo, s=s: e.tensor_add(
                    Xb[0:64, O1C + ilo:O1C + ilo + 256],
                    Xb[0:64, O1C + ilo:O1C + ilo + 256],
                    AR2[0:64, 128 * ((2 * s) % 32):128 * ((2 * s) % 32) + 256]))
            if lastb and attn_stage <= 7:
                trunc[0] = True; break
            dupev = state["av"]
            for i in range(16):
                h2 = i % 2
                p.wait("t", "EV", state["ps1b"][h2])
                p.wait("t", "EV", dupev)
                msd = p.add("t", lambda e, S, i=i, h2=h2: e.matmul(
                    pS1[64:128, 512 * h2:512 * h2 + 512], eyeT,
                    Xb[:, O1C + 512 * i:O1C + 512 * i + 512],
                    start=True, stop=True, tile_position=(0, 64)), "MM", 1)
                p.wait("v", "MM", p.ct["MM"])
                state["ps1b"][h2] = vop(lambda e, S, i=i, h2=h2: e.tensor_copy(
                    Xb[64:128, O1C + 512 * i:O1C + 512 * i + 512],
                    pS1[64:128, 512 * h2:512 * h2 + 512]))
            dup_ev = p.ct["EV"]
            if lastb and attn_stage <= 8:
                trunc[0] = True; break
            p.wait("t", *wl_wog)
            p.wait("t", "EV", dup_ev)
            slot = 0
            yev = 0
            for dc in range(8):
                bk = 2 + (dc % 2)
                h = dc % 2
                p.wait("t", "EV", state["pb23"][h])
                p.wait("t", "EV", state["ps1b"][h])
                for par in range(2):
                    base = 64 * par
                    dst = pb[bk][:, 0:512] if par == 0 else \
                        pS1[:, 512 * h:512 * h + 512]
                    for gi in range(8):
                        g = 2 * gi + par
                        fn_ = mm if (par == 1 and gi == 7) else mm0
                        fn_(dst,
                            WA[base:base + 64,
                               8192 * reg_wog + 1024 * gi + 128 * dc:
                               8192 * reg_wog + 1024 * gi + 128 * dc + 128],
                            Xb[base:base + 64, O1C + g:O1C + g + 8177:16],
                            gi == 0, gi == 7)
                wmm = p.ct["MM"]
                p.wait("s", "MM", wmm)
                se = sop(lambda e, S, h=h: e.activation(
                    AR1f[:, 5120 + 512 * h:5120 + 512 * h + 512],
                    pS1[:, 512 * h:512 * h + 512], AF.Copy))
                p.wait("v", "MM", wmm)
                p.wait("v", "SE", se)
                vop0(lambda e, S, dc=dc, bk=bk, h=h: e.tensor_add(
                    AR1f[:, 2048 + 512 * h:2048 + 512 * h + 512],
                    pb[bk][:, 0:512], AR1f[:, 5120 + 512 * h:5120 + 512 * h + 512]))
                vop0(lambda e, S, dc=dc, h=h: e.tensor_scalar(
                    AR1f[:, 2048 + 512 * h:2048 + 512 * h + 512],
                    AR1f[:, 2048 + 512 * h:2048 + 512 * h + 512],
                    MISC[:, MI_B["bo"] + dc:MI_B["bo"] + dc + 1],
                    None, op0=AL.add))
                yev = vop(lambda e, S, dc=dc, slot=slot, nn=nn, h=h: e.tensor_add(
                    Yf(slot, dc, 0, 512),
                    AR1f[:, 2048 + 512 * h:2048 + 512 * h + 512],
                    XAf(dc, 512 * nn, 512)))
                state["pb23"][h] = yev
                state["ps1b"][h] = yev
            wuse[reg_wog] = p.ct["MM"]
            if debug and nn == 0:
                p.wait("y", "EV", yev)
                y_dma(dbg["dyt"][:, :].bitcast(mybir.dt.uint32),
                      X[:, 12288:14336].bitcast(mybir.dt.uint32), sem="SO")
                y_dma(dbg["do1"][:, :].bitcast(mybir.dt.uint16),
                      Xb[:, O1C:O1C + 8192].bitcast(mybir.dt.uint16), sem="SO")
                y_dma(dbg["do2"][:, :].bitcast(mybir.dt.uint16),
                      AR2[:, 0:4096].bitcast(mybir.dt.uint16), sem="SO")
                p.wait("t", "SO", p.ct["SO"])
            if lastb and attn_stage <= 9:
                trunc[0] = True; break
            ab_ev = ln_rms_block(nn, slot, yev)
            state["ps1b"] = [ab_ev, ab_ev]
            state["pb23"] = [ab_ev, ab_ev]
            state["av"] = ab_ev

        if trunc[0] or attn_nn < 4:
            return p.ct["EV"], xs_done
        wl1, reg1 = next_wload("w1aT")
        wl2, reg2 = next_wload("w2aT")
        p.wait("t", *wl1)
        p.wait("t", "EV", ab_ev)
        pbf = [ab_ev, ab_ev]
        h1f = [0, 0]
        for st in range(4):
            lo = 512 * st
            for ht in range(8):
                bk = 2 + (ht % 2)
                p.wait("t", "EV", pbf[ht % 2])
                for dt_ in range(NT):
                    fn_ = mm if dt_ == NT - 1 else mm0
                    fn_(pb[bk][:, 0:512], WSl(reg1, dt_, 128 * ht, 128),
                        XAf(dt_, lo, 512), dt_ == 0, dt_ == NT - 1)
                hm = p.ct["MM"]
                p.wait("v", "MM", hm)
                pbf[ht % 2] = vop(lambda e, S, bk=bk, ht=ht: e.tensor_scalar(
                    AR2[:, 512 * ht:512 * ht + 512], pb[bk][:, 0:512],
                    MISC[:, MI_B["b1a"] + ht:MI_B["b1a"] + ht + 1], None,
                    op0=AL.add))
            if st == 0:
                p.wait("t", *wl2)
            p.wait("t", "EV", max(pbf))
            for dc in range(8):
                bk = 2 + (dc % 2)
                p.wait("t", "EV", pbf[dc % 2])
                for hti in range(NT):
                    fn_ = mm if hti == NT - 1 else mm0
                    fn_(pb[bk][:, 0:512], WSl(reg2, hti, 128 * dc, 128),
                        AR2[:, 512 * hti:512 * hti + 512], hti == 0,
                        hti == NT - 1)
                zm = p.ct["MM"]
                p.wait("v", "MM", zm)
                vop0(lambda e, S, bk=bk, dc=dc: e.tensor_scalar(
                    AR1f[:, 2048 + 512 * (dc % 2):2048 + 512 * (dc % 2) + 512],
                    pb[bk][:, 0:512],
                    MISC[:, MI_B["b2a"] + dc:MI_B["b2a"] + dc + 1], None,
                    op0=AL.add))
                pbf[dc % 2] = vop(lambda e, S, bk=bk, dc=dc, lo=lo: e.tensor_add(
                    Xf(dc, lo, 512),
                    AR1f[:, 2048 + 512 * (dc % 2):2048 + 512 * (dc % 2) + 512],
                    XAf(dc, lo, 512)))
        z_ev = max(pbf)
        return z_ev, xs_done


    def ln_rms_block(nn, slot, gate_ev):
        p.wait("t", "EV", gate_ev)
        p.wait("s", "MM", p.ct["MM"])
        sq = {}
        for dt_ in range(NT):
            slt = LNSf[dt_ % 2]
            p.wait("s", "MM", sq.get(dt_ - 2, 0))
            p.wait("s", "EV", gate_ev)
            sev = sop(lambda e, S, dt_=dt_, slt=slt, slot=slot: e.activation(
                AR1f[:, slt:slt + 512], Yf(slot, dt_, 0, 512), AF.Square))
            p.wait("t", "SE", sev)
            for c4 in range(4):
                mm0(pb[c4][:, 4 + nn:5 + nn], Yf(slot, dt_, 128 * c4, 128),
                    MISC[:, MI_OCF:MI_OCF + 1], dt_ == 0, dt_ == NT - 1)
                fn_ = mm0 if c4 < 3 else (lambda *a, **k: mm(*a, **k))
                fn_([pS1, pS1, pS2, pS2][c4][:, [0, 512, 0, 512][c4] + 4 + nn:
                                             [0, 512, 0, 512][c4] + 5 + nn],
                    AR1f[:, slt + 128 * c4:slt + 128 * c4 + 128],
                    MISC[:, MI_OCF:MI_OCF + 1], dt_ == 0, dt_ == NT - 1)
            sq[dt_] = p.ct["MM"]
        p.wait("v", "MM", p.ct["MM"])
        SQR = [pS1[:, 0:512], pS1[:, 512:1024], pS2[:, 0:512], pS2[:, 512:1024]]
        for c4 in range(4):
            vop0(lambda e, S, c4=c4, nn=nn: e.tensor_scalar(
                MISC[:, MI_SM + 4 * nn + c4:MI_SM + 4 * nn + c4 + 1],
                pb[c4][:, 4 + nn:5 + nn], 1.0 / 1024, None, op0=AL.mult))
            vop0(lambda e, S, c4=c4, nn=nn: e.tensor_scalar(
                MISC[:, MI_SV + 4 * nn + c4:MI_SV + 4 * nn + c4 + 1],
                SQR[c4][:, 4 + nn:5 + nn], 1.0 / 1024, None, op0=AL.mult))
        cm, cv = MISC[:, MI_SM + 4 * nn:MI_SM + 4 * nn + 4], \
            MISC[:, MI_SV + 4 * nn:MI_SV + 4 * nn + 4]
        vop0(lambda e, S, cm=cm, nn=nn: e.tensor_mul(
            MISC[:, MI_SS:MI_SS + 4], cm, cm))
        sev2 = vop(lambda e, S, cv=cv: e.tensor_sub(
            cv, cv, MISC[:, MI_SS:MI_SS + 4]))
        p.wait("s", "EV", sev2)
        sop(lambda e, S, cv=cv: e.activation(MISC[:, MI_SS:MI_SS + 4], cv,
                                             AF.Ln, bias=0.0))
        rst = sop(lambda e, S, nn=nn: e.activation(
            MISC[:, MI_SR + 4 * nn:MI_SR + 4 * nn + 4],
            MISC[:, MI_SS:MI_SS + 4], AF.Exp, scale=-0.5))
        p.wait("v", "SE", rst)
        m16 = vop(lambda e, S, nn=nn: e.tensor_copy(
            MISCb[:, MB_M + 4 * nn:MB_M + 4 * nn + 4],
            MISC[:, MI_SM + 4 * nn:MI_SM + 4 * nn + 4]))
        m16 = vop(lambda e, S, nn=nn: e.tensor_copy(
            MISCb[:, MB_R + 4 * nn:MB_R + 4 * nn + 4],
            MISC[:, MI_SR + 4 * nn:MI_SR + 4 * nn + 4]))
        p.wait("t", "EV", m16)
        for c4 in range(4):
            c = 4 * nn + c4
            mm0(pb[0][0:1, 128 * c4:128 * c4 + 128], MISCb[:, MB_M + c:MB_M + c + 1],
                CON[:, CON_EYE:CON_EYE + 128], True, True)
            fn_ = mm0 if c4 < 3 else (lambda *a, **k: mm(*a, **k))
            fn_(pb[1][0:1, 128 * c4:128 * c4 + 128], MISCb[:, MB_R + c:MB_R + c + 1],
                CON[:, CON_EYE:CON_EYE + 128], True, True)
        p.wait("v", "MM")
        vop0(lambda e, S: e.tensor_copy(ROW0(0, 512), pb[0][0:1, 0:512]))
        rowev = vop(lambda e, S: e.tensor_copy(ROW0(512, 512), pb[1][0:1, 0:512]))
        p.wait("t", "EV", rowev)
        mm0(pb[0][:, 0:512], MISC[0:1, MI_ONE:MI_ONE + 128], ROW0(0, 512),
            True, True)
        mrr = mm(pb[1][:, 0:512], MISC[0:1, MI_ONE:MI_ONE + 128], ROW0(512, 512),
                 True, True)
        p.wait("v", "MM", mrr)
        ab_ev = 0
        for dt_ in range(NT):
            vop0(lambda e, S, dt_=dt_, slot=slot: e.tensor_sub(
                AR1f[:, STTf[dt_ % 2]:STTf[dt_ % 2] + 512],
                Yf(slot, dt_, 0, 512), pb[0][:, 0:512]))
            ab_ev = vop(lambda e, S, dt_=dt_, nn=nn: e.tensor_mul(
                XAf(dt_, 512 * nn, 512),
                AR1f[:, STTf[dt_ % 2]:STTf[dt_ % 2] + 512], pb[1][:, 0:512]))
        return ab_ev


    def final_residual(gate_ev):
        mr = emit_ln(0.0, False, gate_ev)
        p.wait("y", "XS", p.ct["XS"])
        p.wait("y", "EV", mr)
        xr_free = {}
        for st in range(4):
            lo = 512 * st
            bk = (st % 2) * 2
            for dt_ in range(NT):
                p.wait("y", "EV", xr_free.get((st - 1, dt_), 0))
                y_dma(AR1f[:, 4096 + 512 * dt_:4096 + 512 * dt_ + 512],
                      xspill[128 * dt_:128 * (dt_ + 1), lo:lo + 512], sem="XR")
            xr = p.ct["XR"]
            p.wait("t", "EV", mr if st == 0 else 0)
            for c4 in range(4):
                c = 4 * st + c4
                mm0(pb[bk][0:1, 128 * c4:128 * c4 + 128],
                    MISCb[:, MB_M + c:MB_M + c + 1],
                    CON[:, CON_EYE:CON_EYE + 128], True, True)
                fn_ = mm0 if c4 < 3 else (lambda *a, **k: mm(*a, **k))
                fn_(pb[bk + 1][0:1, 128 * c4:128 * c4 + 128],
                    MISCb[:, MB_R + c:MB_R + c + 1],
                    CON[:, CON_EYE:CON_EYE + 128], True, True)
            p.wait("v", "MM")
            vop0(lambda e, S, bk=bk: e.tensor_copy(ROW0(0, 512), pb[bk][0:1, 0:512]))
            rowev = vop(lambda e, S, bk=bk: e.tensor_copy(ROW0(512, 512),
                                                          pb[bk + 1][0:1, 0:512]))
            p.wait("t", "EV", rowev)
            mm0(pb[bk][:, 0:512], MISC[0:1, MI_ONE:MI_ONE + 128], ROW0(0, 512),
                True, True)
            mrr = mm(pb[bk + 1][:, 0:512], MISC[0:1, MI_ONE:MI_ONE + 128],
                     ROW0(512, 512), True, True)
            p.wait("v", "MM", mrr)
            p.wait("v", "XR", xr)
            last = 0
            for dt_ in range(NT):
                hm = 512 * dt_
                vop0(lambda e, S, dt_=dt_, lo=lo, bk=bk: e.tensor_sub(
                    AR1f[:, LNSf[dt_ % 2]:LNSf[dt_ % 2] + 512],
                    Xf(dt_, lo, 512), pb[bk][:, 0:512]))
                vop0(lambda e, S, dt_=dt_, bk=bk: e.tensor_mul(
                    AR1f[:, LNSf[dt_ % 2]:LNSf[dt_ % 2] + 512],
                    AR1f[:, LNSf[dt_ % 2]:LNSf[dt_ % 2] + 512],
                    pb[bk + 1][:, 0:512]))
                last = vop(lambda e, S, dt_=dt_, lo=lo, hm=hm: e.tensor_add(
                    Xf(dt_, lo, 512),
                    AR1f[:, LNSf[dt_ % 2]:LNSf[dt_ % 2] + 512],
                    AR1f[:, 4096 + hm:4096 + hm + 512]))
                xr_free[(st, dt_)] = last
        return last

    span_free_g = {"c": 0}
    last_ev = emit_ffn(0, ones_v)

    if 'attn' in parts:
        z_ev, _xs = emit_attn(last_ev)
        if attn_nn == 4 and attn_stage >= 10:
            last_ev = final_residual(z_ev)
        if 'ffn2' in parts and attn_nn == 4 and attn_stage >= 10:
            p.wait("g", "MM", p.ct["MM"])
            wf["kw1"] = wload(0, "kw1c")
            wf["kw2"] = wload(1, "kw2c")
            last_ev = emit_ffn(1, last_ev)

    p.wait("y", "EV", p.ct["EV"])
    p.wait("y", "MM", p.ct["MM"])
    p.wait("y", "SE", p.ct["SE"])
    for s_ in ("GS", "GT0", "GT1", "GF0", "GF1", "WB", "WA0", "WA1", "WLC"):
        p.wait("y", s_, p.ct[s_])
    U32 = mybir.dt.uint32
    for dt_ in range(NT):
        y_dma(yout[128 * dt_:128 * (dt_ + 1), :].bitcast(U32) if debug else
              yout[128 * dt_:128 * (dt_ + 1), :],
              Xf(dt_, 0, 2048).bitcast(U32) if debug else Xf(dt_, 0, 2048),
              sem="SO")
    if debug:
        y_dma(dbg["dxa"][:, :], XA[:, :], sem="SO")
        y_dma(dbg["dtp"][:, :], CON[:, CON_TP:CON_TP + 2048], sem="SO")
        if 'gather' in parts:
            y_dma(dbg["doh"][:, 0:256], OHQ[:, :], sem="SO")
        if 'attn' in parts:
            U16 = mybir.dt.uint16
            y_dma(dbg["dqk"][:, :].bitcast(U16), Xb[:, 0:16384].bitcast(U16), sem="SO")
            y_dma(dbg["dvmo"][:, :].bitcast(U16), AR3[:, :].bitcast(U16), sem="SO")
        if 'flash' in parts:
            y_dma(dbg["dfq"][:, :], AR2[:, 2048:4096], sem="SO")
            U16 = mybir.dt.uint16
            y_dma(dbg["de"][:, :].bitcast(U16), AR1[:, :].bitcast(U16), sem="SO")
            y_dma(dbg["dht"][:, :], AR2[:, 0:4096], sem="SO")
    p.wait("y", "SO", p.ct["SO"])

    with nc.Block() as blk:
        semcms = [nc.semaphore(n) for n in p.semnames]
        S = {}
        for n_, cm in zip(p.semnames, semcms):
            S[n_] = cm.__enter__()

        @blk.tensor
        def _(e):
            for fn in p.ops["t"]:
                fn(e, S)

        @blk.vector
        def _(e):
            for fn in p.ops["v"]:
                fn(e, S)

        @blk.scalar
        def _(e):
            for fn in p.ops["s"]:
                fn(e, S)

        @blk.sync
        def _(e):
            for fn in p.ops["y"]:
                fn(e, S)

        @blk.gpsimd
        def _(e):
            for fn in p.ops["g"]:
                fn(e, S)

    return nc



def _wog(wo):
    out = np.zeros((1024, 1024), np.float32)
    w = np.asarray(wo).reshape(1024, 16, 64)
    for g in range(16):
        out[128 * (g // 2) + 64 * (g % 2):128 * (g // 2) + 64 * (g % 2) + 64, :] \
            = w[:, g, :].T
    return np.ascontiguousarray(out)


def _prep_core_inputs(core, inp):
    b, qd = core // 4, core % 4
    sl = slice(2048 * qd, 2048 * (qd + 1))
    C = _get_consts()

    def f32(a):
        return np.ascontiguousarray(np.asarray(a, dtype=np.float32))

    d = {
        "xt": f32(np.asarray(inp["L0_attn_x"])[b, sl, :].T),
        "wqT": f32(np.asarray(inp["L0_attn_Wq"]).T),
        "wkT": f32(np.asarray(inp["L0_attn_Wk"]).T),
        "wvT": f32(np.asarray(inp["L0_attn_Wv"]).T),
        "woT": f32(np.asarray(inp["L0_attn_Wo"]).T),
        "wogT": _wog(np.asarray(inp["L0_attn_Wo"])),
        "w1aT": f32(np.asarray(inp["L0_attn_W1"]).T),
        "w2aT": f32(np.asarray(inp["L0_attn_W2"]).T),
        "relbT": f32(np.asarray(inp["L0_attn_rel_bias"]).T),
        "mflat": C["mflat"], "eyetri": C["eyetri"], "l64": C["l64"],
        "gmask": C["gmask"],
        "fconst": np.tile(np.array([0.0, 1.0, 1e-5, -0.5, 1.0 / 32.0, -40.0,
                                    0.5, 2.0], np.float32), (128, 1)),
    }
    for nm, key in (("bq4", "L0_attn_bq"), ("bk4", "L0_attn_bk"),
                    ("bv4", "L0_attn_bv"), ("bo4", "L0_attn_bo"),
                    ("b1a4", "L0_attn_b1"), ("b2a4", "L0_attn_b2")):
        d[nm] = f32(np.asarray(inp[key]).reshape(8, 128).T)
    for nm in ("bq4", "bk4", "bv4"):
        sh = np.zeros((128, 8), np.float32)
        sh[0:64, :] = d[nm][64:128, :]
        d[nm + "s"] = sh
    for pfx in ("q", "k", "v"):
        d[f"{pfx}nwT"] = f32(np.asarray(inp[f"L0_ffn_{pfx}_node_w"]).T)
        d[f"{pfx}nb"] = f32(np.asarray(inp[f"L0_ffn_{pfx}_node_b"]).reshape(1, 15))
        d[f"{pfx}w1c"] = f32(np.asarray(inp[f"L0_ffn_{pfx}_w1"]).transpose(1, 0, 2)
                             .reshape(1024, 1024))
        d[f"{pfx}b1f4"] = f32(np.asarray(inp[f"L0_ffn_{pfx}_b1"]).reshape(1024)
                              .reshape(8, 128).T)
        d[f"{pfx}w2c"] = f32(np.asarray(inp[f"L0_ffn_{pfx}_w2"]).reshape(1024, 1024))
        d[f"{pfx}b2"] = f32(np.asarray(inp[f"L0_ffn_{pfx}_b2"]))
    hs = np.zeros((16, 4), np.float32)
    for ln in range(4):
        hs[4 * qd + ln, ln] = 1.0
    d["hsel"] = hs
    return d


def kernel(**inputs):
    nc = build_nc()
    in_maps = [_prep_core_inputs(c, inputs) for c in range(NCORE)]
    res = run_bass_kernel_spmd(nc, in_maps, core_ids=list(range(NCORE)))
    out = np.zeros((2, 8192, 1024), np.float32)
    for c in range(NCORE):
        b, qd = c // 4, c % 4
        out[b, 2048 * qd:2048 * (qd + 1), :] = res.results[c]["y"].T
    return out


# version 05
# speedup vs baseline: 1.1002x; 1.0826x over previous
import math
import numpy as np
import concourse.bass as bass
import concourse.mybir as mybir
from concourse.bass_utils import run_bass_kernel_spmd

F32 = mybir.dt.float32
BF16 = mybir.dt.bfloat16
AF = mybir.ActivationFunctionType
AL = mybir.AluOpType

NCORE = 8
T = 2048
D = 1024
NT = 8
SW_F = 256
NSTRIP_F = T // SW_F
NSPAN = 16
EXPSH = -40.0


def _bucket_of(d):
    n = -d
    ret = 16 if n < 0 else 0
    na = abs(n)
    if na < 8:
        return ret + na
    v = 8 + int(math.floor(math.log(na / 8.0) / math.log(16.0) * 8.0))
    return ret + min(v, 15)


def _host_consts():
    c = {}
    mf = np.zeros((96, 16384), np.float32)
    for di, delta in enumerate((-1, 0, 1)):
        for dd in range(-90, 91):
            b = _bucket_of(dd)
            for jl in range(128):
                il = dd + 128 * delta + jl
                if 0 <= il < 128:
                    col = 128 * jl + il
                    mf[32 * di + b, col] += 1.0
                    if dd >= 1:
                        mf[32 * di + 31, col] -= 1.0
                    else:
                        mf[32 * di + 15, col] -= 1.0
    c["mflat"] = mf
    eye = np.zeros((128, 256), np.float32)
    eye[:, 0:128] = np.eye(128, dtype=np.float32)
    eye[:, 128:256] = (np.arange(128)[:, None] < np.arange(128)[None, :]).astype(np.float32)
    c["eyetri"] = eye
    l64 = np.zeros((128, 65), np.float32)
    l64[0:64, 0:64] = (np.arange(64)[:, None] < np.arange(64)[None, :]).astype(np.float32)
    l64[:, 64:65] = 1.0
    c["l64"] = l64
    g = np.zeros((16, 1024), np.float32)
    for ht in range(8):
        for pp in range(128):
            g[2 * ht + pp // 64, 128 * ht + pp] = 1.0
    c["gmask"] = g
    return c


_CONSTS = None


def _get_consts():
    global _CONSTS
    if _CONSTS is None:
        _CONSTS = _host_consts()
    return _CONSTS



ENGS = ("t", "v", "s", "y", "g")


class P:
    def __init__(self):
        self.ops = {e: [] for e in ENGS}
        self.ct = {}
        self.semnames = []

    def sem(self, name):
        if name not in self.ct:
            self.ct[name] = 0
            self.semnames.append(name)

    def add(self, eng, fn, sem=None, inc=0):
        if sem is not None:
            self.sem(sem)
            self.ct[sem] += inc

            def wrapped(e, S, fn=fn, sem=sem, inc=inc):
                fn(e, S).then_inc(S[sem], inc)
            self.ops[eng].append(wrapped)
            return self.ct[sem]
        self.ops[eng].append(fn)
        return None

    def wait(self, eng, sem, val=None):
        self.sem(sem)
        v = self.ct[sem] if val is None else val
        assert v <= self.ct[sem], f"{sem}: wait {v} > emitted {self.ct[sem]}"
        if v > 0:
            self.ops[eng].append(lambda e, S, sem=sem, v=v: e.wait_ge(S[sem], v))
        return v



def build_nc(debug=False, parts=('tp', 'ln', 'kv', 'gather', 'flash', 'attn', 'ffn2'), attn_nn=4, attn_stage=10):
    nc = bass.Bass("TRN2", target_bir_lowering=False)
    p = P()

    din = {}

    def DI(name, shape):
        din[name] = nc.dram_tensor(name, list(shape), F32, kind="ExternalInput").ap()

    DI("xt", (D, T))
    for w in ("wqT", "wkT", "wvT", "woT", "w1aT", "w2aT"):
        DI(w, (D, D))
    for b in ("bq4", "bk4", "bv4", "bo4", "b1a4", "b2a4"):
        DI(b, (128, 8))
    for pfx in ("q", "k", "v"):
        DI(f"{pfx}nwT", (D, 15))
        DI(f"{pfx}nb", (1, 15))
        DI(f"{pfx}w1c", (D, D))
        DI(f"{pfx}b1f4", (128, 8))
        DI(f"{pfx}w2c", (D, D))
        DI(f"{pfx}b2", (16, D))
    DI("relbT", (16, 32))
    DI("wogT", (D, D))
    for b in ("bq4s", "bk4s", "bv4s"):
        DI(b, (128, 8))
    DI("fconst", (128, 8))
    DI("hsel", (16, 4))
    DI("mflat", (96, 16384))
    DI("eyetri", (128, 256))
    DI("l64", (128, 65))
    DI("gmask", (16, 1024))

    yout = nc.dram_tensor("y", [D, T], F32, kind="ExternalOutput").ap()
    dbg = {}
    if debug:
        for nm, shape, dt_ in (
            ("dxa", (128, 16384), BF16), ("dfq", (128, 2048), BF16),
            ("de", (128, 16384), BF16), ("dtp", (128, 2048), BF16),
            ("doh", (128, 256), BF16), ("dht", (128, 4096), BF16),
            ("dqk", (128, 16384), BF16), ("dvmo", (128, 8256), BF16),
            ("dyt", (128, 2048), F32), ("do1", (128, 8192), BF16),
            ("do2", (128, 4096), BF16),
        ):
            dbg[nm] = nc.dram_tensor(nm, list(shape), dt_, kind="ExternalOutput").ap()

    xspill = nc.dram_tensor("xspill", [D, T], F32).ap()
    kb = nc.dram_tensor("kb", [D, T], BF16).ap()
    vb = nc.dram_tensor("vb", [T, D], BF16).ap()
    kg = nc.dram_tensor("kg", [4 * D, T], BF16).ap()
    vg = nc.dram_tensor("vg", [4 * T, D], BF16).ap()
    tpscr = nc.dram_tensor("tpscr", [12, 16384], BF16).ap()

    X = nc.alloc_sbuf_tensor("X", [128, 16384], F32)
    XA = nc.alloc_sbuf_tensor("XA", [128, 16384], BF16)
    AR1 = nc.alloc_sbuf_tensor("AR1", [128, 16384], BF16)
    AR2 = nc.alloc_sbuf_tensor("AR2", [128, 4096], BF16)
    AR3 = nc.alloc_sbuf_tensor("AR3", [128, 8256], BF16)
    WA = nc.alloc_sbuf_tensor("WA", [128, 16384], BF16)
    CON = nc.alloc_sbuf_tensor("CON", [128, 2512], BF16)
    ROWS = nc.alloc_sbuf_tensor("ROWS", [128, 1536], F32)
    SPK = nc.alloc_sbuf_tensor("SPK", [128, 2560], BF16)
    OHQ = nc.alloc_sbuf_tensor("OHQ", [128, 256], BF16)
    NWB = nc.alloc_sbuf_tensor("NWB", [128, 1792], BF16)
    MISC = nc.alloc_sbuf_tensor("MISC", [128, 480], F32)
    FCON = nc.alloc_sbuf_tensor("FCON", [128, 8], F32)
    FCONSTS = [0.0, 1.0, 1e-5, -0.5, 1.0 / 32.0, -40.0, 0.5, 2.0]
    for j, v in enumerate(FCONSTS):
        nc.const_aps.register(F32, v, FCON[:, j:j + 1])

    AR1f = AR1.ap().bitcast(F32)

    def Xf(dt_, lo, n):
        return X[:, 2048 * dt_ + lo: 2048 * dt_ + lo + n]

    def XAf(dt_, lo, n):
        return XA[:, 2048 * dt_ + lo: 2048 * dt_ + lo + n]

    CON_EYE, CON_TRI, CON_L64, CON_ONEC, CON_ONEB, CON_TP = \
        0, 128, 256, 321, 324, 464
    ONEROW = CON[0:1, CON_ONEB:CON_ONEB + 128]
    ONECOL = CON[:, CON_ONEC:CON_ONEC + 1]

    def TPR(di, n):
        o = CON_TP + 128 * (4 * di + n)
        return CON[:, o:o + 128]

    def TSN(n):
        o = CON_TP + 128 * (12 + n)
        return CON[:, o:o + 128]

    def ROW0(lo, n):
        return ROWS[0:1, lo:lo + n]


    def NWT(pi, dt_):
        return NWB[:, 120 * pi + 15 * dt_: 120 * pi + 15 * dt_ + 15]

    def NBROW(pi):
        return NWB[0:1, 360 + 15 * pi: 375 + 15 * pi]

    RELBT = NWB[0:16, 440:472]
    NW_CROW = 480
    NW_REC = 512
    NW_TPS = [768, 1280]

    def B2S(lo, n):
        return SPK[0:16, 1536 + lo:1536 + lo + n]

    def OHT(lo, n):
        return SPK[0:16, 1024 + lo:1024 + lo + n]

    HSEL = NWB[0:16, 412:416]
    RBSEL = NWB[0:32, 408:412]

    MI_B = {"bq": 0, "bk": 8, "bv": 16, "bo": 24, "b1a": 32, "b2a": 40,
            "qb1": 48, "kb1": 56, "vb1": 64, "bqs": 448, "bks": 456,
            "bvs": 464}
    MI_F = 80
    MI_REP = 144
    MI_ONE = 160
    MI_BST = 288
    MI_OCF = 352
    MI_SM, MI_SV, MI_SS, MI_SR = 356, 372, 388, 404
    MISCb = MISC.ap().bitcast(BF16)
    MB_M, MB_R = 856, 872

    def WSl(r, kt, lo, n):
        o = 8192 * r + 1024 * kt + lo
        return WA[:, o:o + n]

    pb = [nc.alloc_psum_tensor(f"pb{i}", [128, 512], F32) for i in range(4)]
    pS1 = nc.alloc_psum_tensor("pS1", [128, 1024], F32)
    pS2 = nc.alloc_psum_tensor("pS2", [128, 1024], F32)

    OHB = 14336
    OH_SC, OH_SCR, OH_KV, OH_Q = OHB, OHB + 256, OHB + 512, -1

    LNSf = [1536, 2048]
    MSKf = [2560, 3072]
    STTf = [3584, 4096]
    BST = [13312, 13824]
    MF = [9216, 11264]

    for s_ in ("XL", "SPL", "WLC", "WA0", "WA1", "WB", "GC", "GF0", "GF1",
               "GT0", "GT1", "GS", "KL0", "KL1", "SB0", "SB1", "SO",
               "MM", "EV", "SE", "CC", "XS", "XR"):
        p.sem(s_)

    def g_dma(dst, src, sem="WLC"):
        return p.add("g", lambda e, S, d=dst, s=src: e.dma_start(out=d, in_=s), sem, 16)

    def y_dma(dst, src, sem="SPL"):
        return p.add("y", lambda e, S, d=dst, s=src: e.dma_start(out=d, in_=s), sem, 16)

    def mm(out, lhsT, rhs, start, stop, tp=None, sem="MM"):
        return p.add("t", lambda e, S, o=out, l=lhsT, r=rhs, a=start, b=stop, tp=tp:
                     e.matmul(o, l, r, start=a, stop=b, tile_position=tp), sem, 1)

    def mm0(out, lhsT, rhs, start, stop, tp=None):
        p.add("t", lambda e, S, o=out, l=lhsT, r=rhs, a=start, b=stop, tp=tp:
              e.matmul(o, l, r, start=a, stop=b, tile_position=tp))

    def ttrans(out, in_, ident):
        p.add("t", lambda e, S, o=out, i=in_, d=ident: e.transpose(o, i, d))

    def tnop(sem, inc=1):
        return p.add("t", lambda e, S: e.nop(), sem, inc)

    def vop(fn, sem="EV", inc=1):
        p.wait("v", "EV", p.ct["EV"])
        return p.add("v", fn, "EV", 1)

    def vop0(fn):
        p.wait("v", "EV", p.ct["EV"])
        p.add("v", fn, "EV", 1)

    def sop(fn, sem="SE", inc=1):
        p.wait("s", "SE", p.ct["SE"])
        return p.add("s", fn, "SE", 1)

    def wload(reg, name):
        src = din[name][:, :].rearrange("(a p) f -> p a f", p=128)
        dst = WA[:, 8192 * reg:8192 * reg + 8192].rearrange("p (a f) -> p a f", a=8)
        return (f"WA{reg}", g_dma(dst, src, sem=f"WA{reg}"))

    for dt_ in range(NT):
        y_dma(Xf(dt_, 0, 2048), din["xt"][128 * dt_:128 * (dt_ + 1), :], sem="XL")
    xload = p.ct["XL"]

    g_dma(CON[:, CON_EYE:CON_EYE + 256], din["eyetri"][:, :])
    g_dma(CON[0:128, CON_L64:CON_L64 + 65], din["l64"][:, :])
    g_dma(SPK[0:16, 0:1024], din["gmask"][:, :])
    for pi, pfx in enumerate(("q", "k", "v")):
        g_dma(NWB[:, 120 * pi:120 * pi + 120].rearrange("p (a n) -> p a n", a=8),
              din[f"{pfx}nwT"][:, :].rearrange("(a p) n -> p a n", p=128))
        g_dma(NBROW(pi), din[f"{pfx}nb"][:, :])
    g_dma(HSEL, din["hsel"][:, :])
    g_dma(RELBT, din["relbT"][:, :])
    con_done = p.ct["WLC"]

    y_dma(FCON[:, :], din["fconst"][:, :], sem="SPL")
    for nm, key in (("bq4", "bq"), ("bk4", "bk"), ("bv4", "bv"), ("bo4", "bo"),
                    ("b1a4", "b1a"), ("b2a4", "b2a")):
        y_dma(MISC[:, MI_B[key]:MI_B[key] + 8], din[nm][:, :], sem="SPL")
    for pfx, key in (("q", "qb1"), ("k", "kb1"), ("v", "vb1")):
        y_dma(MISC[:, MI_B[key]:MI_B[key] + 8], din[f"{pfx}b1f4"][:, :], sem="SPL")
    for nm, key in (("bq4s", "bqs"), ("bk4s", "bks"), ("bv4s", "bvs")):
        y_dma(MISC[:, MI_B[key]:MI_B[key] + 8], din[nm][:, :], sem="SPL")
    smalls = p.ct["SPL"]

    if debug:
        vop0(lambda e, S: e.memset(AR3[:, :], 0.0))
        vop0(lambda e, S: e.memset(CON[:, CON_TP:CON_TP + 2048], 0.0))
        vop0(lambda e, S: e.memset(OHQ[:, :], 0.0))
        vop0(lambda e, S: e.memset(AR1[:, :], 0.0))
        vop0(lambda e, S: e.memset(AR2[:, :], 0.0))
        vop0(lambda e, S: e.memset(XA[:, :], 0.0))
    vop0(lambda e, S: e.memset(CON[:, CON_ONEC:CON_ONEC + 1], 1.0))
    vop0(lambda e, S: e.memset(CON[:, CON_ONEB:CON_ONEB + 128], 1.0))
    vop0(lambda e, S: e.memset(MISC[:, MI_ONE:MI_ONE + 128], 1.0))
    ones_v = vop(lambda e, S: e.memset(MISC[:, MI_OCF:MI_OCF + 1], 1.0))

    wf = {}
    wf["kw1"] = wload(0, "kw1c")
    wf["kw2"] = wload(1, "kw2c")

    def build_tprime():
        p.wait("t", "WLC", con_done)
        p.wait("t", "EV", ones_v)
        m1 = mm(pS2[0:32, 0:4], RELBT, HSEL, True, True)
        p.wait("v", "MM", m1)
        e1 = vop(lambda e, S: e.tensor_copy(RBSEL, pS2[0:32, 0:4]))
        p.wait("g", "EV", e1)
        p.add("g", lambda e, S: e.dma_start(out=NWB[0:1, NW_CROW:NW_CROW + 4],
                                            in_=RBSEL[15:16, :]), "GC", 16)
        c2 = p.add("g", lambda e, S: e.dma_start(out=NWB[0:1, NW_CROW + 4:NW_CROW + 8],
                                                 in_=RBSEL[31:32, :]), "GC", 16)
        p.wait("v", "GC", c2)
        e2 = vop(lambda e, S: e.tensor_sub(NWB[0:1, NW_CROW + 8:NW_CROW + 12],
                                           NWB[0:1, NW_CROW + 4:NW_CROW + 8],
                                           NWB[0:1, NW_CROW:NW_CROW + 4]))
        p.wait("t", "EV", e2)
        last = None
        for ni in range(4):
            mm0(pS2[0:128, 8 + ni:9 + ni], ONEROW,
                NWB[0:1, NW_CROW + ni:NW_CROW + ni + 1], True, True)
            last = mm(pS2[0:128, 12 + ni:13 + ni], ONEROW,
                      NWB[0:1, NW_CROW + 8 + ni:NW_CROW + 9 + ni], True, True)
        p.wait("v", "MM", last)
        er = vop(lambda e, S: e.tensor_copy(MISC[:, MI_REP:MI_REP + 8],
                                            pS2[0:128, 8:16]))
        for ni in range(4):
            vop(lambda e, S, ni=ni: e.tensor_scalar(
                TSN(ni), CON[:, CON_TRI:CON_TRI + 128],
                MISC[:, MI_REP + 4 + ni:MI_REP + 5 + ni], None, op0=AL.mult))
        return er

    def tprime_rtiles(gate_ev):
        ps1_free = [gate_ev, gate_ev]
        tps_free = [0, 0]
        mf_mm = [0, 0]
        for di in range(3):
            for half in range(8):
                hb = half % 2
                p.wait("g", "MM", mf_mm[hb])
                ld = g_dma(AR1[0:32, MF[hb]:MF[hb] + 2048],
                           din["mflat"][32 * di:32 * di + 32,
                                        2048 * half:2048 * half + 2048],
                           sem=f"GF{hb}")
                p.wait("t", f"GF{hb}", ld)
                for cq in range(4):
                    h2 = cq % 2
                    p.wait("t", "EV", ps1_free[h2])
                    m_ = mm(pS1[0:4, 512 * h2:512 * h2 + 512], RBSEL,
                            AR1[0:32, MF[hb] + 512 * cq:MF[hb] + 512 * cq + 512],
                            True, True)
                    p.wait("v", "MM", m_)
                    p.wait("v", f"GT{h2}", tps_free[h2])
                    ps1_free[h2] = vop(lambda e, S, h2=h2: e.tensor_copy(
                        NWB[0:4, NW_TPS[h2]:NW_TPS[h2] + 512],
                        pS1[0:4, 512 * h2:512 * h2 + 512]))
                    p.wait("g", "EV", ps1_free[h2])
                    tps_free[h2] = g_dma(
                        tpscr[4 * di:4 * di + 4,
                              2048 * half + 512 * cq:2048 * half + 512 * cq + 512],
                        NWB[0:4, NW_TPS[h2]:NW_TPS[h2] + 512], sem=f"GT{h2}")
                mf_mm[hb] = p.ct["MM"]
            p.wait("g", "GT0", p.ct["GT0"])
            p.wait("g", "GT1", p.ct["GT1"])
            for ni in range(4):
                g_dma(TPR(di, ni),
                      tpscr[4 * di + ni:4 * di + ni + 1, :]
                      .rearrange("o (pp f) -> (o pp) f", pp=128), sem="GS")
        return p.ct["EV"], p.ct["GS"]

    if 'tp' in parts:
        tp_er = build_tprime()
    else:
        tp_er = p.ct["EV"]
    tp_state = {"ev": p.ct["EV"], "gs": p.ct["GS"]}

    def emit_ln(eps, write_xa, gate_ev, src=None):
        src = src or Xf
        p.wait("s", "XL", xload)
        p.wait("s", "SPL", smalls)
        p.wait("s", "EV", p.ct["EV"])
        p.wait("s", "MM", p.ct["MM"])
        p.wait("t", "SPL", smalls)
        p.wait("t", "WLC", con_done)
        p.wait("t", "XL", xload)
        p.wait("t", "EV", ones_v)
        p.wait("t", "EV", gate_ev)
        SQR = [pS1[:, 0:512], pS1[:, 512:1024], pS2[:, 0:512], pS2[:, 512:1024]]
        sqfree = {}
        for st in range(4):
            for dt_ in range(NT):
                g_ = 8 * st + dt_
                slot = LNSf[g_ % 2]
                p.wait("s", "MM", sqfree.get(g_ - 2, 0))
                sq = sop(lambda e, S, dt_=dt_, st=st, slot=slot: e.activation(
                    AR1f[:, slot:slot + 512], src(dt_, 512 * st, 512), AF.Square))
                p.wait("t", "SE", sq)
                for c4 in range(4):
                    mm0(pb[c4][:, st:st + 1], src(dt_, 512 * st + 128 * c4, 128),
                        MISC[:, MI_OCF:MI_OCF + 1], dt_ == 0, dt_ == NT - 1)
                    fn_ = mm0 if c4 < 3 else (lambda *a, **k: mm(*a, **k))
                    fn_(SQR[c4][:, st:st + 1],
                        AR1f[:, slot + 128 * c4:slot + 128 * c4 + 128],
                        MISC[:, MI_OCF:MI_OCF + 1], dt_ == 0, dt_ == NT - 1)
                sqfree[g_] = p.ct["MM"]
        lastmm = p.ct["MM"]
        p.wait("v", "MM", lastmm)
        for c4 in range(4):
            vop0(lambda e, S, c4=c4: e.tensor_scalar(
                MISC[:, MI_SM + c4:MI_SM + 13 + c4:4], pb[c4][:, 0:4],
                1.0 / 1024, None, op0=AL.mult))
            vop0(lambda e, S, c4=c4: e.tensor_scalar(
                MISC[:, MI_SV + c4:MI_SV + 13 + c4:4], SQR[c4][:, 0:4],
                1.0 / 1024, None, op0=AL.mult))
        vop0(lambda e, S: e.tensor_mul(
            MISC[:, MI_SS:MI_SS + 16], MISC[:, MI_SM:MI_SM + 16],
            MISC[:, MI_SM:MI_SM + 16]))
        stats_ev = vop(lambda e, S: e.tensor_sub(
            MISC[:, MI_SV:MI_SV + 16], MISC[:, MI_SV:MI_SV + 16],
            MISC[:, MI_SS:MI_SS + 16]))
        p.wait("s", "EV", stats_ev)
        sop(lambda e, S: e.activation(MISC[:, MI_SS:MI_SS + 16],
                                      MISC[:, MI_SV:MI_SV + 16], AF.Ln,
                                      bias=float(eps)))
        rstd = sop(lambda e, S: e.activation(MISC[:, MI_SR:MI_SR + 16],
                                             MISC[:, MI_SS:MI_SS + 16], AF.Exp,
                                             scale=-0.5))
        p.wait("v", "SE", rstd)
        mr16 = vop(lambda e, S: e.tensor_copy(MISCb[:, MB_M:MB_M + 16],
                                              MISC[:, MI_SM:MI_SM + 16]))
        mr16 = vop(lambda e, S: e.tensor_copy(MISCb[:, MB_R:MB_R + 16],
                                              MISC[:, MI_SR:MI_SR + 16]))
        if not write_xa:
            return mr16
        p.wait("t", "EV", mr16)
        apply_ev = mr16
        for st in range(4):
            lo = 512 * st
            bk = (st % 2) * 2
            p.wait("t", "EV", apply_ev)
            for c4 in range(4):
                c = 4 * st + c4
                mm0(pb[bk][0:1, 128 * c4:128 * c4 + 128],
                    MISCb[:, MB_M + c:MB_M + c + 1],
                    CON[:, CON_EYE:CON_EYE + 128], True, True)
                fn_ = mm0 if c4 < 3 else (lambda *a, **k: mm(*a, **k))
                fn_(pb[bk + 1][0:1, 128 * c4:128 * c4 + 128],
                    MISCb[:, MB_R + c:MB_R + c + 1],
                    CON[:, CON_EYE:CON_EYE + 128], True, True)
            p.wait("v", "MM")
            vop0(lambda e, S, bk=bk: e.tensor_copy(ROW0(0, 512), pb[bk][0:1, 0:512]))
            rowev = vop(lambda e, S, bk=bk: e.tensor_copy(ROW0(512, 512),
                                                          pb[bk + 1][0:1, 0:512]))
            p.wait("t", "EV", rowev)
            mm0(pb[bk][:, 0:512], MISC[0:1, MI_ONE:MI_ONE + 128],
                ROW0(0, 512), True, True)
            mrr = mm(pb[bk + 1][:, 0:512], MISC[0:1, MI_ONE:MI_ONE + 128],
                     ROW0(512, 512), True, True)
            p.wait("v", "MM", mrr)
            for dt_ in range(NT):
                vop0(lambda e, S, dt_=dt_, lo=lo, bk=bk: e.tensor_sub(
                    AR1f[:, STTf[dt_ % 2]:STTf[dt_ % 2] + 512],
                    src(dt_, lo, 512), pb[bk][:, 0:512]))
                apply_ev = vop(lambda e, S, dt_=dt_, lo=lo, bk=bk: e.tensor_mul(
                    XAf(dt_, lo, 512),
                    AR1f[:, STTf[dt_ % 2]:STTf[dt_ % 2] + 512],
                    pb[bk + 1][:, 0:512]))
        return apply_ev

    def fff_scores_traversal(pi, oh_dst, psum_ready_ev):
        p.wait("t", "WLC", con_done)
        p.wait("t", "EV", psum_ready_ev)
        for c in range(16):
            for dt_ in range(NT):
                mm0(pS2[0:128, 15 * c:15 * c + 15], XAf(dt_, 128 * c, 128),
                    NWT(pi, dt_), dt_ == 0, False)
            last = mm(pS2[0:128, 15 * c:15 * c + 15], ONEROW, NBROW(pi), False, True)
        p.wait("v", "MM", last)
        sc_ev = vop(lambda e, S: e.tensor_copy(AR1[:, OH_SC:OH_SC + 240],
                                               pS2[0:128, 0:240]))

        def sl(i):
            return AR1[:, OH_SCR + 16 * i:OH_SCR + 16 * i + 16]

        def scc(j):
            return AR1[:, OH_SC + j: OH_SC + j + 15 * 16:15]

        B = [sl(i) for i in range(4)]
        E_ = [sl(4 + i) for i in range(4)]
        TMP = [sl(8 + i) for i in range(8)]
        P2 = [AR1[:, OH_SC + 16 * i:OH_SC + 16 * i + 16] for i in range(4)]
        P4 = [AR1[:, OH_SC + 64 + 16 * i:OH_SC + 64 + 16 * i + 16] for i in range(8)]

        def ts1(out, in_, s, op):
            vop0(lambda e, S, o=out, i=in_, s=s, op=op:
                 e.tensor_single_scalar(o, i, s, op=op))

        MXT = sl(24)

        def selv(out, mask, tt, ff):
            vop0(lambda e, S, o=MXT, a=tt, b=ff: e.tensor_sub(o, a, b))
            vop0(lambda e, S, o=MXT, m=mask: e.tensor_mul(o, o, m))
            vop0(lambda e, S, o=out, b=ff: e.tensor_add(o, b, MXT))

        def mulv(out, a, b):
            vop0(lambda e, S, o=out, a=a, b=b: e.tensor_mul(o, a, b))

        ts1(B[0], scc(0), 0.0, AL.is_ge)
        selv(TMP[0], B[0], scc(2), scc(1))
        ts1(B[1], TMP[0], 0.0, AL.is_ge)
        selv(TMP[1], B[1], scc(4), scc(3))
        selv(TMP[2], B[1], scc(6), scc(5))
        selv(TMP[3], B[0], TMP[2], TMP[1])
        ts1(B[2], TMP[3], 0.0, AL.is_ge)
        selv(TMP[4], B[2], scc(8), scc(7))
        selv(TMP[5], B[2], scc(10), scc(9))
        selv(TMP[6], B[2], scc(12), scc(11))
        selv(TMP[7], B[2], scc(14), scc(13))
        selv(TMP[1], B[1], TMP[5], TMP[4])
        selv(TMP[2], B[1], TMP[7], TMP[6])
        selv(TMP[0], B[0], TMP[2], TMP[1])
        ts1(B[3], TMP[0], 0.0, AL.is_ge)
        for i in range(4):
            vop0(lambda e, S, i=i: e.tensor_scalar(
                E_[i], B[i], -1.0, 1.0, op0=AL.mult, op1=AL.add))
        mulv(P2[0], E_[0], E_[1]); mulv(P2[1], E_[0], B[1])
        mulv(P2[2], B[0], E_[1]); mulv(P2[3], B[0], B[1])
        for a in range(4):
            mulv(P4[2 * a], P2[a], E_[2]); mulv(P4[2 * a + 1], P2[a], B[2])
        for l in range(16):
            src_ = P4[l // 2]
            bit = E_[3] if l % 2 == 0 else B[3]
            if oh_dst >= 0:
                dst = AR1[:, oh_dst + l: oh_dst + l + 241:16]
            else:
                dst = OHQ[:, l: l + 241:16]
            vop0(lambda e, S, dst=dst, src_=src_, bit=bit: e.tensor_mul(
                dst, src_, bit))
        done = p.add("v", lambda e, S: e.engine_nop(), "EV", 1)
        return done, sc_ev

    def fff_ohT(oh_dst, chunks, dst_lo, gate_ev):
        pS2b = pS2.ap().bitcast(BF16)
        slots = [pS2b[0:16, 0:128], pS2b[0:16, 1024:1152]]
        slot_free = [gate_ev, gate_ev]
        ev = gate_ev
        for idx, c in enumerate(chunks):
            sl_ = idx % 2
            p.wait("t", "EV", slot_free[sl_])
            if oh_dst >= 0:
                src_oh = AR1[:, oh_dst + 16 * c:oh_dst + 16 * c + 16]
            else:
                src_oh = OHQ[:, 16 * c:16 * c + 16]
            p.add("t", lambda e, S, src_oh=src_oh, sl_=sl_, slots=slots:
                  e.transpose(slots[sl_], src_oh,
                              CON[:, CON_EYE:CON_EYE + 128]), "MM", 1)
            mv = p.ct["MM"]
            p.wait("v", "MM", mv)
            ev = vop(lambda e, S, idx=idx, sl_=sl_, slots=slots, dst_lo=dst_lo:
                     e.tensor_copy(OHT(dst_lo + 128 * idx, 128), slots[sl_]))
            slot_free[sl_] = ev
        return ev

    def fff_hidden(st_lo, n, b1key, oht_lo, pbfree, maskfree, h1s=512):
        for ht in range(8):
            bk = ht % 2
            p.wait("t", "EV", pbfree[bk])
            for dt_ in range(NT):
                mm0(pb[bk][:, 0:n], WSl(0, dt_, 128 * ht, 128),
                    XAf(dt_, st_lo, n), dt_ == 0, dt_ == NT - 1)
            p.wait("t", "SE", maskfree[bk])
            mk = mm(pb[2 + bk][:, 0:n],
                    SPK[0:16, 128 * ht:128 * ht + 128],
                    OHT(oht_lo, n), True, True)
            p.wait("s", "MM", mk)
            maskfree[bk] = sop(lambda e, S, bk=bk, n=n: e.activation(
                AR1f[:, MSKf[bk]:MSKf[bk] + n], pb[2 + bk][:, 0:n], AF.Copy))
            p.wait("v", "SE", maskfree[bk])
            p.wait("v", "MM", mk)
            vop0(lambda e, S, bk=bk, n=n, b1key=b1key, ht=ht: e.scalar_tensor_tensor(
                AR1f[:, STTf[bk]:STTf[bk] + n], pb[bk][:, 0:n],
                MISC[:, MI_B[b1key] + ht:MI_B[b1key] + ht + 1],
                AR1f[:, MSKf[bk]:MSKf[bk] + n], op0=AL.add, op1=AL.mult))
            pbfree[bk] = vop(lambda e, S, bk=bk, n=n, ht=ht, h1s=h1s: e.tensor_scalar(
                AR2[:, h1s * ht:h1s * ht + n],
                AR1f[:, STTf[bk]:STTf[bk] + n], 0.0, None, op0=AL.max))
        return pbfree, maskfree

    def emit_ffn(li, gate_ev):
        if 'ln' not in parts:
            return gate_ev
        ln_ev = emit_ln(1e-5, True, max(gate_ev, tp_er))
        if li == 0 and 'tp' in parts:
            tp_state["ev"], tp_state["gs"] = tprime_rtiles(ln_ev)
        if 'kv' not in parts:
            p.wait("t", "EV", tp_state["ev"])
            p.wait("t", "GS", tp_state["gs"])
            return ln_ev

        pbfree = [ln_ev, ln_ev, ln_ev, ln_ev]
        maskfree = [0, 0]
        scores_gate = tp_er if li == 0 else gate_ev
        bounce_last = 0
        bst_free = [0, 0]
        vb2_pre = [None]
        for pi, pfx, b1key in ((1, "k", "kb1"), (2, "v", "vb1")):
            if vb2_pre[0] is None:
                p.wait("g", "MM", p.ct["MM"])
                b2load = g_dma(B2S(0, 1024), din[f"{pfx}b2"][:, :], sem="WB")
            else:
                b2load = vb2_pre[0]
            trav, sc_ev = fff_scores_traversal(pi, OH_KV, max(scores_gate, ln_ev))
            p.wait("t", "EV", trav)
            p.wait("t", *wf[f"{pfx}w1"])
            p.wait("t", *wf[f"{pfx}w2"])
            p.wait("t", "WB", b2load)
            ofree = [pbfree[0], pbfree[1]]
            ohgate = sc_ev
            for st in range(4):
                lo = 512 * st
                oht_ev = fff_ohT(OH_KV, range(4 * st, 4 * st + 4), 0, ohgate)
                ohgate = oht_ev
                p.wait("t", "EV", oht_ev)
                pbfree, maskfree = fff_hidden(lo, 512, b1key, 0, pbfree, maskfree)
                p.wait("t", "EV", max(pbfree[0], pbfree[1]))
                ofree = [max(ofree[0], pbfree[0]), max(ofree[1], pbfree[1])]
                if pi == 1:
                    jobs = [(dc,) for dc in range(8)]
                else:
                    jobs = [(tc, ds) for tc in range(4) for ds in range(2)]
                for j, job in enumerate(jobs):
                    h2 = j % 2
                    p.wait("t", "EV", ofree[h2])
                    if pi == 1:
                        dc = job[0]
                        for hti in range(NT):
                            mm0(pb[h2][:, 0:512], WSl(1, hti, 128 * dc, 128),
                                AR2[:, 512 * hti:512 * hti + 512], hti == 0, False)
                        lastm = mm(pb[h2][:, 0:512], B2S(128 * dc, 128),
                                   OHT(0, 512), False, True)
                    else:
                        tc, ds = job
                        for hti in range(NT):
                            mm0(pb[h2][:, 0:512],
                                AR2[:, 512 * hti + 128 * tc:512 * hti + 128 * tc + 128],
                                WSl(1, hti, 512 * ds, 512), hti == 0, False)
                        lastm = mm(pb[h2][:, 0:512], OHT(128 * tc, 128),
                                   B2S(512 * ds, 512), False, True)
                    p.wait("v", "MM", lastm)
                    p.wait("v", f"SB{h2}", bst_free[h2])
                    ev = vop(lambda e, S, h2=h2: e.tensor_copy(
                        AR1[:, BST[h2]:BST[h2] + 512], pb[h2][:, 0:512]))
                    ofree[h2] = ev
                    pbfree[h2] = ev
                    p.wait("y", "EV", ev)
                    if pi == 1:
                        dc = job[0]
                        bst_free[h2] = y_dma(kb[128 * dc:128 * dc + 128, lo:lo + 512],
                                             AR1[:, BST[h2]:BST[h2] + 512],
                                             sem=f"SB{h2}")
                    else:
                        tc, ds = job
                        bst_free[h2] = y_dma(
                            vb[lo + 128 * tc:lo + 128 * tc + 128, 512 * ds:512 * ds + 512],
                            AR1[:, BST[h2]:BST[h2] + 512], sem=f"SB{h2}")
            scores_gate = sc_ev
            p.wait("g", "MM", p.ct["MM"])
            if pi == 1:
                wf["vw1"] = wload(0, "vw1c")
                wf["vw2"] = wload(1, "vw2c")
                vb2_pre[0] = g_dma(B2S(0, 1024), din["vb2"][:, :], sem="WB")
                if 'gather' in parts:
                    p.wait("g", "SB0", p.ct["SB0"])
                    p.wait("g", "SB1", p.ct["SB1"])
                    p.add("g", lambda e, S: e.collective_compute(
                        "AllGather", AL.bypass,
                        replica_groups=[[0, 1, 2, 3], [4, 5, 6, 7]],
                        ins=[kb[:, :]], outs=[kg[:, :]]), "CC", 1)
            else:
                wf["qw1"] = wload(0, "qw1c")
                wf["qw2"] = wload(1, "qw2c")
                wf["qb2"] = ("WB", g_dma(B2S(0, 1024), din["qb2"][:, :], sem="WB"))

        if 'gather' not in parts:
            return pbfree[0]
        p.wait("g", "SB0", p.ct["SB0"])
        p.wait("g", "SB1", p.ct["SB1"])
        cc = p.add("g", lambda e, S: e.collective_compute(
            "AllGather", AL.bypass,
            replica_groups=[[0, 1, 2, 3], [4, 5, 6, 7]],
            ins=[vb[:, :]], outs=[vg[:, :]]), "CC", 1)

        travq, sc_ev_q = fff_scores_traversal(0, OH_Q, max(scores_gate, ln_ev))
        p.wait("t", "EV", travq)
        p.wait("t", *wf["qw1"])
        p.wait("t", *wf["qw2"])
        p.wait("t", *wf["qb2"])
        if li == 0:
            p.wait("t", "EV", tp_state["ev"])
            p.wait("t", "GS", tp_state["gs"])
            p.wait("s", "GS", tp_state["gs"])
            p.wait("s", "GF0", p.ct["GF0"])
            p.wait("s", "GF1", p.ct["GF1"])

        if 'flash' not in parts:
            return pbfree[0]
        p.wait("y", "CC", cc)
        p.wait("s", "SB0", p.ct["SB0"])
        p.wait("s", "SB1", p.ct["SB1"])
        span_free = {0: span_free_g["c"], 1: span_free_g["c"]}

        def span_load(which, sp, half):
            p.wait("y", "MM", span_free[half])
            if which == "k":
                r_, lo_ = sp // 4, 512 * (sp % 4)
                src = kg[1024 * r_:1024 * r_ + 1024, lo_:lo_ + 512] \
                    .rearrange("(a p) f -> p a f", p=128)
                na = 8
            else:
                g0 = 512 * sp
                src = vg[g0:g0 + 512, :].rearrange("(a p) f -> p a f", p=128)
                na = 4
            dst = AR3[:, 4096 * half:4096 * half + 4096] \
                .rearrange("p (a f) -> p a f", a=na)
            return y_dma(dst, src, sem=f"KL{half}")

        AVR = [pb[0][:, 0:256], pb[1][:, 0:256], pb[2][:, 0:256], pb[3][:, 0:256],
               pS1[:, 0:256], pS1[:, 512:768], pS2[:, 0:256], pS2[:, 512:768]]
        SUMR = pb[0][0:1, 384:640] if False else pb[0][0:1, 256:512]
        qofree = {0: 0, 1: 0}
        av_ev = 0
        ohgate = sc_ev_q
        for s in range(NSTRIP_F):
            qlo = SW_F * s
            p.wait("t", "EV", av_ev)
            oht_ev = fff_ohT(OH_Q, [2 * s, 2 * s + 1], 0, max(ohgate, av_ev))
            ohgate = oht_ev
            p.wait("t", "EV", oht_ev)
            fff_hidden(qlo, SW_F, "qb1", 0, pbfree, maskfree, h1s=256)
            p.wait("t", "EV", max(pbfree[0], pbfree[1]))
            for dc in range(8):
                h2 = dc % 2
                p.wait("t", "SE", qofree[h2])
                for hti in range(NT):
                    mm0(pS1[:, 512 * h2:512 * h2 + 256], WSl(1, hti, 128 * dc, 128),
                        AR2[:, 256 * hti:256 * hti + 256], hti == 0, False)
                lastm = mm(pS1[:, 512 * h2:512 * h2 + 256],
                           B2S(128 * dc, 128), OHT(0, 256), False, True)
                p.wait("s", "MM", lastm)
                qofree[h2] = sop(lambda e, S, h2=h2, dc=dc: e.activation(
                    AR2[:, 2048 + 256 * dc:2048 + 256 * dc + 256],
                    pS1[:, 512 * h2:512 * h2 + 256], AF.Copy, scale=1.0 / 32.0))
            fq_se = qofree[1]
            p.wait("t", "SE", fq_se)
            p.wait("t", "EV", p.ct["EV"])
            p.wait("t", "SE", p.ct["SE"])
            exp_se = {}
            for sp in range(NSPAN):
                half = sp % 2
                ld = span_load("k", sp, half)
                p.wait("t", f"KL{half}", ld)
                pS = pS1 if half == 0 else pS2
                p.wait("t", "SE", exp_se.get(sp - 2, 0))
                for kc in range(4):
                    for dt_ in range(NT):
                        fn_ = mm if (kc == 3 and dt_ == NT - 1) else mm0
                        fn_(pS[:, 256 * kc:256 * kc + 256],
                            AR3[:, 4096 * half + 512 * dt_ + 128 * kc:
                                4096 * half + 512 * dt_ + 128 * kc + 128],
                            AR2[:, 2048 + 256 * dt_:2048 + 256 * dt_ + 256],
                            dt_ == 0, dt_ == NT - 1)
                mmv = p.ct["MM"]
                span_free[half] = mmv
                p.wait("s", "MM", mmv)
                exp_se[sp] = sop(lambda e, S, sp=sp, pS=pS: e.activation(
                    AR1[:, 1024 * sp:1024 * sp + 1024], pS[:, 0:1024], AF.Exp))
            p.wait("t", "SE", exp_se[NSPAN - 1])
            for sp in range(NSPAN):
                half = sp % 2
                ld = span_load("v", sp, half)
                p.wait("t", f"KL{half}", ld)
                for kc in range(4):
                    first = (sp == 0 and kc == 0)
                    lastf = (sp == NSPAN - 1 and kc == 3)
                    for dc in range(8):
                        fn_ = mm if (kc == 3 and dc == 7) else mm0
                        fn_(AVR[dc],
                            AR3[:, 4096 * half + 1024 * kc + 128 * dc:
                                4096 * half + 1024 * kc + 128 * dc + 128],
                            AR1[:, 1024 * sp + 256 * kc:1024 * sp + 256 * kc + 256],
                            first, lastf)
                span_free[half] = p.ct["MM"]
            span_free_g["c"] = max(span_free.values())
            for K in range(64):
                fn_ = mm if K == 63 else mm0
                fn_(SUMR, ONECOL, AR1[:, 256 * K:256 * K + 256], K == 0, K == 63)
            msum = p.ct["MM"]
            p.wait("v", "MM", msum)
            vop0(lambda e, S: e.reciprocal(ROW0(1024, 256), SUMR))
            vop0(lambda e, S: e.tensor_scalar(ROW0(1280, 256), ROW0(1024, 256),
                                              0.5, None, op0=AL.mult))
            rec_ev = vop(lambda e, S: e.tensor_copy(
                NWB[0:1, NW_REC:NW_REC + 256], ROW0(1280, 256)))
            p.wait("t", "EV", rec_ev)
            mrep = mm(pb[1][:, 256:512], ONEROW, NWB[0:1, NW_REC:NW_REC + 256],
                      True, True)
            p.wait("v", "MM", mrep)
            vop0(lambda e, S: e.tensor_copy(AR1f[:, 0:256], pb[1][:, 256:512]))
            for dc in range(8):
                vop0(lambda e, S, dc=dc: e.tensor_mul(
                    AR1f[:, 256:512], AVR[dc], AR1f[:, 0:256]))
                av_ev = vop(lambda e, S, dc=dc, qlo=qlo: e.tensor_add(
                    Xf(dc, qlo, 256), Xf(dc, qlo, 256), AR1f[:, 256:512]))
            pbfree[0] = pbfree[1] = pbfree[2] = pbfree[3] = av_ev
        return av_ev


    Xb = X.ap().bitcast(BF16)
    QMT, KMT, O1C = 0, 8192, 16384
    SW_A = 256
    eyeT = CON[:, CON_EYE:CON_EYE + 64]
    eyeB = CON[:, CON_EYE + 64:CON_EYE + 128]
    SCRA = NWB[64:128, 768:1280]
    SCRF = NWB[:, 768:1280]
    REPS = NWB[:, 1280:1536]
    BSB = NWB[0:64, 1536:1600]
    BSTB = NWB[0:64, 1600:1664]
    RECROW = NWB[64:65, 512:768]

    def Yf(slot, dt_, lo, n):
        o = 12288 + 512 * dt_ + lo
        return X[:, o:o + n]

    def emit_attn(gate_ev):
        trunc = [False]
        ln_ev = emit_ln(1e-5, True, gate_ev)
        p.wait("y", "EV", ln_ev)
        for dt_ in range(NT):
            y_dma(xspill[128 * dt_:128 * (dt_ + 1), :], Xf(dt_, 0, 2048), sem="XS")
        xs_done = p.ct["XS"]
        p.wait("v", "XS", xs_done)
        ev0 = vop(lambda e, S: e.memset(NWB[0:64, 768:1280], 0.0))
        ev0 = vop(lambda e, S: e.memset(Xb[:, O1C:O1C + 8192], 0.0))
        ev0 = vop(lambda e, S: e.memset(Xb[64:128, QMT:QMT + 8192], 0.0))
        ev0 = vop(lambda e, S: e.memset(Xb[64:128, KMT:KMT + 8192], 0.0))
        wreg = {"i": 0}
        wuse = {0: p.ct["MM"], 1: p.ct["MM"]}

        def next_wload(name):
            r = wreg["i"] % 2
            wreg["i"] += 1
            p.wait("g", "MM", wuse[r])
            p.wait("g", f"WA{r}", p.ct[f"WA{r}"])
            return wload(r, name), r

        state = {"shift": [ev0, ev0], "pb23": [ev0, ev0], "ps1b": [ev0, ev0],
                 "av": ev0, "exp": {}, "qo": 0}

        def proj(nn, wl, reg, bkey, dst_base, dst_tensor):
            p.wait("t", *wl)
            for i in range(8):
                bk = 2 + (i % 2)
                p.wait("t", "EV", state["pb23"][i % 2])
                p.wait("t", "SE", state.get("pb23se", [0, 0])[i % 2])
                for dt_ in range(NT):
                    fn_ = mm if dt_ == NT - 1 else mm0
                    fn_(pb[bk][:, 0:512], WSl(reg, dt_, 128 * i, 128),
                        XAf(dt_, 512 * nn, 512), dt_ == 0, dt_ == NT - 1)
                mmv = p.ct["MM"]
                p.wait("s", "MM", mmv)
                se1 = sop(lambda e, S, bk=bk: e.activation(
                    SCRA, pb[bk][64:128, 0:512], AF.Copy))
                state.setdefault("pb23se", [0, 0])[i % 2] = se1
                p.wait("v", "MM", mmv)
                ev1 = vop(lambda e, S, bk=bk, i=i, bkey=bkey, dst_base=dst_base,
                          dst_tensor=dst_tensor: e.tensor_scalar(
                    dst_tensor[0:64, dst_base + 2 * i:dst_base + 2 * i + 8177:16],
                    pb[bk][0:64, 0:512],
                    MISC[0:64, MI_B[bkey] + i:MI_B[bkey] + i + 1], None,
                    op0=AL.add))
                p.wait("t", "SE", se1)
                h2 = i % 2
                p.wait("t", "EV", state["ps1b"][h2])
                ms = mm(pS1[0:64, 512 * h2:512 * h2 + 512], eyeB, SCRF,
                        True, True)
                p.wait("v", "MM", ms)
                ev2 = vop(lambda e, S, i=i, h2=h2, bkey=bkey, dst_base=dst_base,
                          dst_tensor=dst_tensor: e.tensor_scalar(
                    dst_tensor[0:64, dst_base + 2 * i + 1:dst_base + 2 * i + 8178:16],
                    pS1[0:64, 512 * h2:512 * h2 + 512],
                    MISC[0:64, MI_B[bkey + "s"] + i:MI_B[bkey + "s"] + i + 1],
                    None, op0=AL.add))
                state["ps1b"][h2] = ev2
                state["pb23"][i % 2] = ev1
            wuse[reg] = p.ct["MM"]
            return p.ct["EV"]


        def score_shift(nn):
            p.wait("t", "EV", p.ct["EV"])
            p.wait("t", "SE", p.ct["SE"])
            p.wait("s", "EV", p.ct["EV"])
            p.wait("s", "MM", p.ct["MM"])
            vop0(lambda e, S: e.memset(Xb[64:65, KMT:KMT + 8192], -1.0))
            mz = vop(lambda e, S: e.memset(Xb[64:65, QMT:QMT + 8192], 0.0))
            p.wait("t", "EV", mz)
            expse = {}
            dense_ = {}

            def emit_ones(bb):
                for k4 in range(4):
                    fn_ = mm if (bb == 15 and k4 == 3) else mm0
                    fn_(pb[0][0:1, 0:256], ONECOL,
                        AR1[:, 1024 * (bb % 4) + 256 * k4:
                            1024 * (bb % 4) + 256 * k4 + 256],
                        bb == 0 and k4 == 0, bb == 15 and k4 == 3)

            for s_ in range(32):
                ilo = SW_A * s_
                p.wait("t", "SE", dense_.get("ln", 0))
                for b_ in range(16):
                    pS = pS1 if b_ % 2 == 0 else pS2
                    p.wait("t", "SE", expse.get(b_ - 2, 0))
                    for k4 in range(4):
                        kc = 4 * b_ + k4
                        fn_ = mm if k4 == 3 else mm0
                        fn_(pS[:, 256 * k4:256 * k4 + 256],
                            Xb[0:128, KMT + 128 * kc:KMT + 128 * kc + 128],
                            Xb[0:128, QMT + ilo:QMT + ilo + SW_A],
                            True, True)
                    if b_ >= 2:
                        emit_ones(b_ - 2)
                    mmv = p.ct["MM"]
                    p.wait("s", "MM", mmv)
                    expse[b_] = sop(lambda e, S, b_=b_, pS=pS: e.activation(
                        AR1[:, 1024 * (b_ % 4):1024 * (b_ % 4) + 1024],
                        pS[:, 0:1024], AF.Exp, scale=0.125))
                for bb in (14, 15):
                    p.wait("t", "SE", expse[bb])
                    emit_ones(bb)
                smm = p.ct["MM"]
                p.wait("s", "MM", smm)
                p.wait("s", "EV", p.ct["EV"])
                lnse = sop(lambda e, S: e.activation(ROWS[0:1, 0:256],
                                                     pb[0][0:1, 0:256],
                                                     AF.Ln, bias=0.0))
                dense_["ln"] = lnse
                p.wait("v", "SE", lnse)
                cev = vop(lambda e, S, s_=s_: e.tensor_scalar(
                    AR1[0:1, 8192 + 256 * s_:8192 + 256 * s_ + 256],
                    ROWS[0:1, 0:256], 8.0, -76.0, op0=AL.mult, op1=AL.add))
                expse = {}
            p.wait("g", "EV", cev)
            cdma = g_dma(Xb[64:65, QMT:QMT + 8192], AR1[0:1, 8192:16384],
                         sem="GC")
            p.wait("t", "GC", cdma)
            p.wait("t", "EV", cev)
            p.wait("t", "SE", p.ct["SE"])

        def bias_pass(nn, ihalf, gate):
            o2g = gate
            for ii in range(32):
                I = 32 * ihalf + ii
                h2 = ii % 2
                p.wait("t", "EV", state["ps1b"][h2])
                first = True
                for di, dlt in ((0, -1), (1, 0), (2, 1)):
                    J = I + dlt
                    if J < 0 or J > 63:
                        continue
                    mm0(pS1[0:64, 512 * h2:512 * h2 + 128], AR3[:, 65 * J:65 * J + 64],
                        TPR(di, nn), first, False)
                    first = False
                lmm = mm(pS1[0:64, 512 * h2:512 * h2 + 128], AR3[:, 65 * I:65 * I + 64],
                         TSN(nn), False, True)
                p.wait("v", "MM", lmm)
                o2g = vop(lambda e, S, I=I, h2=h2: e.tensor_scalar(
                    AR2[0:64, 128 * (I % 32):128 * (I % 32) + 128],
                    pS1[0:64, 512 * h2:512 * h2 + 128],
                    MISC[0:64, MI_F + I:MI_F + I + 1], None, op0=AL.add))
                state["ps1b"][h2] = o2g
            return o2g

        ab_ev = 0
        for nn in range(attn_nn):
            lastb = (nn == attn_nn - 1)
            tokb = 512 * nn
            wl_q, reg_q = next_wload("wqT")
            wl_k, reg_k = next_wload("wkT")
            if nn == 0:
                p.wait("t", "EV", ln_ev)
                p.wait("t", "XS", xs_done)
                p.wait("v", "XS", xs_done)
            proj(nn, wl_q, reg_q, "bq", QMT, Xb)
            if lastb and attn_stage <= 1:
                trunc[0] = True; break
            wl_v, reg_v = next_wload("wvT")
            proj(nn, wl_k, reg_k, "bk", KMT, Xb)
            if lastb and attn_stage <= 2:
                trunc[0] = True; break
            score_shift(nn)
            if lastb and attn_stage <= 3:
                trunc[0] = True; break
            wl_wog, reg_wog = next_wload("wogT")
            vev = proj(nn, wl_v, reg_v, "bv", 0, AR3)
            if lastb and attn_stage <= 4:
                trunc[0] = True; break
            pS2b = pS2.ap().bitcast(BF16)
            slots = [pS2b[0:128, 0:64], pS2b[0:128, 1024:1088]]
            sfree = [vev, vev]
            for c in range(64):
                sl_ = c % 2
                p.wait("t", "EV", sfree[sl_])
                p.add("t", lambda e, S, c=c, sl_=sl_: e.transpose(
                    slots[sl_], AR3[0:64, 128 * c:128 * c + 128],
                    CON[0:64, CON_EYE:CON_EYE + 64]), "MM", 1)
                mv = p.ct["MM"]
                p.wait("v", "MM", mv)
                vop0(lambda e, S, c=c, sl_=sl_: e.tensor_copy(
                    AR3[:, 65 * c:65 * c + 64], slots[sl_]))
                sfree[sl_] = vop(lambda e, S, c=c: e.memset(
                    AR3[:, 65 * c + 64:65 * c + 65], 1.0))
            vmo_ev = p.ct["EV"]
            if lastb and attn_stage <= 5:
                trunc[0] = True; break
            p.wait("t", "EV", vmo_ev)
            p.wait("t", "EV", state["pb23"][0])
            p.wait("t", "EV", state["pb23"][1])
            for J in range(64):
                fn_ = mm if J == 63 else mm0
                fn_(pb[2][0:64, J:J + 1], AR3[:, 65 * J:65 * J + 64], ONECOL,
                    True, True)
            bsmm = p.ct["MM"]
            p.wait("v", "MM", bsmm)
            bse = vop(lambda e, S: e.tensor_copy(BSB, pb[2][0:64, 0:64]))
            p.wait("t", "EV", bse)
            p.add("t", lambda e, S: e.transpose(
                pS2.ap().bitcast(BF16)[0:64, 0:64], BSB,
                CON[0:64, CON_EYE:CON_EYE + 64]), "MM", 1)
            btr = p.ct["MM"]
            p.wait("v", "MM", btr)
            bte = vop(lambda e, S: e.tensor_copy(
                BSTB, pS2.ap().bitcast(BF16)[0:64, 0:64]))
            p.wait("t", "EV", bte)
            mm0(pb[3][0:64, 0:64], BSTB, CON[0:64, CON_L64:CON_L64 + 64],
                True, True)
            lsmm = mm(pb[3][0:64, 64:65], BSTB,
                      CON[0:64, CON_L64 + 64:CON_L64 + 65], True, True)
            p.wait("v", "MM", lsmm)
            vop0(lambda e, S, nn=nn: e.tensor_scalar(
                MISC[0:64, 72:73], pb[3][0:64, 64:65],
                MISC[0:64, MI_REP + nn:MI_REP + nn + 1], None, op0=AL.mult))
            fev = vop(lambda e, S, nn=nn: e.tensor_scalar(
                MISC[0:64, MI_F:MI_F + 64], pb[3][0:64, 0:64],
                MISC[0:64, MI_REP + 4 + nn:MI_REP + 4 + nn + 1],
                MISC[0:64, 72:73], op0=AL.mult, op1=AL.add))
            state["pb23"] = [fev, fev]
            if lastb and attn_stage <= 6:
                trunc[0] = True; break
            o2g = bias_pass(nn, 0, fev)
            for s in range(32):
                if s == 16:
                    o2g = bias_pass(nn, 1, o2g)
                ilo = SW_A * s

                def emit_av(bb):
                    if bb == 0:
                        p.wait("t", "EV", state["av"])
                    for k4 in range(4):
                        kc = 4 * bb + k4
                        fn_ = mm if kc == 63 else mm0
                        fn_(pb[0][0:65, 0:256], AR3[:, 65 * kc:65 * kc + 65],
                            AR1[:, 256 * kc:256 * kc + 256], kc == 0, kc == 63)

                for b_ in range(16):
                    pS = pS1 if b_ % 2 == 0 else pS2
                    p.wait("t", "SE", state["exp"].get(("b", nn, s, b_ - 2), 0))
                    if b_ < 2:
                        p.wait("t", "EV", state["ps1b"][0])
                        p.wait("t", "EV", state["ps1b"][1])
                        p.wait("t", "EV", o2g)
                    for k4 in range(4):
                        kc = 4 * b_ + k4
                        fn_ = mm if k4 == 3 else mm0
                        fn_(pS[:, 256 * k4:256 * k4 + 256],
                            Xb[0:128, KMT + 128 * kc:KMT + 128 * kc + 128],
                            Xb[0:128, QMT + ilo:QMT + ilo + SW_A],
                            True, True)
                    if b_ >= 2:
                        emit_av(b_ - 2)
                    mmv = p.ct["MM"]
                    p.wait("s", "MM", mmv)
                    state["exp"][("b", nn, s, b_)] = sop(
                        lambda e, S, b_=b_, pS=pS: e.activation(
                            AR1[:, 1024 * b_:1024 * b_ + 1024], pS[:, 0:1024],
                            AF.Exp, bias=0.0))
                for bb in (14, 15):
                    p.wait("t", "SE", state["exp"][("b", nn, s, bb)])
                    emit_av(bb)
                avm = p.ct["MM"]
                p.wait("v", "MM", avm)
                vop0(lambda e, S: e.tensor_scalar(
                    ROWS[64:65, 1024:1280], pb[0][64:65, 0:256], 1e-30, None,
                    op0=AL.max))
                vop0(lambda e, S: e.reciprocal(ROWS[64:65, 1024:1280],
                                               ROWS[64:65, 1024:1280]))
                rec = vop(lambda e, S: e.tensor_copy(RECROW,
                                                     ROWS[64:65, 1024:1280]))
                p.wait("t", "EV", rec)
                mrep = mm(pb[1][:, 0:256], CON[64:65, CON_ONEB:CON_ONEB + 128],
                          RECROW, True, True)
                p.wait("v", "MM", mrep)
                vop0(lambda e, S: e.tensor_copy(REPS, pb[1][:, 0:256]))
                vop0(lambda e, S, ilo=ilo: e.tensor_mul(
                    Xb[0:64, O1C + ilo:O1C + ilo + 256], pb[0][0:64, 0:256],
                    REPS[0:64, :]))
                state["av"] = vop(lambda e, S, ilo=ilo, s=s: e.tensor_add(
                    Xb[0:64, O1C + ilo:O1C + ilo + 256],
                    Xb[0:64, O1C + ilo:O1C + ilo + 256],
                    AR2[0:64, 128 * ((2 * s) % 32):128 * ((2 * s) % 32) + 256]))
            if lastb and attn_stage <= 7:
                trunc[0] = True; break
            dupev = state["av"]
            for i in range(16):
                h2 = i % 2
                p.wait("t", "EV", state["ps1b"][h2])
                p.wait("t", "EV", dupev)
                msd = p.add("t", lambda e, S, i=i, h2=h2: e.matmul(
                    pS1[64:128, 512 * h2:512 * h2 + 512], eyeT,
                    Xb[:, O1C + 512 * i:O1C + 512 * i + 512],
                    start=True, stop=True, tile_position=(0, 64)), "MM", 1)
                p.wait("v", "MM", p.ct["MM"])
                state["ps1b"][h2] = vop(lambda e, S, i=i, h2=h2: e.tensor_copy(
                    Xb[64:128, O1C + 512 * i:O1C + 512 * i + 512],
                    pS1[64:128, 512 * h2:512 * h2 + 512]))
            dup_ev = p.ct["EV"]
            if lastb and attn_stage <= 8:
                trunc[0] = True; break
            p.wait("t", *wl_wog)
            p.wait("t", "EV", dup_ev)
            slot = 0
            yev = 0
            for dc in range(8):
                bk = 2 + (dc % 2)
                h = dc % 2
                p.wait("t", "EV", state["pb23"][h])
                p.wait("t", "EV", state["ps1b"][h])
                for par in range(2):
                    base = 64 * par
                    dst = pb[bk][:, 0:512] if par == 0 else \
                        pS1[:, 512 * h:512 * h + 512]
                    for gi in range(8):
                        g = 2 * gi + par
                        fn_ = mm if (par == 1 and gi == 7) else mm0
                        fn_(dst,
                            WA[base:base + 64,
                               8192 * reg_wog + 1024 * gi + 128 * dc:
                               8192 * reg_wog + 1024 * gi + 128 * dc + 128],
                            Xb[base:base + 64, O1C + g:O1C + g + 8177:16],
                            gi == 0, gi == 7)
                wmm = p.ct["MM"]
                p.wait("s", "MM", wmm)
                se = sop(lambda e, S, h=h: e.activation(
                    AR1f[:, 5120 + 512 * h:5120 + 512 * h + 512],
                    pS1[:, 512 * h:512 * h + 512], AF.Copy))
                p.wait("v", "MM", wmm)
                p.wait("v", "SE", se)
                vop0(lambda e, S, dc=dc, bk=bk, h=h: e.tensor_add(
                    AR1f[:, 2048 + 512 * h:2048 + 512 * h + 512],
                    pb[bk][:, 0:512], AR1f[:, 5120 + 512 * h:5120 + 512 * h + 512]))
                vop0(lambda e, S, dc=dc, h=h: e.tensor_scalar(
                    AR1f[:, 2048 + 512 * h:2048 + 512 * h + 512],
                    AR1f[:, 2048 + 512 * h:2048 + 512 * h + 512],
                    MISC[:, MI_B["bo"] + dc:MI_B["bo"] + dc + 1],
                    None, op0=AL.add))
                yev = vop(lambda e, S, dc=dc, slot=slot, nn=nn, h=h: e.tensor_add(
                    Yf(slot, dc, 0, 512),
                    AR1f[:, 2048 + 512 * h:2048 + 512 * h + 512],
                    XAf(dc, 512 * nn, 512)))
                state["pb23"][h] = yev
                state["ps1b"][h] = yev
            wuse[reg_wog] = p.ct["MM"]
            if debug and nn == 0:
                p.wait("y", "EV", yev)
                y_dma(dbg["dyt"][:, :].bitcast(mybir.dt.uint32),
                      X[:, 12288:14336].bitcast(mybir.dt.uint32), sem="SO")
                y_dma(dbg["do1"][:, :].bitcast(mybir.dt.uint16),
                      Xb[:, O1C:O1C + 8192].bitcast(mybir.dt.uint16), sem="SO")
                y_dma(dbg["do2"][:, :].bitcast(mybir.dt.uint16),
                      AR2[:, 0:4096].bitcast(mybir.dt.uint16), sem="SO")
                p.wait("t", "SO", p.ct["SO"])
            if lastb and attn_stage <= 9:
                trunc[0] = True; break
            ab_ev = ln_rms_block(nn, slot, yev)
            state["ps1b"] = [ab_ev, ab_ev]
            state["pb23"] = [ab_ev, ab_ev]
            state["av"] = ab_ev

        if trunc[0] or attn_nn < 4:
            return p.ct["EV"], xs_done
        wl1, reg1 = next_wload("w1aT")
        wl2, reg2 = next_wload("w2aT")
        p.wait("t", *wl1)
        p.wait("t", "EV", ab_ev)
        pbf = [ab_ev, ab_ev]
        h1f = [0, 0]
        for st in range(4):
            lo = 512 * st
            for ht in range(8):
                bk = 2 + (ht % 2)
                p.wait("t", "EV", pbf[ht % 2])
                for dt_ in range(NT):
                    fn_ = mm if dt_ == NT - 1 else mm0
                    fn_(pb[bk][:, 0:512], WSl(reg1, dt_, 128 * ht, 128),
                        XAf(dt_, lo, 512), dt_ == 0, dt_ == NT - 1)
                hm = p.ct["MM"]
                p.wait("v", "MM", hm)
                pbf[ht % 2] = vop(lambda e, S, bk=bk, ht=ht: e.tensor_scalar(
                    AR2[:, 512 * ht:512 * ht + 512], pb[bk][:, 0:512],
                    MISC[:, MI_B["b1a"] + ht:MI_B["b1a"] + ht + 1], None,
                    op0=AL.add))
            if st == 0:
                p.wait("t", *wl2)
            p.wait("t", "EV", max(pbf))
            for dc in range(8):
                bk = 2 + (dc % 2)
                p.wait("t", "EV", pbf[dc % 2])
                for hti in range(NT):
                    fn_ = mm if hti == NT - 1 else mm0
                    fn_(pb[bk][:, 0:512], WSl(reg2, hti, 128 * dc, 128),
                        AR2[:, 512 * hti:512 * hti + 512], hti == 0,
                        hti == NT - 1)
                zm = p.ct["MM"]
                p.wait("v", "MM", zm)
                vop0(lambda e, S, bk=bk, dc=dc: e.tensor_scalar(
                    AR1f[:, 2048 + 512 * (dc % 2):2048 + 512 * (dc % 2) + 512],
                    pb[bk][:, 0:512],
                    MISC[:, MI_B["b2a"] + dc:MI_B["b2a"] + dc + 1], None,
                    op0=AL.add))
                pbf[dc % 2] = vop(lambda e, S, bk=bk, dc=dc, lo=lo: e.tensor_add(
                    Xf(dc, lo, 512),
                    AR1f[:, 2048 + 512 * (dc % 2):2048 + 512 * (dc % 2) + 512],
                    XAf(dc, lo, 512)))
        z_ev = max(pbf)
        return z_ev, xs_done


    def ln_rms_block(nn, slot, gate_ev):
        p.wait("t", "EV", gate_ev)
        p.wait("s", "MM", p.ct["MM"])
        sq = {}
        for dt_ in range(NT):
            slt = LNSf[dt_ % 2]
            p.wait("s", "MM", sq.get(dt_ - 2, 0))
            p.wait("s", "EV", gate_ev)
            sev = sop(lambda e, S, dt_=dt_, slt=slt, slot=slot: e.activation(
                AR1f[:, slt:slt + 512], Yf(slot, dt_, 0, 512), AF.Square))
            p.wait("t", "SE", sev)
            for c4 in range(4):
                mm0(pb[c4][:, 4 + nn:5 + nn], Yf(slot, dt_, 128 * c4, 128),
                    MISC[:, MI_OCF:MI_OCF + 1], dt_ == 0, dt_ == NT - 1)
                fn_ = mm0 if c4 < 3 else (lambda *a, **k: mm(*a, **k))
                fn_([pS1, pS1, pS2, pS2][c4][:, [0, 512, 0, 512][c4] + 4 + nn:
                                             [0, 512, 0, 512][c4] + 5 + nn],
                    AR1f[:, slt + 128 * c4:slt + 128 * c4 + 128],
                    MISC[:, MI_OCF:MI_OCF + 1], dt_ == 0, dt_ == NT - 1)
            sq[dt_] = p.ct["MM"]
        p.wait("v", "MM", p.ct["MM"])
        SQR = [pS1[:, 0:512], pS1[:, 512:1024], pS2[:, 0:512], pS2[:, 512:1024]]
        for c4 in range(4):
            vop0(lambda e, S, c4=c4, nn=nn: e.tensor_scalar(
                MISC[:, MI_SM + 4 * nn + c4:MI_SM + 4 * nn + c4 + 1],
                pb[c4][:, 4 + nn:5 + nn], 1.0 / 1024, None, op0=AL.mult))
            vop0(lambda e, S, c4=c4, nn=nn: e.tensor_scalar(
                MISC[:, MI_SV + 4 * nn + c4:MI_SV + 4 * nn + c4 + 1],
                SQR[c4][:, 4 + nn:5 + nn], 1.0 / 1024, None, op0=AL.mult))
        cm, cv = MISC[:, MI_SM + 4 * nn:MI_SM + 4 * nn + 4], \
            MISC[:, MI_SV + 4 * nn:MI_SV + 4 * nn + 4]
        vop0(lambda e, S, cm=cm, nn=nn: e.tensor_mul(
            MISC[:, MI_SS:MI_SS + 4], cm, cm))
        sev2 = vop(lambda e, S, cv=cv: e.tensor_sub(
            cv, cv, MISC[:, MI_SS:MI_SS + 4]))
        p.wait("s", "EV", sev2)
        sop(lambda e, S, cv=cv: e.activation(MISC[:, MI_SS:MI_SS + 4], cv,
                                             AF.Ln, bias=0.0))
        rst = sop(lambda e, S, nn=nn: e.activation(
            MISC[:, MI_SR + 4 * nn:MI_SR + 4 * nn + 4],
            MISC[:, MI_SS:MI_SS + 4], AF.Exp, scale=-0.5))
        p.wait("v", "SE", rst)
        m16 = vop(lambda e, S, nn=nn: e.tensor_copy(
            MISCb[:, MB_M + 4 * nn:MB_M + 4 * nn + 4],
            MISC[:, MI_SM + 4 * nn:MI_SM + 4 * nn + 4]))
        m16 = vop(lambda e, S, nn=nn: e.tensor_copy(
            MISCb[:, MB_R + 4 * nn:MB_R + 4 * nn + 4],
            MISC[:, MI_SR + 4 * nn:MI_SR + 4 * nn + 4]))
        p.wait("t", "EV", m16)
        for c4 in range(4):
            c = 4 * nn + c4
            mm0(pb[0][0:1, 128 * c4:128 * c4 + 128], MISCb[:, MB_M + c:MB_M + c + 1],
                CON[:, CON_EYE:CON_EYE + 128], True, True)
            fn_ = mm0 if c4 < 3 else (lambda *a, **k: mm(*a, **k))
            fn_(pb[1][0:1, 128 * c4:128 * c4 + 128], MISCb[:, MB_R + c:MB_R + c + 1],
                CON[:, CON_EYE:CON_EYE + 128], True, True)
        p.wait("v", "MM")
        vop0(lambda e, S: e.tensor_copy(ROW0(0, 512), pb[0][0:1, 0:512]))
        rowev = vop(lambda e, S: e.tensor_copy(ROW0(512, 512), pb[1][0:1, 0:512]))
        p.wait("t", "EV", rowev)
        mm0(pb[0][:, 0:512], MISC[0:1, MI_ONE:MI_ONE + 128], ROW0(0, 512),
            True, True)
        mrr = mm(pb[1][:, 0:512], MISC[0:1, MI_ONE:MI_ONE + 128], ROW0(512, 512),
                 True, True)
        p.wait("v", "MM", mrr)
        ab_ev = 0
        for dt_ in range(NT):
            vop0(lambda e, S, dt_=dt_, slot=slot: e.tensor_sub(
                AR1f[:, STTf[dt_ % 2]:STTf[dt_ % 2] + 512],
                Yf(slot, dt_, 0, 512), pb[0][:, 0:512]))
            ab_ev = vop(lambda e, S, dt_=dt_, nn=nn: e.tensor_mul(
                XAf(dt_, 512 * nn, 512),
                AR1f[:, STTf[dt_ % 2]:STTf[dt_ % 2] + 512], pb[1][:, 0:512]))
        return ab_ev


    def final_residual(gate_ev):
        mr = emit_ln(0.0, False, gate_ev)
        p.wait("y", "XS", p.ct["XS"])
        p.wait("y", "EV", mr)
        xr_free = {}
        for st in range(4):
            lo = 512 * st
            bk = (st % 2) * 2
            for dt_ in range(NT):
                p.wait("y", "EV", xr_free.get((st - 1, dt_), 0))
                y_dma(AR1f[:, 4096 + 512 * dt_:4096 + 512 * dt_ + 512],
                      xspill[128 * dt_:128 * (dt_ + 1), lo:lo + 512], sem="XR")
            xr = p.ct["XR"]
            p.wait("t", "EV", mr if st == 0 else 0)
            for c4 in range(4):
                c = 4 * st + c4
                mm0(pb[bk][0:1, 128 * c4:128 * c4 + 128],
                    MISCb[:, MB_M + c:MB_M + c + 1],
                    CON[:, CON_EYE:CON_EYE + 128], True, True)
                fn_ = mm0 if c4 < 3 else (lambda *a, **k: mm(*a, **k))
                fn_(pb[bk + 1][0:1, 128 * c4:128 * c4 + 128],
                    MISCb[:, MB_R + c:MB_R + c + 1],
                    CON[:, CON_EYE:CON_EYE + 128], True, True)
            p.wait("v", "MM")
            vop0(lambda e, S, bk=bk: e.tensor_copy(ROW0(0, 512), pb[bk][0:1, 0:512]))
            rowev = vop(lambda e, S, bk=bk: e.tensor_copy(ROW0(512, 512),
                                                          pb[bk + 1][0:1, 0:512]))
            p.wait("t", "EV", rowev)
            mm0(pb[bk][:, 0:512], MISC[0:1, MI_ONE:MI_ONE + 128], ROW0(0, 512),
                True, True)
            mrr = mm(pb[bk + 1][:, 0:512], MISC[0:1, MI_ONE:MI_ONE + 128],
                     ROW0(512, 512), True, True)
            p.wait("v", "MM", mrr)
            p.wait("v", "XR", xr)
            last = 0
            for dt_ in range(NT):
                hm = 512 * dt_
                vop0(lambda e, S, dt_=dt_, lo=lo, bk=bk: e.tensor_sub(
                    AR1f[:, LNSf[dt_ % 2]:LNSf[dt_ % 2] + 512],
                    Xf(dt_, lo, 512), pb[bk][:, 0:512]))
                vop0(lambda e, S, dt_=dt_, bk=bk: e.tensor_mul(
                    AR1f[:, LNSf[dt_ % 2]:LNSf[dt_ % 2] + 512],
                    AR1f[:, LNSf[dt_ % 2]:LNSf[dt_ % 2] + 512],
                    pb[bk + 1][:, 0:512]))
                last = vop(lambda e, S, dt_=dt_, lo=lo, hm=hm: e.tensor_add(
                    Xf(dt_, lo, 512),
                    AR1f[:, LNSf[dt_ % 2]:LNSf[dt_ % 2] + 512],
                    AR1f[:, 4096 + hm:4096 + hm + 512]))
                xr_free[(st, dt_)] = last
        return last

    span_free_g = {"c": 0}
    last_ev = emit_ffn(0, ones_v)

    if 'attn' in parts:
        z_ev, _xs = emit_attn(last_ev)
        if attn_nn == 4 and attn_stage >= 10:
            last_ev = final_residual(z_ev)
        if 'ffn2' in parts and attn_nn == 4 and attn_stage >= 10:
            p.wait("g", "MM", p.ct["MM"])
            wf["kw1"] = wload(0, "kw1c")
            wf["kw2"] = wload(1, "kw2c")
            last_ev = emit_ffn(1, last_ev)

    p.wait("y", "EV", p.ct["EV"])
    p.wait("y", "MM", p.ct["MM"])
    p.wait("y", "SE", p.ct["SE"])
    for s_ in ("GS", "GT0", "GT1", "GF0", "GF1", "WB", "WA0", "WA1", "WLC"):
        p.wait("y", s_, p.ct[s_])
    U32 = mybir.dt.uint32
    for dt_ in range(NT):
        y_dma(yout[128 * dt_:128 * (dt_ + 1), :].bitcast(U32) if debug else
              yout[128 * dt_:128 * (dt_ + 1), :],
              Xf(dt_, 0, 2048).bitcast(U32) if debug else Xf(dt_, 0, 2048),
              sem="SO")
    if debug:
        y_dma(dbg["dxa"][:, :], XA[:, :], sem="SO")
        y_dma(dbg["dtp"][:, :], CON[:, CON_TP:CON_TP + 2048], sem="SO")
        if 'gather' in parts:
            y_dma(dbg["doh"][:, 0:256], OHQ[:, :], sem="SO")
        if 'attn' in parts:
            U16 = mybir.dt.uint16
            y_dma(dbg["dqk"][:, :].bitcast(U16), Xb[:, 0:16384].bitcast(U16), sem="SO")
            y_dma(dbg["dvmo"][:, :].bitcast(U16), AR3[:, :].bitcast(U16), sem="SO")
        if 'flash' in parts:
            y_dma(dbg["dfq"][:, :], AR2[:, 2048:4096], sem="SO")
            U16 = mybir.dt.uint16
            y_dma(dbg["de"][:, :].bitcast(U16), AR1[:, :].bitcast(U16), sem="SO")
            y_dma(dbg["dht"][:, :], AR2[:, 0:4096], sem="SO")
    p.wait("y", "SO", p.ct["SO"])

    with nc.Block() as blk:
        semcms = [nc.semaphore(n) for n in p.semnames]
        S = {}
        for n_, cm in zip(p.semnames, semcms):
            S[n_] = cm.__enter__()

        @blk.tensor
        def _(e):
            for fn in p.ops["t"]:
                fn(e, S)

        @blk.vector
        def _(e):
            for fn in p.ops["v"]:
                fn(e, S)

        @blk.scalar
        def _(e):
            for fn in p.ops["s"]:
                fn(e, S)

        @blk.sync
        def _(e):
            for fn in p.ops["y"]:
                fn(e, S)

        @blk.gpsimd
        def _(e):
            for fn in p.ops["g"]:
                fn(e, S)

    return nc



def _wog(wo):
    out = np.zeros((1024, 1024), np.float32)
    w = np.asarray(wo).reshape(1024, 16, 64)
    for g in range(16):
        out[128 * (g // 2) + 64 * (g % 2):128 * (g // 2) + 64 * (g % 2) + 64, :] \
            = w[:, g, :].T
    return np.ascontiguousarray(out)


def _prep_core_inputs(core, inp):
    b, qd = core // 4, core % 4
    sl = slice(2048 * qd, 2048 * (qd + 1))
    C = _get_consts()

    def f32(a):
        return np.ascontiguousarray(np.asarray(a, dtype=np.float32))

    d = {
        "xt": f32(np.asarray(inp["L0_attn_x"])[b, sl, :].T),
        "wqT": f32(np.asarray(inp["L0_attn_Wq"]).T),
        "wkT": f32(np.asarray(inp["L0_attn_Wk"]).T),
        "wvT": f32(np.asarray(inp["L0_attn_Wv"]).T),
        "woT": f32(np.asarray(inp["L0_attn_Wo"]).T),
        "wogT": _wog(np.asarray(inp["L0_attn_Wo"])),
        "w1aT": f32(np.asarray(inp["L0_attn_W1"]).T),
        "w2aT": f32(np.asarray(inp["L0_attn_W2"]).T),
        "relbT": f32(np.asarray(inp["L0_attn_rel_bias"]).T),
        "mflat": C["mflat"], "eyetri": C["eyetri"], "l64": C["l64"],
        "gmask": C["gmask"],
        "fconst": np.tile(np.array([0.0, 1.0, 1e-5, -0.5, 1.0 / 32.0, -40.0,
                                    0.5, 2.0], np.float32), (128, 1)),
    }
    for nm, key in (("bq4", "L0_attn_bq"), ("bk4", "L0_attn_bk"),
                    ("bv4", "L0_attn_bv"), ("bo4", "L0_attn_bo"),
                    ("b1a4", "L0_attn_b1"), ("b2a4", "L0_attn_b2")):
        d[nm] = f32(np.asarray(inp[key]).reshape(8, 128).T)
    for nm in ("bq4", "bk4", "bv4"):
        sh = np.zeros((128, 8), np.float32)
        sh[0:64, :] = d[nm][64:128, :]
        d[nm + "s"] = sh
    for pfx in ("q", "k", "v"):
        d[f"{pfx}nwT"] = f32(np.asarray(inp[f"L0_ffn_{pfx}_node_w"]).T)
        d[f"{pfx}nb"] = f32(np.asarray(inp[f"L0_ffn_{pfx}_node_b"]).reshape(1, 15))
        d[f"{pfx}w1c"] = f32(np.asarray(inp[f"L0_ffn_{pfx}_w1"]).transpose(1, 0, 2)
                             .reshape(1024, 1024))
        d[f"{pfx}b1f4"] = f32(np.asarray(inp[f"L0_ffn_{pfx}_b1"]).reshape(1024)
                              .reshape(8, 128).T)
        d[f"{pfx}w2c"] = f32(np.asarray(inp[f"L0_ffn_{pfx}_w2"]).reshape(1024, 1024))
        d[f"{pfx}b2"] = f32(np.asarray(inp[f"L0_ffn_{pfx}_b2"]))
    hs = np.zeros((16, 4), np.float32)
    for ln in range(4):
        hs[4 * qd + ln, ln] = 1.0
    d["hsel"] = hs
    return d


def kernel(**inputs):
    nc = build_nc()
    in_maps = [_prep_core_inputs(c, inputs) for c in range(NCORE)]
    res = run_bass_kernel_spmd(nc, in_maps, core_ids=list(range(NCORE)))
    out = np.zeros((2, 8192, 1024), np.float32)
    for c in range(NCORE):
        b, qd = c // 4, c % 4
        out[b, 2048 * qd:2048 * (qd + 1), :] = res.results[c]["y"].T
    return out


# version 06
# speedup vs baseline: 1.1076x; 1.0067x over previous
import math
import numpy as np
import concourse.bass as bass
import concourse.mybir as mybir
from concourse.bass_utils import run_bass_kernel_spmd

F32 = mybir.dt.float32
BF16 = mybir.dt.bfloat16
AF = mybir.ActivationFunctionType
AL = mybir.AluOpType

NCORE = 8
T = 2048
D = 1024
NT = 8
SW_F = 256
NSTRIP_F = T // SW_F
NSPAN = 16
EXPSH = -40.0


def _bucket_of(d):
    n = -d
    ret = 16 if n < 0 else 0
    na = abs(n)
    if na < 8:
        return ret + na
    v = 8 + int(math.floor(math.log(na / 8.0) / math.log(16.0) * 8.0))
    return ret + min(v, 15)


def _host_consts():
    c = {}
    mf = np.zeros((96, 16384), np.float32)
    for di, delta in enumerate((-1, 0, 1)):
        for dd in range(-90, 91):
            b = _bucket_of(dd)
            for jl in range(128):
                il = dd + 128 * delta + jl
                if 0 <= il < 128:
                    col = 128 * jl + il
                    mf[32 * di + b, col] += 1.0
                    if dd >= 1:
                        mf[32 * di + 31, col] -= 1.0
                    else:
                        mf[32 * di + 15, col] -= 1.0
    c["mflat"] = mf
    eye = np.zeros((128, 256), np.float32)
    eye[:, 0:128] = np.eye(128, dtype=np.float32)
    eye[:, 128:256] = (np.arange(128)[:, None] < np.arange(128)[None, :]).astype(np.float32)
    c["eyetri"] = eye
    l64 = np.zeros((128, 65), np.float32)
    l64[0:64, 0:64] = (np.arange(64)[:, None] < np.arange(64)[None, :]).astype(np.float32)
    l64[:, 64:65] = 1.0
    c["l64"] = l64
    g = np.zeros((16, 1024), np.float32)
    for ht in range(8):
        for pp in range(128):
            g[2 * ht + pp // 64, 128 * ht + pp] = 1.0
    c["gmask"] = g
    return c


_CONSTS = None


def _get_consts():
    global _CONSTS
    if _CONSTS is None:
        _CONSTS = _host_consts()
    return _CONSTS



ENGS = ("t", "v", "s", "y", "g")


class P:
    def __init__(self):
        self.ops = {e: [] for e in ENGS}
        self.ct = {}
        self.semnames = []

    def sem(self, name):
        if name not in self.ct:
            self.ct[name] = 0
            self.semnames.append(name)

    def add(self, eng, fn, sem=None, inc=0):
        if sem is not None:
            self.sem(sem)
            self.ct[sem] += inc

            def wrapped(e, S, fn=fn, sem=sem, inc=inc):
                fn(e, S).then_inc(S[sem], inc)
            self.ops[eng].append(wrapped)
            return self.ct[sem]
        self.ops[eng].append(fn)
        return None

    def wait(self, eng, sem, val=None):
        self.sem(sem)
        v = self.ct[sem] if val is None else val
        assert v <= self.ct[sem], f"{sem}: wait {v} > emitted {self.ct[sem]}"
        if v > 0:
            self.ops[eng].append(lambda e, S, sem=sem, v=v: e.wait_ge(S[sem], v))
        return v



def build_nc(debug=False, parts=('tp', 'ln', 'kv', 'gather', 'flash', 'attn', 'ffn2'), attn_nn=4, attn_stage=10):
    nc = bass.Bass("TRN2", target_bir_lowering=False)
    p = P()

    din = {}

    def DI(name, shape):
        din[name] = nc.dram_tensor(name, list(shape), F32, kind="ExternalInput").ap()

    DI("xt", (D, T))
    for w in ("wqT", "wkT", "wvT", "woT", "w1aT", "w2aT"):
        DI(w, (D, D))
    for b in ("bq4", "bk4", "bv4", "bo4", "b1a4", "b2a4"):
        DI(b, (128, 8))
    for pfx in ("q", "k", "v"):
        DI(f"{pfx}nwT", (D, 15))
        DI(f"{pfx}nb", (1, 15))
        DI(f"{pfx}w1c", (D, D))
        DI(f"{pfx}b1f4", (128, 8))
        DI(f"{pfx}w2c", (D, D))
        DI(f"{pfx}b2", (16, D))
    DI("relbT", (16, 32))
    DI("wogT", (D, D))
    for b in ("bq4s", "bk4s", "bv4s"):
        DI(b, (128, 8))
    DI("fconst", (128, 8))
    DI("hsel", (16, 4))
    DI("mflat", (96, 16384))
    DI("eyetri", (128, 256))
    DI("l64", (128, 65))
    DI("gmask", (16, 1024))

    yout = nc.dram_tensor("y", [D, T], F32, kind="ExternalOutput").ap()
    dbg = {}
    if debug:
        for nm, shape, dt_ in (
            ("dxa", (128, 16384), BF16), ("dfq", (128, 2048), BF16),
            ("de", (128, 16384), BF16), ("dtp", (128, 2048), BF16),
            ("doh", (128, 256), BF16), ("dht", (128, 4096), BF16),
            ("dqk", (128, 16384), BF16), ("dvmo", (128, 8256), BF16),
            ("dyt", (128, 2048), F32), ("do1", (128, 8192), BF16),
            ("do2", (128, 4096), BF16),
        ):
            dbg[nm] = nc.dram_tensor(nm, list(shape), dt_, kind="ExternalOutput").ap()

    xspill = nc.dram_tensor("xspill", [D, T], F32).ap()
    kb = nc.dram_tensor("kb", [D, T], BF16).ap()
    vb = nc.dram_tensor("vb", [T, D], BF16).ap()
    kg = nc.dram_tensor("kg", [4 * D, T], BF16).ap()
    vg = nc.dram_tensor("vg", [4 * T, D], BF16).ap()
    tpscr = nc.dram_tensor("tpscr", [12, 16384], BF16).ap()

    X = nc.alloc_sbuf_tensor("X", [128, 16384], F32)
    XA = nc.alloc_sbuf_tensor("XA", [128, 16384], BF16)
    AR1 = nc.alloc_sbuf_tensor("AR1", [128, 16384], BF16)
    AR2 = nc.alloc_sbuf_tensor("AR2", [128, 4096], BF16)
    AR3 = nc.alloc_sbuf_tensor("AR3", [128, 8256], BF16)
    WA = nc.alloc_sbuf_tensor("WA", [128, 16384], BF16)
    CON = nc.alloc_sbuf_tensor("CON", [128, 2512], BF16)
    ROWS = nc.alloc_sbuf_tensor("ROWS", [128, 1536], F32)
    SPK = nc.alloc_sbuf_tensor("SPK", [128, 2560], BF16)
    OHQ = nc.alloc_sbuf_tensor("OHQ", [128, 256], BF16)
    NWB = nc.alloc_sbuf_tensor("NWB", [128, 1792], BF16)
    MISC = nc.alloc_sbuf_tensor("MISC", [128, 480], F32)
    FCON = nc.alloc_sbuf_tensor("FCON", [128, 8], F32)
    FCONSTS = [0.0, 1.0, 1e-5, -0.5, 1.0 / 32.0, -40.0, 0.5, 2.0]
    for j, v in enumerate(FCONSTS):
        nc.const_aps.register(F32, v, FCON[:, j:j + 1])

    AR1f = AR1.ap().bitcast(F32)

    def Xf(dt_, lo, n):
        return X[:, 2048 * dt_ + lo: 2048 * dt_ + lo + n]

    def XAf(dt_, lo, n):
        return XA[:, 2048 * dt_ + lo: 2048 * dt_ + lo + n]

    CON_EYE, CON_TRI, CON_L64, CON_ONEC, CON_ONEB, CON_TP = \
        0, 128, 256, 321, 324, 464
    ONEROW = CON[0:1, CON_ONEB:CON_ONEB + 128]
    ONECOL = CON[:, CON_ONEC:CON_ONEC + 1]

    def TPR(di, n):
        o = CON_TP + 128 * (4 * di + n)
        return CON[:, o:o + 128]

    def TSN(n):
        o = CON_TP + 128 * (12 + n)
        return CON[:, o:o + 128]

    def ROW0(lo, n):
        return ROWS[0:1, lo:lo + n]


    def NWT(pi, dt_):
        return NWB[:, 120 * pi + 15 * dt_: 120 * pi + 15 * dt_ + 15]

    def NBROW(pi):
        return NWB[0:1, 360 + 15 * pi: 375 + 15 * pi]

    RELBT = NWB[0:16, 440:472]
    NW_CROW = 480
    NW_REC = 512
    NW_TPS = [768, 1280]

    def B2S(lo, n):
        return SPK[0:16, 1536 + lo:1536 + lo + n]

    def OHT(lo, n):
        return SPK[0:16, 1024 + lo:1024 + lo + n]

    HSEL = NWB[0:16, 412:416]
    RBSEL = NWB[0:32, 408:412]

    MI_B = {"bq": 0, "bk": 8, "bv": 16, "bo": 24, "b1a": 32, "b2a": 40,
            "qb1": 48, "kb1": 56, "vb1": 64, "bqs": 448, "bks": 456,
            "bvs": 464}
    MI_F = 80
    MI_REP = 144
    MI_ONE = 160
    MI_BST = 288
    MI_OCF = 352
    MI_SM, MI_SV, MI_SS, MI_SR = 356, 372, 388, 404
    MISCb = MISC.ap().bitcast(BF16)
    MB_M, MB_R = 856, 872

    def WSl(r, kt, lo, n):
        o = 8192 * r + 1024 * kt + lo
        return WA[:, o:o + n]

    pb = [nc.alloc_psum_tensor(f"pb{i}", [128, 512], F32) for i in range(4)]
    pS1 = nc.alloc_psum_tensor("pS1", [128, 1024], F32)
    pS2 = nc.alloc_psum_tensor("pS2", [128, 1024], F32)

    OHB = 14336
    OH_SC, OH_SCR, OH_KV, OH_Q = OHB, OHB + 256, OHB + 512, -1

    LNSf = [1536, 2048]
    MSKf = [2560, 3072]
    STTf = [3584, 4096]
    BST = [13312, 13824]
    MF = [9216, 11264]

    for s_ in ("XL", "SPL", "WLC", "WA0", "WA1", "WB", "GC", "GF0", "GF1",
               "GT0", "GT1", "GS", "KL0", "KL1", "SB0", "SB1", "SO",
               "MM", "EV", "SE", "CC", "XS", "XR"):
        p.sem(s_)

    def g_dma(dst, src, sem="WLC"):
        return p.add("g", lambda e, S, d=dst, s=src: e.dma_start(out=d, in_=s), sem, 16)

    def y_dma(dst, src, sem="SPL"):
        return p.add("y", lambda e, S, d=dst, s=src: e.dma_start(out=d, in_=s), sem, 16)

    def mm(out, lhsT, rhs, start, stop, tp=None, sem="MM"):
        return p.add("t", lambda e, S, o=out, l=lhsT, r=rhs, a=start, b=stop, tp=tp:
                     e.matmul(o, l, r, start=a, stop=b, tile_position=tp), sem, 1)

    def mm0(out, lhsT, rhs, start, stop, tp=None):
        p.add("t", lambda e, S, o=out, l=lhsT, r=rhs, a=start, b=stop, tp=tp:
              e.matmul(o, l, r, start=a, stop=b, tile_position=tp))

    def ttrans(out, in_, ident):
        p.add("t", lambda e, S, o=out, i=in_, d=ident: e.transpose(o, i, d))

    def tnop(sem, inc=1):
        return p.add("t", lambda e, S: e.nop(), sem, inc)

    def vop(fn, sem="EV", inc=1):
        p.wait("v", "EV", p.ct["EV"])
        return p.add("v", fn, "EV", 1)

    def vop0(fn):
        p.wait("v", "EV", p.ct["EV"])
        p.add("v", fn, "EV", 1)

    def sop(fn, sem="SE", inc=1):
        p.wait("s", "SE", p.ct["SE"])
        return p.add("s", fn, "SE", 1)

    def wload(reg, name):
        src = din[name][:, :].rearrange("(a p) f -> p a f", p=128)
        dst = WA[:, 8192 * reg:8192 * reg + 8192].rearrange("p (a f) -> p a f", a=8)
        return (f"WA{reg}", g_dma(dst, src, sem=f"WA{reg}"))

    for dt_ in range(NT):
        y_dma(Xf(dt_, 0, 2048), din["xt"][128 * dt_:128 * (dt_ + 1), :], sem="XL")
    xload = p.ct["XL"]

    g_dma(CON[:, CON_EYE:CON_EYE + 256], din["eyetri"][:, :])
    g_dma(CON[0:128, CON_L64:CON_L64 + 65], din["l64"][:, :])
    g_dma(SPK[0:16, 0:1024], din["gmask"][:, :])
    for pi, pfx in enumerate(("q", "k", "v")):
        g_dma(NWB[:, 120 * pi:120 * pi + 120].rearrange("p (a n) -> p a n", a=8),
              din[f"{pfx}nwT"][:, :].rearrange("(a p) n -> p a n", p=128))
        g_dma(NBROW(pi), din[f"{pfx}nb"][:, :])
    g_dma(HSEL, din["hsel"][:, :])
    g_dma(RELBT, din["relbT"][:, :])
    con_done = p.ct["WLC"]

    y_dma(FCON[:, :], din["fconst"][:, :], sem="SPL")
    for nm, key in (("bq4", "bq"), ("bk4", "bk"), ("bv4", "bv"), ("bo4", "bo"),
                    ("b1a4", "b1a"), ("b2a4", "b2a")):
        y_dma(MISC[:, MI_B[key]:MI_B[key] + 8], din[nm][:, :], sem="SPL")
    for pfx, key in (("q", "qb1"), ("k", "kb1"), ("v", "vb1")):
        y_dma(MISC[:, MI_B[key]:MI_B[key] + 8], din[f"{pfx}b1f4"][:, :], sem="SPL")
    for nm, key in (("bq4s", "bqs"), ("bk4s", "bks"), ("bv4s", "bvs")):
        y_dma(MISC[:, MI_B[key]:MI_B[key] + 8], din[nm][:, :], sem="SPL")
    smalls = p.ct["SPL"]

    if debug:
        vop0(lambda e, S: e.memset(AR3[:, :], 0.0))
        vop0(lambda e, S: e.memset(CON[:, CON_TP:CON_TP + 2048], 0.0))
        vop0(lambda e, S: e.memset(OHQ[:, :], 0.0))
        vop0(lambda e, S: e.memset(AR1[:, :], 0.0))
        vop0(lambda e, S: e.memset(AR2[:, :], 0.0))
        vop0(lambda e, S: e.memset(XA[:, :], 0.0))
    vop0(lambda e, S: e.memset(CON[:, CON_ONEC:CON_ONEC + 1], 1.0))
    vop0(lambda e, S: e.memset(CON[:, CON_ONEB:CON_ONEB + 128], 1.0))
    vop0(lambda e, S: e.memset(MISC[:, MI_ONE:MI_ONE + 128], 1.0))
    ones_v = vop(lambda e, S: e.memset(MISC[:, MI_OCF:MI_OCF + 1], 1.0))

    wf = {}
    wf["kw1"] = wload(0, "kw1c")
    wf["kw2"] = wload(1, "kw2c")

    def build_tprime():
        p.wait("t", "WLC", con_done)
        p.wait("t", "EV", ones_v)
        m1 = mm(pS2[0:32, 0:4], RELBT, HSEL, True, True)
        p.wait("v", "MM", m1)
        e1 = vop(lambda e, S: e.tensor_copy(RBSEL, pS2[0:32, 0:4]))
        p.wait("g", "EV", e1)
        p.add("g", lambda e, S: e.dma_start(out=NWB[0:1, NW_CROW:NW_CROW + 4],
                                            in_=RBSEL[15:16, :]), "GC", 16)
        c2 = p.add("g", lambda e, S: e.dma_start(out=NWB[0:1, NW_CROW + 4:NW_CROW + 8],
                                                 in_=RBSEL[31:32, :]), "GC", 16)
        p.wait("v", "GC", c2)
        e2 = vop(lambda e, S: e.tensor_sub(NWB[0:1, NW_CROW + 8:NW_CROW + 12],
                                           NWB[0:1, NW_CROW + 4:NW_CROW + 8],
                                           NWB[0:1, NW_CROW:NW_CROW + 4]))
        p.wait("t", "EV", e2)
        last = None
        for ni in range(4):
            mm0(pS2[0:128, 8 + ni:9 + ni], ONEROW,
                NWB[0:1, NW_CROW + ni:NW_CROW + ni + 1], True, True)
            last = mm(pS2[0:128, 12 + ni:13 + ni], ONEROW,
                      NWB[0:1, NW_CROW + 8 + ni:NW_CROW + 9 + ni], True, True)
        p.wait("v", "MM", last)
        er = vop(lambda e, S: e.tensor_copy(MISC[:, MI_REP:MI_REP + 8],
                                            pS2[0:128, 8:16]))
        for ni in range(4):
            vop(lambda e, S, ni=ni: e.tensor_scalar(
                TSN(ni), CON[:, CON_TRI:CON_TRI + 128],
                MISC[:, MI_REP + 4 + ni:MI_REP + 5 + ni], None, op0=AL.mult))
        return er

    def tprime_rtiles(gate_ev):
        ps1_free = [gate_ev, gate_ev]
        tps_free = [0, 0]
        mf_mm = [0, 0]
        for di in range(3):
            for half in range(8):
                hb = half % 2
                p.wait("g", "MM", mf_mm[hb])
                ld = g_dma(AR1[0:32, MF[hb]:MF[hb] + 2048],
                           din["mflat"][32 * di:32 * di + 32,
                                        2048 * half:2048 * half + 2048],
                           sem=f"GF{hb}")
                p.wait("t", f"GF{hb}", ld)
                for cq in range(4):
                    h2 = cq % 2
                    p.wait("t", "EV", ps1_free[h2])
                    m_ = mm(pS1[0:4, 512 * h2:512 * h2 + 512], RBSEL,
                            AR1[0:32, MF[hb] + 512 * cq:MF[hb] + 512 * cq + 512],
                            True, True)
                    p.wait("v", "MM", m_)
                    p.wait("v", f"GT{h2}", tps_free[h2])
                    ps1_free[h2] = vop(lambda e, S, h2=h2: e.tensor_copy(
                        NWB[0:4, NW_TPS[h2]:NW_TPS[h2] + 512],
                        pS1[0:4, 512 * h2:512 * h2 + 512]))
                    p.wait("g", "EV", ps1_free[h2])
                    tps_free[h2] = g_dma(
                        tpscr[4 * di:4 * di + 4,
                              2048 * half + 512 * cq:2048 * half + 512 * cq + 512],
                        NWB[0:4, NW_TPS[h2]:NW_TPS[h2] + 512], sem=f"GT{h2}")
                mf_mm[hb] = p.ct["MM"]
            p.wait("g", "GT0", p.ct["GT0"])
            p.wait("g", "GT1", p.ct["GT1"])
            for ni in range(4):
                g_dma(TPR(di, ni),
                      tpscr[4 * di + ni:4 * di + ni + 1, :]
                      .rearrange("o (pp f) -> (o pp) f", pp=128), sem="GS")
        return p.ct["EV"], p.ct["GS"]

    if 'tp' in parts:
        tp_er = build_tprime()
    else:
        tp_er = p.ct["EV"]
    tp_state = {"ev": p.ct["EV"], "gs": p.ct["GS"]}

    def emit_ln(eps, write_xa, gate_ev, src=None):
        src = src or Xf
        p.wait("s", "XL", xload)
        p.wait("s", "SPL", smalls)
        p.wait("s", "EV", p.ct["EV"])
        p.wait("s", "MM", p.ct["MM"])
        p.wait("t", "SPL", smalls)
        p.wait("t", "WLC", con_done)
        p.wait("t", "XL", xload)
        p.wait("t", "EV", ones_v)
        p.wait("t", "EV", gate_ev)
        SQR = [pS1[:, 0:512], pS1[:, 512:1024], pS2[:, 0:512], pS2[:, 512:1024]]
        sqfree = {}
        for st in range(4):
            for dt_ in range(NT):
                g_ = 8 * st + dt_
                slot = LNSf[g_ % 2]
                p.wait("s", "MM", sqfree.get(g_ - 2, 0))
                sq = sop(lambda e, S, dt_=dt_, st=st, slot=slot: e.activation(
                    AR1f[:, slot:slot + 512], src(dt_, 512 * st, 512), AF.Square))
                p.wait("t", "SE", sq)
                for c4 in range(4):
                    mm0(pb[c4][:, st:st + 1], src(dt_, 512 * st + 128 * c4, 128),
                        MISC[:, MI_OCF:MI_OCF + 1], dt_ == 0, dt_ == NT - 1)
                    fn_ = mm0 if c4 < 3 else (lambda *a, **k: mm(*a, **k))
                    fn_(SQR[c4][:, st:st + 1],
                        AR1f[:, slot + 128 * c4:slot + 128 * c4 + 128],
                        MISC[:, MI_OCF:MI_OCF + 1], dt_ == 0, dt_ == NT - 1)
                sqfree[g_] = p.ct["MM"]
        lastmm = p.ct["MM"]
        p.wait("v", "MM", lastmm)
        for c4 in range(4):
            vop0(lambda e, S, c4=c4: e.tensor_scalar(
                MISC[:, MI_SM + c4:MI_SM + 13 + c4:4], pb[c4][:, 0:4],
                1.0 / 1024, None, op0=AL.mult))
            vop0(lambda e, S, c4=c4: e.tensor_scalar(
                MISC[:, MI_SV + c4:MI_SV + 13 + c4:4], SQR[c4][:, 0:4],
                1.0 / 1024, None, op0=AL.mult))
        vop0(lambda e, S: e.tensor_mul(
            MISC[:, MI_SS:MI_SS + 16], MISC[:, MI_SM:MI_SM + 16],
            MISC[:, MI_SM:MI_SM + 16]))
        stats_ev = vop(lambda e, S: e.tensor_sub(
            MISC[:, MI_SV:MI_SV + 16], MISC[:, MI_SV:MI_SV + 16],
            MISC[:, MI_SS:MI_SS + 16]))
        p.wait("s", "EV", stats_ev)
        sop(lambda e, S: e.activation(MISC[:, MI_SS:MI_SS + 16],
                                      MISC[:, MI_SV:MI_SV + 16], AF.Ln,
                                      bias=float(eps)))
        rstd = sop(lambda e, S: e.activation(MISC[:, MI_SR:MI_SR + 16],
                                             MISC[:, MI_SS:MI_SS + 16], AF.Exp,
                                             scale=-0.5))
        p.wait("v", "SE", rstd)
        mr16 = vop(lambda e, S: e.tensor_copy(MISCb[:, MB_M:MB_M + 16],
                                              MISC[:, MI_SM:MI_SM + 16]))
        mr16 = vop(lambda e, S: e.tensor_copy(MISCb[:, MB_R:MB_R + 16],
                                              MISC[:, MI_SR:MI_SR + 16]))
        if not write_xa:
            return mr16
        p.wait("t", "EV", mr16)
        apply_ev = mr16
        for st in range(4):
            lo = 512 * st
            bk = (st % 2) * 2
            p.wait("t", "EV", apply_ev)
            for c4 in range(4):
                c = 4 * st + c4
                mm0(pb[bk][0:1, 128 * c4:128 * c4 + 128],
                    MISCb[:, MB_M + c:MB_M + c + 1],
                    CON[:, CON_EYE:CON_EYE + 128], True, True)
                fn_ = mm0 if c4 < 3 else (lambda *a, **k: mm(*a, **k))
                fn_(pb[bk + 1][0:1, 128 * c4:128 * c4 + 128],
                    MISCb[:, MB_R + c:MB_R + c + 1],
                    CON[:, CON_EYE:CON_EYE + 128], True, True)
            p.wait("v", "MM")
            vop0(lambda e, S, bk=bk: e.tensor_copy(ROW0(0, 512), pb[bk][0:1, 0:512]))
            rowev = vop(lambda e, S, bk=bk: e.tensor_copy(ROW0(512, 512),
                                                          pb[bk + 1][0:1, 0:512]))
            p.wait("t", "EV", rowev)
            mm0(pb[bk][:, 0:512], MISC[0:1, MI_ONE:MI_ONE + 128],
                ROW0(0, 512), True, True)
            mrr = mm(pb[bk + 1][:, 0:512], MISC[0:1, MI_ONE:MI_ONE + 128],
                     ROW0(512, 512), True, True)
            p.wait("v", "MM", mrr)
            for dt_ in range(NT):
                vop0(lambda e, S, dt_=dt_, lo=lo, bk=bk: e.tensor_sub(
                    AR1f[:, STTf[dt_ % 2]:STTf[dt_ % 2] + 512],
                    src(dt_, lo, 512), pb[bk][:, 0:512]))
                apply_ev = vop(lambda e, S, dt_=dt_, lo=lo, bk=bk: e.tensor_mul(
                    XAf(dt_, lo, 512),
                    AR1f[:, STTf[dt_ % 2]:STTf[dt_ % 2] + 512],
                    pb[bk + 1][:, 0:512]))
        return apply_ev

    def fff_scores_traversal(pi, oh_dst, psum_ready_ev):
        p.wait("t", "WLC", con_done)
        p.wait("t", "EV", psum_ready_ev)
        for c in range(16):
            for dt_ in range(NT):
                mm0(pS2[0:128, 15 * c:15 * c + 15], XAf(dt_, 128 * c, 128),
                    NWT(pi, dt_), dt_ == 0, False)
            last = mm(pS2[0:128, 15 * c:15 * c + 15], ONEROW, NBROW(pi), False, True)
        p.wait("v", "MM", last)
        sc_ev = vop(lambda e, S: e.tensor_copy(AR1[:, OH_SC:OH_SC + 240],
                                               pS2[0:128, 0:240]))

        def sl(i):
            return AR1[:, OH_SCR + 16 * i:OH_SCR + 16 * i + 16]

        def scc(j):
            return AR1[:, OH_SC + j: OH_SC + j + 15 * 16:15]

        B = [sl(i) for i in range(4)]
        E_ = [sl(4 + i) for i in range(4)]
        TMP = [sl(8 + i) for i in range(8)]
        P2 = [AR1[:, OH_SC + 16 * i:OH_SC + 16 * i + 16] for i in range(4)]
        P4 = [AR1[:, OH_SC + 64 + 16 * i:OH_SC + 64 + 16 * i + 16] for i in range(8)]

        def ts1(out, in_, s, op):
            vop0(lambda e, S, o=out, i=in_, s=s, op=op:
                 e.tensor_single_scalar(o, i, s, op=op))

        MXT = sl(24)

        def selv(out, mask, tt, ff):
            vop0(lambda e, S, o=MXT, a=tt, b=ff: e.tensor_sub(o, a, b))
            vop0(lambda e, S, o=MXT, m=mask: e.tensor_mul(o, o, m))
            vop0(lambda e, S, o=out, b=ff: e.tensor_add(o, b, MXT))

        def mulv(out, a, b):
            vop0(lambda e, S, o=out, a=a, b=b: e.tensor_mul(o, a, b))

        ts1(B[0], scc(0), 0.0, AL.is_ge)
        selv(TMP[0], B[0], scc(2), scc(1))
        ts1(B[1], TMP[0], 0.0, AL.is_ge)
        selv(TMP[1], B[1], scc(4), scc(3))
        selv(TMP[2], B[1], scc(6), scc(5))
        selv(TMP[3], B[0], TMP[2], TMP[1])
        ts1(B[2], TMP[3], 0.0, AL.is_ge)
        selv(TMP[4], B[2], scc(8), scc(7))
        selv(TMP[5], B[2], scc(10), scc(9))
        selv(TMP[6], B[2], scc(12), scc(11))
        selv(TMP[7], B[2], scc(14), scc(13))
        selv(TMP[1], B[1], TMP[5], TMP[4])
        selv(TMP[2], B[1], TMP[7], TMP[6])
        selv(TMP[0], B[0], TMP[2], TMP[1])
        ts1(B[3], TMP[0], 0.0, AL.is_ge)
        for i in range(4):
            vop0(lambda e, S, i=i: e.tensor_scalar(
                E_[i], B[i], -1.0, 1.0, op0=AL.mult, op1=AL.add))
        mulv(P2[0], E_[0], E_[1]); mulv(P2[1], E_[0], B[1])
        mulv(P2[2], B[0], E_[1]); mulv(P2[3], B[0], B[1])
        for a in range(4):
            mulv(P4[2 * a], P2[a], E_[2]); mulv(P4[2 * a + 1], P2[a], B[2])
        for l in range(16):
            src_ = P4[l // 2]
            bit = E_[3] if l % 2 == 0 else B[3]
            if oh_dst >= 0:
                dst = AR1[:, oh_dst + l: oh_dst + l + 241:16]
            else:
                dst = OHQ[:, l: l + 241:16]
            vop0(lambda e, S, dst=dst, src_=src_, bit=bit: e.tensor_mul(
                dst, src_, bit))
        done = p.add("v", lambda e, S: e.engine_nop(), "EV", 1)
        return done, sc_ev

    def fff_ohT(oh_dst, chunks, dst_lo, gate_ev):
        pS2b = pS2.ap().bitcast(BF16)
        slots = [pS2b[0:16, 0:128], pS2b[0:16, 1024:1152]]
        slot_free = [gate_ev, gate_ev]
        ev = gate_ev
        for idx, c in enumerate(chunks):
            sl_ = idx % 2
            p.wait("t", "EV", slot_free[sl_])
            if oh_dst >= 0:
                src_oh = AR1[:, oh_dst + 16 * c:oh_dst + 16 * c + 16]
            else:
                src_oh = OHQ[:, 16 * c:16 * c + 16]
            p.add("t", lambda e, S, src_oh=src_oh, sl_=sl_, slots=slots:
                  e.transpose(slots[sl_], src_oh,
                              CON[:, CON_EYE:CON_EYE + 128]), "MM", 1)
            mv = p.ct["MM"]
            p.wait("v", "MM", mv)
            ev = vop(lambda e, S, idx=idx, sl_=sl_, slots=slots, dst_lo=dst_lo:
                     e.tensor_copy(OHT(dst_lo + 128 * idx, 128), slots[sl_]))
            slot_free[sl_] = ev
        return ev

    def fff_hidden(st_lo, n, b1key, oht_lo, pbfree, maskfree, h1s=512):
        for ht in range(8):
            bk = ht % 2
            p.wait("t", "EV", pbfree[bk])
            for dt_ in range(NT):
                mm0(pb[bk][:, 0:n], WSl(0, dt_, 128 * ht, 128),
                    XAf(dt_, st_lo, n), dt_ == 0, dt_ == NT - 1)
            p.wait("t", "SE", maskfree[bk])
            mk = mm(pb[2 + bk][:, 0:n],
                    SPK[0:16, 128 * ht:128 * ht + 128],
                    OHT(oht_lo, n), True, True)
            p.wait("s", "MM", mk)
            maskfree[bk] = sop(lambda e, S, bk=bk, n=n: e.activation(
                AR1f[:, MSKf[bk]:MSKf[bk] + n], pb[2 + bk][:, 0:n], AF.Copy))
            p.wait("v", "SE", maskfree[bk])
            p.wait("v", "MM", mk)
            vop0(lambda e, S, bk=bk, n=n, b1key=b1key, ht=ht: e.scalar_tensor_tensor(
                AR1f[:, STTf[bk]:STTf[bk] + n], pb[bk][:, 0:n],
                MISC[:, MI_B[b1key] + ht:MI_B[b1key] + ht + 1],
                AR1f[:, MSKf[bk]:MSKf[bk] + n], op0=AL.add, op1=AL.mult))
            pbfree[bk] = vop(lambda e, S, bk=bk, n=n, ht=ht, h1s=h1s: e.tensor_scalar(
                AR2[:, h1s * ht:h1s * ht + n],
                AR1f[:, STTf[bk]:STTf[bk] + n], 0.0, None, op0=AL.max))
        return pbfree, maskfree

    def emit_ffn(li, gate_ev):
        if 'ln' not in parts:
            return gate_ev
        ln_ev = emit_ln(1e-5, True, max(gate_ev, tp_er))
        if li == 0 and 'tp' in parts:
            tp_state["ev"], tp_state["gs"] = tprime_rtiles(ln_ev)
        if 'kv' not in parts:
            p.wait("t", "EV", tp_state["ev"])
            p.wait("t", "GS", tp_state["gs"])
            return ln_ev

        pbfree = [ln_ev, ln_ev, ln_ev, ln_ev]
        maskfree = [0, 0]
        scores_gate = tp_er if li == 0 else gate_ev
        bounce_last = 0
        bst_free = [0, 0]
        vb2_pre = [None]
        for pi, pfx, b1key in ((1, "k", "kb1"), (2, "v", "vb1")):
            if vb2_pre[0] is None:
                p.wait("g", "MM", p.ct["MM"])
                b2load = g_dma(B2S(0, 1024), din[f"{pfx}b2"][:, :], sem="WB")
            else:
                b2load = vb2_pre[0]
            trav, sc_ev = fff_scores_traversal(pi, OH_KV, max(scores_gate, ln_ev))
            p.wait("t", "EV", trav)
            p.wait("t", *wf[f"{pfx}w1"])
            p.wait("t", *wf[f"{pfx}w2"])
            p.wait("t", "WB", b2load)
            ofree = [pbfree[0], pbfree[1]]
            ohgate = sc_ev
            for st in range(4):
                lo = 512 * st
                oht_ev = fff_ohT(OH_KV, range(4 * st, 4 * st + 4), 0, ohgate)
                ohgate = oht_ev
                p.wait("t", "EV", oht_ev)
                pbfree, maskfree = fff_hidden(lo, 512, b1key, 0, pbfree, maskfree)
                p.wait("t", "EV", max(pbfree[0], pbfree[1]))
                ofree = [max(ofree[0], pbfree[0]), max(ofree[1], pbfree[1])]
                if pi == 1:
                    jobs = [(dc,) for dc in range(8)]
                else:
                    jobs = [(tc, ds) for tc in range(4) for ds in range(2)]
                for j, job in enumerate(jobs):
                    h2 = j % 2
                    p.wait("t", "EV", ofree[h2])
                    if pi == 1:
                        dc = job[0]
                        for hti in range(NT):
                            mm0(pb[h2][:, 0:512], WSl(1, hti, 128 * dc, 128),
                                AR2[:, 512 * hti:512 * hti + 512], hti == 0, False)
                        lastm = mm(pb[h2][:, 0:512], B2S(128 * dc, 128),
                                   OHT(0, 512), False, True)
                    else:
                        tc, ds = job
                        for hti in range(NT):
                            mm0(pb[h2][:, 0:512],
                                AR2[:, 512 * hti + 128 * tc:512 * hti + 128 * tc + 128],
                                WSl(1, hti, 512 * ds, 512), hti == 0, False)
                        lastm = mm(pb[h2][:, 0:512], OHT(128 * tc, 128),
                                   B2S(512 * ds, 512), False, True)
                    p.wait("v", "MM", lastm)
                    p.wait("v", f"SB{h2}", bst_free[h2])
                    ev = vop(lambda e, S, h2=h2: e.tensor_copy(
                        AR1[:, BST[h2]:BST[h2] + 512], pb[h2][:, 0:512]))
                    ofree[h2] = ev
                    pbfree[h2] = ev
                    p.wait("y", "EV", ev)
                    if pi == 1:
                        dc = job[0]
                        bst_free[h2] = y_dma(kb[128 * dc:128 * dc + 128, lo:lo + 512],
                                             AR1[:, BST[h2]:BST[h2] + 512],
                                             sem=f"SB{h2}")
                    else:
                        tc, ds = job
                        bst_free[h2] = y_dma(
                            vb[lo + 128 * tc:lo + 128 * tc + 128, 512 * ds:512 * ds + 512],
                            AR1[:, BST[h2]:BST[h2] + 512], sem=f"SB{h2}")
            scores_gate = sc_ev
            p.wait("g", "MM", p.ct["MM"])
            if pi == 1:
                wf["vw1"] = wload(0, "vw1c")
                wf["vw2"] = wload(1, "vw2c")
                vb2_pre[0] = g_dma(B2S(0, 1024), din["vb2"][:, :], sem="WB")
                if 'gather' in parts:
                    p.wait("g", "SB0", p.ct["SB0"])
                    p.wait("g", "SB1", p.ct["SB1"])
                    p.add("g", lambda e, S: e.collective_compute(
                        "AllGather", AL.bypass,
                        replica_groups=[[0, 1, 2, 3], [4, 5, 6, 7]],
                        ins=[kb[:, :]], outs=[kg[:, :]]), "CC", 1)
            else:
                wf["qw1"] = wload(0, "qw1c")
                wf["qw2"] = wload(1, "qw2c")
                wf["qb2"] = ("WB", g_dma(B2S(0, 1024), din["qb2"][:, :], sem="WB"))

        if 'gather' not in parts:
            return pbfree[0]
        p.wait("g", "SB0", p.ct["SB0"])
        p.wait("g", "SB1", p.ct["SB1"])
        cc = p.add("g", lambda e, S: e.collective_compute(
            "AllGather", AL.bypass,
            replica_groups=[[0, 1, 2, 3], [4, 5, 6, 7]],
            ins=[vb[:, :]], outs=[vg[:, :]]), "CC", 1)

        travq, sc_ev_q = fff_scores_traversal(0, OH_Q, max(scores_gate, ln_ev))
        p.wait("t", "EV", travq)
        p.wait("t", *wf["qw1"])
        p.wait("t", *wf["qw2"])
        p.wait("t", *wf["qb2"])
        if li == 0:
            p.wait("t", "EV", tp_state["ev"])
            p.wait("t", "GS", tp_state["gs"])
            p.wait("s", "GS", tp_state["gs"])
            p.wait("s", "GF0", p.ct["GF0"])
            p.wait("s", "GF1", p.ct["GF1"])

        if 'flash' not in parts:
            return pbfree[0]
        p.wait("y", "CC", cc)
        p.wait("s", "SB0", p.ct["SB0"])
        p.wait("s", "SB1", p.ct["SB1"])
        span_free = {0: span_free_g["c"], 1: span_free_g["c"]}

        def span_load(which, sp, half):
            p.wait("y", "MM", span_free[half])
            if which == "k":
                r_, lo_ = sp // 4, 512 * (sp % 4)
                src = kg[1024 * r_:1024 * r_ + 1024, lo_:lo_ + 512] \
                    .rearrange("(a p) f -> p a f", p=128)
                na = 8
            else:
                g0 = 512 * sp
                src = vg[g0:g0 + 512, :].rearrange("(a p) f -> p a f", p=128)
                na = 4
            dst = AR3[:, 4096 * half:4096 * half + 4096] \
                .rearrange("p (a f) -> p a f", a=na)
            return y_dma(dst, src, sem=f"KL{half}")

        AVR = [pb[0][:, 0:256], pb[1][:, 0:256], pb[2][:, 0:256], pb[3][:, 0:256],
               pS1[:, 0:256], pS1[:, 512:768], pS2[:, 0:256], pS2[:, 512:768]]
        SUMR = pb[0][0:1, 384:640] if False else pb[0][0:1, 256:512]
        qofree = {0: 0, 1: 0}
        av_ev = 0
        ohgate = sc_ev_q
        for s in range(NSTRIP_F):
            qlo = SW_F * s
            p.wait("t", "EV", av_ev)
            oht_ev = fff_ohT(OH_Q, [2 * s, 2 * s + 1], 0, max(ohgate, av_ev))
            ohgate = oht_ev
            p.wait("t", "EV", oht_ev)
            fff_hidden(qlo, SW_F, "qb1", 0, pbfree, maskfree, h1s=256)
            p.wait("t", "EV", max(pbfree[0], pbfree[1]))
            for dc in range(8):
                h2 = dc % 2
                p.wait("t", "SE", qofree[h2])
                for hti in range(NT):
                    mm0(pS1[:, 512 * h2:512 * h2 + 256], WSl(1, hti, 128 * dc, 128),
                        AR2[:, 256 * hti:256 * hti + 256], hti == 0, False)
                lastm = mm(pS1[:, 512 * h2:512 * h2 + 256],
                           B2S(128 * dc, 128), OHT(0, 256), False, True)
                p.wait("s", "MM", lastm)
                qofree[h2] = sop(lambda e, S, h2=h2, dc=dc: e.activation(
                    AR2[:, 2048 + 256 * dc:2048 + 256 * dc + 256],
                    pS1[:, 512 * h2:512 * h2 + 256], AF.Copy, scale=1.0 / 32.0))
            fq_se = qofree[1]
            p.wait("t", "SE", fq_se)
            p.wait("t", "EV", p.ct["EV"])
            p.wait("t", "SE", p.ct["SE"])
            exp_se = {}
            for sp in range(NSPAN):
                half = sp % 2
                ld = span_load("k", sp, half)
                p.wait("t", f"KL{half}", ld)
                pS = pS1 if half == 0 else pS2
                p.wait("t", "SE", exp_se.get(sp - 2, 0))
                for kc in range(4):
                    for dt_ in range(NT):
                        fn_ = mm if (kc == 3 and dt_ == NT - 1) else mm0
                        fn_(pS[:, 256 * kc:256 * kc + 256],
                            AR3[:, 4096 * half + 512 * dt_ + 128 * kc:
                                4096 * half + 512 * dt_ + 128 * kc + 128],
                            AR2[:, 2048 + 256 * dt_:2048 + 256 * dt_ + 256],
                            dt_ == 0, dt_ == NT - 1)
                mmv = p.ct["MM"]
                span_free[half] = mmv
                p.wait("s", "MM", mmv)
                exp_se[sp] = sop(lambda e, S, sp=sp, pS=pS: e.activation(
                    AR1[:, 1024 * sp:1024 * sp + 1024], pS[:, 0:1024], AF.Exp))
            p.wait("t", "SE", exp_se[NSPAN - 1])
            for sp in range(NSPAN):
                half = sp % 2
                ld = span_load("v", sp, half)
                p.wait("t", f"KL{half}", ld)
                for kc in range(4):
                    first = (sp == 0 and kc == 0)
                    lastf = (sp == NSPAN - 1 and kc == 3)
                    for dc in range(8):
                        fn_ = mm if (kc == 3 and dc == 7) else mm0
                        fn_(AVR[dc],
                            AR3[:, 4096 * half + 1024 * kc + 128 * dc:
                                4096 * half + 1024 * kc + 128 * dc + 128],
                            AR1[:, 1024 * sp + 256 * kc:1024 * sp + 256 * kc + 256],
                            first, lastf)
                span_free[half] = p.ct["MM"]
            span_free_g["c"] = max(span_free.values())
            for K in range(64):
                fn_ = mm if K == 63 else mm0
                fn_(SUMR, ONECOL, AR1[:, 256 * K:256 * K + 256], K == 0, K == 63)
            msum = p.ct["MM"]
            p.wait("v", "MM", msum)
            vop0(lambda e, S: e.reciprocal(ROW0(1024, 256), SUMR))
            vop0(lambda e, S: e.tensor_scalar(ROW0(1280, 256), ROW0(1024, 256),
                                              0.5, None, op0=AL.mult))
            rec_ev = vop(lambda e, S: e.tensor_copy(
                NWB[0:1, NW_REC:NW_REC + 256], ROW0(1280, 256)))
            p.wait("t", "EV", rec_ev)
            mrep = mm(pb[1][:, 256:512], ONEROW, NWB[0:1, NW_REC:NW_REC + 256],
                      True, True)
            p.wait("v", "MM", mrep)
            vop0(lambda e, S: e.tensor_copy(AR1f[:, 0:256], pb[1][:, 256:512]))
            for dc in range(8):
                vop0(lambda e, S, dc=dc: e.tensor_mul(
                    AR1f[:, 256:512], AVR[dc], AR1f[:, 0:256]))
                av_ev = vop(lambda e, S, dc=dc, qlo=qlo: e.tensor_add(
                    Xf(dc, qlo, 256), Xf(dc, qlo, 256), AR1f[:, 256:512]))
            pbfree[0] = pbfree[1] = pbfree[2] = pbfree[3] = av_ev
        return av_ev


    Xb = X.ap().bitcast(BF16)
    QMT, KMT, O1C = 0, 8192, 16384
    SW_A = 256
    eyeT = CON[:, CON_EYE:CON_EYE + 64]
    eyeB = CON[:, CON_EYE + 64:CON_EYE + 128]
    SCRA = NWB[64:128, 768:1280]
    SCRF = NWB[:, 768:1280]
    REPS = NWB[:, 1280:1536]
    BSB = NWB[0:64, 1536:1600]
    BSTB = NWB[0:64, 1600:1664]
    RECROW = NWB[64:65, 512:768]

    def Yf(slot, dt_, lo, n):
        o = 12288 + 512 * dt_ + lo
        return X[:, o:o + n]

    def emit_attn(gate_ev):
        trunc = [False]
        ln_ev = emit_ln(1e-5, True, gate_ev)
        p.wait("y", "EV", ln_ev)
        for dt_ in range(NT):
            y_dma(xspill[128 * dt_:128 * (dt_ + 1), :], Xf(dt_, 0, 2048), sem="XS")
        xs_done = p.ct["XS"]
        p.wait("v", "XS", xs_done)
        ev0 = vop(lambda e, S: e.memset(NWB[0:64, 768:1280], 0.0))
        ev0 = vop(lambda e, S: e.memset(Xb[:, O1C:O1C + 8192], 0.0))
        ev0 = vop(lambda e, S: e.memset(Xb[64:128, QMT:QMT + 8192], 0.0))
        ev0 = vop(lambda e, S: e.memset(Xb[64:128, KMT:KMT + 8192], 0.0))
        wreg = {"i": 0}
        wuse = {0: p.ct["MM"], 1: p.ct["MM"]}

        def next_wload(name):
            r = wreg["i"] % 2
            wreg["i"] += 1
            p.wait("g", "MM", wuse[r])
            p.wait("g", f"WA{r}", p.ct[f"WA{r}"])
            return wload(r, name), r

        state = {"shift": [ev0, ev0], "pb23": [ev0, ev0], "ps1b": [ev0, ev0],
                 "av": ev0, "exp": {}, "qo": 0}

        def proj(nn, wl, reg, bkey, dst_base, dst_tensor):
            p.wait("t", *wl)
            for i in range(8):
                bk = 2 + (i % 2)
                p.wait("t", "EV", state["pb23"][i % 2])
                p.wait("t", "SE", state.get("pb23se", [0, 0])[i % 2])
                for dt_ in range(NT):
                    fn_ = mm if dt_ == NT - 1 else mm0
                    fn_(pb[bk][:, 0:512], WSl(reg, dt_, 128 * i, 128),
                        XAf(dt_, 512 * nn, 512), dt_ == 0, dt_ == NT - 1)
                mmv = p.ct["MM"]
                p.wait("s", "MM", mmv)
                se1 = sop(lambda e, S, bk=bk: e.activation(
                    SCRA, pb[bk][64:128, 0:512], AF.Copy))
                state.setdefault("pb23se", [0, 0])[i % 2] = se1
                p.wait("v", "MM", mmv)
                ev1 = vop(lambda e, S, bk=bk, i=i, bkey=bkey, dst_base=dst_base,
                          dst_tensor=dst_tensor: e.tensor_scalar(
                    dst_tensor[0:64, dst_base + 2 * i:dst_base + 2 * i + 8177:16],
                    pb[bk][0:64, 0:512],
                    MISC[0:64, MI_B[bkey] + i:MI_B[bkey] + i + 1], None,
                    op0=AL.add))
                p.wait("t", "SE", se1)
                h2 = i % 2
                p.wait("t", "EV", state["ps1b"][h2])
                ms = mm(pS1[0:64, 512 * h2:512 * h2 + 512], eyeB, SCRF,
                        True, True)
                p.wait("v", "MM", ms)
                ev2 = vop(lambda e, S, i=i, h2=h2, bkey=bkey, dst_base=dst_base,
                          dst_tensor=dst_tensor: e.tensor_scalar(
                    dst_tensor[0:64, dst_base + 2 * i + 1:dst_base + 2 * i + 8178:16],
                    pS1[0:64, 512 * h2:512 * h2 + 512],
                    MISC[0:64, MI_B[bkey + "s"] + i:MI_B[bkey + "s"] + i + 1],
                    None, op0=AL.add))
                state["ps1b"][h2] = ev2
                state["pb23"][i % 2] = ev1
            wuse[reg] = p.ct["MM"]
            return p.ct["EV"]


        def score_shift(nn):
            p.wait("t", "EV", p.ct["EV"])
            p.wait("t", "SE", p.ct["SE"])
            p.wait("s", "EV", p.ct["EV"])
            p.wait("s", "MM", p.ct["MM"])
            vop0(lambda e, S: e.memset(Xb[64:65, KMT:KMT + 8192], -1.0))
            mz = vop(lambda e, S: e.memset(Xb[64:65, QMT:QMT + 8192], 0.0))
            p.wait("t", "EV", mz)
            expse = {}
            dense_ = {}

            def emit_ones(bb):
                for k4 in range(4):
                    fn_ = mm if (bb == 15 and k4 == 3) else mm0
                    fn_(pb[0][0:1, 0:256], ONECOL,
                        AR1[:, 1024 * (bb % 4) + 256 * k4:
                            1024 * (bb % 4) + 256 * k4 + 256],
                        bb == 0 and k4 == 0, bb == 15 and k4 == 3)

            for s_ in range(32):
                ilo = SW_A * s_
                for b_ in range(16):
                    pS = pS1 if b_ % 2 == 0 else pS2
                    p.wait("t", "SE", expse.get(b_ - 2, 0))
                    for k4 in range(4):
                        kc = 4 * b_ + k4
                        fn_ = mm if k4 == 3 else mm0
                        fn_(pS[:, 256 * k4:256 * k4 + 256],
                            Xb[0:128, KMT + 128 * kc:KMT + 128 * kc + 128],
                            Xb[0:128, QMT + ilo:QMT + ilo + SW_A],
                            True, True)
                    if b_ >= 2:
                        if b_ == 2:
                            p.wait("t", "SE", dense_.get("ln", 0))
                        emit_ones(b_ - 2)
                    mmv = p.ct["MM"]
                    p.wait("s", "MM", mmv)
                    expse[b_] = sop(lambda e, S, b_=b_, pS=pS: e.activation(
                        AR1[:, 1024 * (b_ % 4):1024 * (b_ % 4) + 1024],
                        pS[:, 0:1024], AF.Exp, scale=0.125))
                for bb in (14, 15):
                    p.wait("t", "SE", expse[bb])
                    emit_ones(bb)
                smm = p.ct["MM"]
                p.wait("s", "MM", smm)
                p.wait("s", "EV", p.ct["EV"])
                lnse = sop(lambda e, S: e.activation(ROWS[0:1, 0:256],
                                                     pb[0][0:1, 0:256],
                                                     AF.Ln, bias=0.0))
                dense_["ln"] = lnse
                p.wait("v", "SE", lnse)
                cev = vop(lambda e, S, s_=s_: e.tensor_scalar(
                    AR1[0:1, 8192 + 256 * s_:8192 + 256 * s_ + 256],
                    ROWS[0:1, 0:256], 8.0, -76.0, op0=AL.mult, op1=AL.add))
                expse = {}
            p.wait("g", "EV", cev)
            cdma = g_dma(Xb[64:65, QMT:QMT + 8192], AR1[0:1, 8192:16384],
                         sem="GC")
            p.wait("t", "GC", cdma)
            p.wait("t", "EV", cev)
            p.wait("t", "SE", p.ct["SE"])

        def bias_pass(nn, ihalf, gate):
            o2g = gate
            for ii in range(32):
                I = 32 * ihalf + ii
                h2 = ii % 2
                p.wait("t", "EV", state["ps1b"][h2])
                first = True
                for di, dlt in ((0, -1), (1, 0), (2, 1)):
                    J = I + dlt
                    if J < 0 or J > 63:
                        continue
                    mm0(pS1[0:64, 512 * h2:512 * h2 + 128], AR3[:, 65 * J:65 * J + 64],
                        TPR(di, nn), first, False)
                    first = False
                lmm = mm(pS1[0:64, 512 * h2:512 * h2 + 128], AR3[:, 65 * I:65 * I + 64],
                         TSN(nn), False, True)
                p.wait("v", "MM", lmm)
                o2g = vop(lambda e, S, I=I, h2=h2: e.tensor_scalar(
                    AR2[0:64, 128 * (I % 32):128 * (I % 32) + 128],
                    pS1[0:64, 512 * h2:512 * h2 + 128],
                    MISC[0:64, MI_F + I:MI_F + I + 1], None, op0=AL.add))
                state["ps1b"][h2] = o2g
            return o2g

        ab_ev = 0
        for nn in range(attn_nn):
            lastb = (nn == attn_nn - 1)
            tokb = 512 * nn
            wl_q, reg_q = next_wload("wqT")
            wl_k, reg_k = next_wload("wkT")
            if nn == 0:
                p.wait("t", "EV", ln_ev)
                p.wait("t", "XS", xs_done)
                p.wait("v", "XS", xs_done)
            proj(nn, wl_q, reg_q, "bq", QMT, Xb)
            if lastb and attn_stage <= 1:
                trunc[0] = True; break
            wl_v, reg_v = next_wload("wvT")
            proj(nn, wl_k, reg_k, "bk", KMT, Xb)
            if lastb and attn_stage <= 2:
                trunc[0] = True; break
            score_shift(nn)
            if lastb and attn_stage <= 3:
                trunc[0] = True; break
            wl_wog, reg_wog = next_wload("wogT")
            vev = proj(nn, wl_v, reg_v, "bv", 0, AR3)
            if lastb and attn_stage <= 4:
                trunc[0] = True; break
            pS2b = pS2.ap().bitcast(BF16)
            slots = [pS2b[0:128, 0:64], pS2b[0:128, 1024:1088]]
            sfree = [vev, vev]
            for c in range(64):
                sl_ = c % 2
                p.wait("t", "EV", sfree[sl_])
                p.add("t", lambda e, S, c=c, sl_=sl_: e.transpose(
                    slots[sl_], AR3[0:64, 128 * c:128 * c + 128],
                    CON[0:64, CON_EYE:CON_EYE + 64]), "MM", 1)
                mv = p.ct["MM"]
                p.wait("v", "MM", mv)
                vop0(lambda e, S, c=c, sl_=sl_: e.tensor_copy(
                    AR3[:, 65 * c:65 * c + 64], slots[sl_]))
                sfree[sl_] = vop(lambda e, S, c=c: e.memset(
                    AR3[:, 65 * c + 64:65 * c + 65], 1.0))
            vmo_ev = p.ct["EV"]
            if lastb and attn_stage <= 5:
                trunc[0] = True; break
            p.wait("t", "EV", vmo_ev)
            p.wait("t", "EV", state["pb23"][0])
            p.wait("t", "EV", state["pb23"][1])
            for J in range(64):
                fn_ = mm if J == 63 else mm0
                fn_(pb[2][0:64, J:J + 1], AR3[:, 65 * J:65 * J + 64], ONECOL,
                    True, True)
            bsmm = p.ct["MM"]
            p.wait("v", "MM", bsmm)
            bse = vop(lambda e, S: e.tensor_copy(BSB, pb[2][0:64, 0:64]))
            p.wait("t", "EV", bse)
            p.add("t", lambda e, S: e.transpose(
                pS2.ap().bitcast(BF16)[0:64, 0:64], BSB,
                CON[0:64, CON_EYE:CON_EYE + 64]), "MM", 1)
            btr = p.ct["MM"]
            p.wait("v", "MM", btr)
            bte = vop(lambda e, S: e.tensor_copy(
                BSTB, pS2.ap().bitcast(BF16)[0:64, 0:64]))
            p.wait("t", "EV", bte)
            mm0(pb[3][0:64, 0:64], BSTB, CON[0:64, CON_L64:CON_L64 + 64],
                True, True)
            lsmm = mm(pb[3][0:64, 64:65], BSTB,
                      CON[0:64, CON_L64 + 64:CON_L64 + 65], True, True)
            p.wait("v", "MM", lsmm)
            vop0(lambda e, S, nn=nn: e.tensor_scalar(
                MISC[0:64, 72:73], pb[3][0:64, 64:65],
                MISC[0:64, MI_REP + nn:MI_REP + nn + 1], None, op0=AL.mult))
            fev = vop(lambda e, S, nn=nn: e.tensor_scalar(
                MISC[0:64, MI_F:MI_F + 64], pb[3][0:64, 0:64],
                MISC[0:64, MI_REP + 4 + nn:MI_REP + 4 + nn + 1],
                MISC[0:64, 72:73], op0=AL.mult, op1=AL.add))
            state["pb23"] = [fev, fev]
            if lastb and attn_stage <= 6:
                trunc[0] = True; break
            o2g = bias_pass(nn, 0, fev)
            for s in range(32):
                if s == 16:
                    o2g = bias_pass(nn, 1, o2g)
                ilo = SW_A * s

                def emit_av(bb):
                    if bb == 0:
                        p.wait("t", "EV", state["av"])
                    for k4 in range(4):
                        kc = 4 * bb + k4
                        fn_ = mm if kc == 63 else mm0
                        fn_(pb[0][0:65, 0:256], AR3[:, 65 * kc:65 * kc + 65],
                            AR1[:, 256 * kc:256 * kc + 256], kc == 0, kc == 63)

                for b_ in range(16):
                    pS = pS1 if b_ % 2 == 0 else pS2
                    p.wait("t", "SE", state["exp"].get(("b", nn, s, b_ - 2), 0))
                    if b_ < 2:
                        p.wait("t", "EV", state["ps1b"][0])
                        p.wait("t", "EV", state["ps1b"][1])
                        p.wait("t", "EV", o2g)
                    for k4 in range(4):
                        kc = 4 * b_ + k4
                        fn_ = mm if k4 == 3 else mm0
                        fn_(pS[:, 256 * k4:256 * k4 + 256],
                            Xb[0:128, KMT + 128 * kc:KMT + 128 * kc + 128],
                            Xb[0:128, QMT + ilo:QMT + ilo + SW_A],
                            True, True)
                    if b_ >= 2:
                        emit_av(b_ - 2)
                    mmv = p.ct["MM"]
                    p.wait("s", "MM", mmv)
                    state["exp"][("b", nn, s, b_)] = sop(
                        lambda e, S, b_=b_, pS=pS: e.activation(
                            AR1[:, 1024 * b_:1024 * b_ + 1024], pS[:, 0:1024],
                            AF.Exp, bias=0.0))
                for bb in (14, 15):
                    p.wait("t", "SE", state["exp"][("b", nn, s, bb)])
                    emit_av(bb)
                avm = p.ct["MM"]
                p.wait("v", "MM", avm)
                vop0(lambda e, S: e.tensor_scalar(
                    ROWS[64:65, 1024:1280], pb[0][64:65, 0:256], 1e-30, None,
                    op0=AL.max))
                vop0(lambda e, S: e.reciprocal(ROWS[64:65, 1024:1280],
                                               ROWS[64:65, 1024:1280]))
                rec = vop(lambda e, S: e.tensor_copy(RECROW,
                                                     ROWS[64:65, 1024:1280]))
                p.wait("t", "EV", rec)
                mrep = mm(pb[1][:, 0:256], CON[64:65, CON_ONEB:CON_ONEB + 128],
                          RECROW, True, True)
                p.wait("v", "MM", mrep)
                vop0(lambda e, S: e.tensor_copy(REPS, pb[1][:, 0:256]))
                vop0(lambda e, S, ilo=ilo: e.tensor_mul(
                    Xb[0:64, O1C + ilo:O1C + ilo + 256], pb[0][0:64, 0:256],
                    REPS[0:64, :]))
                state["av"] = vop(lambda e, S, ilo=ilo, s=s: e.tensor_add(
                    Xb[0:64, O1C + ilo:O1C + ilo + 256],
                    Xb[0:64, O1C + ilo:O1C + ilo + 256],
                    AR2[0:64, 128 * ((2 * s) % 32):128 * ((2 * s) % 32) + 256]))
            if lastb and attn_stage <= 7:
                trunc[0] = True; break
            dupev = state["av"]
            for i in range(16):
                h2 = i % 2
                p.wait("t", "EV", state["ps1b"][h2])
                p.wait("t", "EV", dupev)
                msd = p.add("t", lambda e, S, i=i, h2=h2: e.matmul(
                    pS1[64:128, 512 * h2:512 * h2 + 512], eyeT,
                    Xb[:, O1C + 512 * i:O1C + 512 * i + 512],
                    start=True, stop=True, tile_position=(0, 64)), "MM", 1)
                p.wait("v", "MM", p.ct["MM"])
                state["ps1b"][h2] = vop(lambda e, S, i=i, h2=h2: e.tensor_copy(
                    Xb[64:128, O1C + 512 * i:O1C + 512 * i + 512],
                    pS1[64:128, 512 * h2:512 * h2 + 512]))
            dup_ev = p.ct["EV"]
            if lastb and attn_stage <= 8:
                trunc[0] = True; break
            p.wait("t", *wl_wog)
            p.wait("t", "EV", dup_ev)
            slot = 0
            yev = 0
            for dc in range(8):
                bk = 2 + (dc % 2)
                h = dc % 2
                p.wait("t", "EV", state["pb23"][h])
                p.wait("t", "EV", state["ps1b"][h])
                for par in range(2):
                    base = 64 * par
                    dst = pb[bk][:, 0:512] if par == 0 else \
                        pS1[:, 512 * h:512 * h + 512]
                    for gi in range(8):
                        g = 2 * gi + par
                        fn_ = mm if (par == 1 and gi == 7) else mm0
                        fn_(dst,
                            WA[base:base + 64,
                               8192 * reg_wog + 1024 * gi + 128 * dc:
                               8192 * reg_wog + 1024 * gi + 128 * dc + 128],
                            Xb[base:base + 64, O1C + g:O1C + g + 8177:16],
                            gi == 0, gi == 7)
                wmm = p.ct["MM"]
                p.wait("s", "MM", wmm)
                se = sop(lambda e, S, h=h: e.activation(
                    AR1f[:, 5120 + 512 * h:5120 + 512 * h + 512],
                    pS1[:, 512 * h:512 * h + 512], AF.Copy))
                p.wait("v", "MM", wmm)
                p.wait("v", "SE", se)
                vop0(lambda e, S, dc=dc, bk=bk, h=h: e.tensor_add(
                    AR1f[:, 2048 + 512 * h:2048 + 512 * h + 512],
                    pb[bk][:, 0:512], AR1f[:, 5120 + 512 * h:5120 + 512 * h + 512]))
                vop0(lambda e, S, dc=dc, h=h: e.tensor_scalar(
                    AR1f[:, 2048 + 512 * h:2048 + 512 * h + 512],
                    AR1f[:, 2048 + 512 * h:2048 + 512 * h + 512],
                    MISC[:, MI_B["bo"] + dc:MI_B["bo"] + dc + 1],
                    None, op0=AL.add))
                yev = vop(lambda e, S, dc=dc, slot=slot, nn=nn, h=h: e.tensor_add(
                    Yf(slot, dc, 0, 512),
                    AR1f[:, 2048 + 512 * h:2048 + 512 * h + 512],
                    XAf(dc, 512 * nn, 512)))
                state["pb23"][h] = yev
                state["ps1b"][h] = yev
            wuse[reg_wog] = p.ct["MM"]
            if debug and nn == 0:
                p.wait("y", "EV", yev)
                y_dma(dbg["dyt"][:, :].bitcast(mybir.dt.uint32),
                      X[:, 12288:14336].bitcast(mybir.dt.uint32), sem="SO")
                y_dma(dbg["do1"][:, :].bitcast(mybir.dt.uint16),
                      Xb[:, O1C:O1C + 8192].bitcast(mybir.dt.uint16), sem="SO")
                y_dma(dbg["do2"][:, :].bitcast(mybir.dt.uint16),
                      AR2[:, 0:4096].bitcast(mybir.dt.uint16), sem="SO")
                p.wait("t", "SO", p.ct["SO"])
            if lastb and attn_stage <= 9:
                trunc[0] = True; break
            ab_ev = ln_rms_block(nn, slot, yev)
            state["ps1b"] = [ab_ev, ab_ev]
            state["pb23"] = [ab_ev, ab_ev]
            state["av"] = ab_ev

        if trunc[0] or attn_nn < 4:
            return p.ct["EV"], xs_done
        wl1, reg1 = next_wload("w1aT")
        wl2, reg2 = next_wload("w2aT")
        p.wait("t", *wl1)
        p.wait("t", "EV", ab_ev)
        pbf = [ab_ev, ab_ev]
        h1f = [0, 0]
        for st in range(4):
            lo = 512 * st
            for ht in range(8):
                bk = 2 + (ht % 2)
                p.wait("t", "EV", pbf[ht % 2])
                for dt_ in range(NT):
                    fn_ = mm if dt_ == NT - 1 else mm0
                    fn_(pb[bk][:, 0:512], WSl(reg1, dt_, 128 * ht, 128),
                        XAf(dt_, lo, 512), dt_ == 0, dt_ == NT - 1)
                hm = p.ct["MM"]
                p.wait("v", "MM", hm)
                pbf[ht % 2] = vop(lambda e, S, bk=bk, ht=ht: e.tensor_scalar(
                    AR2[:, 512 * ht:512 * ht + 512], pb[bk][:, 0:512],
                    MISC[:, MI_B["b1a"] + ht:MI_B["b1a"] + ht + 1], None,
                    op0=AL.add))
            if st == 0:
                p.wait("t", *wl2)
            p.wait("t", "EV", max(pbf))
            for dc in range(8):
                bk = 2 + (dc % 2)
                p.wait("t", "EV", pbf[dc % 2])
                for hti in range(NT):
                    fn_ = mm if hti == NT - 1 else mm0
                    fn_(pb[bk][:, 0:512], WSl(reg2, hti, 128 * dc, 128),
                        AR2[:, 512 * hti:512 * hti + 512], hti == 0,
                        hti == NT - 1)
                zm = p.ct["MM"]
                p.wait("v", "MM", zm)
                vop0(lambda e, S, bk=bk, dc=dc: e.tensor_scalar(
                    AR1f[:, 2048 + 512 * (dc % 2):2048 + 512 * (dc % 2) + 512],
                    pb[bk][:, 0:512],
                    MISC[:, MI_B["b2a"] + dc:MI_B["b2a"] + dc + 1], None,
                    op0=AL.add))
                pbf[dc % 2] = vop(lambda e, S, bk=bk, dc=dc, lo=lo: e.tensor_add(
                    Xf(dc, lo, 512),
                    AR1f[:, 2048 + 512 * (dc % 2):2048 + 512 * (dc % 2) + 512],
                    XAf(dc, lo, 512)))
        z_ev = max(pbf)
        return z_ev, xs_done


    def ln_rms_block(nn, slot, gate_ev):
        p.wait("t", "EV", gate_ev)
        p.wait("s", "MM", p.ct["MM"])
        sq = {}
        for dt_ in range(NT):
            slt = LNSf[dt_ % 2]
            p.wait("s", "MM", sq.get(dt_ - 2, 0))
            p.wait("s", "EV", gate_ev)
            sev = sop(lambda e, S, dt_=dt_, slt=slt, slot=slot: e.activation(
                AR1f[:, slt:slt + 512], Yf(slot, dt_, 0, 512), AF.Square))
            p.wait("t", "SE", sev)
            for c4 in range(4):
                mm0(pb[c4][:, 4 + nn:5 + nn], Yf(slot, dt_, 128 * c4, 128),
                    MISC[:, MI_OCF:MI_OCF + 1], dt_ == 0, dt_ == NT - 1)
                fn_ = mm0 if c4 < 3 else (lambda *a, **k: mm(*a, **k))
                fn_([pS1, pS1, pS2, pS2][c4][:, [0, 512, 0, 512][c4] + 4 + nn:
                                             [0, 512, 0, 512][c4] + 5 + nn],
                    AR1f[:, slt + 128 * c4:slt + 128 * c4 + 128],
                    MISC[:, MI_OCF:MI_OCF + 1], dt_ == 0, dt_ == NT - 1)
            sq[dt_] = p.ct["MM"]
        p.wait("v", "MM", p.ct["MM"])
        SQR = [pS1[:, 0:512], pS1[:, 512:1024], pS2[:, 0:512], pS2[:, 512:1024]]
        for c4 in range(4):
            vop0(lambda e, S, c4=c4, nn=nn: e.tensor_scalar(
                MISC[:, MI_SM + 4 * nn + c4:MI_SM + 4 * nn + c4 + 1],
                pb[c4][:, 4 + nn:5 + nn], 1.0 / 1024, None, op0=AL.mult))
            vop0(lambda e, S, c4=c4, nn=nn: e.tensor_scalar(
                MISC[:, MI_SV + 4 * nn + c4:MI_SV + 4 * nn + c4 + 1],
                SQR[c4][:, 4 + nn:5 + nn], 1.0 / 1024, None, op0=AL.mult))
        cm, cv = MISC[:, MI_SM + 4 * nn:MI_SM + 4 * nn + 4], \
            MISC[:, MI_SV + 4 * nn:MI_SV + 4 * nn + 4]
        vop0(lambda e, S, cm=cm, nn=nn: e.tensor_mul(
            MISC[:, MI_SS:MI_SS + 4], cm, cm))
        sev2 = vop(lambda e, S, cv=cv: e.tensor_sub(
            cv, cv, MISC[:, MI_SS:MI_SS + 4]))
        p.wait("s", "EV", sev2)
        sop(lambda e, S, cv=cv: e.activation(MISC[:, MI_SS:MI_SS + 4], cv,
                                             AF.Ln, bias=0.0))
        rst = sop(lambda e, S, nn=nn: e.activation(
            MISC[:, MI_SR + 4 * nn:MI_SR + 4 * nn + 4],
            MISC[:, MI_SS:MI_SS + 4], AF.Exp, scale=-0.5))
        p.wait("v", "SE", rst)
        m16 = vop(lambda e, S, nn=nn: e.tensor_copy(
            MISCb[:, MB_M + 4 * nn:MB_M + 4 * nn + 4],
            MISC[:, MI_SM + 4 * nn:MI_SM + 4 * nn + 4]))
        m16 = vop(lambda e, S, nn=nn: e.tensor_copy(
            MISCb[:, MB_R + 4 * nn:MB_R + 4 * nn + 4],
            MISC[:, MI_SR + 4 * nn:MI_SR + 4 * nn + 4]))
        p.wait("t", "EV", m16)
        for c4 in range(4):
            c = 4 * nn + c4
            mm0(pb[0][0:1, 128 * c4:128 * c4 + 128], MISCb[:, MB_M + c:MB_M + c + 1],
                CON[:, CON_EYE:CON_EYE + 128], True, True)
            fn_ = mm0 if c4 < 3 else (lambda *a, **k: mm(*a, **k))
            fn_(pb[1][0:1, 128 * c4:128 * c4 + 128], MISCb[:, MB_R + c:MB_R + c + 1],
                CON[:, CON_EYE:CON_EYE + 128], True, True)
        p.wait("v", "MM")
        vop0(lambda e, S: e.tensor_copy(ROW0(0, 512), pb[0][0:1, 0:512]))
        rowev = vop(lambda e, S: e.tensor_copy(ROW0(512, 512), pb[1][0:1, 0:512]))
        p.wait("t", "EV", rowev)
        mm0(pb[0][:, 0:512], MISC[0:1, MI_ONE:MI_ONE + 128], ROW0(0, 512),
            True, True)
        mrr = mm(pb[1][:, 0:512], MISC[0:1, MI_ONE:MI_ONE + 128], ROW0(512, 512),
                 True, True)
        p.wait("v", "MM", mrr)
        ab_ev = 0
        for dt_ in range(NT):
            vop0(lambda e, S, dt_=dt_, slot=slot: e.tensor_sub(
                AR1f[:, STTf[dt_ % 2]:STTf[dt_ % 2] + 512],
                Yf(slot, dt_, 0, 512), pb[0][:, 0:512]))
            ab_ev = vop(lambda e, S, dt_=dt_, nn=nn: e.tensor_mul(
                XAf(dt_, 512 * nn, 512),
                AR1f[:, STTf[dt_ % 2]:STTf[dt_ % 2] + 512], pb[1][:, 0:512]))
        return ab_ev


    def final_residual(gate_ev):
        mr = emit_ln(0.0, False, gate_ev)
        p.wait("y", "XS", p.ct["XS"])
        p.wait("y", "EV", mr)
        xr_free = {}
        for st in range(4):
            lo = 512 * st
            bk = (st % 2) * 2
            for dt_ in range(NT):
                p.wait("y", "EV", xr_free.get((st - 1, dt_), 0))
                y_dma(AR1f[:, 4096 + 512 * dt_:4096 + 512 * dt_ + 512],
                      xspill[128 * dt_:128 * (dt_ + 1), lo:lo + 512], sem="XR")
            xr = p.ct["XR"]
            p.wait("t", "EV", mr if st == 0 else 0)
            for c4 in range(4):
                c = 4 * st + c4
                mm0(pb[bk][0:1, 128 * c4:128 * c4 + 128],
                    MISCb[:, MB_M + c:MB_M + c + 1],
                    CON[:, CON_EYE:CON_EYE + 128], True, True)
                fn_ = mm0 if c4 < 3 else (lambda *a, **k: mm(*a, **k))
                fn_(pb[bk + 1][0:1, 128 * c4:128 * c4 + 128],
                    MISCb[:, MB_R + c:MB_R + c + 1],
                    CON[:, CON_EYE:CON_EYE + 128], True, True)
            p.wait("v", "MM")
            vop0(lambda e, S, bk=bk: e.tensor_copy(ROW0(0, 512), pb[bk][0:1, 0:512]))
            rowev = vop(lambda e, S, bk=bk: e.tensor_copy(ROW0(512, 512),
                                                          pb[bk + 1][0:1, 0:512]))
            p.wait("t", "EV", rowev)
            mm0(pb[bk][:, 0:512], MISC[0:1, MI_ONE:MI_ONE + 128], ROW0(0, 512),
                True, True)
            mrr = mm(pb[bk + 1][:, 0:512], MISC[0:1, MI_ONE:MI_ONE + 128],
                     ROW0(512, 512), True, True)
            p.wait("v", "MM", mrr)
            p.wait("v", "XR", xr)
            last = 0
            for dt_ in range(NT):
                hm = 512 * dt_
                vop0(lambda e, S, dt_=dt_, lo=lo, bk=bk: e.tensor_sub(
                    AR1f[:, LNSf[dt_ % 2]:LNSf[dt_ % 2] + 512],
                    Xf(dt_, lo, 512), pb[bk][:, 0:512]))
                vop0(lambda e, S, dt_=dt_, bk=bk: e.tensor_mul(
                    AR1f[:, LNSf[dt_ % 2]:LNSf[dt_ % 2] + 512],
                    AR1f[:, LNSf[dt_ % 2]:LNSf[dt_ % 2] + 512],
                    pb[bk + 1][:, 0:512]))
                last = vop(lambda e, S, dt_=dt_, lo=lo, hm=hm: e.tensor_add(
                    Xf(dt_, lo, 512),
                    AR1f[:, LNSf[dt_ % 2]:LNSf[dt_ % 2] + 512],
                    AR1f[:, 4096 + hm:4096 + hm + 512]))
                xr_free[(st, dt_)] = last
        return last

    span_free_g = {"c": 0}
    last_ev = emit_ffn(0, ones_v)

    if 'attn' in parts:
        z_ev, _xs = emit_attn(last_ev)
        if attn_nn == 4 and attn_stage >= 10:
            last_ev = final_residual(z_ev)
        if 'ffn2' in parts and attn_nn == 4 and attn_stage >= 10:
            p.wait("g", "MM", p.ct["MM"])
            wf["kw1"] = wload(0, "kw1c")
            wf["kw2"] = wload(1, "kw2c")
            last_ev = emit_ffn(1, last_ev)

    p.wait("y", "EV", p.ct["EV"])
    p.wait("y", "MM", p.ct["MM"])
    p.wait("y", "SE", p.ct["SE"])
    for s_ in ("GS", "GT0", "GT1", "GF0", "GF1", "WB", "WA0", "WA1", "WLC"):
        p.wait("y", s_, p.ct[s_])
    U32 = mybir.dt.uint32
    for dt_ in range(NT):
        y_dma(yout[128 * dt_:128 * (dt_ + 1), :].bitcast(U32) if debug else
              yout[128 * dt_:128 * (dt_ + 1), :],
              Xf(dt_, 0, 2048).bitcast(U32) if debug else Xf(dt_, 0, 2048),
              sem="SO")
    if debug:
        y_dma(dbg["dxa"][:, :], XA[:, :], sem="SO")
        y_dma(dbg["dtp"][:, :], CON[:, CON_TP:CON_TP + 2048], sem="SO")
        if 'gather' in parts:
            y_dma(dbg["doh"][:, 0:256], OHQ[:, :], sem="SO")
        if 'attn' in parts:
            U16 = mybir.dt.uint16
            y_dma(dbg["dqk"][:, :].bitcast(U16), Xb[:, 0:16384].bitcast(U16), sem="SO")
            y_dma(dbg["dvmo"][:, :].bitcast(U16), AR3[:, :].bitcast(U16), sem="SO")
        if 'flash' in parts:
            y_dma(dbg["dfq"][:, :], AR2[:, 2048:4096], sem="SO")
            U16 = mybir.dt.uint16
            y_dma(dbg["de"][:, :].bitcast(U16), AR1[:, :].bitcast(U16), sem="SO")
            y_dma(dbg["dht"][:, :], AR2[:, 0:4096], sem="SO")
    p.wait("y", "SO", p.ct["SO"])

    with nc.Block() as blk:
        semcms = [nc.semaphore(n) for n in p.semnames]
        S = {}
        for n_, cm in zip(p.semnames, semcms):
            S[n_] = cm.__enter__()

        @blk.tensor
        def _(e):
            for fn in p.ops["t"]:
                fn(e, S)

        @blk.vector
        def _(e):
            for fn in p.ops["v"]:
                fn(e, S)

        @blk.scalar
        def _(e):
            for fn in p.ops["s"]:
                fn(e, S)

        @blk.sync
        def _(e):
            for fn in p.ops["y"]:
                fn(e, S)

        @blk.gpsimd
        def _(e):
            for fn in p.ops["g"]:
                fn(e, S)

    return nc



def _wog(wo):
    out = np.zeros((1024, 1024), np.float32)
    w = np.asarray(wo).reshape(1024, 16, 64)
    for g in range(16):
        out[128 * (g // 2) + 64 * (g % 2):128 * (g // 2) + 64 * (g % 2) + 64, :] \
            = w[:, g, :].T
    return np.ascontiguousarray(out)


def _prep_core_inputs(core, inp):
    b, qd = core // 4, core % 4
    sl = slice(2048 * qd, 2048 * (qd + 1))
    C = _get_consts()

    def f32(a):
        return np.ascontiguousarray(np.asarray(a, dtype=np.float32))

    d = {
        "xt": f32(np.asarray(inp["L0_attn_x"])[b, sl, :].T),
        "wqT": f32(np.asarray(inp["L0_attn_Wq"]).T),
        "wkT": f32(np.asarray(inp["L0_attn_Wk"]).T),
        "wvT": f32(np.asarray(inp["L0_attn_Wv"]).T),
        "woT": f32(np.asarray(inp["L0_attn_Wo"]).T),
        "wogT": _wog(np.asarray(inp["L0_attn_Wo"])),
        "w1aT": f32(np.asarray(inp["L0_attn_W1"]).T),
        "w2aT": f32(np.asarray(inp["L0_attn_W2"]).T),
        "relbT": f32(np.asarray(inp["L0_attn_rel_bias"]).T),
        "mflat": C["mflat"], "eyetri": C["eyetri"], "l64": C["l64"],
        "gmask": C["gmask"],
        "fconst": np.tile(np.array([0.0, 1.0, 1e-5, -0.5, 1.0 / 32.0, -40.0,
                                    0.5, 2.0], np.float32), (128, 1)),
    }
    for nm, key in (("bq4", "L0_attn_bq"), ("bk4", "L0_attn_bk"),
                    ("bv4", "L0_attn_bv"), ("bo4", "L0_attn_bo"),
                    ("b1a4", "L0_attn_b1"), ("b2a4", "L0_attn_b2")):
        d[nm] = f32(np.asarray(inp[key]).reshape(8, 128).T)
    for nm in ("bq4", "bk4", "bv4"):
        sh = np.zeros((128, 8), np.float32)
        sh[0:64, :] = d[nm][64:128, :]
        d[nm + "s"] = sh
    for pfx in ("q", "k", "v"):
        d[f"{pfx}nwT"] = f32(np.asarray(inp[f"L0_ffn_{pfx}_node_w"]).T)
        d[f"{pfx}nb"] = f32(np.asarray(inp[f"L0_ffn_{pfx}_node_b"]).reshape(1, 15))
        d[f"{pfx}w1c"] = f32(np.asarray(inp[f"L0_ffn_{pfx}_w1"]).transpose(1, 0, 2)
                             .reshape(1024, 1024))
        d[f"{pfx}b1f4"] = f32(np.asarray(inp[f"L0_ffn_{pfx}_b1"]).reshape(1024)
                              .reshape(8, 128).T)
        d[f"{pfx}w2c"] = f32(np.asarray(inp[f"L0_ffn_{pfx}_w2"]).reshape(1024, 1024))
        d[f"{pfx}b2"] = f32(np.asarray(inp[f"L0_ffn_{pfx}_b2"]))
    hs = np.zeros((16, 4), np.float32)
    for ln in range(4):
        hs[4 * qd + ln, ln] = 1.0
    d["hsel"] = hs
    return d


def kernel(**inputs):
    nc = build_nc()
    in_maps = [_prep_core_inputs(c, inputs) for c in range(NCORE)]
    res = run_bass_kernel_spmd(nc, in_maps, core_ids=list(range(NCORE)))
    out = np.zeros((2, 8192, 1024), np.float32)
    for c in range(NCORE):
        b, qd = c // 4, c % 4
        out[b, 2048 * qd:2048 * (qd + 1), :] = res.results[c]["y"].T
    return out


# version 07 (final)
# speedup vs baseline: 1.1276x; 1.0180x over previous
import math
import numpy as np
import concourse.bass as bass
import concourse.mybir as mybir
from concourse.bass_utils import run_bass_kernel_spmd

F32 = mybir.dt.float32
BF16 = mybir.dt.bfloat16
AF = mybir.ActivationFunctionType
AL = mybir.AluOpType

NCORE = 8
T = 2048
D = 1024
NT = 8
SW_F = 256
NSTRIP_F = T // SW_F
NSPAN = 16
EXPSH = -40.0


def _bucket_of(d):
    n = -d
    ret = 16 if n < 0 else 0
    na = abs(n)
    if na < 8:
        return ret + na
    v = 8 + int(math.floor(math.log(na / 8.0) / math.log(16.0) * 8.0))
    return ret + min(v, 15)


def _host_consts():
    c = {}
    mf = np.zeros((96, 16384), np.float32)
    for di, delta in enumerate((-1, 0, 1)):
        for dd in range(-90, 91):
            b = _bucket_of(dd)
            for jl in range(128):
                il = dd + 128 * delta + jl
                if 0 <= il < 128:
                    col = 128 * jl + il
                    mf[32 * di + b, col] += 1.0
                    if dd >= 1:
                        mf[32 * di + 31, col] -= 1.0
                    else:
                        mf[32 * di + 15, col] -= 1.0
    c["mflat"] = mf
    eye = np.zeros((128, 256), np.float32)
    eye[:, 0:128] = np.eye(128, dtype=np.float32)
    eye[:, 128:256] = (np.arange(128)[:, None] < np.arange(128)[None, :]).astype(np.float32)
    c["eyetri"] = eye
    l64 = np.zeros((128, 65), np.float32)
    l64[0:64, 0:64] = (np.arange(64)[:, None] < np.arange(64)[None, :]).astype(np.float32)
    l64[:, 64:65] = 1.0
    c["l64"] = l64
    g = np.zeros((16, 1024), np.float32)
    for ht in range(8):
        for pp in range(128):
            g[2 * ht + pp // 64, 128 * ht + pp] = 1.0
    c["gmask"] = g
    return c


_CONSTS = None


def _get_consts():
    global _CONSTS
    if _CONSTS is None:
        _CONSTS = _host_consts()
    return _CONSTS



ENGS = ("t", "v", "s", "y", "g")


class P:
    def __init__(self):
        self.ops = {e: [] for e in ENGS}
        self.ct = {}
        self.semnames = []

    def sem(self, name):
        if name not in self.ct:
            self.ct[name] = 0
            self.semnames.append(name)

    def add(self, eng, fn, sem=None, inc=0):
        if sem is not None:
            self.sem(sem)
            self.ct[sem] += inc

            def wrapped(e, S, fn=fn, sem=sem, inc=inc):
                fn(e, S).then_inc(S[sem], inc)
            self.ops[eng].append(wrapped)
            return self.ct[sem]
        self.ops[eng].append(fn)
        return None

    def wait(self, eng, sem, val=None):
        self.sem(sem)
        v = self.ct[sem] if val is None else val
        assert v <= self.ct[sem], f"{sem}: wait {v} > emitted {self.ct[sem]}"
        if v > 0:
            self.ops[eng].append(lambda e, S, sem=sem, v=v: e.wait_ge(S[sem], v))
        return v



def build_nc(debug=False, parts=('tp', 'ln', 'kv', 'gather', 'flash', 'attn', 'ffn2'), attn_nn=4, attn_stage=10):
    nc = bass.Bass("TRN2", target_bir_lowering=False)
    p = P()

    din = {}

    def DI(name, shape):
        din[name] = nc.dram_tensor(name, list(shape), F32, kind="ExternalInput").ap()

    DI("xt", (D, T))
    for w in ("wqT", "wkT", "wvT", "woT", "w1aT", "w2aT"):
        DI(w, (D, D))
    for b in ("bq4", "bk4", "bv4", "bo4", "b1a4", "b2a4"):
        DI(b, (128, 8))
    for pfx in ("q", "k", "v"):
        DI(f"{pfx}nwT", (D, 15))
        DI(f"{pfx}nb", (1, 15))
        DI(f"{pfx}w1c", (D, D))
        DI(f"{pfx}b1f4", (128, 8))
        DI(f"{pfx}w2c", (D, D))
        DI(f"{pfx}b2", (16, D))
    DI("relbT", (16, 32))
    DI("wogT", (D, D))
    for b in ("bq4s", "bk4s", "bv4s"):
        DI(b, (128, 8))
    DI("fconst", (128, 8))
    DI("hsel", (16, 4))
    DI("mflat", (96, 16384))
    DI("eyetri", (128, 256))
    DI("l64", (128, 65))
    DI("gmask", (16, 1024))

    yout = nc.dram_tensor("y", [D, T], F32, kind="ExternalOutput").ap()
    dbg = {}
    if debug:
        for nm, shape, dt_ in (
            ("dxa", (128, 16384), BF16), ("dfq", (128, 2048), BF16),
            ("de", (128, 16384), BF16), ("dtp", (128, 2048), BF16),
            ("doh", (128, 256), BF16), ("dht", (128, 4096), BF16),
            ("dqk", (128, 16384), BF16), ("dvmo", (128, 8256), BF16),
            ("dyt", (128, 2048), F32), ("do1", (128, 8192), BF16),
            ("do2", (128, 4096), BF16),
        ):
            dbg[nm] = nc.dram_tensor(nm, list(shape), dt_, kind="ExternalOutput").ap()

    xspill = nc.dram_tensor("xspill", [D, T], F32).ap()
    kb = nc.dram_tensor("kb", [D, T], BF16).ap()
    vb = nc.dram_tensor("vb", [T, D], BF16).ap()
    kg = nc.dram_tensor("kg", [4 * D, T], BF16).ap()
    vg = nc.dram_tensor("vg", [4 * T, D], BF16).ap()
    tpscr = nc.dram_tensor("tpscr", [12, 16384], BF16).ap()

    X = nc.alloc_sbuf_tensor("X", [128, 16384], F32)
    XA = nc.alloc_sbuf_tensor("XA", [128, 16384], BF16)
    AR1 = nc.alloc_sbuf_tensor("AR1", [128, 16384], BF16)
    AR2 = nc.alloc_sbuf_tensor("AR2", [128, 4096], BF16)
    AR3 = nc.alloc_sbuf_tensor("AR3", [128, 8256], BF16)
    WA = nc.alloc_sbuf_tensor("WA", [128, 16384], BF16)
    CON = nc.alloc_sbuf_tensor("CON", [128, 2512], BF16)
    ROWS = nc.alloc_sbuf_tensor("ROWS", [128, 1536], F32)
    SPK = nc.alloc_sbuf_tensor("SPK", [128, 2560], BF16)
    OHQ = nc.alloc_sbuf_tensor("OHQ", [128, 256], BF16)
    NWB = nc.alloc_sbuf_tensor("NWB", [128, 1792], BF16)
    MISC = nc.alloc_sbuf_tensor("MISC", [128, 480], F32)
    FCON = nc.alloc_sbuf_tensor("FCON", [128, 8], F32)
    FCONSTS = [0.0, 1.0, 1e-5, -0.5, 1.0 / 32.0, -40.0, 0.5, 2.0]
    for j, v in enumerate(FCONSTS):
        nc.const_aps.register(F32, v, FCON[:, j:j + 1])

    AR1f = AR1.ap().bitcast(F32)

    def Xf(dt_, lo, n):
        return X[:, 2048 * dt_ + lo: 2048 * dt_ + lo + n]

    def XAf(dt_, lo, n):
        return XA[:, 2048 * dt_ + lo: 2048 * dt_ + lo + n]

    CON_EYE, CON_TRI, CON_L64, CON_ONEC, CON_ONEB, CON_TP = \
        0, 128, 256, 321, 324, 464
    ONEROW = CON[0:1, CON_ONEB:CON_ONEB + 128]
    ONECOL = CON[:, CON_ONEC:CON_ONEC + 1]

    def TPR(di, n):
        o = CON_TP + 128 * (4 * di + n)
        return CON[:, o:o + 128]

    def TSN(n):
        o = CON_TP + 128 * (12 + n)
        return CON[:, o:o + 128]

    def ROW0(lo, n):
        return ROWS[0:1, lo:lo + n]


    def NWT(pi, dt_):
        return NWB[:, 120 * pi + 15 * dt_: 120 * pi + 15 * dt_ + 15]

    def NBROW(pi):
        return NWB[0:1, 360 + 15 * pi: 375 + 15 * pi]

    RELBT = NWB[0:16, 440:472]
    NW_CROW = 480
    NW_REC = 512
    NW_TPS = [768, 1280]

    def B2S(lo, n):
        return SPK[0:16, 1536 + lo:1536 + lo + n]

    def OHT(lo, n):
        return SPK[0:16, 1024 + lo:1024 + lo + n]

    HSEL = NWB[0:16, 412:416]
    RBSEL = NWB[0:32, 408:412]

    MI_B = {"bq": 0, "bk": 8, "bv": 16, "bo": 24, "b1a": 32, "b2a": 40,
            "qb1": 48, "kb1": 56, "vb1": 64, "bqs": 448, "bks": 456,
            "bvs": 464}
    MI_F = 80
    MI_REP = 144
    MI_ONE = 160
    MI_BST = 288
    MI_OCF = 352
    MI_SM, MI_SV, MI_SS, MI_SR = 356, 372, 388, 404
    MISCb = MISC.ap().bitcast(BF16)
    MB_M, MB_R = 856, 872

    def WSl(r, kt, lo, n):
        o = 8192 * r + 1024 * kt + lo
        return WA[:, o:o + n]

    pb = [nc.alloc_psum_tensor(f"pb{i}", [128, 512], F32) for i in range(4)]
    pS1 = nc.alloc_psum_tensor("pS1", [128, 1024], F32)
    pS2 = nc.alloc_psum_tensor("pS2", [128, 1024], F32)

    OHB = 14336
    OH_SC, OH_SCR, OH_KV, OH_Q = OHB, OHB + 256, OHB + 512, -1

    LNSf = [1536, 2048]
    MSKf = [2560, 3072]
    STTf = [3584, 4096]
    BST = [13312, 13824]
    MF = [9216, 11264]

    for s_ in ("XL", "SPL", "WLC", "WA0", "WA1", "WB", "GC", "GF0", "GF1",
               "GT0", "GT1", "GS", "KL0", "KL1", "SB0", "SB1", "SO",
               "MM", "EV", "SE", "CC", "XS", "XR"):
        p.sem(s_)

    def g_dma(dst, src, sem="WLC"):
        return p.add("g", lambda e, S, d=dst, s=src: e.dma_start(out=d, in_=s), sem, 16)

    def y_dma(dst, src, sem="SPL"):
        return p.add("y", lambda e, S, d=dst, s=src: e.dma_start(out=d, in_=s), sem, 16)

    def mm(out, lhsT, rhs, start, stop, tp=None, sem="MM"):
        return p.add("t", lambda e, S, o=out, l=lhsT, r=rhs, a=start, b=stop, tp=tp:
                     e.matmul(o, l, r, start=a, stop=b, tile_position=tp), sem, 1)

    def mm0(out, lhsT, rhs, start, stop, tp=None):
        p.add("t", lambda e, S, o=out, l=lhsT, r=rhs, a=start, b=stop, tp=tp:
              e.matmul(o, l, r, start=a, stop=b, tile_position=tp))

    def ttrans(out, in_, ident):
        p.add("t", lambda e, S, o=out, i=in_, d=ident: e.transpose(o, i, d))

    def tnop(sem, inc=1):
        return p.add("t", lambda e, S: e.nop(), sem, inc)

    def vop(fn, sem="EV", inc=1):
        p.wait("v", "EV", p.ct["EV"])
        return p.add("v", fn, "EV", 1)

    def vop0(fn):
        p.wait("v", "EV", p.ct["EV"])
        p.add("v", fn, "EV", 1)

    def sop(fn, sem="SE", inc=1):
        p.wait("s", "SE", p.ct["SE"])
        return p.add("s", fn, "SE", 1)

    def wload(reg, name):
        src = din[name][:, :].rearrange("(a p) f -> p a f", p=128)
        dst = WA[:, 8192 * reg:8192 * reg + 8192].rearrange("p (a f) -> p a f", a=8)
        return (f"WA{reg}", g_dma(dst, src, sem=f"WA{reg}"))

    for dt_ in range(NT):
        y_dma(Xf(dt_, 0, 2048), din["xt"][128 * dt_:128 * (dt_ + 1), :], sem="XL")
    xload = p.ct["XL"]

    g_dma(CON[:, CON_EYE:CON_EYE + 256], din["eyetri"][:, :])
    g_dma(CON[0:128, CON_L64:CON_L64 + 65], din["l64"][:, :])
    g_dma(SPK[0:16, 0:1024], din["gmask"][:, :])
    for pi, pfx in enumerate(("q", "k", "v")):
        g_dma(NWB[:, 120 * pi:120 * pi + 120].rearrange("p (a n) -> p a n", a=8),
              din[f"{pfx}nwT"][:, :].rearrange("(a p) n -> p a n", p=128))
        g_dma(NBROW(pi), din[f"{pfx}nb"][:, :])
    g_dma(HSEL, din["hsel"][:, :])
    g_dma(RELBT, din["relbT"][:, :])
    con_done = p.ct["WLC"]

    y_dma(FCON[:, :], din["fconst"][:, :], sem="SPL")
    for nm, key in (("bq4", "bq"), ("bk4", "bk"), ("bv4", "bv"), ("bo4", "bo"),
                    ("b1a4", "b1a"), ("b2a4", "b2a")):
        y_dma(MISC[:, MI_B[key]:MI_B[key] + 8], din[nm][:, :], sem="SPL")
    for pfx, key in (("q", "qb1"), ("k", "kb1"), ("v", "vb1")):
        y_dma(MISC[:, MI_B[key]:MI_B[key] + 8], din[f"{pfx}b1f4"][:, :], sem="SPL")
    for nm, key in (("bq4s", "bqs"), ("bk4s", "bks"), ("bv4s", "bvs")):
        y_dma(MISC[:, MI_B[key]:MI_B[key] + 8], din[nm][:, :], sem="SPL")
    smalls = p.ct["SPL"]

    if debug:
        vop0(lambda e, S: e.memset(AR3[:, :], 0.0))
        vop0(lambda e, S: e.memset(CON[:, CON_TP:CON_TP + 2048], 0.0))
        vop0(lambda e, S: e.memset(OHQ[:, :], 0.0))
        vop0(lambda e, S: e.memset(AR1[:, :], 0.0))
        vop0(lambda e, S: e.memset(AR2[:, :], 0.0))
        vop0(lambda e, S: e.memset(XA[:, :], 0.0))
    vop0(lambda e, S: e.memset(CON[:, CON_ONEC:CON_ONEC + 1], 1.0))
    vop0(lambda e, S: e.memset(CON[:, CON_ONEB:CON_ONEB + 128], 1.0))
    vop0(lambda e, S: e.memset(MISC[:, MI_ONE:MI_ONE + 128], 1.0))
    ones_v = vop(lambda e, S: e.memset(MISC[:, MI_OCF:MI_OCF + 1], 1.0))

    wf = {}
    wf["kw1"] = wload(0, "kw1c")
    wf["kw2"] = wload(1, "kw2c")

    def build_tprime():
        p.wait("t", "WLC", con_done)
        p.wait("t", "EV", ones_v)
        m1 = mm(pS2[0:32, 0:4], RELBT, HSEL, True, True)
        p.wait("v", "MM", m1)
        e1 = vop(lambda e, S: e.tensor_copy(RBSEL, pS2[0:32, 0:4]))
        p.wait("g", "EV", e1)
        p.add("g", lambda e, S: e.dma_start(out=NWB[0:1, NW_CROW:NW_CROW + 4],
                                            in_=RBSEL[15:16, :]), "GC", 16)
        c2 = p.add("g", lambda e, S: e.dma_start(out=NWB[0:1, NW_CROW + 4:NW_CROW + 8],
                                                 in_=RBSEL[31:32, :]), "GC", 16)
        p.wait("v", "GC", c2)
        e2 = vop(lambda e, S: e.tensor_sub(NWB[0:1, NW_CROW + 8:NW_CROW + 12],
                                           NWB[0:1, NW_CROW + 4:NW_CROW + 8],
                                           NWB[0:1, NW_CROW:NW_CROW + 4]))
        p.wait("t", "EV", e2)
        last = None
        for ni in range(4):
            mm0(pS2[0:128, 8 + ni:9 + ni], ONEROW,
                NWB[0:1, NW_CROW + ni:NW_CROW + ni + 1], True, True)
            last = mm(pS2[0:128, 12 + ni:13 + ni], ONEROW,
                      NWB[0:1, NW_CROW + 8 + ni:NW_CROW + 9 + ni], True, True)
        p.wait("v", "MM", last)
        er = vop(lambda e, S: e.tensor_copy(MISC[:, MI_REP:MI_REP + 8],
                                            pS2[0:128, 8:16]))
        for ni in range(4):
            vop(lambda e, S, ni=ni: e.tensor_scalar(
                TSN(ni), CON[:, CON_TRI:CON_TRI + 128],
                MISC[:, MI_REP + 4 + ni:MI_REP + 5 + ni], None, op0=AL.mult))
        return er

    def tprime_rtiles(gate_ev):
        ps1_free = [gate_ev, gate_ev]
        tps_free = [0, 0]
        mf_mm = [0, 0]
        for di in range(3):
            for half in range(8):
                hb = half % 2
                p.wait("g", "MM", mf_mm[hb])
                ld = g_dma(AR1[0:32, MF[hb]:MF[hb] + 2048],
                           din["mflat"][32 * di:32 * di + 32,
                                        2048 * half:2048 * half + 2048],
                           sem=f"GF{hb}")
                p.wait("t", f"GF{hb}", ld)
                for cq in range(4):
                    h2 = cq % 2
                    p.wait("t", "EV", ps1_free[h2])
                    m_ = mm(pS1[0:4, 512 * h2:512 * h2 + 512], RBSEL,
                            AR1[0:32, MF[hb] + 512 * cq:MF[hb] + 512 * cq + 512],
                            True, True)
                    p.wait("v", "MM", m_)
                    p.wait("v", f"GT{h2}", tps_free[h2])
                    ps1_free[h2] = vop(lambda e, S, h2=h2: e.tensor_copy(
                        NWB[0:4, NW_TPS[h2]:NW_TPS[h2] + 512],
                        pS1[0:4, 512 * h2:512 * h2 + 512]))
                    p.wait("g", "EV", ps1_free[h2])
                    tps_free[h2] = g_dma(
                        tpscr[4 * di:4 * di + 4,
                              2048 * half + 512 * cq:2048 * half + 512 * cq + 512],
                        NWB[0:4, NW_TPS[h2]:NW_TPS[h2] + 512], sem=f"GT{h2}")
                mf_mm[hb] = p.ct["MM"]
            p.wait("g", "GT0", p.ct["GT0"])
            p.wait("g", "GT1", p.ct["GT1"])
            for ni in range(4):
                g_dma(TPR(di, ni),
                      tpscr[4 * di + ni:4 * di + ni + 1, :]
                      .rearrange("o (pp f) -> (o pp) f", pp=128), sem="GS")
        return p.ct["EV"], p.ct["GS"]

    if 'tp' in parts:
        tp_er = build_tprime()
    else:
        tp_er = p.ct["EV"]
    tp_state = {"ev": p.ct["EV"], "gs": p.ct["GS"]}

    def emit_ln(eps, write_xa, gate_ev, src=None):
        src = src or Xf
        p.wait("s", "XL", xload)
        p.wait("s", "SPL", smalls)
        p.wait("s", "EV", p.ct["EV"])
        p.wait("s", "MM", p.ct["MM"])
        p.wait("t", "SPL", smalls)
        p.wait("t", "WLC", con_done)
        p.wait("t", "XL", xload)
        p.wait("t", "EV", ones_v)
        p.wait("t", "EV", gate_ev)
        SQR = [pS1[:, 0:512], pS1[:, 512:1024], pS2[:, 0:512], pS2[:, 512:1024]]
        sqfree = {}
        for st in range(4):
            for dt_ in range(NT):
                g_ = 8 * st + dt_
                slot = LNSf[g_ % 2]
                p.wait("s", "MM", sqfree.get(g_ - 2, 0))
                sq = sop(lambda e, S, dt_=dt_, st=st, slot=slot: e.activation(
                    AR1f[:, slot:slot + 512], src(dt_, 512 * st, 512), AF.Square))
                p.wait("t", "SE", sq)
                for c4 in range(4):
                    mm0(pb[c4][:, st:st + 1], src(dt_, 512 * st + 128 * c4, 128),
                        MISC[:, MI_OCF:MI_OCF + 1], dt_ == 0, dt_ == NT - 1)
                    fn_ = mm0 if c4 < 3 else (lambda *a, **k: mm(*a, **k))
                    fn_(SQR[c4][:, st:st + 1],
                        AR1f[:, slot + 128 * c4:slot + 128 * c4 + 128],
                        MISC[:, MI_OCF:MI_OCF + 1], dt_ == 0, dt_ == NT - 1)
                sqfree[g_] = p.ct["MM"]
        lastmm = p.ct["MM"]
        p.wait("v", "MM", lastmm)
        for c4 in range(4):
            vop0(lambda e, S, c4=c4: e.tensor_scalar(
                MISC[:, MI_SM + c4:MI_SM + 13 + c4:4], pb[c4][:, 0:4],
                1.0 / 1024, None, op0=AL.mult))
            vop0(lambda e, S, c4=c4: e.tensor_scalar(
                MISC[:, MI_SV + c4:MI_SV + 13 + c4:4], SQR[c4][:, 0:4],
                1.0 / 1024, None, op0=AL.mult))
        vop0(lambda e, S: e.tensor_mul(
            MISC[:, MI_SS:MI_SS + 16], MISC[:, MI_SM:MI_SM + 16],
            MISC[:, MI_SM:MI_SM + 16]))
        stats_ev = vop(lambda e, S: e.tensor_sub(
            MISC[:, MI_SV:MI_SV + 16], MISC[:, MI_SV:MI_SV + 16],
            MISC[:, MI_SS:MI_SS + 16]))
        p.wait("s", "EV", stats_ev)
        sop(lambda e, S: e.activation(MISC[:, MI_SS:MI_SS + 16],
                                      MISC[:, MI_SV:MI_SV + 16], AF.Ln,
                                      bias=float(eps)))
        rstd = sop(lambda e, S: e.activation(MISC[:, MI_SR:MI_SR + 16],
                                             MISC[:, MI_SS:MI_SS + 16], AF.Exp,
                                             scale=-0.5))
        p.wait("v", "SE", rstd)
        mr16 = vop(lambda e, S: e.tensor_copy(MISCb[:, MB_M:MB_M + 16],
                                              MISC[:, MI_SM:MI_SM + 16]))
        mr16 = vop(lambda e, S: e.tensor_copy(MISCb[:, MB_R:MB_R + 16],
                                              MISC[:, MI_SR:MI_SR + 16]))
        if not write_xa:
            return mr16
        p.wait("t", "EV", mr16)
        apply_ev = mr16
        for st in range(4):
            lo = 512 * st
            bk = (st % 2) * 2
            p.wait("t", "EV", apply_ev)
            for c4 in range(4):
                c = 4 * st + c4
                mm0(pb[bk][0:1, 128 * c4:128 * c4 + 128],
                    MISCb[:, MB_M + c:MB_M + c + 1],
                    CON[:, CON_EYE:CON_EYE + 128], True, True)
                fn_ = mm0 if c4 < 3 else (lambda *a, **k: mm(*a, **k))
                fn_(pb[bk + 1][0:1, 128 * c4:128 * c4 + 128],
                    MISCb[:, MB_R + c:MB_R + c + 1],
                    CON[:, CON_EYE:CON_EYE + 128], True, True)
            p.wait("v", "MM")
            vop0(lambda e, S, bk=bk: e.tensor_copy(ROW0(0, 512), pb[bk][0:1, 0:512]))
            rowev = vop(lambda e, S, bk=bk: e.tensor_copy(ROW0(512, 512),
                                                          pb[bk + 1][0:1, 0:512]))
            p.wait("t", "EV", rowev)
            mm0(pb[bk][:, 0:512], MISC[0:1, MI_ONE:MI_ONE + 128],
                ROW0(0, 512), True, True)
            mrr = mm(pb[bk + 1][:, 0:512], MISC[0:1, MI_ONE:MI_ONE + 128],
                     ROW0(512, 512), True, True)
            p.wait("v", "MM", mrr)
            for dt_ in range(NT):
                vop0(lambda e, S, dt_=dt_, lo=lo, bk=bk: e.tensor_sub(
                    AR1f[:, STTf[dt_ % 2]:STTf[dt_ % 2] + 512],
                    src(dt_, lo, 512), pb[bk][:, 0:512]))
                apply_ev = vop(lambda e, S, dt_=dt_, lo=lo, bk=bk: e.tensor_mul(
                    XAf(dt_, lo, 512),
                    AR1f[:, STTf[dt_ % 2]:STTf[dt_ % 2] + 512],
                    pb[bk + 1][:, 0:512]))
        return apply_ev

    def fff_scores_traversal(pi, oh_dst, psum_ready_ev):
        p.wait("t", "WLC", con_done)
        p.wait("t", "EV", psum_ready_ev)
        for c in range(16):
            for dt_ in range(NT):
                mm0(pS2[0:128, 15 * c:15 * c + 15], XAf(dt_, 128 * c, 128),
                    NWT(pi, dt_), dt_ == 0, False)
            last = mm(pS2[0:128, 15 * c:15 * c + 15], ONEROW, NBROW(pi), False, True)
        p.wait("v", "MM", last)
        sc_ev = vop(lambda e, S: e.tensor_copy(AR1[:, OH_SC:OH_SC + 240],
                                               pS2[0:128, 0:240]))

        def sl(i):
            return AR1[:, OH_SCR + 16 * i:OH_SCR + 16 * i + 16]

        def scc(j):
            return AR1[:, OH_SC + j: OH_SC + j + 15 * 16:15]

        B = [sl(i) for i in range(4)]
        E_ = [sl(4 + i) for i in range(4)]
        TMP = [sl(8 + i) for i in range(8)]
        P2 = [AR1[:, OH_SC + 16 * i:OH_SC + 16 * i + 16] for i in range(4)]
        P4 = [AR1[:, OH_SC + 64 + 16 * i:OH_SC + 64 + 16 * i + 16] for i in range(8)]

        def ts1(out, in_, s, op):
            vop0(lambda e, S, o=out, i=in_, s=s, op=op:
                 e.tensor_single_scalar(o, i, s, op=op))

        MXT = sl(24)

        def selv(out, mask, tt, ff):
            vop0(lambda e, S, o=MXT, a=tt, b=ff: e.tensor_sub(o, a, b))
            vop0(lambda e, S, o=MXT, m=mask: e.tensor_mul(o, o, m))
            vop0(lambda e, S, o=out, b=ff: e.tensor_add(o, b, MXT))

        def mulv(out, a, b):
            vop0(lambda e, S, o=out, a=a, b=b: e.tensor_mul(o, a, b))

        ts1(B[0], scc(0), 0.0, AL.is_ge)
        selv(TMP[0], B[0], scc(2), scc(1))
        ts1(B[1], TMP[0], 0.0, AL.is_ge)
        selv(TMP[1], B[1], scc(4), scc(3))
        selv(TMP[2], B[1], scc(6), scc(5))
        selv(TMP[3], B[0], TMP[2], TMP[1])
        ts1(B[2], TMP[3], 0.0, AL.is_ge)
        selv(TMP[4], B[2], scc(8), scc(7))
        selv(TMP[5], B[2], scc(10), scc(9))
        selv(TMP[6], B[2], scc(12), scc(11))
        selv(TMP[7], B[2], scc(14), scc(13))
        selv(TMP[1], B[1], TMP[5], TMP[4])
        selv(TMP[2], B[1], TMP[7], TMP[6])
        selv(TMP[0], B[0], TMP[2], TMP[1])
        ts1(B[3], TMP[0], 0.0, AL.is_ge)
        for i in range(4):
            vop0(lambda e, S, i=i: e.tensor_scalar(
                E_[i], B[i], -1.0, 1.0, op0=AL.mult, op1=AL.add))
        mulv(P2[0], E_[0], E_[1]); mulv(P2[1], E_[0], B[1])
        mulv(P2[2], B[0], E_[1]); mulv(P2[3], B[0], B[1])
        for a in range(4):
            mulv(P4[2 * a], P2[a], E_[2]); mulv(P4[2 * a + 1], P2[a], B[2])
        for l in range(16):
            src_ = P4[l // 2]
            bit = E_[3] if l % 2 == 0 else B[3]
            if oh_dst >= 0:
                dst = AR1[:, oh_dst + l: oh_dst + l + 241:16]
            else:
                dst = OHQ[:, l: l + 241:16]
            vop0(lambda e, S, dst=dst, src_=src_, bit=bit: e.tensor_mul(
                dst, src_, bit))
        done = p.add("v", lambda e, S: e.engine_nop(), "EV", 1)
        return done, sc_ev

    def fff_ohT(oh_dst, chunks, dst_lo, gate_ev):
        pS2b = pS2.ap().bitcast(BF16)
        slots = [pS2b[0:16, 0:128], pS2b[0:16, 1024:1152]]
        slot_free = [gate_ev, gate_ev]
        ev = gate_ev
        for idx, c in enumerate(chunks):
            sl_ = idx % 2
            p.wait("t", "EV", slot_free[sl_])
            if oh_dst >= 0:
                src_oh = AR1[:, oh_dst + 16 * c:oh_dst + 16 * c + 16]
            else:
                src_oh = OHQ[:, 16 * c:16 * c + 16]
            p.add("t", lambda e, S, src_oh=src_oh, sl_=sl_, slots=slots:
                  e.transpose(slots[sl_], src_oh,
                              CON[:, CON_EYE:CON_EYE + 128]), "MM", 1)
            mv = p.ct["MM"]
            p.wait("v", "MM", mv)
            ev = vop(lambda e, S, idx=idx, sl_=sl_, slots=slots, dst_lo=dst_lo:
                     e.tensor_copy(OHT(dst_lo + 128 * idx, 128), slots[sl_]))
            slot_free[sl_] = ev
        return ev

    def fff_hidden(st_lo, n, b1key, oht_lo, pbfree, maskfree, h1s=512):
        for ht in range(8):
            bk = ht % 2
            p.wait("t", "EV", pbfree[bk])
            for dt_ in range(NT):
                mm0(pb[bk][:, 0:n], WSl(0, dt_, 128 * ht, 128),
                    XAf(dt_, st_lo, n), dt_ == 0, dt_ == NT - 1)
            p.wait("t", "SE", maskfree[bk])
            mk = mm(pb[2 + bk][:, 0:n],
                    SPK[0:16, 128 * ht:128 * ht + 128],
                    OHT(oht_lo, n), True, True)
            p.wait("s", "MM", mk)
            maskfree[bk] = sop(lambda e, S, bk=bk, n=n: e.activation(
                AR1f[:, MSKf[bk]:MSKf[bk] + n], pb[2 + bk][:, 0:n], AF.Copy))
            p.wait("v", "SE", maskfree[bk])
            p.wait("v", "MM", mk)
            vop0(lambda e, S, bk=bk, n=n, b1key=b1key, ht=ht: e.scalar_tensor_tensor(
                AR1f[:, STTf[bk]:STTf[bk] + n], pb[bk][:, 0:n],
                MISC[:, MI_B[b1key] + ht:MI_B[b1key] + ht + 1],
                AR1f[:, MSKf[bk]:MSKf[bk] + n], op0=AL.add, op1=AL.mult))
            pbfree[bk] = vop(lambda e, S, bk=bk, n=n, ht=ht, h1s=h1s: e.tensor_scalar(
                AR2[:, h1s * ht:h1s * ht + n],
                AR1f[:, STTf[bk]:STTf[bk] + n], 0.0, None, op0=AL.max))
        return pbfree, maskfree

    def emit_ffn(li, gate_ev):
        if 'ln' not in parts:
            return gate_ev
        ln_ev = emit_ln(1e-5, True, max(gate_ev, tp_er))
        if li == 0 and 'tp' in parts:
            tp_state["ev"], tp_state["gs"] = tprime_rtiles(ln_ev)
        if 'kv' not in parts:
            p.wait("t", "EV", tp_state["ev"])
            p.wait("t", "GS", tp_state["gs"])
            return ln_ev

        pbfree = [ln_ev, ln_ev, ln_ev, ln_ev]
        maskfree = [0, 0]
        scores_gate = tp_er if li == 0 else gate_ev
        bounce_last = 0
        bst_free = [0, 0]
        vb2_pre = [None]
        for pi, pfx, b1key in ((1, "k", "kb1"), (2, "v", "vb1")):
            if vb2_pre[0] is None:
                p.wait("g", "MM", p.ct["MM"])
                b2load = g_dma(B2S(0, 1024), din[f"{pfx}b2"][:, :], sem="WB")
            else:
                b2load = vb2_pre[0]
            trav, sc_ev = fff_scores_traversal(pi, OH_KV, max(scores_gate, ln_ev))
            p.wait("t", "EV", trav)
            p.wait("t", *wf[f"{pfx}w1"])
            p.wait("t", *wf[f"{pfx}w2"])
            p.wait("t", "WB", b2load)
            ofree = [pbfree[0], pbfree[1]]
            ohgate = sc_ev
            for st in range(4):
                lo = 512 * st
                oht_ev = fff_ohT(OH_KV, range(4 * st, 4 * st + 4), 0, ohgate)
                ohgate = oht_ev
                p.wait("t", "EV", oht_ev)
                pbfree, maskfree = fff_hidden(lo, 512, b1key, 0, pbfree, maskfree)
                p.wait("t", "EV", max(pbfree[0], pbfree[1]))
                ofree = [max(ofree[0], pbfree[0]), max(ofree[1], pbfree[1])]
                if pi == 1:
                    jobs = [(dc,) for dc in range(8)]
                else:
                    jobs = [(tc, ds) for tc in range(4) for ds in range(2)]
                for j, job in enumerate(jobs):
                    h2 = j % 2
                    p.wait("t", "EV", ofree[h2])
                    if pi == 1:
                        dc = job[0]
                        for hti in range(NT):
                            mm0(pb[h2][:, 0:512], WSl(1, hti, 128 * dc, 128),
                                AR2[:, 512 * hti:512 * hti + 512], hti == 0, False)
                        lastm = mm(pb[h2][:, 0:512], B2S(128 * dc, 128),
                                   OHT(0, 512), False, True)
                    else:
                        tc, ds = job
                        for hti in range(NT):
                            mm0(pb[h2][:, 0:512],
                                AR2[:, 512 * hti + 128 * tc:512 * hti + 128 * tc + 128],
                                WSl(1, hti, 512 * ds, 512), hti == 0, False)
                        lastm = mm(pb[h2][:, 0:512], OHT(128 * tc, 128),
                                   B2S(512 * ds, 512), False, True)
                    p.wait("v", "MM", lastm)
                    p.wait("v", f"SB{h2}", bst_free[h2])
                    ev = vop(lambda e, S, h2=h2: e.tensor_copy(
                        AR1[:, BST[h2]:BST[h2] + 512], pb[h2][:, 0:512]))
                    ofree[h2] = ev
                    pbfree[h2] = ev
                    p.wait("y", "EV", ev)
                    if pi == 1:
                        dc = job[0]
                        bst_free[h2] = y_dma(kb[128 * dc:128 * dc + 128, lo:lo + 512],
                                             AR1[:, BST[h2]:BST[h2] + 512],
                                             sem=f"SB{h2}")
                    else:
                        tc, ds = job
                        bst_free[h2] = y_dma(
                            vb[lo + 128 * tc:lo + 128 * tc + 128, 512 * ds:512 * ds + 512],
                            AR1[:, BST[h2]:BST[h2] + 512], sem=f"SB{h2}")
            scores_gate = sc_ev
            p.wait("g", "MM", p.ct["MM"])
            if pi == 1:
                wf["vw1"] = wload(0, "vw1c")
                wf["vw2"] = wload(1, "vw2c")
                vb2_pre[0] = g_dma(B2S(0, 1024), din["vb2"][:, :], sem="WB")
                if 'gather' in parts:
                    p.wait("g", "SB0", p.ct["SB0"])
                    p.wait("g", "SB1", p.ct["SB1"])
                    p.add("g", lambda e, S: e.collective_compute(
                        "AllGather", AL.bypass,
                        replica_groups=[[0, 1, 2, 3], [4, 5, 6, 7]],
                        ins=[kb[:, :]], outs=[kg[:, :]]), "CC", 1)
            else:
                wf["qw1"] = wload(0, "qw1c")
                wf["qw2"] = wload(1, "qw2c")
                wf["qb2"] = ("WB", g_dma(B2S(0, 1024), din["qb2"][:, :], sem="WB"))

        if 'gather' not in parts:
            return pbfree[0]
        p.wait("g", "SB0", p.ct["SB0"])
        p.wait("g", "SB1", p.ct["SB1"])
        cc = p.add("g", lambda e, S: e.collective_compute(
            "AllGather", AL.bypass,
            replica_groups=[[0, 1, 2, 3], [4, 5, 6, 7]],
            ins=[vb[:, :]], outs=[vg[:, :]]), "CC", 1)

        travq, sc_ev_q = fff_scores_traversal(0, OH_Q, max(scores_gate, ln_ev))
        p.wait("t", "EV", travq)
        p.wait("t", *wf["qw1"])
        p.wait("t", *wf["qw2"])
        p.wait("t", *wf["qb2"])
        if li == 0:
            p.wait("t", "EV", tp_state["ev"])
            p.wait("t", "GS", tp_state["gs"])
            p.wait("s", "GS", tp_state["gs"])
            p.wait("s", "GF0", p.ct["GF0"])
            p.wait("s", "GF1", p.ct["GF1"])

        if 'flash' not in parts:
            return pbfree[0]
        p.wait("y", "CC", cc)
        p.wait("s", "SB0", p.ct["SB0"])
        p.wait("s", "SB1", p.ct["SB1"])
        span_free = {0: span_free_g["c"], 1: span_free_g["c"]}

        def span_load(which, sp, half):
            p.wait("y", "MM", span_free[half])
            if which == "k":
                r_, lo_ = sp // 4, 512 * (sp % 4)
                src = kg[1024 * r_:1024 * r_ + 1024, lo_:lo_ + 512] \
                    .rearrange("(a p) f -> p a f", p=128)
                na = 8
            else:
                g0 = 512 * sp
                src = vg[g0:g0 + 512, :].rearrange("(a p) f -> p a f", p=128)
                na = 4
            dst = AR3[:, 4096 * half:4096 * half + 4096] \
                .rearrange("p (a f) -> p a f", a=na)
            return y_dma(dst, src, sem=f"KL{half}")

        AVR = [pb[0][:, 0:256], pb[1][:, 0:256], pb[2][:, 0:256], pb[3][:, 0:256],
               pS1[:, 0:256], pS1[:, 512:768], pS2[:, 0:256], pS2[:, 512:768]]
        SUMR = pb[0][0:1, 384:640] if False else pb[0][0:1, 256:512]
        qofree = {0: 0, 1: 0}
        av_ev = 0
        ohgate = sc_ev_q
        for s in range(NSTRIP_F):
            qlo = SW_F * s
            p.wait("t", "EV", av_ev)
            oht_ev = fff_ohT(OH_Q, [2 * s, 2 * s + 1], 0, max(ohgate, av_ev))
            ohgate = oht_ev
            p.wait("t", "EV", oht_ev)
            fff_hidden(qlo, SW_F, "qb1", 0, pbfree, maskfree, h1s=256)
            p.wait("t", "EV", max(pbfree[0], pbfree[1]))
            for dc in range(8):
                h2 = dc % 2
                p.wait("t", "SE", qofree[h2])
                for hti in range(NT):
                    mm0(pS1[:, 512 * h2:512 * h2 + 256], WSl(1, hti, 128 * dc, 128),
                        AR2[:, 256 * hti:256 * hti + 256], hti == 0, False)
                lastm = mm(pS1[:, 512 * h2:512 * h2 + 256],
                           B2S(128 * dc, 128), OHT(0, 256), False, True)
                p.wait("s", "MM", lastm)
                qofree[h2] = sop(lambda e, S, h2=h2, dc=dc: e.activation(
                    AR2[:, 2048 + 256 * dc:2048 + 256 * dc + 256],
                    pS1[:, 512 * h2:512 * h2 + 256], AF.Copy, scale=1.0 / 32.0))
            fq_se = qofree[1]
            p.wait("t", "SE", fq_se)
            p.wait("t", "EV", p.ct["EV"])
            p.wait("t", "SE", p.ct["SE"])
            exp_se = {}
            for sp in range(NSPAN):
                half = sp % 2
                ld = span_load("k", sp, half)
                p.wait("t", f"KL{half}", ld)
                pS = pS1 if half == 0 else pS2
                p.wait("t", "SE", exp_se.get(sp - 2, 0))
                for kc in range(4):
                    for dt_ in range(NT):
                        fn_ = mm if (kc == 3 and dt_ == NT - 1) else mm0
                        fn_(pS[:, 256 * kc:256 * kc + 256],
                            AR3[:, 4096 * half + 512 * dt_ + 128 * kc:
                                4096 * half + 512 * dt_ + 128 * kc + 128],
                            AR2[:, 2048 + 256 * dt_:2048 + 256 * dt_ + 256],
                            dt_ == 0, dt_ == NT - 1)
                mmv = p.ct["MM"]
                span_free[half] = mmv
                p.wait("s", "MM", mmv)
                exp_se[sp] = sop(lambda e, S, sp=sp, pS=pS: e.activation(
                    AR1[:, 1024 * sp:1024 * sp + 1024], pS[:, 0:1024], AF.Exp))
            p.wait("t", "SE", exp_se[NSPAN - 1])
            for sp in range(NSPAN):
                half = sp % 2
                ld = span_load("v", sp, half)
                p.wait("t", f"KL{half}", ld)
                for kc in range(4):
                    first = (sp == 0 and kc == 0)
                    lastf = (sp == NSPAN - 1 and kc == 3)
                    for dc in range(8):
                        fn_ = mm if (kc == 3 and dc == 7) else mm0
                        fn_(AVR[dc],
                            AR3[:, 4096 * half + 1024 * kc + 128 * dc:
                                4096 * half + 1024 * kc + 128 * dc + 128],
                            AR1[:, 1024 * sp + 256 * kc:1024 * sp + 256 * kc + 256],
                            first, lastf)
                span_free[half] = p.ct["MM"]
            span_free_g["c"] = max(span_free.values())
            for K in range(64):
                fn_ = mm if K == 63 else mm0
                fn_(SUMR, ONECOL, AR1[:, 256 * K:256 * K + 256], K == 0, K == 63)
            msum = p.ct["MM"]
            p.wait("v", "MM", msum)
            vop0(lambda e, S: e.reciprocal(ROW0(1024, 256), SUMR))
            vop0(lambda e, S: e.tensor_scalar(ROW0(1280, 256), ROW0(1024, 256),
                                              0.5, None, op0=AL.mult))
            rec_ev = vop(lambda e, S: e.tensor_copy(
                NWB[0:1, NW_REC:NW_REC + 256], ROW0(1280, 256)))
            p.wait("t", "EV", rec_ev)
            mrep = mm(pb[1][:, 256:512], ONEROW, NWB[0:1, NW_REC:NW_REC + 256],
                      True, True)
            p.wait("v", "MM", mrep)
            vop0(lambda e, S: e.tensor_copy(AR1f[:, 0:256], pb[1][:, 256:512]))
            for dc in range(8):
                vop0(lambda e, S, dc=dc: e.tensor_mul(
                    AR1f[:, 256:512], AVR[dc], AR1f[:, 0:256]))
                av_ev = vop(lambda e, S, dc=dc, qlo=qlo: e.tensor_add(
                    Xf(dc, qlo, 256), Xf(dc, qlo, 256), AR1f[:, 256:512]))
            pbfree[0] = pbfree[1] = pbfree[2] = pbfree[3] = av_ev
        return av_ev


    Xb = X.ap().bitcast(BF16)
    QMT, KMT, O1C = 0, 8192, 16384
    SW_A = 256
    eyeT = CON[:, CON_EYE:CON_EYE + 64]
    eyeB = CON[:, CON_EYE + 64:CON_EYE + 128]
    SCRA = NWB[64:128, 768:1280]
    SCRF = NWB[:, 768:1280]
    REPS = NWB[:, 1280:1536]
    BSB = NWB[0:64, 1536:1600]
    BSTB = NWB[0:64, 1600:1664]
    RECROW = NWB[64:65, 512:768]

    def Yf(slot, dt_, lo, n):
        o = 12288 + 512 * dt_ + lo
        return X[:, o:o + n]

    def emit_attn(gate_ev):
        trunc = [False]
        ln_ev = emit_ln(1e-5, True, gate_ev)
        p.wait("y", "EV", ln_ev)
        for dt_ in range(NT):
            y_dma(xspill[128 * dt_:128 * (dt_ + 1), :], Xf(dt_, 0, 2048), sem="XS")
        xs_done = p.ct["XS"]
        p.wait("v", "XS", xs_done)
        ev0 = vop(lambda e, S: e.memset(NWB[0:64, 768:1280], 0.0))
        ev0 = vop(lambda e, S: e.memset(Xb[:, O1C:O1C + 8192], 0.0))
        ev0 = vop(lambda e, S: e.memset(Xb[64:128, QMT:QMT + 8192], 0.0))
        ev0 = vop(lambda e, S: e.memset(Xb[64:128, KMT:KMT + 8192], 0.0))
        wreg = {"i": 0}
        wuse = {0: p.ct["MM"], 1: p.ct["MM"]}

        def next_wload(name):
            r = wreg["i"] % 2
            wreg["i"] += 1
            p.wait("g", "MM", wuse[r])
            p.wait("g", f"WA{r}", p.ct[f"WA{r}"])
            return wload(r, name), r

        state = {"shift": [ev0, ev0], "pb23": [ev0, ev0], "ps1b": [ev0, ev0],
                 "av": ev0, "pb0free": ev0, "ep": None, "exp": {}, "qo": 0}

        def proj(nn, wl, reg, bkey, dst_base, dst_tensor):
            p.wait("t", *wl)
            for i in range(8):
                bk = 2 + (i % 2)
                p.wait("t", "EV", state["pb23"][i % 2])
                p.wait("t", "SE", state.get("pb23se", [0, 0])[i % 2])
                for dt_ in range(NT):
                    fn_ = mm if dt_ == NT - 1 else mm0
                    fn_(pb[bk][:, 0:512], WSl(reg, dt_, 128 * i, 128),
                        XAf(dt_, 512 * nn, 512), dt_ == 0, dt_ == NT - 1)
                mmv = p.ct["MM"]
                p.wait("s", "MM", mmv)
                se1 = sop(lambda e, S, bk=bk: e.activation(
                    SCRA, pb[bk][64:128, 0:512], AF.Copy))
                state.setdefault("pb23se", [0, 0])[i % 2] = se1
                p.wait("v", "MM", mmv)
                ev1 = vop(lambda e, S, bk=bk, i=i, bkey=bkey, dst_base=dst_base,
                          dst_tensor=dst_tensor: e.tensor_scalar(
                    dst_tensor[0:64, dst_base + 2 * i:dst_base + 2 * i + 8177:16],
                    pb[bk][0:64, 0:512],
                    MISC[0:64, MI_B[bkey] + i:MI_B[bkey] + i + 1], None,
                    op0=AL.add))
                p.wait("t", "SE", se1)
                h2 = i % 2
                p.wait("t", "EV", state["ps1b"][h2])
                ms = mm(pS1[0:64, 512 * h2:512 * h2 + 512], eyeB, SCRF,
                        True, True)
                p.wait("v", "MM", ms)
                ev2 = vop(lambda e, S, i=i, h2=h2, bkey=bkey, dst_base=dst_base,
                          dst_tensor=dst_tensor: e.tensor_scalar(
                    dst_tensor[0:64, dst_base + 2 * i + 1:dst_base + 2 * i + 8178:16],
                    pS1[0:64, 512 * h2:512 * h2 + 512],
                    MISC[0:64, MI_B[bkey + "s"] + i:MI_B[bkey + "s"] + i + 1],
                    None, op0=AL.add))
                state["ps1b"][h2] = ev2
                state["pb23"][i % 2] = ev1
            wuse[reg] = p.ct["MM"]
            return p.ct["EV"]


        def score_shift(nn):
            p.wait("t", "EV", p.ct["EV"])
            p.wait("t", "SE", p.ct["SE"])
            p.wait("s", "EV", p.ct["EV"])
            p.wait("s", "MM", p.ct["MM"])
            vop0(lambda e, S: e.memset(Xb[64:65, KMT:KMT + 8192], -1.0))
            mz = vop(lambda e, S: e.memset(Xb[64:65, QMT:QMT + 8192], 0.0))
            p.wait("t", "EV", mz)
            expse = {}
            dense_ = {}

            def emit_ones(bb):
                for k4 in range(4):
                    fn_ = mm if (bb == 15 and k4 == 3) else mm0
                    fn_(pb[0][0:1, 0:256], ONECOL,
                        AR1[:, 1024 * (bb % 4) + 256 * k4:
                            1024 * (bb % 4) + 256 * k4 + 256],
                        bb == 0 and k4 == 0, bb == 15 and k4 == 3)

            for s_ in range(32):
                ilo = SW_A * s_
                for b_ in range(16):
                    pS = pS1 if b_ % 2 == 0 else pS2
                    p.wait("t", "SE", expse.get(b_ - 2, 0))
                    for k4 in range(4):
                        kc = 4 * b_ + k4
                        fn_ = mm if k4 == 3 else mm0
                        fn_(pS[:, 256 * k4:256 * k4 + 256],
                            Xb[0:128, KMT + 128 * kc:KMT + 128 * kc + 128],
                            Xb[0:128, QMT + ilo:QMT + ilo + SW_A],
                            True, True)
                    if b_ >= 2:
                        if b_ == 2:
                            p.wait("t", "SE", dense_.get("ln", 0))
                        emit_ones(b_ - 2)
                    mmv = p.ct["MM"]
                    p.wait("s", "MM", mmv)
                    expse[b_] = sop(lambda e, S, b_=b_, pS=pS: e.activation(
                        AR1[:, 1024 * (b_ % 4):1024 * (b_ % 4) + 1024],
                        pS[:, 0:1024], AF.Exp, scale=0.125))
                for bb in (14, 15):
                    p.wait("t", "SE", expse[bb])
                    emit_ones(bb)
                smm = p.ct["MM"]
                p.wait("s", "MM", smm)
                p.wait("s", "EV", p.ct["EV"])
                lnse = sop(lambda e, S: e.activation(ROWS[0:1, 0:256],
                                                     pb[0][0:1, 0:256],
                                                     AF.Ln, bias=0.0))
                dense_["ln"] = lnse
                p.wait("v", "SE", lnse)
                cev = vop(lambda e, S, s_=s_: e.tensor_scalar(
                    AR1[0:1, 8192 + 256 * s_:8192 + 256 * s_ + 256],
                    ROWS[0:1, 0:256], 8.0, -76.0, op0=AL.mult, op1=AL.add))
                expse = {}
            p.wait("g", "EV", cev)
            cdma = g_dma(Xb[64:65, QMT:QMT + 8192], AR1[0:1, 8192:16384],
                         sem="GC")
            p.wait("t", "GC", cdma)
            p.wait("t", "EV", cev)
            p.wait("t", "SE", p.ct["SE"])

        def bias_pass(nn, ihalf, gate):
            o2g = gate
            for ii in range(32):
                I = 32 * ihalf + ii
                h2 = ii % 2
                p.wait("t", "EV", state["ps1b"][h2])
                first = True
                for di, dlt in ((0, -1), (1, 0), (2, 1)):
                    J = I + dlt
                    if J < 0 or J > 63:
                        continue
                    mm0(pS1[0:64, 512 * h2:512 * h2 + 128], AR3[:, 65 * J:65 * J + 64],
                        TPR(di, nn), first, False)
                    first = False
                lmm = mm(pS1[0:64, 512 * h2:512 * h2 + 128], AR3[:, 65 * I:65 * I + 64],
                         TSN(nn), False, True)
                p.wait("v", "MM", lmm)
                o2g = vop(lambda e, S, I=I, h2=h2: e.tensor_scalar(
                    AR2[0:64, 128 * (I % 32):128 * (I % 32) + 128],
                    pS1[0:64, 512 * h2:512 * h2 + 128],
                    MISC[0:64, MI_F + I:MI_F + I + 1], None, op0=AL.add))
                state["ps1b"][h2] = o2g
            return o2g

        ab_ev = 0
        for nn in range(attn_nn):
            lastb = (nn == attn_nn - 1)
            tokb = 512 * nn
            wl_q, reg_q = next_wload("wqT")
            wl_k, reg_k = next_wload("wkT")
            if nn == 0:
                p.wait("t", "EV", ln_ev)
                p.wait("t", "XS", xs_done)
                p.wait("v", "XS", xs_done)
            proj(nn, wl_q, reg_q, "bq", QMT, Xb)
            if lastb and attn_stage <= 1:
                trunc[0] = True; break
            wl_v, reg_v = next_wload("wvT")
            proj(nn, wl_k, reg_k, "bk", KMT, Xb)
            if lastb and attn_stage <= 2:
                trunc[0] = True; break
            score_shift(nn)
            if lastb and attn_stage <= 3:
                trunc[0] = True; break
            wl_wog, reg_wog = next_wload("wogT")
            vev = proj(nn, wl_v, reg_v, "bv", 0, AR3)
            if lastb and attn_stage <= 4:
                trunc[0] = True; break
            pS2b = pS2.ap().bitcast(BF16)
            slots = [pS2b[0:128, 0:64], pS2b[0:128, 1024:1088]]
            sfree = [vev, vev]
            for c in range(64):
                sl_ = c % 2
                p.wait("t", "EV", sfree[sl_])
                p.add("t", lambda e, S, c=c, sl_=sl_: e.transpose(
                    slots[sl_], AR3[0:64, 128 * c:128 * c + 128],
                    CON[0:64, CON_EYE:CON_EYE + 64]), "MM", 1)
                mv = p.ct["MM"]
                p.wait("v", "MM", mv)
                vop0(lambda e, S, c=c, sl_=sl_: e.tensor_copy(
                    AR3[:, 65 * c:65 * c + 64], slots[sl_]))
                sfree[sl_] = vop(lambda e, S, c=c: e.memset(
                    AR3[:, 65 * c + 64:65 * c + 65], 1.0))
            vmo_ev = p.ct["EV"]
            if lastb and attn_stage <= 5:
                trunc[0] = True; break
            p.wait("t", "EV", vmo_ev)
            p.wait("t", "EV", state["pb23"][0])
            p.wait("t", "EV", state["pb23"][1])
            for J in range(64):
                fn_ = mm if J == 63 else mm0
                fn_(pb[2][0:64, J:J + 1], AR3[:, 65 * J:65 * J + 64], ONECOL,
                    True, True)
            bsmm = p.ct["MM"]
            p.wait("v", "MM", bsmm)
            bse = vop(lambda e, S: e.tensor_copy(BSB, pb[2][0:64, 0:64]))
            p.wait("t", "EV", bse)
            p.add("t", lambda e, S: e.transpose(
                pS2.ap().bitcast(BF16)[0:64, 0:64], BSB,
                CON[0:64, CON_EYE:CON_EYE + 64]), "MM", 1)
            btr = p.ct["MM"]
            p.wait("v", "MM", btr)
            bte = vop(lambda e, S: e.tensor_copy(
                BSTB, pS2.ap().bitcast(BF16)[0:64, 0:64]))
            p.wait("t", "EV", bte)
            mm0(pb[3][0:64, 0:64], BSTB, CON[0:64, CON_L64:CON_L64 + 64],
                True, True)
            lsmm = mm(pb[3][0:64, 64:65], BSTB,
                      CON[0:64, CON_L64 + 64:CON_L64 + 65], True, True)
            p.wait("v", "MM", lsmm)
            vop0(lambda e, S, nn=nn: e.tensor_scalar(
                MISC[0:64, 72:73], pb[3][0:64, 64:65],
                MISC[0:64, MI_REP + nn:MI_REP + nn + 1], None, op0=AL.mult))
            fev = vop(lambda e, S, nn=nn: e.tensor_scalar(
                MISC[0:64, MI_F:MI_F + 64], pb[3][0:64, 0:64],
                MISC[0:64, MI_REP + 4 + nn:MI_REP + 4 + nn + 1],
                MISC[0:64, 72:73], op0=AL.mult, op1=AL.add))
            state["pb23"] = [fev, fev]
            if lastb and attn_stage <= 6:
                trunc[0] = True; break
            o2g = bias_pass(nn, 0, fev)
            for s in range(32):
                if s == 16:
                    if state.get("ep"):
                        state["ep"]()
                        state["ep"] = None
                    o2g = bias_pass(nn, 1, o2g)
                ilo = SW_A * s

                def emit_av(bb):
                    if bb == 0:
                        p.wait("t", "EV", state["pb0free"])
                    for k4 in range(4):
                        kc = 4 * bb + k4
                        fn_ = mm if kc == 63 else mm0
                        fn_(pb[0][0:65, 0:256], AR3[:, 65 * kc:65 * kc + 65],
                            AR1[:, 256 * kc:256 * kc + 256], kc == 0, kc == 63)

                for b_ in range(16):
                    pS = pS1 if b_ % 2 == 0 else pS2
                    p.wait("t", "SE", state["exp"].get(("b", nn, s, b_ - 2), 0))
                    if b_ < 2:
                        p.wait("t", "EV", state["ps1b"][0])
                        p.wait("t", "EV", state["ps1b"][1])
                        p.wait("t", "EV", o2g)
                    for k4 in range(4):
                        kc = 4 * b_ + k4
                        fn_ = mm if k4 == 3 else mm0
                        fn_(pS[:, 256 * k4:256 * k4 + 256],
                            Xb[0:128, KMT + 128 * kc:KMT + 128 * kc + 128],
                            Xb[0:128, QMT + ilo:QMT + ilo + SW_A],
                            True, True)
                    if b_ >= 2:
                        if b_ == 2 and state.get("ep"):
                            state["ep"]()
                            state["ep"] = None
                        emit_av(b_ - 2)
                    mmv = p.ct["MM"]
                    p.wait("s", "MM", mmv)
                    state["exp"][("b", nn, s, b_)] = sop(
                        lambda e, S, b_=b_, pS=pS: e.activation(
                            AR1[:, 1024 * b_:1024 * b_ + 1024], pS[:, 0:1024],
                            AF.Exp, bias=0.0))
                for bb in (14, 15):
                    p.wait("t", "SE", state["exp"][("b", nn, s, bb)])
                    emit_av(bb)
                avm = p.ct["MM"]
                p.wait("v", "MM", avm)
                vop0(lambda e, S: e.tensor_scalar(
                    ROWS[64:65, 1024:1280], pb[0][64:65, 0:256], 1e-30, None,
                    op0=AL.max))
                vop0(lambda e, S: e.reciprocal(ROWS[64:65, 1024:1280],
                                               ROWS[64:65, 1024:1280]))
                rec = vop(lambda e, S: e.tensor_copy(RECROW,
                                                     ROWS[64:65, 1024:1280]))

                def ep(ilo=ilo, s=s, rec=rec):
                    p.wait("t", "EV", rec)
                    mrep = mm(pb[1][:, 0:256],
                              CON[64:65, CON_ONEB:CON_ONEB + 128],
                              RECROW, True, True)
                    p.wait("v", "MM", mrep)
                    vop0(lambda e, S: e.tensor_copy(REPS, pb[1][:, 0:256]))
                    state["pb0free"] = vop(lambda e, S: e.tensor_mul(
                        Xb[0:64, O1C + ilo:O1C + ilo + 256],
                        pb[0][0:64, 0:256], REPS[0:64, :]))
                    state["av"] = vop(lambda e, S: e.tensor_add(
                        Xb[0:64, O1C + ilo:O1C + ilo + 256],
                        Xb[0:64, O1C + ilo:O1C + ilo + 256],
                        AR2[0:64, 128 * ((2 * s) % 32):
                            128 * ((2 * s) % 32) + 256]))
                state["ep"] = ep
            if state.get("ep"):
                state["ep"]()
                state["ep"] = None
            if lastb and attn_stage <= 7:
                trunc[0] = True; break
            dupev = state["av"]
            for i in range(16):
                h2 = i % 2
                p.wait("t", "EV", state["ps1b"][h2])
                p.wait("t", "EV", dupev)
                msd = p.add("t", lambda e, S, i=i, h2=h2: e.matmul(
                    pS1[64:128, 512 * h2:512 * h2 + 512], eyeT,
                    Xb[:, O1C + 512 * i:O1C + 512 * i + 512],
                    start=True, stop=True, tile_position=(0, 64)), "MM", 1)
                p.wait("v", "MM", p.ct["MM"])
                state["ps1b"][h2] = vop(lambda e, S, i=i, h2=h2: e.tensor_copy(
                    Xb[64:128, O1C + 512 * i:O1C + 512 * i + 512],
                    pS1[64:128, 512 * h2:512 * h2 + 512]))
            dup_ev = p.ct["EV"]
            if lastb and attn_stage <= 8:
                trunc[0] = True; break
            p.wait("t", *wl_wog)
            p.wait("t", "EV", dup_ev)
            slot = 0
            yev = 0
            for dc in range(8):
                bk = 2 + (dc % 2)
                h = dc % 2
                p.wait("t", "EV", state["pb23"][h])
                p.wait("t", "EV", state["ps1b"][h])
                for par in range(2):
                    base = 64 * par
                    dst = pb[bk][:, 0:512] if par == 0 else \
                        pS1[:, 512 * h:512 * h + 512]
                    for gi in range(8):
                        g = 2 * gi + par
                        fn_ = mm if (par == 1 and gi == 7) else mm0
                        fn_(dst,
                            WA[base:base + 64,
                               8192 * reg_wog + 1024 * gi + 128 * dc:
                               8192 * reg_wog + 1024 * gi + 128 * dc + 128],
                            Xb[base:base + 64, O1C + g:O1C + g + 8177:16],
                            gi == 0, gi == 7)
                wmm = p.ct["MM"]
                p.wait("s", "MM", wmm)
                se = sop(lambda e, S, h=h: e.activation(
                    AR1f[:, 5120 + 512 * h:5120 + 512 * h + 512],
                    pS1[:, 512 * h:512 * h + 512], AF.Copy))
                p.wait("v", "MM", wmm)
                p.wait("v", "SE", se)
                vop0(lambda e, S, dc=dc, bk=bk, h=h: e.tensor_add(
                    AR1f[:, 2048 + 512 * h:2048 + 512 * h + 512],
                    pb[bk][:, 0:512], AR1f[:, 5120 + 512 * h:5120 + 512 * h + 512]))
                vop0(lambda e, S, dc=dc, h=h: e.tensor_scalar(
                    AR1f[:, 2048 + 512 * h:2048 + 512 * h + 512],
                    AR1f[:, 2048 + 512 * h:2048 + 512 * h + 512],
                    MISC[:, MI_B["bo"] + dc:MI_B["bo"] + dc + 1],
                    None, op0=AL.add))
                yev = vop(lambda e, S, dc=dc, slot=slot, nn=nn, h=h: e.tensor_add(
                    Yf(slot, dc, 0, 512),
                    AR1f[:, 2048 + 512 * h:2048 + 512 * h + 512],
                    XAf(dc, 512 * nn, 512)))
                state["pb23"][h] = yev
                state["ps1b"][h] = yev
            wuse[reg_wog] = p.ct["MM"]
            if debug and nn == 0:
                p.wait("y", "EV", yev)
                y_dma(dbg["dyt"][:, :].bitcast(mybir.dt.uint32),
                      X[:, 12288:14336].bitcast(mybir.dt.uint32), sem="SO")
                y_dma(dbg["do1"][:, :].bitcast(mybir.dt.uint16),
                      Xb[:, O1C:O1C + 8192].bitcast(mybir.dt.uint16), sem="SO")
                y_dma(dbg["do2"][:, :].bitcast(mybir.dt.uint16),
                      AR2[:, 0:4096].bitcast(mybir.dt.uint16), sem="SO")
                p.wait("t", "SO", p.ct["SO"])
            if lastb and attn_stage <= 9:
                trunc[0] = True; break
            ab_ev = ln_rms_block(nn, slot, yev)
            state["ps1b"] = [ab_ev, ab_ev]
            state["pb23"] = [ab_ev, ab_ev]
            state["av"] = ab_ev
            state["pb0free"] = ab_ev

        if trunc[0] or attn_nn < 4:
            return p.ct["EV"], xs_done
        wl1, reg1 = next_wload("w1aT")
        wl2, reg2 = next_wload("w2aT")
        p.wait("t", *wl1)
        p.wait("t", "EV", ab_ev)
        pbf = [ab_ev, ab_ev]
        h1f = [0, 0]
        for st in range(4):
            lo = 512 * st
            for ht in range(8):
                bk = 2 + (ht % 2)
                p.wait("t", "EV", pbf[ht % 2])
                for dt_ in range(NT):
                    fn_ = mm if dt_ == NT - 1 else mm0
                    fn_(pb[bk][:, 0:512], WSl(reg1, dt_, 128 * ht, 128),
                        XAf(dt_, lo, 512), dt_ == 0, dt_ == NT - 1)
                hm = p.ct["MM"]
                p.wait("v", "MM", hm)
                pbf[ht % 2] = vop(lambda e, S, bk=bk, ht=ht: e.tensor_scalar(
                    AR2[:, 512 * ht:512 * ht + 512], pb[bk][:, 0:512],
                    MISC[:, MI_B["b1a"] + ht:MI_B["b1a"] + ht + 1], None,
                    op0=AL.add))
            if st == 0:
                p.wait("t", *wl2)
            p.wait("t", "EV", max(pbf))
            for dc in range(8):
                bk = 2 + (dc % 2)
                p.wait("t", "EV", pbf[dc % 2])
                for hti in range(NT):
                    fn_ = mm if hti == NT - 1 else mm0
                    fn_(pb[bk][:, 0:512], WSl(reg2, hti, 128 * dc, 128),
                        AR2[:, 512 * hti:512 * hti + 512], hti == 0,
                        hti == NT - 1)
                zm = p.ct["MM"]
                p.wait("v", "MM", zm)
                vop0(lambda e, S, bk=bk, dc=dc: e.tensor_scalar(
                    AR1f[:, 2048 + 512 * (dc % 2):2048 + 512 * (dc % 2) + 512],
                    pb[bk][:, 0:512],
                    MISC[:, MI_B["b2a"] + dc:MI_B["b2a"] + dc + 1], None,
                    op0=AL.add))
                pbf[dc % 2] = vop(lambda e, S, bk=bk, dc=dc, lo=lo: e.tensor_add(
                    Xf(dc, lo, 512),
                    AR1f[:, 2048 + 512 * (dc % 2):2048 + 512 * (dc % 2) + 512],
                    XAf(dc, lo, 512)))
        z_ev = max(pbf)
        return z_ev, xs_done


    def ln_rms_block(nn, slot, gate_ev):
        p.wait("t", "EV", gate_ev)
        p.wait("s", "MM", p.ct["MM"])
        sq = {}
        for dt_ in range(NT):
            slt = LNSf[dt_ % 2]
            p.wait("s", "MM", sq.get(dt_ - 2, 0))
            p.wait("s", "EV", gate_ev)
            sev = sop(lambda e, S, dt_=dt_, slt=slt, slot=slot: e.activation(
                AR1f[:, slt:slt + 512], Yf(slot, dt_, 0, 512), AF.Square))
            p.wait("t", "SE", sev)
            for c4 in range(4):
                mm0(pb[c4][:, 4 + nn:5 + nn], Yf(slot, dt_, 128 * c4, 128),
                    MISC[:, MI_OCF:MI_OCF + 1], dt_ == 0, dt_ == NT - 1)
                fn_ = mm0 if c4 < 3 else (lambda *a, **k: mm(*a, **k))
                fn_([pS1, pS1, pS2, pS2][c4][:, [0, 512, 0, 512][c4] + 4 + nn:
                                             [0, 512, 0, 512][c4] + 5 + nn],
                    AR1f[:, slt + 128 * c4:slt + 128 * c4 + 128],
                    MISC[:, MI_OCF:MI_OCF + 1], dt_ == 0, dt_ == NT - 1)
            sq[dt_] = p.ct["MM"]
        p.wait("v", "MM", p.ct["MM"])
        SQR = [pS1[:, 0:512], pS1[:, 512:1024], pS2[:, 0:512], pS2[:, 512:1024]]
        for c4 in range(4):
            vop0(lambda e, S, c4=c4, nn=nn: e.tensor_scalar(
                MISC[:, MI_SM + 4 * nn + c4:MI_SM + 4 * nn + c4 + 1],
                pb[c4][:, 4 + nn:5 + nn], 1.0 / 1024, None, op0=AL.mult))
            vop0(lambda e, S, c4=c4, nn=nn: e.tensor_scalar(
                MISC[:, MI_SV + 4 * nn + c4:MI_SV + 4 * nn + c4 + 1],
                SQR[c4][:, 4 + nn:5 + nn], 1.0 / 1024, None, op0=AL.mult))
        cm, cv = MISC[:, MI_SM + 4 * nn:MI_SM + 4 * nn + 4], \
            MISC[:, MI_SV + 4 * nn:MI_SV + 4 * nn + 4]
        vop0(lambda e, S, cm=cm, nn=nn: e.tensor_mul(
            MISC[:, MI_SS:MI_SS + 4], cm, cm))
        sev2 = vop(lambda e, S, cv=cv: e.tensor_sub(
            cv, cv, MISC[:, MI_SS:MI_SS + 4]))
        p.wait("s", "EV", sev2)
        sop(lambda e, S, cv=cv: e.activation(MISC[:, MI_SS:MI_SS + 4], cv,
                                             AF.Ln, bias=0.0))
        rst = sop(lambda e, S, nn=nn: e.activation(
            MISC[:, MI_SR + 4 * nn:MI_SR + 4 * nn + 4],
            MISC[:, MI_SS:MI_SS + 4], AF.Exp, scale=-0.5))
        p.wait("v", "SE", rst)
        m16 = vop(lambda e, S, nn=nn: e.tensor_copy(
            MISCb[:, MB_M + 4 * nn:MB_M + 4 * nn + 4],
            MISC[:, MI_SM + 4 * nn:MI_SM + 4 * nn + 4]))
        m16 = vop(lambda e, S, nn=nn: e.tensor_copy(
            MISCb[:, MB_R + 4 * nn:MB_R + 4 * nn + 4],
            MISC[:, MI_SR + 4 * nn:MI_SR + 4 * nn + 4]))
        p.wait("t", "EV", m16)
        for c4 in range(4):
            c = 4 * nn + c4
            mm0(pb[0][0:1, 128 * c4:128 * c4 + 128], MISCb[:, MB_M + c:MB_M + c + 1],
                CON[:, CON_EYE:CON_EYE + 128], True, True)
            fn_ = mm0 if c4 < 3 else (lambda *a, **k: mm(*a, **k))
            fn_(pb[1][0:1, 128 * c4:128 * c4 + 128], MISCb[:, MB_R + c:MB_R + c + 1],
                CON[:, CON_EYE:CON_EYE + 128], True, True)
        p.wait("v", "MM")
        vop0(lambda e, S: e.tensor_copy(ROW0(0, 512), pb[0][0:1, 0:512]))
        rowev = vop(lambda e, S: e.tensor_copy(ROW0(512, 512), pb[1][0:1, 0:512]))
        p.wait("t", "EV", rowev)
        mm0(pb[0][:, 0:512], MISC[0:1, MI_ONE:MI_ONE + 128], ROW0(0, 512),
            True, True)
        mrr = mm(pb[1][:, 0:512], MISC[0:1, MI_ONE:MI_ONE + 128], ROW0(512, 512),
                 True, True)
        p.wait("v", "MM", mrr)
        ab_ev = 0
        for dt_ in range(NT):
            vop0(lambda e, S, dt_=dt_, slot=slot: e.tensor_sub(
                AR1f[:, STTf[dt_ % 2]:STTf[dt_ % 2] + 512],
                Yf(slot, dt_, 0, 512), pb[0][:, 0:512]))
            ab_ev = vop(lambda e, S, dt_=dt_, nn=nn: e.tensor_mul(
                XAf(dt_, 512 * nn, 512),
                AR1f[:, STTf[dt_ % 2]:STTf[dt_ % 2] + 512], pb[1][:, 0:512]))
        return ab_ev


    def final_residual(gate_ev):
        mr = emit_ln(0.0, False, gate_ev)
        p.wait("y", "XS", p.ct["XS"])
        p.wait("y", "EV", mr)
        xr_free = {}
        for st in range(4):
            lo = 512 * st
            bk = (st % 2) * 2
            for dt_ in range(NT):
                p.wait("y", "EV", xr_free.get((st - 1, dt_), 0))
                y_dma(AR1f[:, 4096 + 512 * dt_:4096 + 512 * dt_ + 512],
                      xspill[128 * dt_:128 * (dt_ + 1), lo:lo + 512], sem="XR")
            xr = p.ct["XR"]
            p.wait("t", "EV", mr if st == 0 else 0)
            for c4 in range(4):
                c = 4 * st + c4
                mm0(pb[bk][0:1, 128 * c4:128 * c4 + 128],
                    MISCb[:, MB_M + c:MB_M + c + 1],
                    CON[:, CON_EYE:CON_EYE + 128], True, True)
                fn_ = mm0 if c4 < 3 else (lambda *a, **k: mm(*a, **k))
                fn_(pb[bk + 1][0:1, 128 * c4:128 * c4 + 128],
                    MISCb[:, MB_R + c:MB_R + c + 1],
                    CON[:, CON_EYE:CON_EYE + 128], True, True)
            p.wait("v", "MM")
            vop0(lambda e, S, bk=bk: e.tensor_copy(ROW0(0, 512), pb[bk][0:1, 0:512]))
            rowev = vop(lambda e, S, bk=bk: e.tensor_copy(ROW0(512, 512),
                                                          pb[bk + 1][0:1, 0:512]))
            p.wait("t", "EV", rowev)
            mm0(pb[bk][:, 0:512], MISC[0:1, MI_ONE:MI_ONE + 128], ROW0(0, 512),
                True, True)
            mrr = mm(pb[bk + 1][:, 0:512], MISC[0:1, MI_ONE:MI_ONE + 128],
                     ROW0(512, 512), True, True)
            p.wait("v", "MM", mrr)
            p.wait("v", "XR", xr)
            last = 0
            for dt_ in range(NT):
                hm = 512 * dt_
                vop0(lambda e, S, dt_=dt_, lo=lo, bk=bk: e.tensor_sub(
                    AR1f[:, LNSf[dt_ % 2]:LNSf[dt_ % 2] + 512],
                    Xf(dt_, lo, 512), pb[bk][:, 0:512]))
                vop0(lambda e, S, dt_=dt_, bk=bk: e.tensor_mul(
                    AR1f[:, LNSf[dt_ % 2]:LNSf[dt_ % 2] + 512],
                    AR1f[:, LNSf[dt_ % 2]:LNSf[dt_ % 2] + 512],
                    pb[bk + 1][:, 0:512]))
                last = vop(lambda e, S, dt_=dt_, lo=lo, hm=hm: e.tensor_add(
                    Xf(dt_, lo, 512),
                    AR1f[:, LNSf[dt_ % 2]:LNSf[dt_ % 2] + 512],
                    AR1f[:, 4096 + hm:4096 + hm + 512]))
                xr_free[(st, dt_)] = last
        return last

    span_free_g = {"c": 0}
    last_ev = emit_ffn(0, ones_v)

    if 'attn' in parts:
        z_ev, _xs = emit_attn(last_ev)
        if attn_nn == 4 and attn_stage >= 10:
            last_ev = final_residual(z_ev)
        if 'ffn2' in parts and attn_nn == 4 and attn_stage >= 10:
            p.wait("g", "MM", p.ct["MM"])
            wf["kw1"] = wload(0, "kw1c")
            wf["kw2"] = wload(1, "kw2c")
            last_ev = emit_ffn(1, last_ev)

    p.wait("y", "EV", p.ct["EV"])
    p.wait("y", "MM", p.ct["MM"])
    p.wait("y", "SE", p.ct["SE"])
    for s_ in ("GS", "GT0", "GT1", "GF0", "GF1", "WB", "WA0", "WA1", "WLC"):
        p.wait("y", s_, p.ct[s_])
    U32 = mybir.dt.uint32
    for dt_ in range(NT):
        y_dma(yout[128 * dt_:128 * (dt_ + 1), :].bitcast(U32) if debug else
              yout[128 * dt_:128 * (dt_ + 1), :],
              Xf(dt_, 0, 2048).bitcast(U32) if debug else Xf(dt_, 0, 2048),
              sem="SO")
    if debug:
        y_dma(dbg["dxa"][:, :], XA[:, :], sem="SO")
        y_dma(dbg["dtp"][:, :], CON[:, CON_TP:CON_TP + 2048], sem="SO")
        if 'gather' in parts:
            y_dma(dbg["doh"][:, 0:256], OHQ[:, :], sem="SO")
        if 'attn' in parts:
            U16 = mybir.dt.uint16
            y_dma(dbg["dqk"][:, :].bitcast(U16), Xb[:, 0:16384].bitcast(U16), sem="SO")
            y_dma(dbg["dvmo"][:, :].bitcast(U16), AR3[:, :].bitcast(U16), sem="SO")
        if 'flash' in parts:
            y_dma(dbg["dfq"][:, :], AR2[:, 2048:4096], sem="SO")
            U16 = mybir.dt.uint16
            y_dma(dbg["de"][:, :].bitcast(U16), AR1[:, :].bitcast(U16), sem="SO")
            y_dma(dbg["dht"][:, :], AR2[:, 0:4096], sem="SO")
    p.wait("y", "SO", p.ct["SO"])

    with nc.Block() as blk:
        semcms = [nc.semaphore(n) for n in p.semnames]
        S = {}
        for n_, cm in zip(p.semnames, semcms):
            S[n_] = cm.__enter__()

        @blk.tensor
        def _(e):
            for fn in p.ops["t"]:
                fn(e, S)

        @blk.vector
        def _(e):
            for fn in p.ops["v"]:
                fn(e, S)

        @blk.scalar
        def _(e):
            for fn in p.ops["s"]:
                fn(e, S)

        @blk.sync
        def _(e):
            for fn in p.ops["y"]:
                fn(e, S)

        @blk.gpsimd
        def _(e):
            for fn in p.ops["g"]:
                fn(e, S)

    return nc



def _wog(wo):
    out = np.zeros((1024, 1024), np.float32)
    w = np.asarray(wo).reshape(1024, 16, 64)
    for g in range(16):
        out[128 * (g // 2) + 64 * (g % 2):128 * (g // 2) + 64 * (g % 2) + 64, :] \
            = w[:, g, :].T
    return np.ascontiguousarray(out)


def _prep_core_inputs(core, inp):
    b, qd = core // 4, core % 4
    sl = slice(2048 * qd, 2048 * (qd + 1))
    C = _get_consts()

    def f32(a):
        return np.ascontiguousarray(np.asarray(a, dtype=np.float32))

    d = {
        "xt": f32(np.asarray(inp["L0_attn_x"])[b, sl, :].T),
        "wqT": f32(np.asarray(inp["L0_attn_Wq"]).T),
        "wkT": f32(np.asarray(inp["L0_attn_Wk"]).T),
        "wvT": f32(np.asarray(inp["L0_attn_Wv"]).T),
        "woT": f32(np.asarray(inp["L0_attn_Wo"]).T),
        "wogT": _wog(np.asarray(inp["L0_attn_Wo"])),
        "w1aT": f32(np.asarray(inp["L0_attn_W1"]).T),
        "w2aT": f32(np.asarray(inp["L0_attn_W2"]).T),
        "relbT": f32(np.asarray(inp["L0_attn_rel_bias"]).T),
        "mflat": C["mflat"], "eyetri": C["eyetri"], "l64": C["l64"],
        "gmask": C["gmask"],
        "fconst": np.tile(np.array([0.0, 1.0, 1e-5, -0.5, 1.0 / 32.0, -40.0,
                                    0.5, 2.0], np.float32), (128, 1)),
    }
    for nm, key in (("bq4", "L0_attn_bq"), ("bk4", "L0_attn_bk"),
                    ("bv4", "L0_attn_bv"), ("bo4", "L0_attn_bo"),
                    ("b1a4", "L0_attn_b1"), ("b2a4", "L0_attn_b2")):
        d[nm] = f32(np.asarray(inp[key]).reshape(8, 128).T)
    for nm in ("bq4", "bk4", "bv4"):
        sh = np.zeros((128, 8), np.float32)
        sh[0:64, :] = d[nm][64:128, :]
        d[nm + "s"] = sh
    for pfx in ("q", "k", "v"):
        d[f"{pfx}nwT"] = f32(np.asarray(inp[f"L0_ffn_{pfx}_node_w"]).T)
        d[f"{pfx}nb"] = f32(np.asarray(inp[f"L0_ffn_{pfx}_node_b"]).reshape(1, 15))
        d[f"{pfx}w1c"] = f32(np.asarray(inp[f"L0_ffn_{pfx}_w1"]).transpose(1, 0, 2)
                             .reshape(1024, 1024))
        d[f"{pfx}b1f4"] = f32(np.asarray(inp[f"L0_ffn_{pfx}_b1"]).reshape(1024)
                              .reshape(8, 128).T)
        d[f"{pfx}w2c"] = f32(np.asarray(inp[f"L0_ffn_{pfx}_w2"]).reshape(1024, 1024))
        d[f"{pfx}b2"] = f32(np.asarray(inp[f"L0_ffn_{pfx}_b2"]))
    hs = np.zeros((16, 4), np.float32)
    for ln in range(4):
        hs[4 * qd + ln, ln] = 1.0
    d["hsel"] = hs
    return d


def kernel(**inputs):
    nc = build_nc()
    in_maps = [_prep_core_inputs(c, inputs) for c in range(NCORE)]
    res = run_bass_kernel_spmd(nc, in_maps, core_ids=list(range(NCORE)))
    out = np.zeros((2, 8192, 1024), np.float32)
    for c in range(NCORE):
        b, qd = c // 4, c % 4
        out[b, 2048 * qd:2048 * (qd + 1), :] = res.results[c]["y"].T
    return out
